# version 15
# baseline (speedup 1.0000x reference)
import numpy as np
import ml_dtypes
import concourse.bass as bass
import concourse.mybir as mybir
from concourse.bass_utils import run_bass_kernel_spmd

F32 = mybir.dt.float32
BF16 = mybir.dt.bfloat16
AF = mybir.ActivationFunctionType
ALU = mybir.AluOpType
NBF = ml_dtypes.bfloat16

T, D = 4096, 1024
N1, N2 = 128, 32
K2F, K2I = 17, 16
NCORES = 8

def _fft_consts():
    n1 = np.arange(N1); k1 = np.arange(N1)
    F128 = np.exp(-2j * np.pi * np.outer(n1, k1) / N1)
    n2 = np.arange(N2); k2 = np.arange(K2F)
    W32 = np.exp(-2j * np.pi * np.outer(n2, k2) / N2)
    tw = np.exp(-2j * np.pi * np.outer(n2, np.arange(N1)) / T)
    M2 = tw.T[:, :, None] * W32[None, :, :]
    t5 = np.arange(N2)
    EI = np.exp(2j * np.pi * np.outer(np.arange(K2I), t5) / N2)
    m = np.arange(N1)
    Finv = np.exp(2j * np.pi * np.outer(np.arange(N1), m) / N1)
    twi = np.exp(2j * np.pi * np.outer(t5, np.arange(N1)) / T)
    G2 = twi[:, :, None] * Finv[None, :, :]
    return F128, M2, EI, G2

_F128, _M2, _EI, _G2 = _fft_consts()

def _pack_s2_fwd(M2, scale=1.0):
    out = np.zeros((N1, 128, 128), np.float32)
    M2re = (M2.real * scale).astype(np.float32)
    M2im = (M2.imag * scale).astype(np.float32)
    for n2 in range(N2):
        for ci in range(2):
            for s2 in range(2):
                p = n2 * 4 + ci * 2 + s2
                for k2 in range(K2F):
                    out[:, p, 0 * 64 + s2 * 32 + k2] = M2re[:, n2, k2] if ci == 0 else -M2im[:, n2, k2]
                    out[:, p, 1 * 64 + s2 * 32 + k2] = M2im[:, n2, k2] if ci == 0 else M2re[:, n2, k2]
    return out.astype(NBF)

def _pack_s1_inv(EI):
    out = np.zeros((2, 128, 128), np.float32)
    EIre = EI.real.astype(np.float32); EIim = EI.imag.astype(np.float32)
    for k2 in range(K2I):
        for ci in range(2):
            for s4 in range(4):
                p = ci * 64 + k2 * 4 + s4
                for t5 in range(N2):
                    i = s4 * 32 + t5
                    out[0, p, i] = EIre[k2, t5] if ci == 0 else -EIim[k2, t5]
                    out[1, p, i] = EIim[k2, t5] if ci == 0 else EIre[k2, t5]
    return out.astype(NBF)

def _bfly():
    b1 = np.zeros((128, 128), np.float32)
    b2 = np.zeros((128, 128), np.float32)
    for i in range(64):
        b1[i, i] = 1.0
        b1[i + 64, i] = -1.0
        b2[i, i + 64] = 1.0
        b2[i + 64, i + 64] = 1.0
    bh = np.zeros((128, 128), np.float32)
    bh2 = np.zeros((128, 128), np.float32)
    for s in range(4):
        bh[s, s] = 1.0
        bh[s + 4, s] = -1.0
        bh2[s, s + 4] = 1.0
        bh2[s + 4, s + 4] = 1.0
    return b1.astype(NBF), b2.astype(NBF), bh.astype(NBF), bh2.astype(NBF)

_B1, _B2, _BH1, _BH2 = _bfly()

CONST = {
    "BFLY1": _B1, "BFLY2": _B2, "BFLYH1": _BH1, "BFLYH2": _BH2,
    "F128RE": _F128.real.astype(np.float32).astype(NBF),
    "F128IM": _F128.imag.astype(np.float32).astype(NBF),
    "S2FWD": _pack_s2_fwd(_M2, 1.0),
    "S2FILT": _pack_s2_fwd(_M2, 2.0 / T),
    "S1INV": _pack_s1_inv(_EI),
    "G2RE": _G2.real.astype(np.float32).astype(NBF),
    "G2NIM": (-_G2.imag).astype(np.float32).astype(NBF),
    "IDENT": np.eye(128, dtype=NBF),
    "SIGNONE": np.stack([np.ones(128, np.float32),
                         -np.ones(128, np.float32)]).astype(NBF),
}

class Prog:
    def __init__(self, name):
        self.nc = bass.Bass("TRN2", target_bir_lowering=False, debug=False)
        self.name = name
        self.ins = {}
        self.outs = {}

    def inp(self, name, shape, dtype=F32):
        ap = self.nc.dram_tensor(name, list(shape), dtype, kind="ExternalInput").ap()
        self.ins[name] = ap
        return ap

    def out(self, name, shape, dtype=F32):
        ap = self.nc.dram_tensor(name, list(shape), dtype, kind="ExternalOutput").ap()
        self.outs[name] = ap
        return ap

    def sb(self, name, shape, dtype=F32):
        return self.nc.alloc_sbuf_tensor(name, list(shape), dtype).ap()

    def ps(self, name, shape, dtype=F32):
        return self.nc.alloc_psum_tensor(name, list(shape), dtype).ap()

    def sem(self, name):
        return self.nc.alloc_semaphore(name=name)


def run_launch(prog, per_core_maps, trace=False):
    maps = [{k: per_core_maps[c][k] for k in prog.ins} for c in range(len(per_core_maps))]
    return run_bass_kernel_spmd(prog.nc, maps, core_ids=list(range(len(per_core_maps))), trace=trace)


def build_ln(name):
    p = Prog(name)
    x = p.inp("x", [T, D])
    ident = p.inp("IDENT", [128, 128], BF16)
    xnT = p.out("xnT", [D, T], BF16)

    xt = p.sb("xt", [128, D], F32)
    xnb = p.sb("xnb", [128, D], BF16)
    s1 = p.sb("s1", [128, 1], F32)
    mneg = p.sb("mneg", [128, 1], F32)
    msq = p.sb("msq", [128, 1], F32)
    ssq = p.sb("ssq", [128, 1], F32)
    var = p.sb("var", [128, 1], F32)
    rstd = p.sb("rstd", [128, 1], F32)
    dummy = p.sb("scrd", [128, D], F32)
    idt = p.sb("idt", [128, 128], BF16)
    trs = p.sb("trs", [128, 128], BF16)
    pst = p.ps("pst", [128, 128], BF16)

    eps = p.sb("eps", [128, 1], F32)
    ld = p.sem("ld"); sc = p.sem("sc"); ve = p.sem("ve")
    pe = p.sem("pe"); ev = p.sem("ev"); st = p.sem("st"); ini = p.sem("ini")
    NT = T // 128; ND = D // 128

    block_cm = p.nc.Block(); block = block_cm.__enter__()

    @block.sync
    def _(sync):
        sync.dma_start(out=idt[:], in_=ident[:]).then_inc(ld, 16)
        for i in range(NT):
            if i >= 1:
                sync.wait_ge(pe, ND * i)
            sync.dma_start(out=xt[:], in_=x[i * 128:(i + 1) * 128, :]).then_inc(ld, 16)
            for j in range(ND):
                sync.wait_ge(ev, i * ND + j + 1)
                sync.dma_start(out=xnT[j * 128:(j + 1) * 128, i * 128:(i + 1) * 128],
                               in_=trs[:]).then_inc(st, 16)
        sync.wait_ge(st, 16 * ND * NT)

    @block.gpsimd
    def _(gpsimd):
        gpsimd.memset(eps[:], 1e-5).then_inc(ini, 1)

    @block.scalar
    def _(scalar):
        scalar.wait_ge(ini, 1)
        for i in range(NT):
            scalar.wait_ge(ld, 16 * (i + 2))
            if i >= 1:
                scalar.wait_ge(ve, 2 * i)
            scalar.activation(out=dummy[:], in_=xt[:], func=AF.Copy, accum_out=s1[:])
            scalar.mul(out=mneg[:], in_=s1[:], mul=-1.0 / D)
            scalar.activation(out=msq[:], in_=mneg[:], func=AF.Square)
            scalar.activation(out=dummy[:], in_=xt[:], func=AF.Square, accum_out=ssq[:]).then_inc(sc, 1)
            scalar.wait_ge(ve, 2 * i + 1)
            scalar.activation(out=rstd[:], in_=var[:], func=AF.Ln, bias=eps[:])
            scalar.activation(out=rstd[:], in_=rstd[:], func=AF.Exp, scale=-0.5).then_inc(sc, 1)

    @block.vector
    def _(vector):
        for i in range(NT):
            vector.wait_ge(sc, 2 * i + 1)
            vector.tensor_scalar(var[:], ssq[:], 1.0 / D, msq[:],
                                 ALU.mult, ALU.subtract).then_inc(ve, 1)
            vector.wait_ge(sc, 2 * i + 2)
            if i >= 1:
                vector.wait_ge(pe, ND * i)
            vector.tensor_scalar(xnb[:], xt[:], mneg[:], rstd[:], ALU.add, ALU.mult).then_inc(ve, 1)
            for j in range(ND):
                vector.wait_ge(pe, i * ND + j + 1)
                if i * ND + j >= 1:
                    vector.wait_ge(st, 16 * (i * ND + j))
                vector.tensor_copy(trs[:], pst[:]).then_inc(ev, 1)

    @block.tensor
    def _(tensor):
        for i in range(NT):
            tensor.wait_ge(ve, 2 * i + 2)
            for j in range(ND):
                if i * ND + j >= 1:
                    tensor.wait_ge(ev, i * ND + j)
                tensor.transpose(pst[:], xnb[:, j * 128:(j + 1) * 128], idt[:]).then_inc(pe, 1)

    block_cm.__exit__(None, None, None)
    return p


def build_l0proj():
    p = Prog("l0proj")
    xnT = p.inp("xnT", [D, T], BF16)
    pw = p.inp("proj_w", [D, 3072])
    pb = p.inp("proj_b", [3072])
    ptok = p.out("p_tok", [T, 3072], BF16)

    KC = D // 128
    NJ = 3072 // 512
    xsb = [p.sb(f"x{k}", [128, T], BF16) for k in range(KC)]
    wsb = [p.sb(f"w{k}", [128, 3072], BF16) for k in range(KC)]
    brep = p.sb("brep", [128, 3072], BF16)
    stg = p.sb("stg", [128, 3072], BF16)
    pss = [p.ps(f"ps{j}", [128, 512], F32) for j in range(NJ)]

    ld = p.sem("ld"); pe = p.sem("pe"); ev = p.sem("ev"); st = p.sem("st")
    NT = T // 128

    block_cm = p.nc.Block(); block = block_cm.__enter__()

    @block.gpsimd
    def _(g):
        for k in range(KC):
            g.dma_start(out=wsb[k][:], in_=pw[k * 128:(k + 1) * 128, :]).then_inc(ld, 16)
        g.dma_start(out=brep[:], in_=pb[None, :].broadcast_to([128, 3072])).then_inc(ld, 16)

    @block.sync
    def _(sync):
        for k in range(KC):
            sync.dma_start(out=xsb[k][:], in_=xnT[k * 128:(k + 1) * 128, :]).then_inc(ld, 16)
        for i in range(NT):
            sync.wait_ge(ev, i + 1)
            sync.dma_start(out=ptok[i * 128:(i + 1) * 128, :], in_=stg[:]).then_inc(st, 16)
        sync.wait_ge(st, 16 * NT)

    @block.tensor
    def _(tensor):
        tensor.wait_ge(ld, 16 * (2 * KC + 1))
        for i in range(NT):
            if i >= 1:
                tensor.wait_ge(ev, i)
            for k in range(KC):
                for j in range(NJ):
                    tensor.matmul(pss[j][:], xsb[k][:, i * 128:(i + 1) * 128],
                                  wsb[k][:, j * 512:(j + 1) * 512],
                                  start=(k == 0), stop=(k == KC - 1)).then_inc(pe, 1)

    @block.vector
    def _(vector):
        for i in range(NT):
            vector.wait_ge(pe, NJ * KC * (i + 1))
            if i >= 1:
                vector.wait_ge(st, 16 * i)
            for j in range(NJ):
                vector.tensor_tensor(stg[:, j * 512:(j + 1) * 512], pss[j][:],
                                     brep[:, j * 512:(j + 1) * 512],
                                     ALU.add).then_inc(ev, 1 if j == NJ - 1 else 0)

    block_cm.__exit__(None, None, None)
    return p


class Sched:
    def __init__(self, p):
        self.p = p
        self.ops = {"sync": [], "gpsimd": [], "scalar": [], "vector": [], "tensor": []}
        self.cnt = {}

    def sem(self, name):
        s = self.p.sem(name)
        self.cnt[name] = 0
        return s

    def emit(self, eng, fn):
        self.ops[eng].append(fn)

    def bump(self, name, by=1):
        self.cnt[name] += by
        return self.cnt[name]

    def install(self, block):
        for eng_name in ["sync", "gpsimd", "scalar", "vector", "tensor"]:
            fns = self.ops[eng_name]
            def mk(fns):
                def run(eng):
                    for f in fns:
                        f(eng)
                return run
            getattr(block, eng_name)(mk(fns))


def build_hyena(stage, phases=5):
    p = Prog(f"hyena{stage}")
    vw = 3072 if stage == 0 else D
    vsrc = p.inp("vsrc", [T, vw], BF16)
    ptok = p.inp("p_tok", [T, 3072], BF16)
    filt = p.inp("filt", [2048, D])
    f128re = p.inp("F128RE", [128, 128], BF16)
    f128im = p.inp("F128IM", [128, 128], BF16)
    s2fwd = p.inp("S2FWD", [128, 128, 128], BF16)
    s2filt = p.inp("S2FILT", [128, 128, 128], BF16)
    s1inv = p.inp("S1INV", [2, 128, 128], BF16)
    g2re = p.inp("G2RE", [32, 128, 128], BF16)
    g2nim = p.inp("G2NIM", [32, 128, 128], BF16)
    signone = p.inp("SIGNONE", [2, 128], BF16)
    vout = p.out("vout", [T, D], BF16)

    fw1 = p.nc.dram_tensor("fw1", [128, N2, 2, D], BF16, kind="Internal").ap()
    Fsp = p.nc.dram_tensor("Fsp", [128, K2F, 2, D], BF16, kind="Internal").ap()
    Zsp = p.nc.dram_tensor("Zsp", [128, K2F, 2, D], BF16, kind="Internal").ap()
    Csp = p.nc.dram_tensor("Csp", [128, K2F, 2, D], BF16, kind="Internal").ap()
    Bsp = p.nc.dram_tensor("Bsp", [N2, 2, 4, 128, 256], BF16, kind="Internal").ap()

    la = p.sb("la", [128, N2 * D], BF16)
    cre = p.sb("cre", [128, 128], BF16)
    cim = p.sb("cim", [128, 128], BF16)
    s2sb = p.sb("s2sb", [128, 128 * 32], BF16)
    stg1 = p.sb("stg1", [128, 2 * D], BF16)
    mid = p.sb("mid", [128, 32 * 512], BF16)
    stg2 = p.sb("stg2", [128, 512], BF16)
    DQ = 256
    zt = la[:, 0:K2F * 2 * DQ]
    ft = la[:, K2F * 2 * DQ:2 * K2F * 2 * DQ]
    ct = la[:, 2 * K2F * 2 * DQ:3 * K2F * 2 * DQ]
    s1sb = p.sb("s1sb", [128, 256], BF16)
    cti = la[:, 0:32 * 256]
    stg3 = p.sb("stg3", [128, 512], BF16)
    g2t = p.sb("g2t", [128, 256], BF16)
    bre = p.sb("bre", [128, D], BF16)
    bim = p.sb("bim", [128, D], BF16)
    c2k = p.sb("c2k", [1, D], BF16)
    c2kp = p.sb("c2kp", [1, D], BF16)
    c2km = p.sb("c2km", [1, D], BF16)
    ones1 = p.sb("ones1", [1, 128], BF16)
    pg = p.sb("pg", [128, D], BF16)
    ot = p.sb("ot", [128, D], BF16)
    ps = [p.ps(f"ps{i}", [128, 512], F32) for i in range(6)]

    S = Sched(p)
    ld = S.sem("ld"); pe = S.sem("pe"); ev = S.sem("ev"); st = S.sem("st"); vv = S.sem("vv")
    C = S.cnt

    def f(s):
        s.dma_start(out=cre[:], in_=f128re[:]).then_inc(ld, 16)
        s.dma_start(out=cim[:], in_=f128im[:]).then_inc(ld, 16)
        s.dma_start(out=s1sb[:, 0:128], in_=s1inv[0]).then_inc(ld, 16)
        s.dma_start(out=s1sb[:, 128:256], in_=s1inv[1]).then_inc(ld, 16)
        s.dma_start(out=ones1[:], in_=signone[0:1, :]).then_inc(ld, 16)
    S.emit("sync", f)
    S.bump("ld", 80)
    def f(g):
        g.dma_start(out=la[0:64, :], in_=filt.rearrange("(a b) d -> a (b d)", b=N2)).then_inc(ld, 16)
    S.emit("gpsimd", f)
    S.bump("ld", 16)

    def fwd(src_is_filter):
        kh = 64 if src_is_filter else 128
        outsp = Fsp if src_is_filter else Zsp
        s2c = s2filt if src_is_filter else s2fwd
        if not src_is_filter:
            pe_need = C["pe"]
            vla = vsrc[:, 0:D].rearrange("(a b) d -> a b d", b=N2) if stage == 0 \
                else vsrc.rearrange("(a b) d -> a (b d)", b=N2)
            def f(s, pe_need=pe_need, vla=vla):
                s.wait_ge(pe, pe_need)
                s.dma_start(out=la[:], in_=vla).then_inc(ld, 16)
            S.emit("sync", f)
            S.bump("ld", 16)
        ld_need = C["ld"]
        st_base = C["st"]; ev_base = C["ev"]
        for n2 in range(N2):
            ev_need = ev_base + 4 * n2
            def fmm(t, n2=n2, ld_need=ld_need, ev_need=ev_need, kh=kh):
                t.wait_ge(ld, ld_need)
                if ev_need > 0:
                    t.wait_ge(ev, ev_need)
                for dh in range(2):
                    t.matmul(ps[0 + dh][:], cre[0:kh, :],
                             la[0:kh, n2 * D + dh * 512: n2 * D + dh * 512 + 512],
                             start=True, stop=True).then_inc(pe, 1)
                    t.matmul(ps[2 + dh][:], cim[0:kh, :],
                             la[0:kh, n2 * D + dh * 512: n2 * D + dh * 512 + 512],
                             start=True, stop=True).then_inc(pe, 1)
            S.emit("tensor", fmm)
            pe_now = S.bump("pe", 4)
            st_need = st_base + 16 * n2
            def fev(sc, pe_now=pe_now, st_need=st_need):
                sc.wait_ge(pe, pe_now)
                if st_need > 0:
                    sc.wait_ge(st, st_need)
                for dh in range(2):
                    sc.copy(out=stg1[:, 0 * D + dh * 512: 0 * D + dh * 512 + 512], in_=ps[0 + dh][:])
                c1 = sc.copy(out=stg1[:, 1 * D + 0 * 512: 1 * D + 0 * 512 + 512], in_=ps[2][:])
                c2 = sc.copy(out=stg1[:, 1 * D + 1 * 512: 1 * D + 1 * 512 + 512], in_=ps[3][:])
                c2.then_inc(ev, 4)
            S.emit("scalar", fev)
            ev_now = S.bump("ev", 4)
            def fst(s, n2=n2, ev_now=ev_now):
                s.wait_ge(ev, ev_now)
                s.dma_start(out=fw1[:, n2, :, :].rearrange("k c d -> k (c d)"),
                            in_=stg1[:]).then_inc(st, 16)
            S.emit("sync", fst)
            S.bump("st", 16)
        s1_done_st = C["st"]
        for kc in range(4):
            ld_mid = C["ld"]
            def fldc(s, kc=kc, s1_done_st=s1_done_st, s2c=s2c):
                s.wait_ge(st, s1_done_st)
                s.dma_start(out=s2sb[:],
                            in_=s2c[kc * 32:(kc + 1) * 32, :, :]
                            .rearrange("k p i -> p k i")).then_inc(ld, 16)
                s.dma_start(out=mid[:],
                            in_=fw1.rearrange("k n c (s e) -> (n c s) k e", s=2)
                            [:, kc * 32:(kc + 1) * 32, :]).then_inc(ld, 16)
            S.emit("sync", fldc)
            ldc_now = S.bump("ld", 32)
            for k1i in range(32):
                k1 = kc * 32 + k1i
                ev_need = C["ev"]
                def fmm(t, k1i=k1i, ldc_now=ldc_now, ev_need=ev_need):
                    t.wait_ge(ld, ldc_now)
                    t.wait_ge(ev, ev_need)
                    t.matmul(ps[4][:], s2sb[:, k1i * 128:(k1i + 1) * 128],
                             mid[:, k1i * 512:(k1i + 1) * 512],
                             start=True, stop=True).then_inc(pe, 1)
                S.emit("tensor", fmm)
                pe_now = S.bump("pe", 1)
                st_need = C["st"]
                def fev(sc, pe_now=pe_now, st_need=st_need):
                    sc.wait_ge(pe, pe_now)
                    sc.wait_ge(st, st_need)
                    sc.copy(out=stg2[:], in_=ps[4][:]).then_inc(ev, 1)
                S.emit("scalar", fev)
                ev_now = S.bump("ev", 1)
                def fst(s, k1=k1, ev_now=ev_now, outsp=outsp):
                    s.wait_ge(ev, ev_now)
                    s.dma_start(out=outsp[k1, :, 0, :].rearrange("k (s e) -> k s e", s=2),
                                in_=stg2[0:34, :]).then_inc(st, 16)
                    s.dma_start(out=outsp[k1, :, 1, :].rearrange("k (s e) -> k s e", s=2),
                                in_=stg2[64:98, :]).then_inc(st, 16)
                S.emit("sync", fst)
                S.bump("st", 32)
            pe_need2 = C["pe"]
            def fbar(s, pe_need2=pe_need2):
                s.wait_ge(pe, pe_need2)
            S.emit("sync", fbar)
        return outsp

    fwd(True)
    if phases >= 2:
        fwd(False)
    if phases < 3:
        final_st = C["st"]
        def f(s, final_st=final_st):
            s.wait_ge(st, final_st)
        S.emit("sync", f)
        block_cm = p.nc.Block(); block = block_cm.__enter__()
        S.install(block)
        block_cm.__exit__(None, None, None)
        return p

    pw_st_need = C["st"]
    for dh in range(4):
        ld_now = C["ld"]
        vv_prev = C["vv"]
        def fld(s, dh=dh, pw_st_need=pw_st_need, vv_prev=vv_prev):
            s.wait_ge(st, pw_st_need)
            if dh >= 1:
                s.wait_ge(vv, vv_prev)
            s.dma_start(out=zt[:], in_=Zsp[:, :, :, dh * 256:(dh + 1) * 256]).then_inc(ld, 16)
            s.dma_start(out=ft[:], in_=Fsp[:, :, :, dh * 256:(dh + 1) * 256]).then_inc(ld, 16)
        S.emit("sync", fld)
        ld_now = S.bump("ld", 32)
        z3 = zt.rearrange("k (a c e) -> k a c e", a=K2F, c=2)
        f3 = ft.rearrange("k (a c e) -> k a c e", a=K2F, c=2)
        c3 = ct.rearrange("k (a c e) -> k a c e", a=K2F, c=2)
        vv_base = C["vv"]
        def fpw(v, ld_now=ld_now, z3=z3, f3=f3, c3=c3, dh=dh, vv_base=vv_base):
            v.wait_ge(ld, ld_now)
            v.tensor_tensor(c3[:, :, 0, :], z3[:, :, 0, :], f3[:, :, 0, :], ALU.mult).then_inc(vv, 1)
            v.tensor_tensor(c3[:, :, 1, :], z3[:, :, 1, :], f3[:, :, 1, :], ALU.mult).then_inc(vv, 1)
            v.wait_ge(vv, vv_base + 2)
            v.tensor_tensor(c3[:, :, 0, :], c3[:, :, 0, :], c3[:, :, 1, :], ALU.subtract).then_inc(vv, 1)
            v.tensor_tensor(z3[:, :, 0, :], z3[:, :, 0, :], f3[:, :, 1, :], ALU.mult).then_inc(vv, 1)
            v.tensor_tensor(z3[:, :, 1, :], z3[:, :, 1, :], f3[:, :, 0, :], ALU.mult).then_inc(vv, 1)
            v.wait_ge(vv, vv_base + 5)
            v.tensor_tensor(c3[:, :, 1, :], z3[:, :, 0, :], z3[:, :, 1, :], ALU.add).then_inc(vv, 1)
            v.wait_ge(vv, vv_base + 6)
            v.tensor_scalar(c3[0:1, 0, :, :].rearrange("k c e -> k (c e)"),
                            c3[0:1, 0, :, :].rearrange("k c e -> k (c e)"),
                            0.5, None, ALU.mult).then_inc(vv, 1)
        S.emit("vector", fpw)
        vv_now = S.bump("vv", 7)
        def fstc(s, dh=dh, vv_now=vv_now):
            s.wait_ge(vv, vv_now)
            s.dma_start(out=Csp[:, :, :, dh * 256:(dh + 1) * 256], in_=ct[:]).then_inc(st, 16)
        S.emit("sync", fstc)
        S.bump("st", 16)
    pw_done_st = C["st"]
    if phases < 4:
        def f(s, pw_done_st=pw_done_st):
            s.wait_ge(st, pw_done_st)
        S.emit("sync", f)
        block_cm = p.nc.Block(); block = block_cm.__enter__()
        S.install(block)
        block_cm.__exit__(None, None, None)
        return p

    def f(s, pw_done_st=pw_done_st):
        s.wait_ge(st, pw_done_st)
        s.dma_start(out=c2k[:], in_=Csp[0:1, 16, 0, :]).then_inc(ld, 16)
    S.emit("sync", f)
    c2k_ld = S.bump("ld", 16)
    def f(sc, c2k_ld=c2k_ld):
        sc.wait_ge(ld, c2k_ld)
        sc.mul(out=c2kp[:], in_=c2k[:], mul=0.5)
        sc.mul(out=c2km[:], in_=c2k[:], mul=-0.5).then_inc(ev, 1)
    S.emit("scalar", f)
    c2k_ev = S.bump("ev", 1)

    for kc in range(4):
        pe_guard = C["pe"]
        ld_now = C["ld"]
        def fld(s, kc=kc, pw_done_st=pw_done_st, pe_guard=pe_guard):
            s.wait_ge(st, pw_done_st)
            s.wait_ge(pe, pe_guard)
            s.dma_start(out=cti[:],
                        in_=Csp.rearrange("k a c (s e) -> (a c s) k e", s=4)
                        [0:128, kc * 32:(kc + 1) * 32, :]).then_inc(ld, 16)
        S.emit("sync", fld)
        ld_now = S.bump("ld", 16)
        for co in range(2):
            for js in range(16):
                ev_need = C["ev"]
                def fmm(t, co=co, js=js, ld_now=ld_now, ev_need=ev_need):
                    t.wait_ge(ld, ld_now)
                    t.wait_ge(ev, ev_need)
                    t.matmul(ps[5][:], s1sb[:, co * 128:(co + 1) * 128],
                             cti[:, js * 512:(js + 1) * 512],
                             start=True, stop=True).then_inc(pe, 1)
                S.emit("tensor", fmm)
                pe_now = S.bump("pe", 1)
                st_need = C["st"]
                def fev(sc, pe_now=pe_now, st_need=st_need):
                    sc.wait_ge(pe, pe_now)
                    sc.wait_ge(st, st_need)
                    sc.copy(out=stg3[:], in_=ps[5][:]).then_inc(ev, 1)
                S.emit("scalar", fev)
                ev_now = S.bump("ev", 1)
                k1b = kc * 32 + js * 2
                def fst(s, co=co, k1b=k1b, ev_now=ev_now):
                    s.wait_ge(ev, ev_now)
                    s.dma_start(out=Bsp[:, co, :, k1b:k1b + 2, :]
                                .rearrange("a s k e -> a s (k e)"),
                                in_=stg3[:]).then_inc(st, 16)
                S.emit("sync", fst)
                S.bump("st", 16)
    inv1_done_st = C["st"]
    if phases < 5:
        def f(s, inv1_done_st=inv1_done_st):
            s.wait_ge(st, inv1_done_st)
            s.wait_ge(ev, C_ev_final)
        C_ev_final = C["ev"]
        S.emit("sync", f)
        block_cm = p.nc.Block(); block = block_cm.__enter__()
        S.install(block)
        block_cm.__exit__(None, None, None)
        return p

    for t5 in range(N2):
        ld_now = C["ld"]
        pe_guard = C["pe"]; vv_guard = C["vv"]
        def fld(s, t5=t5, inv1_done_st=inv1_done_st, pe_guard=pe_guard, vv_guard=vv_guard):
            s.wait_ge(st, inv1_done_st)
            s.wait_ge(pe, pe_guard)
            s.wait_ge(vv, vv_guard)
            s.dma_start(out=g2t[:, 0:128], in_=g2re[t5]).then_inc(ld, 16)
            s.dma_start(out=g2t[:, 128:256], in_=g2nim[t5]).then_inc(ld, 16)
            s.dma_start(out=bre[:], in_=Bsp[t5, 0].rearrange("s k e -> k s e")).then_inc(ld, 16)
            s.dma_start(out=bim[:], in_=Bsp[t5, 1].rearrange("s k e -> k s e")).then_inc(ld, 16)
            s.dma_start(out=pg[:], in_=ptok.rearrange("(m b) e -> m (b e)", b=N2)
                        [:, t5 * 3072 + (stage + 1) * D: t5 * 3072 + (stage + 2) * D]).then_inc(ld, 16)
        S.emit("sync", fld)
        ld_now = S.bump("ld", 80)
        for dh in range(2):
            vv_need = C["vv"] - (1 + dh) if t5 > 0 or dh > 0 else 0
            vv_need = max(0, C["vv"] - 2 + dh) if (t5 > 0) else (C["vv"] if dh == 1 else 0)
            vv_need = C["vv"]
            def fmm(t, dh=dh, t5=t5, ld_now=ld_now, vv_need=vv_need, c2k_ev=c2k_ev):
                t.wait_ge(ld, ld_now)
                t.wait_ge(ev, c2k_ev)
                t.wait_ge(vv, vv_need)
                t.matmul(ps[4 + dh][:], g2t[:, 0:128], bre[:, dh * 512:(dh + 1) * 512],
                         start=True, stop=False)
                t.matmul(ps[4 + dh][:], g2t[:, 128:256], bim[:, dh * 512:(dh + 1) * 512],
                         start=False, stop=False)
                t.matmul(ps[4 + dh][:], ones1[:],
                         (c2kp if t5 % 2 == 0 else c2km)[:, dh * 512:(dh + 1) * 512],
                         start=False, stop=True).then_inc(pe, 1)
            S.emit("tensor", fmm)
            pe_now = S.bump("pe", 1)
            def fgate(v, dh=dh, pe_now=pe_now):
                v.wait_ge(pe, pe_now)
                v.tensor_tensor(ot[:, dh * 512:(dh + 1) * 512], ps[4 + dh][:],
                                pg[:, dh * 512:(dh + 1) * 512], ALU.mult).then_inc(vv, 1)
            S.emit("vector", fgate)
            S.bump("vv", 1)
        vv_now = C["vv"]
        def fst(s, t5=t5, vv_now=vv_now):
            s.wait_ge(vv, vv_now)
            s.dma_start(out=vout.rearrange("(m b) d -> m (b d)", b=N2)
                        [:, t5 * D:(t5 + 1) * D], in_=ot[:]).then_inc(st, 16)
        S.emit("sync", fst)
        S.bump("st", 16)
    final_st = C["st"]
    def f(s, final_st=final_st):
        s.wait_ge(st, final_st)
    S.emit("sync", f)

    block_cm = p.nc.Block(); block = block_cm.__enter__()
    S.install(block)
    block_cm.__exit__(None, None, None)
    return p


def build_l0ffn(dbg=False):
    p = Prog("l0ffn")
    xnT = p.inp("xnT", [D, T], BF16)
    Wr = p.inp("W_r", [D, D]); Wk = p.inp("W_k", [D, D]); Wv = p.inp("W_v", [D, D])
    mur = p.inp("mu_r", [D]); muk = p.inp("mu_k", [D])
    ident = p.inp("IDENT", [128, 128], BF16)
    gtok = p.out("g_tok", [T, D], BF16)
    if dbg:
        dwt = p.out("dwt", [D, D], BF16)
        dxr = p.out("dxr", [D, 1024], BF16)
        drt = p.out("drt", [D, 1024], BF16)
        dka = p.out("dka", [D, 1024], BF16)

    KC = 8
    wload = p.sb("wload", [128, D], BF16)
    wT = {n: p.sb(f"wT{n}", [128, KC * D], BF16) for n in ["r", "k", "v"]}
    mu = p.sb("mu", [128, 16], F32)
    idt = p.sb("idt", [128, 128], BF16)
    xnc = p.sb("xnc", [128, KC * 1025], BF16)
    tmp = p.sb("tmpv", [128, 1024], F32)
    xr = p.sb("xr", [128, KC * 1024], BF16)
    xk = p.sb("xk", [128, KC * 1024], BF16)
    rt = p.sb("rt", [128, KC * 1024], BF16)
    kact = p.sb("kact", [128, KC * 1024], BF16)
    sq = p.sb("sq", [128, 512], F32)
    gt = p.sb("gt", [128, KC * 1024], BF16)
    trs = p.sb("trs2", [128, 128], BF16)
    pst = p.ps("pst2", [128, 128], BF16)
    pss = [p.ps(f"psf{j}", [128, 512], F32) for j in range(2)]

    S = Sched(p)
    ld = S.sem("ld"); pe = S.sem("pe"); ev = S.sem("ev"); st = S.sem("st"); vv = S.sem("vv")
    C = S.cnt

    def f(s):
        s.dma_start(out=idt[:], in_=ident[:]).then_inc(ld, 16)
    S.emit("sync", f)
    S.bump("ld", 16)
    def f(g):
        g.dma_start(out=mu[:, 0:8], in_=mur.rearrange("(g p) -> p g", p=128), allow_slow_non_contiguous=True).then_inc(ld, 16)
        g.dma_start(out=mu[:, 8:16], in_=muk.rearrange("(g p) -> p g", p=128), allow_slow_non_contiguous=True).then_inc(ld, 16)
    S.emit("gpsimd", f)
    S.bump("ld", 32)

    for n, W in [("r", Wr), ("k", Wk), ("v", Wv)]:
        for bi in range(8):
            ld_now = C["ld"]
            pe_g = C["pe"]
            def f(g, W=W, bi=bi, pe_g=pe_g):
                g.wait_ge(pe, pe_g)
                g.dma_start(out=wload[:], in_=W[bi * 128:(bi + 1) * 128, :]).then_inc(ld, 16)
            S.emit("gpsimd", f)
            ld_now = S.bump("ld", 16)
            for bj in range(8):
                ev_need = C["ev"]
                def fmm(t, bj=bj, ld_now=ld_now, ev_need=ev_need):
                    t.wait_ge(ld, ld_now)
                    t.wait_ge(ev, ev_need)
                    t.transpose(pst[:], wload[:, bj * 128:(bj + 1) * 128], idt[:]).then_inc(pe, 1)
                S.emit("tensor", fmm)
                pe_now = S.bump("pe", 1)
                def fev(sc, n=n, bi=bi, bj=bj, pe_now=pe_now):
                    sc.wait_ge(pe, pe_now)
                    sc.copy(out=wT[n][:, bj * 1024 + bi * 128: bj * 1024 + (bi + 1) * 128],
                            in_=pst[:]).then_inc(ev, 1)
                S.emit("scalar", fev)
                S.bump("ev", 1)
    wprep_ev = C["ev"]
    if dbg:
        def fdw(s, wprep_ev=wprep_ev):
            s.wait_ge(ev, wprep_ev)
            for bj in range(8):
                s.dma_start(out=dwt[bj * 128:(bj + 1) * 128, :],
                            in_=wT["r"][:, bj * 1024:(bj + 1) * 1024]).then_inc(st, 16)
        S.emit("sync", fdw)
        S.bump("st", 16 * 8)

    NCH = 4; TC = 1024
    for ch in range(NCH):
        t0 = ch * TC
        ld_now = C["ld"]
        vv_g = C["vv"]
        pe_gd = C["pe"]
        def f(s, ch=ch, t0=t0, vv_g=vv_g, pe_gd=pe_gd):
            s.wait_ge(vv, vv_g)
            s.wait_ge(pe, pe_gd)
            for g in range(KC):
                if ch == 0:
                    s.dma_start(out=xnc[:, g * 1025 + 1: (g + 1) * 1025],
                                in_=xnT[g * 128:(g + 1) * 128, 0:TC]).then_inc(ld, 16)
                else:
                    s.dma_start(out=xnc[:, g * 1025: (g + 1) * 1025],
                                in_=xnT[g * 128:(g + 1) * 128, t0 - 1: t0 + TC]).then_inc(ld, 16)
        S.emit("sync", f)
        ld_now = S.bump("ld", 16 * KC)
        if ch == 0:
            def f(v, ld_now=ld_now):
                v.wait_ge(ld, ld_now)
                for g in range(KC):
                    v.memset(xnc[:, g * 1025: g * 1025 + 1], 0.0)
                v.engine_nop().then_inc(vv, 1)
            S.emit("vector", f)
            S.bump("vv", 1)
        for g in range(KC):
            vv_base = C["vv"]
            ev_g = C["ev"]
            def f(v, g=g, ld_now=ld_now, vv_base=vv_base, ev_g=ev_g):
                v.wait_ge(ld, ld_now)
                if g == 0:
                    v.wait_ge(vv, vv_base)
                    v.wait_ge(ev, ev_g)
                xp = xnc[:, g * 1025: g * 1025 + 1024]
                xc = xnc[:, g * 1025 + 1: g * 1025 + 1025]
                v.tensor_tensor(tmp[:], xc, xp, ALU.subtract).then_inc(vv, 1)
                v.wait_ge(vv, vv_base + 1)
                v.scalar_tensor_tensor(xr[:, g * 1024:(g + 1) * 1024], tmp[:],
                                       mu[:, g:g + 1], xp, ALU.mult, ALU.add).then_inc(vv, 1)
                v.scalar_tensor_tensor(xk[:, g * 1024:(g + 1) * 1024], tmp[:],
                                       mu[:, 8 + g:9 + g], xp, ALU.mult, ALU.add).then_inc(vv, 1)
            S.emit("vector", f)
            S.bump("vv", 3)
        mix_vv = C["vv"]

        for (nm, src, dstname) in [("r", xr, "rt"), ("k", xk, "kact")]:
            for bi in range(8):
                for j in range(2):
                    ev_need = C["ev"]
                    def fmm(t, nm=nm, src=src, bi=bi, j=j, mix_vv=mix_vv, ev_need=ev_need,
                            wprep_ev=wprep_ev):
                        t.wait_ge(vv, mix_vv)
                        t.wait_ge(ev, max(ev_need, wprep_ev))
                        for kb in range(8):
                            mm = t.matmul(pss[j][:],
                                     wT[nm][:, kb * 1024 + bi * 128: kb * 1024 + (bi + 1) * 128],
                                     src[:, kb * 1024 + j * 512: kb * 1024 + j * 512 + 512],
                                     start=(kb == 0), stop=(kb == 7))
                        mm.then_inc(pe, 1)
                    S.emit("tensor", fmm)
                    pe_now = S.bump("pe", 1)
                    vv_need = C["vv"]
                    def fev(sc, nm=nm, dstname=dstname, bi=bi, j=j, pe_now=pe_now, vv_need=vv_need):
                        sc.wait_ge(pe, pe_now)
                        sc.wait_ge(vv, vv_need)
                        dst = (rt if dstname == "rt" else kact)
                        piece = dst[:, bi * 1024 + j * 512: bi * 1024 + j * 512 + 512]
                        if nm == "r":
                            sc.activation(out=piece, in_=pss[j][:], func=AF.Sigmoid).then_inc(ev, 1)
                        else:
                            sc.activation(out=sq[:], in_=pss[j][:], func=AF.Relu)
                            sc.activation(out=piece, in_=sq[:], func=AF.Square).then_inc(ev, 1)
                    S.emit("scalar", fev)
                    S.bump("ev", 1)
        rk_ev = C["ev"]
        for bi in range(8):
            for j in range(2):
                ev_need = C["ev"]
                def fmm(t, bi=bi, j=j, rk_ev=rk_ev, ev_need=ev_need):
                    t.wait_ge(ev, max(rk_ev, ev_need))
                    for kb in range(8):
                        mm = t.matmul(pss[j][:],
                                 wT["v"][:, kb * 1024 + bi * 128: kb * 1024 + (bi + 1) * 128],
                                 kact[:, kb * 1024 + j * 512: kb * 1024 + j * 512 + 512],
                                 start=(kb == 0), stop=(kb == 7))
                    mm.then_inc(pe, 1)
                S.emit("tensor", fmm)
                pe_now = S.bump("pe", 1)
                ev_need2 = C["ev"]
                def fgv(v, bi=bi, j=j, pe_now=pe_now, ev_need2=ev_need2):
                    v.wait_ge(pe, pe_now)
                    v.wait_ge(ev, ev_need2)
                    piece = rt[:, bi * 1024 + j * 512: bi * 1024 + j * 512 + 512]
                    v.tensor_tensor(tmp[:, 0:512], pss[j][:], piece, ALU.mult).then_inc(vv, 1)
                S.emit("vector", fgv)
                vv_now = S.bump("vv", 1)
                pe_g2 = C["pe"]
                def fge(sc, bi=bi, j=j, vv_now=vv_now, pe_g2=pe_g2):
                    sc.wait_ge(vv, vv_now)
                    sc.wait_ge(pe, pe_g2)
                    sc.activation(out=gt[:, bi * 1024 + j * 512: bi * 1024 + j * 512 + 512],
                                  in_=tmp[:, 0:512], func=AF.Silu).then_inc(ev, 1)
                S.emit("scalar", fge)
                S.bump("ev", 1)
        g_ev = C["ev"]
        if dbg and ch == 0:
            def fdbg(s, g_ev=g_ev, mix_vv=mix_vv):
                s.wait_ge(ev, g_ev)
                s.wait_ge(vv, mix_vv)
                for g in range(KC):
                    s.dma_start(out=dxr[g * 128:(g + 1) * 128, :],
                                in_=xr[:, g * 1024:(g + 1) * 1024]).then_inc(st, 16)
                    s.dma_start(out=drt[g * 128:(g + 1) * 128, :],
                                in_=rt[:, g * 1024:(g + 1) * 1024]).then_inc(st, 16)
                    s.dma_start(out=dka[g * 128:(g + 1) * 128, :],
                                in_=kact[:, g * 1024:(g + 1) * 1024]).then_inc(st, 16)
            S.emit("sync", fdbg)
            S.bump("st", 16 * 24)
        for it in range(8):
            for jd in range(8):
                ev_need = C["ev"]
                def fmm(t, it=it, jd=jd, g_ev=g_ev, ev_need=ev_need):
                    t.wait_ge(ev, max(g_ev, ev_need))
                    t.transpose(pst[:], gt[:, jd * 1024 + it * 128: jd * 1024 + (it + 1) * 128],
                                idt[:]).then_inc(pe, 1)
                S.emit("tensor", fmm)
                pe_now = S.bump("pe", 1)
                st_need = C["st"]
                def fev(sc, pe_now=pe_now, st_need=st_need):
                    sc.wait_ge(pe, pe_now)
                    sc.wait_ge(st, st_need)
                    sc.copy(out=trs[:], in_=pst[:]).then_inc(ev, 1)
                S.emit("scalar", fev)
                ev_now = S.bump("ev", 1)
                def fst(s, it=it, jd=jd, t0=t0, ev_now=ev_now):
                    s.wait_ge(ev, ev_now)
                    s.dma_start(out=gtok[t0 + it * 128: t0 + (it + 1) * 128,
                                         jd * 128:(jd + 1) * 128], in_=trs[:]).then_inc(st, 16)
                S.emit("sync", fst)
                S.bump("st", 16)
    final_st = C["st"]
    def f(s, final_st=final_st):
        s.wait_ge(st, final_st)
    S.emit("sync", f)

    block_cm = p.nc.Block(); block = block_cm.__enter__()
    S.install(block)
    block_cm.__exit__(None, None, None)
    return p


def build_resid_ln():
    p = Prog("residln")
    x = p.inp("x", [T, D])
    stok = p.inp("s_tok", [T, D], BF16)
    gtok = p.inp("g_tok", [T, D], BF16)
    ident = p.inp("IDENT", [128, 128], BF16)
    x1o = p.out("x1", [T, D])
    xnT = p.out("xn1T", [D, T], BF16)

    xt = p.sb("xt", [128, D], F32)
    stt = p.sb("stt", [128, D], BF16)
    gtt = p.sb("gtt", [128, D], BF16)
    sg = p.sb("sg", [128, D], BF16)
    x1t = p.sb("x1t", [128, D], F32)
    xnb = p.sb("xnb", [128, D], BF16)
    s1 = p.sb("s1", [128, 1], F32)
    mneg = p.sb("mneg", [128, 1], F32)
    msq = p.sb("msq", [128, 1], F32)
    ssq = p.sb("ssq", [128, 1], F32)
    var = p.sb("var", [128, 1], F32)
    rstd = p.sb("rstd", [128, 1], F32)
    dummy = p.sb("scrd", [128, D], F32)
    eps = p.sb("eps", [128, 1], F32)
    idt = p.sb("idt", [128, 128], BF16)
    trs = p.sb("trs", [128, 128], BF16)
    pst = p.ps("pst", [128, 128], BF16)

    S = Sched(p)
    ld = S.sem("ld"); sc = S.sem("sc"); ve = S.sem("ve")
    pe = S.sem("pe"); ev = S.sem("ev"); st = S.sem("st")
    C = S.cnt
    NT = T // 128; ND = D // 128

    def f(s):
        s.dma_start(out=idt[:], in_=ident[:]).then_inc(ld, 16)
    S.emit("sync", f)
    S.bump("ld", 16)
    def f(g):
        g.memset(eps[:], 1e-5).then_inc(ve, 1)
    S.emit("gpsimd", f)
    S.bump("ve", 1)

    for i in range(NT):
        ld_now = C["ld"]
        ve_g = C["ve"]; pe_g = C["pe"]; sc_gl = C["sc"]
        def f(s, i=i, ve_g=ve_g, pe_g=pe_g, sc_gl=sc_gl):
            s.wait_ge(ve, ve_g)
            s.wait_ge(pe, pe_g)
            s.wait_ge(sc, sc_gl)
            s.dma_start(out=xt[:], in_=x[i * 128:(i + 1) * 128, :]).then_inc(ld, 16)
            s.dma_start(out=stt[:], in_=stok[i * 128:(i + 1) * 128, :]).then_inc(ld, 16)
            s.dma_start(out=gtt[:], in_=gtok[i * 128:(i + 1) * 128, :]).then_inc(ld, 16)
        S.emit("sync", f)
        ld_now = S.bump("ld", 48)
        st_g = C["st"]
        vbase = C["ve"]
        def fv(v, ld_now=ld_now, vbase=vbase, st_g=st_g):
            v.wait_ge(ld, ld_now)
            v.wait_ge(st, st_g)
            v.tensor_tensor(sg[:], stt[:], gtt[:], ALU.mult).then_inc(ve, 1)
            v.wait_ge(ve, vbase + 1)
            v.tensor_tensor(x1t[:], xt[:], sg[:], ALU.add).then_inc(ve, 1)
        S.emit("vector", fv)
        ve_now = S.bump("ve", 2)
        def fst1(s, i=i, ve_now=ve_now):
            s.wait_ge(ve, ve_now)
            s.dma_start(out=x1o[i * 128:(i + 1) * 128, :], in_=x1t[:]).then_inc(st, 16)
        S.emit("sync", fst1)
        S.bump("st", 16)
        sc_b = C["sc"]
        def fsc1(scl, ve_now=ve_now, sc_b=sc_b):
            scl.wait_ge(ve, ve_now)
            scl.activation(out=dummy[:], in_=x1t[:], func=AF.Copy, accum_out=s1[:])
            scl.mul(out=mneg[:], in_=s1[:], mul=-1.0 / D)
            scl.activation(out=msq[:], in_=mneg[:], func=AF.Square)
            scl.activation(out=dummy[:], in_=x1t[:], func=AF.Square, accum_out=ssq[:]).then_inc(sc, 1)
        S.emit("scalar", fsc1)
        sc_now = S.bump("sc", 1)
        def fv2(v, sc_now=sc_now):
            v.wait_ge(sc, sc_now)
            v.tensor_scalar(var[:], ssq[:], 1.0 / D, msq[:], ALU.mult, ALU.subtract).then_inc(ve, 1)
        S.emit("vector", fv2)
        ve2 = S.bump("ve", 1)
        def fsc2(scl, ve2=ve2):
            scl.wait_ge(ve, ve2)
            scl.activation(out=rstd[:], in_=var[:], func=AF.Ln, bias=eps[:])
            scl.activation(out=rstd[:], in_=rstd[:], func=AF.Exp, scale=-0.5).then_inc(sc, 1)
        S.emit("scalar", fsc2)
        sc2 = S.bump("sc", 1)
        pe_g2 = C["pe"]
        def fv3(v, sc2=sc2, pe_g2=pe_g2):
            v.wait_ge(sc, sc2)
            v.wait_ge(pe, pe_g2)
            v.tensor_scalar(xnb[:], x1t[:], mneg[:], rstd[:], ALU.add, ALU.mult).then_inc(ve, 1)
        S.emit("vector", fv3)
        ve3 = S.bump("ve", 1)
        for j in range(ND):
            ev_need = C["ev"]
            def fmm(t, j=j, ve3=ve3, ev_need=ev_need):
                t.wait_ge(ve, ve3)
                t.wait_ge(ev, ev_need)
                t.transpose(pst[:], xnb[:, j * 128:(j + 1) * 128], idt[:]).then_inc(pe, 1)
            S.emit("tensor", fmm)
            pe_now = S.bump("pe", 1)
            st_need = C["st"]
            def fev(scl, pe_now=pe_now, st_need=st_need):
                scl.wait_ge(pe, pe_now)
                scl.wait_ge(st, st_need)
                scl.copy(out=trs[:], in_=pst[:]).then_inc(ev, 1)
            S.emit("scalar", fev)
            ev_now = S.bump("ev", 1)
            def fst(s, i=i, j=j, ev_now=ev_now):
                s.wait_ge(ev, ev_now)
                s.dma_start(out=xnT[j * 128:(j + 1) * 128, i * 128:(i + 1) * 128],
                            in_=trs[:]).then_inc(st, 16)
            S.emit("sync", fst)
            S.bump("st", 16)
    final_st = C["st"]
    def f(s, final_st=final_st):
        s.wait_ge(st, final_st)
    S.emit("sync", f)
    block_cm = p.nc.Block(); block = block_cm.__enter__()
    S.install(block)
    block_cm.__exit__(None, None, None)
    return p


def build_l1proj():
    p = Prog("l1proj")
    xnT = p.inp("xn1T", [D, T], BF16)
    W = p.inp("W_xproj", [D, 1056])
    b = p.inp("b_xproj", [1056])
    dT = p.out("deltaT", [D, T])
    bc = p.out("bc", [32, T])

    KC = 8
    xsb = [p.sb(f"x{k}", [128, T], BF16) for k in range(KC)]
    wsb = [p.sb(f"w{k}", [128, 1056], BF16) for k in range(KC)]
    bcol = p.sb("bcol", [128, 9], F32)
    t1 = p.sb("t1", [128, 512], F32)
    t2 = p.sb("t2", [128, 512], F32)
    stg = p.sb("stg", [128, 512], F32)
    pss = [p.ps(f"ps{j}", [128, 512], F32) for j in range(2)]

    S = Sched(p)
    ld = S.sem("ld"); pe = S.sem("pe"); ev = S.sem("ev"); st = S.sem("st"); vv = S.sem("vv")
    C = S.cnt
    def f(g):
        for k in range(KC):
            g.dma_start(out=wsb[k][:], in_=W[k * 128:(k + 1) * 128, :]).then_inc(ld, 16)
        g.dma_start(out=bcol[:, 0:8], in_=b[0:1024].rearrange("(g p) -> p g", p=128),
                    allow_slow_non_contiguous=True).then_inc(ld, 16)
        g.dma_start(out=bcol[0:32, 8:9], in_=b[1024:1056].unsqueeze(1),
                    allow_slow_non_contiguous=True).then_inc(ld, 16)
    S.emit("gpsimd", f)
    S.bump("ld", 16 * (KC + 2))
    def f(s):
        for k in range(KC):
            s.dma_start(out=xsb[k][:], in_=xnT[k * 128:(k + 1) * 128, :]).then_inc(ld, 16)
    S.emit("sync", f)
    ld_all = S.bump("ld", 16 * KC)

    NJ = T // 512
    for i in range(9):
        hi = 128 if i < 8 else 32
        for j in range(NJ):
            jb = j % 2
            ev_need = C["ev"]
            vv_need = C["vv"]
            def fmm(t, i=i, j=j, jb=jb, hi=hi, ld_all=ld_all, ev_need=ev_need, vv_need=vv_need):
                t.wait_ge(ld, ld_all)
                t.wait_ge(ev, ev_need)
                t.wait_ge(vv, vv_need)
                for k in range(KC):
                    mm = t.matmul(pss[jb][0:hi, :], wsb[k][:, i * 128:i * 128 + hi],
                                  xsb[k][:, j * 512:(j + 1) * 512],
                                  start=(k == 0), stop=(k == KC - 1))
                mm.then_inc(pe, 1)
            S.emit("tensor", fmm)
            pe_now = S.bump("pe", 1)
            if i < 8:
                st_need = C["st"]
                def fev(scl, i=i, jb=jb, pe_now=pe_now, st_need=st_need):
                    scl.wait_ge(pe, pe_now)
                    scl.wait_ge(st, st_need)
                    scl.activation(out=t1[:], in_=pss[jb][:], func=AF.Abs, bias=bcol[:, i:i + 1])
                    scl.activation(out=t1[:], in_=t1[:], func=AF.Exp, scale=-1.0)
                    scl.activation(out=t1[:], in_=t1[:], func=AF.Ln, bias=1.0)
                    scl.activation(out=t2[:], in_=pss[jb][:], func=AF.Relu,
                                   bias=bcol[:, i:i + 1]).then_inc(ev, 1)
                S.emit("scalar", fev)
                ev_now = S.bump("ev", 1)
                def fv(v, ev_now=ev_now):
                    v.wait_ge(ev, ev_now)
                    v.tensor_tensor(stg[:], t1[:], t2[:], ALU.add).then_inc(vv, 1)
                S.emit("vector", fv)
                vv_now = S.bump("vv", 1)
                def fst(s, i=i, j=j, vv_now=vv_now):
                    s.wait_ge(vv, vv_now)
                    s.dma_start(out=dT[i * 128:(i + 1) * 128, j * 512:(j + 1) * 512],
                                in_=stg[:]).then_inc(st, 16)
                S.emit("sync", fst)
                S.bump("st", 16)
            else:
                st_need = C["st"]
                def fev(v, jb=jb, pe_now=pe_now, st_need=st_need):
                    v.wait_ge(pe, pe_now)
                    v.wait_ge(st, st_need)
                    v.tensor_scalar(stg[0:32, :], pss[jb][0:32, :], bcol[0:32, 8:9], None,
                                    ALU.add).then_inc(vv, 1)
                S.emit("vector", fev)
                vv_now = S.bump("vv", 1)
                def fst(s, j=j, vv_now=vv_now):
                    s.wait_ge(vv, vv_now)
                    s.dma_start(out=bc[:, j * 512:(j + 1) * 512], in_=stg[0:32, :]).then_inc(st, 16)
                S.emit("sync", fst)
                S.bump("st", 16)
    final_st = C["st"]
    def f(s, final_st=final_st):
        s.wait_ge(st, final_st)
    S.emit("sync", f)
    block_cm = p.nc.Block(); block = block_cm.__enter__()
    S.install(block)
    block_cm.__exit__(None, None, None)
    return p


def build_scan():
    p = Prog("scan")
    dT = p.inp("deltaT", [D, T])
    xnT = p.inp("xn1T", [D, T], BF16)
    bc = p.inp("bc", [32, T])
    Alog = p.inp("A_log", [D, 16])
    Dp = p.inp("Dp", [D])
    yT = p.out("yT", [D, T], BF16)

    at = p.sb("at", [128, 16 * 8], F32)
    dpc = p.sb("dpc", [128, 8], F32)
    dg = p.sb("dg", [128, T], F32)
    xg = p.sb("xg", [128, T], BF16)
    dx = p.sb("dx", [128, T], BF16)
    brep = p.sb("brep", [128, T], BF16)
    crep = p.sb("crep", [128, T], BF16)
    atile = p.sb("atile", [128, T], F32)
    util = p.sb("util", [128, T], BF16)
    htile = p.sb("htile", [128, T], BF16)
    yacc = p.sb("yacc", [128, T], F32)
    yout = p.sb("yout", [128, T], BF16)

    S = Sched(p)
    ld = S.sem("ld"); sc = S.sem("sc"); vv = S.sem("vv"); st = S.sem("st")
    C = S.cnt

    def f(g):
        g.dma_start(out=at[:], in_=Alog.rearrange("(g p) n -> p g n", p=128),
                    allow_slow_non_contiguous=True).then_inc(ld, 16)
        g.dma_start(out=dpc[:], in_=Dp.rearrange("(g p) -> p g", p=128),
                    allow_slow_non_contiguous=True).then_inc(ld, 16)
    S.emit("gpsimd", f)
    S.bump("ld", 32)
    def f(scl):
        scl.wait_ge(ld, 32)
        scl.activation(out=at[:], in_=at[:], func=AF.Exp, scale=1.0).then_inc(sc, 1)
    S.emit("scalar", f)
    S.bump("sc", 1)
    def f(v):
        v.wait_ge(sc, 1)
        v.tensor_scalar(at[:], at[:], -1.0, None, ALU.mult).then_inc(vv, 1)
    S.emit("vector", f)
    S.bump("vv", 1)
    aprep_vv = C["vv"]

    for g in range(8):
        ld_now = C["ld"]
        vv_g = C["vv"]; sc_g = C["sc"]
        def f(s, g=g, vv_g=vv_g, sc_g=sc_g):
            s.wait_ge(vv, vv_g)
            s.wait_ge(sc, sc_g)
            s.dma_start(out=dg[:], in_=dT[g * 128:(g + 1) * 128, :]).then_inc(ld, 16)
            s.dma_start(out=xg[:], in_=xnT[g * 128:(g + 1) * 128, :]).then_inc(ld, 16)
        S.emit("sync", f)
        ld_now = S.bump("ld", 32)
        vbase = C["vv"]
        def f(v, ld_now=ld_now, vbase=vbase):
            v.wait_ge(ld, ld_now)
            v.tensor_tensor(dx[:], dg[:], xg[:], ALU.mult).then_inc(vv, 1)
        S.emit("vector", f)
        S.bump("vv", 1)
        for n in range(16):
            ld_bc = C["ld"]
            vv_bc = C["vv"]
            def f(gp, n=n, vv_bc=vv_bc):
                gp.wait_ge(vv, vv_bc)
                gp.dma_start(out=brep[:], in_=bc[n:n + 1, :].broadcast_to([128, T])).then_inc(ld, 16)
                gp.dma_start(out=crep[:], in_=bc[16 + n:17 + n, :].broadcast_to([128, T])).then_inc(ld, 16)
            S.emit("gpsimd", f)
            ld_bc = S.bump("ld", 32)
            sc_a = C["sc"]
            vv_a = C["vv"]
            def f(scl, g=g, n=n, ld_now=ld_now, vv_a=vv_a):
                scl.wait_ge(ld, ld_now)
                scl.wait_ge(vv, max(vv_a, aprep_vv))
                scl.activation(out=atile[:], in_=dg[:], func=AF.Exp,
                               scale=at[:, g * 16 + n: g * 16 + n + 1]).then_inc(sc, 1)
            S.emit("scalar", f)
            sc_now = S.bump("sc", 1)
            vb = C["vv"]
            def f(v, g=g, n=n, ld_bc=ld_bc, sc_now=sc_now, vb=vb):
                v.wait_ge(ld, ld_bc)
                v.wait_ge(sc, sc_now)
                v.tensor_tensor(util[:], dx[:], brep[:], ALU.mult).then_inc(vv, 1)
                v.wait_ge(vv, vb + 1)
                v.tensor_tensor_scan(htile[:], atile[:], util[:], 0.0,
                                     ALU.mult, ALU.add).then_inc(vv, 1)
                v.wait_ge(vv, vb + 2)
                if n == 0:
                    v.tensor_tensor(yacc[:], htile[:], crep[:], ALU.mult).then_inc(vv, 1)
                else:
                    v.tensor_tensor(util[:], htile[:], crep[:], ALU.mult).then_inc(vv, 1)
                    v.wait_ge(vv, vb + 3)
                    v.tensor_tensor(yacc[:], yacc[:], util[:], ALU.add).then_inc(vv, 1)
            S.emit("vector", f)
            S.bump("vv", 3 if n == 0 else 4)
        st_g = C["st"]
        vb2 = C["vv"]
        def f(v, g=g, st_g=st_g, vb2=vb2):
            v.wait_ge(st, st_g)
            v.wait_ge(vv, vb2)
            v.scalar_tensor_tensor(yout[:], xg[:], dpc[:, g:g + 1], yacc[:],
                                   ALU.mult, ALU.add).then_inc(vv, 1)
        S.emit("vector", f)
        vv_y = S.bump("vv", 1)
        def f(s, g=g, vv_y=vv_y):
            s.wait_ge(vv, vv_y)
            s.dma_start(out=yT[g * 128:(g + 1) * 128, :], in_=yout[:]).then_inc(st, 16)
        S.emit("sync", f)
        S.bump("st", 16)
    final_st = C["st"]
    def f(s, final_st=final_st):
        s.wait_ge(st, final_st)
    S.emit("sync", f)
    block_cm = p.nc.Block(); block = block_cm.__enter__()
    S.install(block)
    block_cm.__exit__(None, None, None)
    return p


def build_finale():
    p = Prog("finale")
    x1 = p.inp("x1", [T, D])
    xnT = p.inp("xn1T", [D, T], BF16)
    yT = p.inp("yT", [D, T], BF16)
    Wout = p.inp("W_out", [D, D])
    bout = p.inp("b_out", [D])
    weight = p.inp("weight", [D, D])
    disha = p.inp("disha", [64, D])
    ident = p.inp("IDENT", [128, 128], BF16)
    OUT = p.out("out", [T, D])

    KC = 8
    wo = [p.sb(f"wo{k}", [128, D], BF16) for k in range(KC)]
    we = [p.sb(f"we{k}", [128, D], BF16) for k in range(KC)]
    wload = p.sb("wload", [128, D], BF16)
    dr = p.sb("dr", [128, D], BF16)
    brep = p.sb("brep", [128, D], BF16)
    idt = p.sb("idt", [128, 128], BF16)
    lhs_y = p.sb("lhs_y", [128, D], BF16)
    lhs_x = p.sb("lhs_x", [128, D], BF16)
    s1t = p.sb("s1t", [128, D], BF16)
    g1t = p.sb("g1t", [128, D], BF16)
    x1t = p.sb("x1t", [128, D], F32)
    sgt = p.sb("sgt", [128, D], BF16)
    ott = p.sb("ott", [128, D], F32)
    pst = p.ps("pst", [128, 128], BF16)
    pss = [p.ps(f"ps{j}", [128, 512], F32) for j in range(4)]

    S = Sched(p)
    ld = S.sem("ld"); pe = S.sem("pe"); ev = S.sem("ev"); st = S.sem("st"); vv = S.sem("vv")
    C = S.cnt

    def f(s):
        s.dma_start(out=idt[:], in_=ident[:]).then_inc(ld, 16)
    S.emit("sync", f)
    S.bump("ld", 16)
    def f(g):
        for k in range(KC):
            g.dma_start(out=wo[k][:], in_=Wout[k * 128:(k + 1) * 128, :]).then_inc(ld, 16)
        g.dma_start(out=brep[:], in_=bout[None, :].broadcast_to([128, D])).then_inc(ld, 16)
        g.dma_start(out=dr[0:64, :], in_=disha[:]).then_inc(ld, 16)
        g.dma_start(out=dr[64:128, :], in_=disha[:]).then_inc(ld, 16)
    S.emit("gpsimd", f)
    S.bump("ld", 16 * (KC + 3))

    for bi in range(8):
        ld_now = C["ld"]
        pe_g = C["pe"]
        def f(g, bi=bi, pe_g=pe_g):
            g.wait_ge(pe, pe_g)
            g.dma_start(out=wload[:], in_=weight[bi * 128:(bi + 1) * 128, :]).then_inc(ld, 16)
        S.emit("gpsimd", f)
        ld_now = S.bump("ld", 16)
        for bj in range(8):
            ev_need = C["ev"]
            def fmm(t, bj=bj, ld_now=ld_now, ev_need=ev_need):
                t.wait_ge(ld, ld_now)
                t.wait_ge(ev, ev_need)
                t.transpose(pst[:], wload[:, bj * 128:(bj + 1) * 128], idt[:]).then_inc(pe, 1)
            S.emit("tensor", fmm)
            pe_now = S.bump("pe", 1)
            def fev(scl, bi=bi, bj=bj, pe_now=pe_now):
                scl.wait_ge(pe, pe_now)
                scl.copy(out=we[bj][:, bi * 128:(bi + 1) * 128], in_=pst[:]).then_inc(ev, 1)
            S.emit("scalar", fev)
            S.bump("ev", 1)
    wt_ev = C["ev"]
    def f(v, wt_ev=wt_ev):
        v.wait_ge(ev, wt_ev)
        v.wait_ge(ld, 16 * (1 + KC + 3))
        for k in range(KC):
            v.tensor_tensor(we[k][:], we[k][:], dr[:], ALU.add)
        v.engine_nop().then_inc(vv, 1)
    S.emit("vector", f)
    weff_vv = S.bump("vv", 1)

    NT = T // 128
    for i in range(NT):
        ld_now = C["ld"]
        pe_g = C["pe"]; vv_g = C["vv"]
        def f(s, i=i, pe_g=pe_g, vv_g=vv_g):
            s.wait_ge(pe, pe_g)
            s.wait_ge(vv, vv_g)
            for k in range(KC):
                s.dma_start(out=lhs_y[:, k * 128:(k + 1) * 128],
                            in_=yT[k * 128:(k + 1) * 128, i * 128:(i + 1) * 128]).then_inc(ld, 16)
                s.dma_start(out=lhs_x[:, k * 128:(k + 1) * 128],
                            in_=xnT[k * 128:(k + 1) * 128, i * 128:(i + 1) * 128]).then_inc(ld, 16)
            s.dma_start(out=x1t[:], in_=x1[i * 128:(i + 1) * 128, :]).then_inc(ld, 16)
        S.emit("sync", f)
        ld_now = S.bump("ld", 16 * (2 * KC + 1))
        for j in range(2):
            ev_need = C["ev"]
            def fmm(t, i=i, j=j, ld_now=ld_now, ev_need=ev_need, weff_vv=weff_vv):
                t.wait_ge(ld, ld_now)
                t.wait_ge(vv, weff_vv)
                t.wait_ge(ev, ev_need)
                for k in range(KC):
                    mm = t.matmul(pss[j][:], lhs_y[:, k * 128:(k + 1) * 128],
                                  wo[k][:, j * 512:(j + 1) * 512],
                                  start=(k == 0), stop=(k == KC - 1))
                mm.then_inc(pe, 1)
                for k in range(KC):
                    mm = t.matmul(pss[2 + j][:], lhs_x[:, k * 128:(k + 1) * 128],
                                  we[k][:, j * 512:(j + 1) * 512],
                                  start=(k == 0), stop=(k == KC - 1))
                mm.then_inc(pe, 1)
            S.emit("tensor", fmm)
            pe_now = S.bump("pe", 2)
            vb = C["vv"]
            def fev(v, j=j, pe_now=pe_now, vb=vb):
                v.wait_ge(pe, pe_now)
                v.tensor_tensor(s1t[:, j * 512:(j + 1) * 512], pss[j][:],
                                brep[:, j * 512:(j + 1) * 512], ALU.add).then_inc(vv, 1)
            S.emit("vector", fev)
            S.bump("vv", 1)
            def fg(scl, j=j, pe_now=pe_now):
                scl.wait_ge(pe, pe_now)
                scl.activation(out=g1t[:, j * 512:(j + 1) * 512], in_=pss[2 + j][:],
                               func=AF.Silu).then_inc(ev, 1)
            S.emit("scalar", fg)
            S.bump("ev", 1)
        mm_done_vv = C["vv"]; mm_done_ev = C["ev"]
        st_g = C["st"]
        vb2 = C["vv"]
        def fout(v, mm_done_vv=mm_done_vv, mm_done_ev=mm_done_ev, st_g=st_g, vb2=vb2):
            v.wait_ge(vv, mm_done_vv)
            v.wait_ge(ev, mm_done_ev)
            v.wait_ge(st, st_g)
            v.tensor_tensor(sgt[:], s1t[:], g1t[:], ALU.mult).then_inc(vv, 1)
            v.wait_ge(vv, vb2 + 1)
            v.tensor_tensor(ott[:], x1t[:], sgt[:], ALU.add).then_inc(vv, 1)
        S.emit("vector", fout)
        vv_out = S.bump("vv", 2)
        def fst(s, i=i, vv_out=vv_out):
            s.wait_ge(vv, vv_out)
            s.dma_start(out=OUT[i * 128:(i + 1) * 128, :], in_=ott[:]).then_inc(st, 16)
        S.emit("sync", fst)
        S.bump("st", 16)
    final_st = C["st"]
    def f(s, final_st=final_st):
        s.wait_ge(st, final_st)
    S.emit("sync", f)
    block_cm = p.nc.Block(); block = block_cm.__enter__()
    S.install(block)
    block_cm.__exit__(None, None, None)
    return p


def kernel_multilaunch(**inputs):
    ident = CONST["IDENT"]
    x = np.ascontiguousarray(np.asarray(inputs["L0_attn_x"], dtype=np.float32))
    nco = x.shape[0]
    cores = list(range(nco))

    def rl(prog, maps):
        return run_launch(prog, maps)

    p_ln = build_ln("ln0")
    r = rl(p_ln, [{"x": x[c], "IDENT": ident} for c in cores])
    xnT = [np.asarray(r.results[c]["xnT"]) for c in cores]

    p_pr = build_l0proj()
    r = rl(p_pr, [{"xnT": xnT[c], "proj_w": inputs["L0_attn_proj_w"],
                   "proj_b": inputs["L0_attn_proj_b"]} for c in cores])
    ptok = [np.asarray(r.results[c]["p_tok"]) for c in cores]

    consts_h = {k: CONST[k] for k in ["F128RE", "F128IM", "S2FWD", "S2FILT",
                                      "S1INV", "G2RE", "G2NIM", "SIGNONE"]}
    filt = np.asarray(inputs["L0_attn_filter_time"], dtype=np.float32)

    p_h0 = build_hyena(0)
    r = rl(p_h0, [{"vsrc": ptok[c], "p_tok": ptok[c], "filt": filt[0, 0], **consts_h}
                  for c in cores])
    v2 = [np.asarray(r.results[c]["vout"]) for c in cores]

    p_h1 = build_hyena(1)
    r = rl(p_h1, [{"vsrc": v2[c], "p_tok": ptok[c], "filt": filt[1, 0], **consts_h}
                  for c in cores])
    stok = [np.asarray(r.results[c]["vout"]) for c in cores]

    p_ff = build_l0ffn()
    r = rl(p_ff, [{"xnT": xnT[c], "W_r": inputs["L0_ffn_W_r"], "W_k": inputs["L0_ffn_W_k"],
                   "W_v": inputs["L0_ffn_W_v"], "mu_r": inputs["L0_ffn_mu_r"],
                   "mu_k": inputs["L0_ffn_mu_k"], "IDENT": ident} for c in cores])
    gtok = [np.asarray(r.results[c]["g_tok"]) for c in cores]

    p_rl = build_resid_ln()
    r = rl(p_rl, [{"x": x[c], "s_tok": stok[c], "g_tok": gtok[c], "IDENT": ident}
                  for c in cores])
    x1 = [np.asarray(r.results[c]["x1"]) for c in cores]
    xn1T = [np.asarray(r.results[c]["xn1T"]) for c in cores]

    p_l1 = build_l1proj()
    r = rl(p_l1, [{"xn1T": xn1T[c], "W_xproj": inputs["L1_attn_W_xproj"],
                   "b_xproj": inputs["L1_attn_b_xproj"]} for c in cores])
    deltaT = [np.asarray(r.results[c]["deltaT"]) for c in cores]
    bc = [np.asarray(r.results[c]["bc"]) for c in cores]

    p_sc = build_scan()
    r = rl(p_sc, [{"deltaT": deltaT[c], "xn1T": xn1T[c], "bc": bc[c],
                   "A_log": inputs["L1_attn_A_log"], "Dp": inputs["L1_attn_Dp"]}
                  for c in cores])
    yT = [np.asarray(r.results[c]["yT"]) for c in cores]

    p_fi = build_finale()
    r = rl(p_fi, [{"x1": x1[c], "xn1T": xn1T[c], "yT": yT[c],
                   "W_out": inputs["L1_attn_W_out"], "b_out": inputs["L1_attn_b_out"],
                   "weight": inputs["L1_ffn_weight"], "disha": inputs["L1_ffn_disha"],
                   "IDENT": ident} for c in cores])
    out = np.stack([np.asarray(r.results[c]["out"], dtype=np.float32) for c in cores])
    return out


class Pool:
    def __init__(self, p, name, cols, dtype):
        self.ap = p.sb(name, [128, cols], dtype)
        self.cols = cols
        self.off = 0

    def alloc(self, n):
        assert self.off + n <= self.cols, (self.off, n, self.cols)
        ap = self.ap[:, self.off:self.off + n]
        self.off += n
        return ap

    def reset(self):
        self.off = 0

    def reset_to(self, off):
        self.off = off


def S_barrier(S, sems):
    snap = {k: S.cnt[k] for k in S.cnt}
    for eng in ["sync", "gpsimd", "scalar", "vector", "tensor"]:
        def f(e, snap=snap, sems=sems):
            for k, v in snap.items():
                if v > 0:
                    e.wait_ge(sems[k], v)
        S.emit(eng, f)


def build_mega(dbg=False):
    p = Prog("mega")
    x_in = p.inp("x", [T, D])
    pw = p.inp("proj_w", [D, 3072])
    pb = p.inp("proj_b", [3072])
    filt_in = p.inp("filt", [2, 2048, D])
    Wr_in = p.inp("W_r", [D, D]); Wk_in = p.inp("W_k", [D, D]); Wv_in = p.inp("W_v", [D, D])
    mur_in = p.inp("mu_r", [D]); muk_in = p.inp("mu_k", [D])
    Alog_in = p.inp("A_log", [D, 16]); Dp_in = p.inp("Dp", [D])
    Wxp_in = p.inp("W_xproj", [D, 1056]); bxp_in = p.inp("b_xproj", [1056])
    Wout_in = p.inp("W_out", [D, D]); bout_in = p.inp("b_out", [D])
    wei_in = p.inp("weight", [D, D]); dis_in = p.inp("disha", [64, D])
    ident_in = p.inp("IDENT", [128, 128], BF16)
    f128re_in = p.inp("F128RE", [128, 128], BF16)
    f128im_in = p.inp("F128IM", [128, 128], BF16)
    s2fwd_in = p.inp("S2FWD", [128, 128, 128], BF16)
    s2filt_in = p.inp("S2FILT", [128, 128, 128], BF16)
    s1inv_in = p.inp("S1INV", [2, 128, 128], BF16)
    g2re_in = p.inp("G2RE", [32, 128, 128], BF16)
    g2nim_in = p.inp("G2NIM", [32, 128, 128], BF16)
    signone_in = p.inp("SIGNONE", [2, 128], BF16)
    bfly1_in = p.inp("BFLY1", [128, 128], BF16)
    bfly2_in = p.inp("BFLY2", [128, 128], BF16)
    bflyh1_in = p.inp("BFLYH1", [128, 128], BF16)
    bflyh2_in = p.inp("BFLYH2", [128, 128], BF16)
    OUT = p.out("out", [T, D])

    dram = lambda n, sh, dt=BF16: p.nc.dram_tensor(n, list(sh), dt, kind="Internal").ap()
    xnT_s = dram("xnT_s", [D, T])
    ptok_s = dram("ptok_s", [T, 3072])
    fw1 = dram("fw1", [128, 128, 512])
    Fsp = dram("Fsp_s", [136, 128, 256])
    Zsp = dram("Zsp_s", [136, 128, 256])
    Csp = dram("Csp_s", [136, 128, 256])
    Bsp = dram("Bsp_s", [N2, 2, 128, D])
    v2_s = dram("v2_s", [T, D])
    stok_s = dram("stok_s", [T, D])
    gtok_s = dram("gtok_s", [T, D])
    x1_s = dram("x1_s", [T, D], F32)
    xn1T_s = dram("xn1T_s", [D, T])
    g1_s = dram("g1_s", [T, D])
    if dbg:
        dT_s = p.out("dT_s", [D, T])
        bc_s = p.out("bc_s", [32, T])
        yT_s = p.out("yT_s", [D, T], BF16)
        xn1T_d = p.out("xn1T_d", [D, T], BF16)
    else:
        dT_s = dram("dT_s", [D, T], F32)
        bc_s = dram("bc_s", [32, T], F32)
        yT_s = dram("yT_s", [D, T])
    c2_s = dram("c2_s", [4, 256])

    PB = Pool(p, "PB", 59392, BF16)
    PF = Pool(p, "PF", 16640, F32)
    idt = p.sb("idt", [128, 128], BF16)
    sm = p.sb("smalls", [128, 256], F32)
    sm2 = p.sb("smalls2", [128, 160], F32)
    c2row = p.sb("c2row", [1, 3 * D], BF16)
    ones1 = p.sb("ones1", [1, 128], BF16)
    pst = p.ps("pst", [128, 128], BF16)
    pst2 = p.ps("pst2", [128, 128], BF16)
    ps = [p.ps(f"ps{i}", [128, 512], F32) for i in range(6)]

    S = Sched(p)
    sems = {}
    for nsem in ["ld", "pe", "ev", "st", "vv", "sc", "lg", "st2", "sg"]:
        sems[nsem] = S.sem(nsem)
    ld = sems["ld"]; pe = sems["pe"]; ev = sems["ev"]
    st = sems["st"]; vv = sems["vv"]; sc = sems["sc"]
    lg = sems["lg"]; st2 = sems["st2"]; sg = sems["sg"]
    C = S.cnt

    def bar():
        S_barrier(S, sems)

    def f(s):
        s.dma_start(out=idt[:], in_=ident_in[:]).then_inc(ld, 16)
        s.dma_start(out=ones1[:], in_=signone_in[0:1, :]).then_inc(ld, 16)
    S.emit("sync", f)
    S.bump("ld", 32)
    eps = sm[:, 0:1]
    def f(g):
        g.memset(eps, 1e-5).then_inc(vv, 1)
    S.emit("gpsimd", f)
    S.bump("vv", 1)

    def emit_ln(x_dram, xnT_dram, resid=None):
        PB.reset(); PF.reset()
        res = [PB.alloc(T) for _ in range(8)]
        xnb = [PB.alloc(1024) for _ in range(2)]
        stt = [PB.alloc(1024) for _ in range(2)] if resid else None
        gtt = [PB.alloc(1024) for _ in range(2)] if resid else None
        sgt = PB.alloc(1024) if resid else None
        xt = [PF.alloc(1024) for _ in range(2)]
        x1t = [PF.alloc(1024) for _ in range(2)] if resid else xt
        scrd = PF.alloc(1024)
        stats = [sm[:, 1 + 8 * q: 9 + 8 * q] for q in range(2)]
        NT = T // 128; ND = D // 128
        ev_done = {}
        vv_done = {}
        for i in range(NT):
            par = i % 2
            s1 = stats[par][:, 0:1]; mneg = stats[par][:, 1:2]; msq = stats[par][:, 2:3]
            ssq = stats[par][:, 3:4]; var = stats[par][:, 4:5]; rstd = stats[par][:, 5:6]
            ld_now = C["ld"]
            vv_g = vv_done.get(i - 2, 0)
            sc_g = C["sc"] - 4 if i >= 2 else 0
            def f(s, i=i, par=par, vv_g=vv_g, sc_g=sc_g, x_dram=x_dram, resid=resid):
                if vv_g > 0:
                    s.wait_ge(vv, vv_g)
                if sc_g > 0:
                    s.wait_ge(sc, sc_g)
                s.dma_start(out=xt[par][:], in_=x_dram[i * 128:(i + 1) * 128, :]).then_inc(ld, 16)
                if resid:
                    s.dma_start(out=stt[par][:], in_=resid[0][i * 128:(i + 1) * 128, :]).then_inc(ld, 16)
                    s.dma_start(out=gtt[par][:], in_=resid[1][i * 128:(i + 1) * 128, :]).then_inc(ld, 16)
            S.emit("sync", f)
            ld_now = S.bump("ld", 48 if resid else 16)
            if resid:
                st_g = C["st"] - 16 if i >= 2 else 0
                vb = C["vv"]
                def fv(v, par=par, ld_now=ld_now, vb=vb, st_g=st_g):
                    v.wait_ge(ld, ld_now)
                    if st_g > 0:
                        v.wait_ge(st, st_g)
                    v.tensor_tensor(sgt[:], stt[par][:], gtt[par][:], ALU.mult).then_inc(vv, 1)
                    v.wait_ge(vv, vb + 1)
                    v.tensor_tensor(x1t[par][:], xt[par][:], sgt[:], ALU.add).then_inc(vv, 1)
                S.emit("vector", fv)
                ve_now = S.bump("vv", 2)
                def fst1(s, i=i, par=par, ve_now=ve_now, resid=resid):
                    s.wait_ge(vv, ve_now)
                    s.dma_start(out=resid[2][i * 128:(i + 1) * 128, :],
                                in_=x1t[par][:]).then_inc(st, 16)
                S.emit("sync", fst1)
                S.bump("st", 16)
                ready = ve_now
                rsem = vv
            else:
                ready = ld_now
                rsem = ld
            def fsc1(scl, par=par, ready=ready, rsem=rsem, s1=s1, mneg=mneg, msq=msq, ssq=ssq):
                scl.wait_ge(rsem, ready)
                scl.activation(out=scrd[:], in_=x1t[par][:], func=AF.Copy, accum_out=s1)
                scl.mul(out=mneg, in_=s1, mul=-1.0 / D)
                scl.activation(out=msq, in_=mneg, func=AF.Square)
                scl.activation(out=scrd[:], in_=x1t[par][:], func=AF.Square, accum_out=ssq).then_inc(sc, 1)
            S.emit("scalar", fsc1)
            sc_now = S.bump("sc", 1)
            def fv2(v, sc_now=sc_now, var=var, ssq=ssq, msq=msq):
                v.wait_ge(sc, sc_now)
                v.tensor_scalar(var, ssq, 1.0 / D, msq, ALU.mult, ALU.subtract).then_inc(vv, 1)
            S.emit("vector", fv2)
            ve2 = S.bump("vv", 1)
            def fsc2(scl, ve2=ve2, var=var, rstd=rstd):
                scl.wait_ge(vv, ve2)
                scl.activation(out=rstd, in_=var, func=AF.Ln, bias=eps)
                scl.activation(out=rstd, in_=rstd, func=AF.Exp, scale=-0.5).then_inc(sc, 1)
            S.emit("scalar", fsc2)
            sc2 = S.bump("sc", 1)
            pe_g2 = C["pe"] - ND if i >= 2 else 0
            def fv3(v, par=par, sc2=sc2, pe_g2=pe_g2, mneg=mneg, rstd=rstd):
                v.wait_ge(sc, sc2)
                if pe_g2 > 0:
                    v.wait_ge(pe, pe_g2)
                v.tensor_scalar(xnb[par][:], x1t[par][:], mneg, rstd, ALU.add, ALU.mult).then_inc(vv, 1)
            S.emit("vector", f_ := fv3)
            ve3 = S.bump("vv", 1)
            vv_done[i] = ve3
            for j in range(ND):
                pp = (i * ND + j) % 2
                pstj = pst if pp == 0 else pst2
                ev_need = ev_done.get(i * ND + j - 2, 0)
                def fmm(t, j=j, pstj=pstj, ve3=ve3, ev_need=ev_need, par=par):
                    t.wait_ge(vv, ve3)
                    if ev_need > 0:
                        t.wait_ge(ev, ev_need)
                    t.transpose(pstj[:], xnb[par][:, j * 128:(j + 1) * 128], idt[:]).then_inc(pe, 1)
                S.emit("tensor", fmm)
                pe_now = S.bump("pe", 1)
                def fev(scl, i=i, j=j, pstj=pstj, pe_now=pe_now):
                    scl.wait_ge(pe, pe_now)
                    scl.copy(out=res[j][:, i * 128:(i + 1) * 128], in_=pstj[:]).then_inc(ev, 1)
                S.emit("scalar", fev)
                ev_done[i * ND + j] = S.bump("ev", 1)
        fin_ev = C["ev"]
        def fstall(s, fin_ev=fin_ev, xnT_dram=xnT_dram):
            s.wait_ge(ev, fin_ev)
            for j in range(8):
                s.dma_start(out=xnT_dram[j * 128:(j + 1) * 128, :], in_=res[j][:]).then_inc(st, 16)
        S.emit("sync", fstall)
        S.bump("st", 16 * 8)
        return res

    xn_res = emit_ln(x_in, xnT_s)
    bar()

    def emit_proj():
        PB.reset_to(8 * T)
        wsb = [PB.alloc(1536) for _ in range(8)]
        brep = PB.alloc(3072)
        stg = [PB.alloc(1536) for _ in range(2)]
        NJ = 3
        def f(g):
            g.dma_start(out=brep[:], in_=pb[None, :].broadcast_to([128, 3072])).then_inc(lg, 16)
        S.emit("gpsimd", f)
        b_lg = S.bump("lg", 16)
        for douth in range(2):
            pe_gw = C["pe"]
            def fw(g, douth=douth, pe_gw=pe_gw):
                g.wait_ge(pe, pe_gw)
                for k in range(8):
                    g.dma_start(out=wsb[k][:],
                                in_=pw[k * 128:(k + 1) * 128,
                                       douth * 1536:(douth + 1) * 1536]).then_inc(lg, 16)
            S.emit("gpsimd", fw)
            w_lg = S.bump("lg", 16 * 8)
            for i in range(T // 128):
                vv_need = C["vv"]
                def fmm(t, i=i, w_lg=w_lg, vv_need=vv_need):
                    t.wait_ge(lg, w_lg)
                    t.wait_ge(vv, vv_need)
                    for k in range(8):
                        for j in range(NJ):
                            t.matmul(ps[j][:], xn_res[k][:, i * 128:(i + 1) * 128],
                                     wsb[k][:, j * 512:(j + 1) * 512],
                                     start=(k == 0), stop=(k == 7)).then_inc(pe, 1)
                S.emit("tensor", fmm)
                pe_now = S.bump("pe", 8 * NJ)
                par = i % 2
                st_need = C["st"] - 16 if i >= 2 else 0
                def fevv(v, par=par, douth=douth, pe_now=pe_now, st_need=st_need, b_lg=b_lg):
                    v.wait_ge(pe, pe_now)
                    v.wait_ge(lg, b_lg)
                    if st_need > 0:
                        v.wait_ge(st, st_need)
                    for j in range(NJ):
                        v.tensor_tensor(stg[par][:, j * 512:(j + 1) * 512], ps[j][:],
                                        brep[:, douth * 1536 + j * 512: douth * 1536 + (j + 1) * 512],
                                        ALU.add).then_inc(vv, 1 if j == NJ - 1 else 0)
                S.emit("vector", fevv)
                vv_now = S.bump("vv", 1)
                def fst(s, i=i, par=par, douth=douth, vv_now=vv_now):
                    s.wait_ge(vv, vv_now)
                    s.dma_start(out=ptok_s[i * 128:(i + 1) * 128,
                                           douth * 1536:(douth + 1) * 1536],
                                in_=stg[par][:]).then_inc(st, 16)
                S.emit("sync", fst)
                S.bump("st", 16)
    emit_proj()


    bar()

    def emit_hyena(stage):
        PB.reset(); PF.reset()
        la = PB.alloc(N2 * 1024)
        mid = [la[:, 0:8192], la[:, 8192:16384]]
        s2sb = [la[:, 16384:18432], la[:, 18432:20480]]
        stg2 = [la[:, 20480:24576], la[:, 24576:28672]]
        stg1 = [PB.alloc(2 * 1024) for _ in range(2)]
        s1sb = PB.alloc(256)
        g2t = [PB.alloc(256) for _ in range(4)]
        bre = [PB.alloc(1024) for _ in range(4)]
        bim = [PB.alloc(1024) for _ in range(4)]
        pg = [PB.alloc(1024) for _ in range(4)]
        ot = [PB.alloc(1024) for _ in range(4)]
        cre = PB.alloc(128)
        cim = PB.alloc(128)
        DQ = 256
        zt = la[:, 0:K2F * 2 * DQ]
        ft_ = la[:, K2F * 2 * DQ:2 * K2F * 2 * DQ]
        ct_ = la[:, 2 * K2F * 2 * DQ:3 * K2F * 2 * DQ]
        cti = [la[:, 0:32 * 256], la[:, 32 * 256:64 * 256]]
        stg3 = [la[:, 64 * 256:64 * 256 + 2048], la[:, 64 * 256 + 2048:64 * 256 + 4096]]
        c2k = c2row[:, 0:D]; c2kp = c2row[:, D:2 * D]; c2km = c2row[:, 2 * D:3 * D]
        filt = filt_in[stage]
        vsrc = ptok_s if stage == 0 else v2_s
        vout = v2_s if stage == 0 else stok_s

        def f(s):
            s.dma_start(out=cre[:], in_=f128re_in[:]).then_inc(ld, 16)
            s.dma_start(out=cim[:], in_=f128im_in[:]).then_inc(ld, 16)
            s.dma_start(out=s1sb[:, 0:128], in_=s1inv_in[0]).then_inc(ld, 16)
            s.dma_start(out=s1sb[:, 128:256], in_=s1inv_in[1]).then_inc(ld, 16)
        S.emit("sync", f)
        S.bump("ld", 64)
        def f(g):
            g.dma_start(out=la[0:64, :], in_=filt.rearrange("(a b) d -> a (b d)", b=N2)).then_inc(lg, 16)
        S.emit("gpsimd", f)
        S.bump("lg", 16)

        def fwd(src_is_filter):
            kh = 64 if src_is_filter else 128
            outsp = Fsp if src_is_filter else Zsp
            s2c = s2filt_in if src_is_filter else s2fwd_in
            if not src_is_filter:
                pe_need = C["pe"]
                st_need0 = C["st2"]
                sg_need0 = C["sg"]
                vla = vsrc[:, 0:D].rearrange("(a b) d -> a b d", b=N2) if stage == 0 \
                    else vsrc.rearrange("(a b) d -> a (b d)", b=N2)
                def f(s, pe_need=pe_need, st_need0=st_need0, sg_need0=sg_need0, vla=vla):
                    s.wait_ge(pe, pe_need)
                    s.wait_ge(st2, st_need0)
                    if sg_need0 > 0:
                        s.wait_ge(sg, sg_need0)
                    s.dma_start(out=la[:], in_=vla).then_inc(ld, 16)
                S.emit("sync", f)
                S.bump("ld", 16)
            ld_need = C["ld"]; lg_need = C["lg"]
            ev_base = C["ev"]; st_base = C["st2"]
            for n2 in range(N2):
                par = n2 % 2
                ev_need = ev_base + 4 * n2
                def fmm(t, n2=n2, ld_need=ld_need, lg_need=lg_need, ev_need=ev_need, kh=kh):
                    t.wait_ge(ld, ld_need)
                    t.wait_ge(lg, lg_need)
                    if ev_need > 0:
                        t.wait_ge(ev, ev_need)
                    for dh in range(2):
                        t.matmul(ps[0 + dh][:], cre[0:kh, :],
                                 la[0:kh, n2 * D + dh * 512: n2 * D + dh * 512 + 512],
                                 start=True, stop=True).then_inc(pe, 1)
                        t.matmul(ps[2 + dh][:], cim[0:kh, :],
                                 la[0:kh, n2 * D + dh * 512: n2 * D + dh * 512 + 512],
                                 start=True, stop=True).then_inc(pe, 1)
                S.emit("tensor", fmm)
                pe_now = S.bump("pe", 4)
                st_need = st_base + 16 * (n2 - 1) if n2 >= 2 else 0
                def fev(v, par=par, pe_now=pe_now, st_need=st_need):
                    v.wait_ge(pe, pe_now)
                    if st_need > 0:
                        v.wait_ge(st2, st_need)
                    v.tensor_copy(stg1[par][:, 0:512], ps[0][:])
                    v.tensor_copy(stg1[par][:, 512:1024], ps[1][:])
                    v.tensor_copy(stg1[par][:, 1024:1536], ps[2][:])
                    v.tensor_copy(stg1[par][:, 1536:2048], ps[3][:]).then_inc(ev, 4)
                S.emit("vector", fev)
                ev_now = S.bump("ev", 4)
                def fst(g, n2=n2, par=par, ev_now=ev_now):
                    g.wait_ge(ev, ev_now)
                    for c_ in range(2):
                        g.dma_start(out=fw1[n2 * 4 + c_ * 2: n2 * 4 + c_ * 2 + 2, :, :]
                                    .rearrange("s k e -> k s e"),
                                    in_=stg1[par][:, c_ * 1024:(c_ + 1) * 1024]
                                    .rearrange("k (s e) -> k s e", s=2)).then_inc(st2, 16)
                S.emit("scalar", fst)
                S.bump("st2", 32)
            s1_done_st = C["st2"]
            for kc in range(8):
                par = kc % 2
                pe_guard = C["pe"] - 16 if kc >= 2 else 0
                def fldc(s, kc=kc, par=par, s1_done_st=s1_done_st, s2c=s2c, pe_guard=pe_guard):
                    s.wait_ge(st2, s1_done_st)
                    if pe_guard > 0:
                        s.wait_ge(pe, pe_guard)
                    s.dma_start(out=s2sb[par][:],
                                in_=s2c[kc * 16:(kc + 1) * 16, :, :]
                                .rearrange("k p i -> p k i")).then_inc(ld, 16)
                    s.dma_start(out=mid[par][:],
                                in_=fw1[:, kc * 16:(kc + 1) * 16, :]).then_inc(ld, 16)
                S.emit("sync", fldc)
                ldc_now = S.bump("ld", 32)
                for half in range(2):
                    spar = (kc * 2 + half) % 2
                    stb = stg2[spar]
                    qname = "st2"
                    qsem = st2
                    for k1h in range(8):
                        k1i = half * 8 + k1h
                        k1 = kc * 16 + k1i
                        bank = 2 + (k1i % 4)
                        ev_need = C["ev"] if (kc == 0 and half == 0 and k1h <= 3) else C["ev"] - 3
                        def fmm(t, par=par, k1i=k1i, bank=bank, ldc_now=ldc_now, ev_need=ev_need):
                            t.wait_ge(ld, ldc_now)
                            if ev_need > 0:
                                t.wait_ge(ev, ev_need)
                            t.matmul(ps[bank][:], s2sb[par][:, k1i * 128:(k1i + 1) * 128],
                                     mid[par][:, k1i * 512:(k1i + 1) * 512],
                                     start=True, stop=True).then_inc(pe, 1)
                        S.emit("tensor", fmm)
                        pe_now = S.bump("pe", 1)
                        st_need = C["st2"] - 256 if (kc * 2 + half) >= 2 else 0
                        def fev(v, stb=stb, k1h=k1h, bank=bank, pe_now=pe_now,
                                st_need=st_need, qsem=qsem):
                            v.wait_ge(pe, pe_now)
                            if st_need > 0 and k1h == 0:
                                v.wait_ge(qsem, st_need)
                            v.tensor_copy(stb[:, k1h * 512:(k1h + 1) * 512], ps[bank][:]).then_inc(ev, 1)
                        S.emit("vector", fev)
                        ev_now = S.bump("ev", 1)
                    k1b = kc * 16 + half * 8
                    ev_now = C["ev"]
                    def fst(scl, k1b=k1b, stb=stb, ev_now=ev_now, outsp=outsp, qsem=qsem):
                        scl.wait_ge(ev, ev_now)
                        for co in range(2):
                            for s2 in range(2):
                                for eh in range(2):
                                    scl.dma_start(
                                        out=outsp.rearrange("r k e -> r (k e)")
                                        [co * 64 + s2 * 2 + eh: co * 64 + s2 * 2 + eh + 61: 4,
                                         k1b * 256:(k1b + 8) * 256]
                                        .rearrange("r (q e) -> r q e", q=8),
                                        in_=stb[co * 64 + s2 * 32: co * 64 + s2 * 32 + 16, :]
                                        .rearrange("p (q h e) -> p q h e", q=8, h=2)
                                        [:, :, eh, :]).then_inc(st2, 16)
                                    scl.dma_start(
                                        out=outsp[128 + co * 4 + s2 * 2 + eh,
                                                  k1b:k1b + 8, :].unsqueeze(0),
                                        in_=stb[co * 64 + s2 * 32 + 16: co * 64 + s2 * 32 + 17, :]
                                        .rearrange("p (q h e) -> p q h e", q=8, h=2)
                                        [:, :, eh, :]).then_inc(qsem, 16)
                    S.emit("scalar", fst)
                    S.bump(qname, 256)

        fwd(True)
        fwd(False)

        pwz = la[:, 0:4096]
        pwf = la[:, 4096:8192]
        pwfs = la[:, 8192:12288]
        pwt1 = la[:, 12288:16384]
        pwt2 = la[:, 16384:20480]
        pwct = la[:, 20480:24576]
        zhi = la[:, 24576:24832]
        fhi = la[:, 24832:25088]
        fshi = la[:, 25088:25344]
        thi1 = la[:, 25344:25600]
        thi2 = la[:, 25600:25856]
        c2st = la[:, 25856:26112]
        bfly = PB.alloc(512)
        pw_st_need = C["st2"]
        pw_sg_need = C["sg"]
        def fldb(s):
            s.dma_start(out=bfly[:, 0:128], in_=bfly1_in[:]).then_inc(ld, 16)
            s.dma_start(out=bfly[:, 128:256], in_=bfly2_in[:]).then_inc(ld, 16)
            s.dma_start(out=bfly[:, 256:384], in_=bflyh1_in[:]).then_inc(ld, 16)
            s.dma_start(out=bfly[:, 384:512], in_=bflyh2_in[:]).then_inc(ld, 16)
        S.emit("sync", fldb)
        bf_ld = S.bump("ld", 64)
        Zv = Zsp.rearrange("r k e -> r (k e)")
        Fv = Fsp.rearrange("r k e -> r (k e)")
        Cv = Csp.rearrange("r k e -> r (k e)")
        for kc in range(8):
            c0 = kc * 16 * 256
            vv_prev = C["vv"]
            pe_prev = C["pe"]
            def fld(s, kc=kc, c0=c0, pw_st_need=pw_st_need, pw_sg_need=pw_sg_need,
                    vv_prev=vv_prev, pe_prev=pe_prev):
                s.wait_ge(st2, pw_st_need)
                if pw_sg_need > 0:
                    s.wait_ge(sg, pw_sg_need)
                if kc >= 1:
                    s.wait_ge(vv, vv_prev)
                    s.wait_ge(pe, pe_prev)
                s.dma_start(out=pwz[:], in_=Zv[0:128, c0:c0 + 4096]).then_inc(ld, 16)
                s.dma_start(out=pwf[:], in_=Fv[0:128, c0:c0 + 4096]).then_inc(ld, 16)
                s.dma_start(out=pwfs[0:64, :], in_=Fv[64:128, c0:c0 + 4096]).then_inc(ld, 16)
                s.dma_start(out=pwfs[64:128, :], in_=Fv[0:64, c0:c0 + 4096]).then_inc(ld, 16)
            S.emit("sync", fld)
            ld_now = S.bump("ld", 64)
            vb = C["vv"]
            def fmul(v, ld_now=ld_now, vb=vb):
                v.wait_ge(ld, ld_now)
                v.tensor_tensor(pwt1[:], pwz[:], pwf[:], ALU.mult).then_inc(vv, 1)
                v.tensor_tensor(pwt2[:], pwz[:], pwfs[:], ALU.mult).then_inc(vv, 1)
            S.emit("vector", fmul)
            mul_vv = S.bump("vv", 2)
            for js in range(8):
                bank = 4 + (js % 2)
                ev_need = C["ev"]
                def fbf(t, js=js, bank=bank, mul_vv=mul_vv, ev_need=ev_need, bf_ld=bf_ld):
                    t.wait_ge(vv, mul_vv)
                    t.wait_ge(ld, bf_ld)
                    t.wait_ge(ev, ev_need)
                    t.matmul(ps[bank][:], bfly[:, 0:128], pwt1[:, js * 512:(js + 1) * 512],
                             start=True, stop=False)
                    t.matmul(ps[bank][:], bfly[:, 128:256], pwt2[:, js * 512:(js + 1) * 512],
                             start=False, stop=True).then_inc(pe, 1)
                S.emit("tensor", fbf)
                pe_now = S.bump("pe", 1)
                st_need2 = C["st"]
                def fev(scl, js=js, bank=bank, pe_now=pe_now, st_need2=st_need2):
                    scl.wait_ge(pe, pe_now)
                    if js == 0:
                        scl.wait_ge(st, st_need2)
                    scl.copy(out=pwct[:, js * 512:(js + 1) * 512], in_=ps[bank][:]).then_inc(ev, 1)
                S.emit("scalar", fev)
                S.bump("ev", 1)
            ev_h = C["ev"]
            if kc == 0:
                def fhv(v, ev_h=ev_h):
                    v.wait_ge(ev, ev_h)
                    v.tensor_scalar(pwct[0:4, 0:256], pwct[0:4, 0:256], 0.5, None, ALU.mult).then_inc(vv, 1)
                    v.tensor_scalar(pwct[64:68, 0:256], pwct[64:68, 0:256], 0.5, None, ALU.mult).then_inc(vv, 1)
                S.emit("vector", fhv)
                S.bump("vv", 2)
            vv_now = C["vv"]; ev_now = C["ev"]
            def fstc(s, kc=kc, c0=c0, vv_now=vv_now, ev_now=ev_now):
                s.wait_ge(vv, vv_now)
                s.wait_ge(ev, ev_now)
                s.dma_start(out=Cv[0:128, c0:c0 + 4096], in_=pwct[:]).then_inc(st, 16)
            S.emit("sync", fstc)
            S.bump("st", 16)
        def fldh(s, pw_st_need=pw_st_need, pw_sg_need=pw_sg_need):
            s.wait_ge(st2, pw_st_need)
            if pw_sg_need > 0:
                s.wait_ge(sg, pw_sg_need)
            s.dma_start(out=zhi[0:8, :], in_=Zsp[128:136, 0, :]).then_inc(ld, 16)
            s.dma_start(out=fhi[0:8, :], in_=Fsp[128:136, 0, :]).then_inc(ld, 16)
            s.dma_start(out=fshi[0:4, :], in_=Fsp[132:136, 0, :]).then_inc(ld, 16)
            s.dma_start(out=fshi[4:8, :], in_=Fsp[128:132, 0, :]).then_inc(ld, 16)
        S.emit("sync", fldh)
        hi_ld = S.bump("ld", 64)
        vbh = C["vv"]
        def fmh(v, hi_ld=hi_ld, vbh=vbh):
            v.wait_ge(ld, hi_ld)
            v.tensor_tensor(thi1[0:8, :], zhi[0:8, :], fhi[0:8, :], ALU.mult).then_inc(vv, 1)
            v.tensor_tensor(thi2[0:8, :], zhi[0:8, :], fshi[0:8, :], ALU.mult).then_inc(vv, 1)
        S.emit("vector", fmh)
        mh_vv = S.bump("vv", 2)
        ev_need = C["ev"]
        def fbh(t, mh_vv=mh_vv, ev_need=ev_need, bf_ld=bf_ld):
            t.wait_ge(vv, mh_vv)
            t.wait_ge(ev, ev_need)
            t.matmul(ps[4][0:8, 0:256], bfly[0:8, 256:264], thi1[0:8, :],
                     start=True, stop=False)
            t.matmul(ps[4][0:8, 0:256], bfly[0:8, 384:392], thi2[0:8, :],
                     start=False, stop=True).then_inc(pe, 1)
        S.emit("tensor", fbh)
        peh = S.bump("pe", 1)
        def fevh(scl, peh=peh):
            scl.wait_ge(pe, peh)
            scl.copy(out=c2st[0:8, :], in_=ps[4][0:8, 0:256]).then_inc(ev, 1)
        S.emit("scalar", fevh)
        evh = S.bump("ev", 1)
        def fgat(s, evh=evh):
            s.wait_ge(ev, evh)
            s.dma_start(out=c2_s[:], in_=c2st[0:4, :]).then_inc(ld, 16)
        S.emit("sync", fgat)
        gat_ld = S.bump("ld", 16)
        def fgat2(s, gat_ld=gat_ld):
            s.wait_ge(ld, gat_ld)
            s.dma_start(out=c2k[:], in_=c2_s.rearrange("s e -> (s e)").unsqueeze(0)).then_inc(ld, 16)
        S.emit("sync", fgat2)
        c2k_ld = S.bump("ld", 16)
        def f(scl, c2k_ld=c2k_ld):
            scl.wait_ge(ld, c2k_ld)
            scl.mul(out=c2kp[:], in_=c2k[:], mul=0.5)
            scl.mul(out=c2km[:], in_=c2k[:], mul=-0.5).then_inc(ev, 1)
        S.emit("scalar", f)
        c2k_ev = S.bump("ev", 1)
        pw_done_st = C["st"]

        for kc in range(4):
            par = kc % 2
            pe_guard = C["pe"] - 32 if kc >= 2 else 0
            def fld(s, kc=kc, par=par, pw_done_st=pw_done_st, pe_guard=pe_guard):
                s.wait_ge(st, pw_done_st)
                if pe_guard > 0:
                    s.wait_ge(pe, pe_guard)
                s.dma_start(out=cti[par][:],
                            in_=Csp[0:128, kc * 32:(kc + 1) * 32, :]).then_inc(ld, 16)
            S.emit("sync", fld)
            ld_now = S.bump("ld", 16)
            for co in range(2):
                for jg in range(4):
                    m = kc * 8 + co * 4 + jg
                    spar = m % 2
                    stb = stg3[spar]
                    qname = "st2"
                    qsem = st2
                    for jh in range(4):
                        js = jg * 4 + jh
                        bank = 2 + (js % 4)
                        ev_need = C["ev"] if (kc == 0 and co == 0 and js <= 3) else C["ev"] - 3
                        def fmm(t, par=par, co=co, js=js, bank=bank, ld_now=ld_now, ev_need=ev_need):
                            t.wait_ge(ld, ld_now)
                            if ev_need > 0:
                                t.wait_ge(ev, ev_need)
                            t.matmul(ps[bank][:], s1sb[:, co * 128:(co + 1) * 128],
                                     cti[par][:, js * 512:(js + 1) * 512],
                                     start=True, stop=True).then_inc(pe, 1)
                        S.emit("tensor", fmm)
                        pe_now = S.bump("pe", 1)
                        st_need = C["st2"] - 64 if m >= 2 else 0
                        def fev(v, stb=stb, jh=jh, bank=bank, pe_now=pe_now,
                                st_need=st_need, qsem=qsem):
                            v.wait_ge(pe, pe_now)
                            if st_need > 0 and jh == 0:
                                v.wait_ge(qsem, st_need)
                            v.tensor_copy(stb[:, jh * 512:(jh + 1) * 512], ps[bank][:]).then_inc(ev, 1)
                        S.emit("vector", fev)
                        S.bump("ev", 1)
                    ev_now = C["ev"]
                    k1b = kc * 32 + jg * 8
                    def fst(scl, co=co, k1b=k1b, stb=stb, ev_now=ev_now, qsem=qsem):
                        scl.wait_ge(ev, ev_now)
                        for s4 in range(4):
                            scl.dma_start(out=Bsp[:, co, k1b:k1b + 8, s4 * 256:(s4 + 1) * 256],
                                          in_=stb[s4 * 32:(s4 + 1) * 32, :]
                                          .rearrange("a (k e) -> a k e", k=8)).then_inc(qsem, 16)
                    S.emit("scalar", fst)
                    S.bump(qname, 64)
        inv1_done_st = C["st2"]
        inv1_done_sg = C["sg"]

        gate_vv = {}
        mm_pe = {}
        st_done = {}
        inv1_ev = C["ev"]
        for t5 in range(N2):
            par = t5 % 4
            pe_guard = mm_pe.get(t5 - 4, 0)
            vv_guard = gate_vv.get((t5 - 4, 1), 0)
            def fld(s, t5=t5, par=par, inv1_done_st=inv1_done_st,
                    pe_guard=pe_guard, vv_guard=vv_guard):
                s.wait_ge(st2, inv1_done_st)
                if pe_guard > 0:
                    s.wait_ge(pe, pe_guard)
                if vv_guard > 0:
                    s.wait_ge(vv, vv_guard)
                s.dma_start(out=g2t[par][:, 0:128], in_=g2re_in[t5]).then_inc(ld, 16)
                s.dma_start(out=g2t[par][:, 128:256], in_=g2nim_in[t5]).then_inc(ld, 16)
                s.dma_start(out=bre[par][:], in_=Bsp[t5, 0]).then_inc(ld, 16)
                s.dma_start(out=bim[par][:], in_=Bsp[t5, 1]).then_inc(ld, 16)
                s.dma_start(out=pg[par][:], in_=ptok_s.rearrange("(m b) e -> m (b e)", b=N2)
                            [:, t5 * 3072 + (stage + 1) * D: t5 * 3072 + (stage + 2) * D]).then_inc(ld, 16)
            S.emit("sync", fld)
            ld_now = S.bump("ld", 80)
            for dh in range(2):
                bank = 2 + ((t5 * 2 + dh) % 4)
                vv_need = gate_vv.get((t5 - 2, dh), 0)
                def fmm(t, dh=dh, t5=t5, par=par, bank=bank, ld_now=ld_now, vv_need=vv_need,
                        c2k_ev=c2k_ev, inv1_ev=inv1_ev):
                    t.wait_ge(ld, ld_now)
                    t.wait_ge(ev, max(c2k_ev, inv1_ev))
                    if vv_need > 0:
                        t.wait_ge(vv, vv_need)
                    t.matmul(ps[bank][:], g2t[par][:, 0:128], bre[par][:, dh * 512:(dh + 1) * 512],
                             start=True, stop=False)
                    t.matmul(ps[bank][:], g2t[par][:, 128:256], bim[par][:, dh * 512:(dh + 1) * 512],
                             start=False, stop=False)
                    t.matmul(ps[bank][:], ones1[:],
                             (c2kp if t5 % 2 == 0 else c2km)[:, dh * 512:(dh + 1) * 512],
                             start=False, stop=True).then_inc(pe, 1)
                S.emit("tensor", fmm)
                pe_now = S.bump("pe", 1)
                st_g = st_done.get(t5 - 4, 0)
                def fgate(v, dh=dh, par=par, bank=bank, pe_now=pe_now, st_g=st_g):
                    v.wait_ge(pe, pe_now)
                    if st_g > 0 and dh == 0:
                        v.wait_ge(st, st_g)
                    v.tensor_tensor(ot[par][:, dh * 512:(dh + 1) * 512], ps[bank][:],
                                    pg[par][:, dh * 512:(dh + 1) * 512], ALU.mult).then_inc(vv, 1)
                S.emit("vector", fgate)
                gate_vv[(t5, dh)] = S.bump("vv", 1)
            mm_pe[t5] = C["pe"]
            vv_now = C["vv"]
            def fst(s, t5=t5, par=par, vv_now=vv_now, vout=vout):
                s.wait_ge(vv, vv_now)
                s.dma_start(out=vout.rearrange("(m b) d -> m (b d)", b=N2)
                            [:, t5 * D:(t5 + 1) * D], in_=ot[par][:]).then_inc(st, 16)
            S.emit("sync", fst)
            st_done[t5] = S.bump("st", 16)

    emit_hyena(0)
    bar()
    emit_hyena(1)
    bar()

    def emit_l0ffn():
        PB.reset(); PF.reset()
        wTr = PB.alloc(8 * 1024)
        wTk = PB.alloc(8 * 1024)
        wTv = PB.alloc(8 * 1024)
        wT = {"r": wTr, "k": wTk, "v": wTv}
        wload = PB.alloc(1024)
        xnc = PB.alloc(8 * 1025)
        xr = PB.alloc(8 * 1024)
        xk = PB.alloc(8 * 1024)
        rt = PB.alloc(8 * 1024)
        kact = xr
        gt = xnc
        trs = PB.alloc(256)
        tmp = PF.alloc(1024)
        sq = PF.alloc(512)
        mu = sm2[:, 0:16]
        def f(g):
            g.dma_start(out=mu[:, 0:8], in_=mur_in.rearrange("(g p) -> p g", p=128),
                        allow_slow_non_contiguous=True).then_inc(lg, 16)
            g.dma_start(out=mu[:, 8:16], in_=muk_in.rearrange("(g p) -> p g", p=128),
                        allow_slow_non_contiguous=True).then_inc(lg, 16)
        S.emit("gpsimd", f)
        mu_lg = S.bump("lg", 32)
        for n, W in [("r", Wr_in), ("k", Wk_in), ("v", Wv_in)]:
            for bi in range(8):
                pe_g = C["pe"]
                def f(g, W=W, bi=bi, pe_g=pe_g):
                    g.wait_ge(pe, pe_g)
                    g.dma_start(out=wload[:], in_=W[bi * 128:(bi + 1) * 128, :]).then_inc(lg, 16)
                S.emit("gpsimd", f)
                lg_now = S.bump("lg", 16)
                for bj in range(8):
                    pstj = pst if bj % 2 == 0 else pst2
                    ev_need = C["ev"] if bj <= 1 and bi == 0 and n == "r" else C["ev"] - 1
                    def fmm(t, bj=bj, pstj=pstj, lg_now=lg_now, ev_need=ev_need):
                        t.wait_ge(lg, lg_now)
                        if ev_need > 0:
                            t.wait_ge(ev, ev_need)
                        t.transpose(pstj[:], wload[:, bj * 128:(bj + 1) * 128], idt[:]).then_inc(pe, 1)
                    S.emit("tensor", fmm)
                    pe_now = S.bump("pe", 1)
                    def fev(scl, n=n, bi=bi, bj=bj, pstj=pstj, pe_now=pe_now):
                        scl.wait_ge(pe, pe_now)
                        scl.copy(out=wT[n][:, bj * 1024 + bi * 128: bj * 1024 + (bi + 1) * 128],
                                 in_=pstj[:]).then_inc(ev, 1)
                    S.emit("scalar", fev)
                    S.bump("ev", 1)
        wprep_ev = C["ev"]
        NCH = 4; TC = 1024
        for ch in range(NCH):
            t0 = ch * TC
            vv_g = C["vv"]; pe_gd = C["pe"]; ev_g0 = C["ev"]
            def f(s, ch=ch, t0=t0, vv_g=vv_g, pe_gd=pe_gd, ev_g0=ev_g0):
                s.wait_ge(vv, vv_g)
                s.wait_ge(pe, pe_gd)
                s.wait_ge(ev, ev_g0)
                for g in range(8):
                    if ch == 0:
                        s.dma_start(out=xnc[:, g * 1025 + 1: (g + 1) * 1025],
                                    in_=xnT_s[g * 128:(g + 1) * 128, 0:TC]).then_inc(ld, 16)
                    else:
                        s.dma_start(out=xnc[:, g * 1025: (g + 1) * 1025],
                                    in_=xnT_s[g * 128:(g + 1) * 128, t0 - 1: t0 + TC]).then_inc(ld, 16)
            S.emit("sync", f)
            ld_now = S.bump("ld", 16 * 8)
            if ch == 0:
                def f(v, ld_now=ld_now):
                    v.wait_ge(ld, ld_now)
                    for g in range(8):
                        v.memset(xnc[:, g * 1025: g * 1025 + 1], 0.0)
                    v.engine_nop().then_inc(vv, 1)
                S.emit("vector", f)
                S.bump("vv", 1)
            for g in range(8):
                vv_base = C["vv"]
                ev_g = C["ev"]
                def f(v, g=g, ld_now=ld_now, vv_base=vv_base, ev_g=ev_g):
                    v.wait_ge(ld, ld_now)
                    if g == 0:
                        v.wait_ge(vv, vv_base)
                        v.wait_ge(ev, ev_g)
                        v.wait_ge(lg, mu_lg)
                    xp = xnc[:, g * 1025: g * 1025 + 1024]
                    xc = xnc[:, g * 1025 + 1: g * 1025 + 1025]
                    v.tensor_tensor(tmp[:], xc, xp, ALU.subtract).then_inc(vv, 1)
                    v.wait_ge(vv, vv_base + 1)
                    v.scalar_tensor_tensor(xr[:, g * 1024:(g + 1) * 1024], tmp[:],
                                           mu[:, g:g + 1], xp, ALU.mult, ALU.add).then_inc(vv, 1)
                    v.scalar_tensor_tensor(xk[:, g * 1024:(g + 1) * 1024], tmp[:],
                                           mu[:, 8 + g:9 + g], xp, ALU.mult, ALU.add).then_inc(vv, 1)
                S.emit("vector", f)
                S.bump("vv", 3)
            mix_vv = C["vv"]
            for (nm, srcb, dstb) in [("r", xr, rt), ("k", xk, kact)]:
                for bi in range(8):
                    for j in range(2):
                        ev_need = C["ev"]
                        def fmm(t, nm=nm, srcb=srcb, bi=bi, j=j, mix_vv=mix_vv,
                                ev_need=ev_need, wprep_ev=wprep_ev):
                            t.wait_ge(vv, mix_vv)
                            t.wait_ge(ev, max(ev_need, wprep_ev))
                            for kb in range(8):
                                mm = t.matmul(ps[j][:],
                                              wT[nm][:, kb * 1024 + bi * 128: kb * 1024 + (bi + 1) * 128],
                                              srcb[:, kb * 1024 + j * 512: kb * 1024 + j * 512 + 512],
                                              start=(kb == 0), stop=(kb == 7))
                            mm.then_inc(pe, 1)
                        S.emit("tensor", fmm)
                        pe_now = S.bump("pe", 1)
                        vv_need = C["vv"]
                        def fev(scl, nm=nm, dstb=dstb, bi=bi, j=j, pe_now=pe_now, vv_need=vv_need):
                            scl.wait_ge(pe, pe_now)
                            scl.wait_ge(vv, vv_need)
                            piece = dstb[:, bi * 1024 + j * 512: bi * 1024 + j * 512 + 512]
                            if nm == "r":
                                scl.activation(out=piece, in_=ps[j][:], func=AF.Sigmoid).then_inc(ev, 1)
                            else:
                                scl.activation(out=sq[:], in_=ps[j][:], func=AF.Relu)
                                scl.activation(out=piece, in_=sq[:], func=AF.Square).then_inc(ev, 1)
                        S.emit("scalar", fev)
                        S.bump("ev", 1)
            rk_ev = C["ev"]
            for bi in range(8):
                for j in range(2):
                    ev_need = C["ev"]
                    def fmm(t, bi=bi, j=j, rk_ev=rk_ev, ev_need=ev_need):
                        t.wait_ge(ev, max(rk_ev, ev_need))
                        for kb in range(8):
                            mm = t.matmul(ps[j][:],
                                          wT["v"][:, kb * 1024 + bi * 128: kb * 1024 + (bi + 1) * 128],
                                          kact[:, kb * 1024 + j * 512: kb * 1024 + j * 512 + 512],
                                          start=(kb == 0), stop=(kb == 7))
                        mm.then_inc(pe, 1)
                    S.emit("tensor", fmm)
                    pe_now = S.bump("pe", 1)
                    ev_need2 = C["ev"]
                    def fgv(v, bi=bi, j=j, pe_now=pe_now, ev_need2=ev_need2):
                        v.wait_ge(pe, pe_now)
                        v.wait_ge(ev, ev_need2)
                        piece = rt[:, bi * 1024 + j * 512: bi * 1024 + j * 512 + 512]
                        v.tensor_tensor(tmp[:, 0:512], ps[j][:], piece, ALU.mult).then_inc(vv, 1)
                    S.emit("vector", fgv)
                    vv_now = S.bump("vv", 1)
                    pe_g2 = C["pe"]
                    def fge(scl, bi=bi, j=j, vv_now=vv_now, pe_g2=pe_g2):
                        scl.wait_ge(vv, vv_now)
                        scl.wait_ge(pe, pe_g2)
                        scl.activation(out=gt[:, bi * 1025 + j * 512: bi * 1025 + j * 512 + 512],
                                       in_=tmp[:, 0:512], func=AF.Silu).then_inc(ev, 1)
                    S.emit("scalar", fge)
                    S.bump("ev", 1)
            g_ev = C["ev"]
            for it in range(8):
                for jd in range(8):
                    idx2 = it * 8 + jd
                    pstj = pst if idx2 % 2 == 0 else pst2
                    ev_need = C["ev"] if idx2 <= 1 else C["ev"] - 1
                    def fmm(t, it=it, jd=jd, pstj=pstj, g_ev=g_ev, ev_need=ev_need):
                        t.wait_ge(ev, max(g_ev, ev_need))
                        t.transpose(pstj[:], gt[:, jd * 1025 + it * 128: jd * 1025 + (it + 1) * 128],
                                    idt[:]).then_inc(pe, 1)
                    S.emit("tensor", fmm)
                    pe_now = S.bump("pe", 1)
                    st_need = C["st"] - 16 if idx2 >= 2 else 0
                    trsj = trs[:, 0:128] if idx2 % 2 == 0 else trs[:, 128:256]
                    def fev(scl, pstj=pstj, trsj=trsj, pe_now=pe_now, st_need=st_need):
                        scl.wait_ge(pe, pe_now)
                        if st_need > 0:
                            scl.wait_ge(st, st_need)
                        scl.copy(out=trsj, in_=pstj[:]).then_inc(ev, 1)
                    S.emit("scalar", fev)
                    ev_now = S.bump("ev", 1)
                    def fst(s, it=it, jd=jd, t0=t0, trsj=trsj, ev_now=ev_now):
                        s.wait_ge(ev, ev_now)
                        s.dma_start(out=gtok_s[t0 + it * 128: t0 + (it + 1) * 128,
                                               jd * 128:(jd + 1) * 128], in_=trsj).then_inc(st, 16)
                    S.emit("sync", fst)
                    S.bump("st", 16)
    emit_l0ffn()
    bar()

    xn1_res = emit_ln(x_in, xn1T_s, resid=(stok_s, gtok_s, x1_s))
    bar()
    if dbg:
        def fdbgx(s):
            for jj in range(8):
                s.dma_start(out=xn1T_d[jj * 128:(jj + 1) * 128, :],
                            in_=xn1T_s[jj * 128:(jj + 1) * 128, :]).then_inc(st, 16)
        S.emit("sync", fdbgx)
        S.bump("st", 16 * 8)
        bar()

    def emit_l1proj():
        PB.reset_to(8 * T); PF.reset()
        wsb = [PB.alloc(1056) for _ in range(8)]
        t1 = PF.alloc(512)
        t2 = PF.alloc(512)
        stg = PF.alloc(512)
        bcol = sm2[:, 16:25]
        def f(g):
            for k in range(8):
                g.dma_start(out=wsb[k][:], in_=Wxp_in[k * 128:(k + 1) * 128, :]).then_inc(lg, 16)
            g.dma_start(out=bcol[:, 0:8], in_=bxp_in[0:1024].rearrange("(g p) -> p g", p=128),
                        allow_slow_non_contiguous=True).then_inc(lg, 16)
            g.dma_start(out=bcol[0:32, 8:9], in_=bxp_in[1024:1056].unsqueeze(1),
                        allow_slow_non_contiguous=True).then_inc(lg, 16)
        S.emit("gpsimd", f)
        w_ld = S.bump("lg", 16 * 10)
        for j in range(T // 512):
            ld_now = C["ld"]
            for i in range(9):
                hi = 128 if i < 8 else 32
                jb = i % 2
                ev_need = C["ev"]; vv_need = C["vv"]
                def fmm(t, i=i, j=j, jb=jb, hi=hi, w_ld=w_ld,
                        ev_need=ev_need, vv_need=vv_need):
                    t.wait_ge(lg, w_ld)
                    t.wait_ge(ev, ev_need)
                    t.wait_ge(vv, vv_need)
                    for k in range(8):
                        mm = t.matmul(ps[jb][0:hi, :], wsb[k][:, i * 128:i * 128 + hi],
                                      xn1_res[k][:, j * 512:(j + 1) * 512],
                                      start=(k == 0), stop=(k == 7))
                    mm.then_inc(pe, 1)
                S.emit("tensor", fmm)
                pe_now = S.bump("pe", 1)
                if i < 8:
                    st_need = C["st"]
                    def fev(scl, i=i, jb=jb, pe_now=pe_now, st_need=st_need):
                        scl.wait_ge(pe, pe_now)
                        scl.wait_ge(st, st_need)
                        scl.activation(out=t1[:], in_=ps[jb][:], func=AF.Abs, bias=bcol[:, i:i + 1])
                        scl.activation(out=t1[:], in_=t1[:], func=AF.Exp, scale=-1.0)
                        scl.activation(out=t1[:], in_=t1[:], func=AF.Ln, bias=1.0)
                        scl.activation(out=t2[:], in_=ps[jb][:], func=AF.Relu,
                                       bias=bcol[:, i:i + 1]).then_inc(ev, 1)
                    S.emit("scalar", fev)
                    ev_now = S.bump("ev", 1)
                    def fv(v, ev_now=ev_now):
                        v.wait_ge(ev, ev_now)
                        v.tensor_tensor(stg[:], t1[:], t2[:], ALU.add).then_inc(vv, 1)
                    S.emit("vector", fv)
                    vv_now = S.bump("vv", 1)
                    def fst(s, i=i, j=j, vv_now=vv_now):
                        s.wait_ge(vv, vv_now)
                        s.dma_start(out=dT_s[i * 128:(i + 1) * 128, j * 512:(j + 1) * 512],
                                    in_=stg[:]).then_inc(st, 16)
                    S.emit("sync", fst)
                    S.bump("st", 16)
                else:
                    st_need = C["st"]
                    def fev(v, jb=jb, pe_now=pe_now, st_need=st_need):
                        v.wait_ge(pe, pe_now)
                        v.wait_ge(st, st_need)
                        v.tensor_scalar(stg[0:32, :], ps[jb][0:32, :], bcol[0:32, 8:9], None,
                                        ALU.add).then_inc(vv, 1)
                    S.emit("vector", fev)
                    vv_now = S.bump("vv", 1)
                    def fst(s, j=j, vv_now=vv_now):
                        s.wait_ge(vv, vv_now)
                        s.dma_start(out=bc_s[:, j * 512:(j + 1) * 512],
                                    in_=stg[0:32, :]).then_inc(st, 16)
                    S.emit("sync", fst)
                    S.bump("st", 16)
    emit_l1proj()
    bar()

    def emit_scan():
        PB.reset(); PF.reset()
        at_ = PB.alloc(256).bitcast(F32)
        xg = PB.alloc(T)
        dx = PB.alloc(T)
        brepb = [PB.alloc(T), PB.alloc(T)]
        crepb = [PB.alloc(T), PB.alloc(T)]
        util = PB.alloc(T)
        htile = PB.alloc(T)
        yout = PB.alloc(T)
        dpc8 = PB.alloc(16).bitcast(F32)
        PF.reset()
        dg = PF.alloc(4096)
        atileb = [PF.alloc(4096), PF.alloc(4096)]
        yacc = PF.alloc(4096)
        we = [PB.alloc(1024) for _ in range(8)]
        wload = PB.alloc(1024)
        dr = PB.alloc(1024)
        bout_sb = PB.alloc(1024)
        xnh = [PB.alloc(1024) for _ in range(2)]
        g1t = [PB.alloc(1024) for _ in range(2)]
        wo = [brepb[k // 4][:, (k % 4) * 1024:(k % 4 + 1) * 1024] for k in range(8)]
        yh = [crepb[0][:, 0:1024], crepb[0][:, 1024:2048]]
        s1t = [crepb[0][:, 2048:3072], crepb[0][:, 3072:4096]]
        x1f = crepb[1].bitcast(F32)
        x1t = [x1f[:, 0:1024], x1f[:, 1024:2048]]
        otf = xg.bitcast(F32)
        ott = [otf[:, 0:1024], otf[:, 1024:2048]]
        g1r = [dx[:, 0:1024], dx[:, 1024:2048]]
        sgt = dx[:, 2048:3072]
        ffn_state = {"mm_pe": {}, "silu_ev": {}, "g1_st": {}, "weadd_vv": 0}

        def emit_weight_prep():
            def f(g):
                g.dma_start(out=dr[0:64, :], in_=dis_in[:]).then_inc(lg, 16)
                g.dma_start(out=dr[64:128, :], in_=dis_in[:]).then_inc(lg, 16)
                g.dma_start(out=bout_sb[0:1, :], in_=bout_in[None, :]).then_inc(lg, 16)
            S.emit("gpsimd", f)
            dr_lg = S.bump("lg", 16 * 3)
            for bi in range(8):
                pe_g = C["pe"]
                def f(g, bi=bi, pe_g=pe_g):
                    g.wait_ge(pe, pe_g)
                    g.dma_start(out=wload[:], in_=wei_in[bi * 128:(bi + 1) * 128, :]).then_inc(lg, 16)
                S.emit("gpsimd", f)
                lg_now = S.bump("lg", 16)
                for bj in range(8):
                    pstj = pst if bj % 2 == 0 else pst2
                    ev_need = C["ev"] if bi == 0 and bj <= 1 else C["ev"] - 1
                    def fmm(t, bj=bj, pstj=pstj, lg_now=lg_now, ev_need=ev_need):
                        t.wait_ge(lg, lg_now)
                        if ev_need > 0:
                            t.wait_ge(ev, ev_need)
                        t.transpose(pstj[:], wload[:, bj * 128:(bj + 1) * 128], idt[:]).then_inc(pe, 1)
                    S.emit("tensor", fmm)
                    pe_now = S.bump("pe", 1)
                    def fev(scl, bi=bi, bj=bj, pstj=pstj, pe_now=pe_now):
                        scl.wait_ge(pe, pe_now)
                        scl.copy(out=we[bj][:, bi * 128:(bi + 1) * 128], in_=pstj[:]).then_inc(ev, 1)
                    S.emit("scalar", fev)
                    S.bump("ev", 1)
            wt_ev = C["ev"]
            def f(v, wt_ev=wt_ev, dr_lg=dr_lg):
                v.wait_ge(ev, wt_ev)
                v.wait_ge(lg, dr_lg)
                for k in range(8):
                    v.tensor_tensor(we[k][:], we[k][:], dr[:], ALU.add).then_inc(vv, 1 if k == 7 else 0)
            S.emit("vector", f)
            ffn_state["weadd_vv"] = S.bump("vv", 1)

        def emit_ffn_tile(t):
            par = t % 2
            pe_guard = ffn_state["mm_pe"].get(t - 2, 0)
            def fldx(s, t=t, par=par, pe_guard=pe_guard):
                if pe_guard > 0:
                    s.wait_ge(pe, pe_guard)
                for k in range(8):
                    s.dma_start(out=xnh[par][:, k * 128:(k + 1) * 128],
                                in_=xn1T_s[k * 128:(k + 1) * 128,
                                           t * 128:(t + 1) * 128]).then_inc(ld, 16)
            S.emit("sync", fldx)
            ld_now = S.bump("ld", 16 * 8)
            for j in range(2):
                ev_need = ffn_state["silu_ev"].get((t - 2, j), 0)
                weadd_vv = ffn_state["weadd_vv"]
                def fmm(tn, t=t, j=j, par=par, ld_now=ld_now, ev_need=ev_need, weadd_vv=weadd_vv):
                    tn.wait_ge(ld, ld_now)
                    tn.wait_ge(vv, weadd_vv)
                    if ev_need > 0:
                        tn.wait_ge(ev, ev_need)
                    for k in range(8):
                        mm = tn.matmul(ps[2 + j][:], xnh[par][:, k * 128:(k + 1) * 128],
                                       we[k][:, j * 512:(j + 1) * 512],
                                       start=(k == 0), stop=(k == 7))
                    mm.then_inc(pe, 1)
                S.emit("tensor", fmm)
                pe_now = S.bump("pe", 1)
                st_guard = ffn_state["g1_st"].get(t - 2, 0)
                def fsil(scl, t=t, j=j, par=par, pe_now=pe_now, st_guard=st_guard):
                    scl.wait_ge(pe, pe_now)
                    if st_guard > 0 and j == 0:
                        scl.wait_ge(st2, st_guard)
                    scl.activation(out=g1t[par][:, j * 512:(j + 1) * 512],
                                   in_=ps[2 + j][:], func=AF.Silu).then_inc(ev, 1)
                S.emit("scalar", fsil)
                ffn_state["silu_ev"][(t, j)] = S.bump("ev", 1)
            ffn_state["mm_pe"][t] = C["pe"]
            ev_now = C["ev"]
            def fstg(scl, t=t, par=par, ev_now=ev_now):
                scl.wait_ge(ev, ev_now)
                scl.dma_start(out=g1_s[t * 128:(t + 1) * 128, :], in_=g1t[par][:]).then_inc(st2, 16)
            S.emit("sync", fstg)
            ffn_state["g1_st"][t] = S.bump("st2", 16)
        def f(g):
            g.dma_start(out=at_[:], in_=Alog_in.rearrange("(g p) n -> p g n", p=128),
                        allow_slow_non_contiguous=True).then_inc(lg, 16)
            g.dma_start(out=dpc8[:], in_=Dp_in.rearrange("(g p) -> p g", p=128),
                        allow_slow_non_contiguous=True).then_inc(lg, 16)
        S.emit("gpsimd", f)
        a_ld = S.bump("lg", 32)
        def f(scl, a_ld=a_ld):
            scl.wait_ge(lg, a_ld)
            scl.activation(out=at_[:], in_=at_[:], func=AF.Exp, scale=1.0).then_inc(sc, 1)
        S.emit("scalar", f)
        sc_now = S.bump("sc", 1)
        def f(v, sc_now=sc_now):
            v.wait_ge(sc, sc_now)
            v.tensor_scalar(at_[:], at_[:], -1.0, None, ALU.mult).then_inc(vv, 1)
        S.emit("vector", f)
        S.bump("vv", 1)
        aprep_vv = C["vv"]
        bc_done = {}
        scan_done = {}
        for g in range(8):
            vv_g = C["vv"]; sc_g = C["sc"]
            def f(s, g=g, vv_g=vv_g, sc_g=sc_g):
                s.wait_ge(vv, vv_g)
                s.wait_ge(sc, sc_g)
                s.dma_start(out=dg[:], in_=dT_s[g * 128:(g + 1) * 128, :]).then_inc(ld, 16)
                s.dma_start(out=xg[:], in_=xn1T_s[g * 128:(g + 1) * 128, :]).then_inc(ld, 16)
            S.emit("sync", f)
            ld_now = S.bump("ld", 32)
            def f(v, ld_now=ld_now):
                v.wait_ge(ld, ld_now)
                v.tensor_tensor(dx[:], dg[:], xg[:], ALU.mult).then_inc(vv, 1)
            S.emit("vector", f)
            S.bump("vv", 1)
            for n in range(16):
                par = (g * 16 + n) % 2
                brep = brepb[par]; crep = crepb[par]
                i2 = g * 16 + n - 2
                vv_bc = bc_done.get(i2, 0)
                def f(gp, n=n, par=par, vv_bc=vv_bc):
                    if vv_bc > 0:
                        gp.wait_ge(vv, vv_bc)
                    gp.dma_start(out=brepb[par][:], in_=bc_s[n:n + 1, :].broadcast_to([128, T])).then_inc(lg, 16)
                    gp.dma_start(out=crepb[par][:], in_=bc_s[16 + n:17 + n, :].broadcast_to([128, T])).then_inc(lg, 16)
                S.emit("gpsimd", f)
                ld_bc = S.bump("lg", 32)
                atile = atileb[par]
                vv_a = max(scan_done.get(g * 16 + n - 2, 0), aprep_vv)
                def f(scl, g=g, n=n, ld_now=ld_now, vv_a=vv_a, atile=atile):
                    scl.wait_ge(ld, ld_now)
                    scl.wait_ge(vv, vv_a)
                    scl.activation(out=atile[:], in_=dg[:], func=AF.Exp,
                                   scale=at_[:, g * 16 + n: g * 16 + n + 1]).then_inc(sc, 1)
                S.emit("scalar", f)
                sc_now = S.bump("sc", 1)
                vb = C["vv"]
                def f(v, g=g, n=n, ld_bc=ld_bc, sc_now=sc_now, vb=vb, brep=brep, crep=crep, atile=atile):
                    v.wait_ge(lg, ld_bc)
                    v.tensor_tensor(util[:], dx[:], brep[:], ALU.mult).then_inc(vv, 1)
                    v.wait_ge(vv, vb + 1)
                    v.wait_ge(sc, sc_now)
                    v.tensor_tensor_scan(htile[:], atile[:], util[:], 0.0,
                                         ALU.mult, ALU.add).then_inc(vv, 1)
                    v.wait_ge(vv, vb + 2)
                    if n == 0:
                        v.tensor_tensor(yacc[:], htile[:], crep[:], ALU.mult).then_inc(vv, 1)
                    else:
                        v.tensor_tensor(util[:], htile[:], crep[:], ALU.mult).then_inc(vv, 1)
                        v.wait_ge(vv, vb + 3)
                        v.tensor_tensor(yacc[:], yacc[:], util[:], ALU.add).then_inc(vv, 1)
                S.emit("vector", f)
                scan_done[g * 16 + n] = vb + 2
                bc_done[g * 16 + n] = S.bump("vv", 3 if n == 0 else 4)
            st_g = C["st"]
            vb2 = C["vv"]
            def f(v, g=g, st_g=st_g, vb2=vb2):
                if st_g > 0:
                    v.wait_ge(st, st_g)
                v.wait_ge(vv, vb2)
                v.scalar_tensor_tensor(yout[:], xg[:], dpc8[:, g:g + 1], yacc[:],
                                       ALU.mult, ALU.add).then_inc(vv, 1)
            S.emit("vector", f)
            vv_y = S.bump("vv", 1)
            def f(s, g=g, vv_y=vv_y):
                s.wait_ge(vv, vv_y)
                s.dma_start(out=yT_s[g * 128:(g + 1) * 128, :], in_=yout[:]).then_inc(st, 16)
            S.emit("sync", f)
            S.bump("st", 16)
            if g == 0:
                emit_weight_prep()
            else:
                for t_ in range((g - 1) * 5, min(32, g * 5)):
                    emit_ffn_tile(t_)

        scan_st = C["st"]
        def fwo(g, scan_st=scan_st):
            g.wait_ge(st, scan_st)
            for k in range(8):
                g.dma_start(out=wo[k][:], in_=Wout_in[k * 128:(k + 1) * 128, :]).then_inc(lg, 16)
        S.emit("gpsimd", fwo)
        wo_lg = S.bump("lg", 16 * 8)
        op_pe = {}
        s1_ev = {}
        comb_vv = {}
        out_st = {}
        for t in range(32):
            par = t % 2
            pe_g = op_pe.get(t - 2, 0)
            def fldy(s, t=t, par=par, scan_st=scan_st, pe_g=pe_g):
                s.wait_ge(st, scan_st)
                if pe_g > 0:
                    s.wait_ge(pe, pe_g)
                for k in range(8):
                    s.dma_start(out=yh[par][:, k * 128:(k + 1) * 128],
                                in_=yT_s[k * 128:(k + 1) * 128,
                                         t * 128:(t + 1) * 128]).then_inc(ld, 16)
            S.emit("sync", fldy)
            ldy_now = S.bump("ld", 16 * 8)
            cg = comb_vv.get(t - 2, 0)
            def fldx1(s, t=t, par=par, cg=cg):
                if cg > 0:
                    s.wait_ge(vv, cg)
                s.dma_start(out=x1t[par][:], in_=x1_s[t * 128:(t + 1) * 128, :]).then_inc(ld, 16)
            S.emit("sync", fldx1)
            ldx_now = S.bump("ld", 16)
            g1_need = ffn_state["g1_st"][t]
            def fldg1(scl, t=t, par=par, cg=cg, g1_need=g1_need):
                scl.wait_ge(st2, g1_need)
                if cg > 0:
                    scl.wait_ge(vv, cg)
                scl.dma_start(out=g1r[par][:], in_=g1_s[t * 128:(t + 1) * 128, :]).then_inc(ld, 16)
            S.emit("sync", fldg1)
            ldg_now = S.bump("ld", 16)
            for j in range(2):
                ev_need = s1_ev.get((t - 2, j), 0)
                def fmo(tn, t=t, j=j, par=par, ldy_now=ldy_now, ev_need=ev_need, wo_lg=wo_lg):
                    tn.wait_ge(ld, ldy_now)
                    tn.wait_ge(lg, wo_lg)
                    if ev_need > 0:
                        tn.wait_ge(ev, ev_need)
                    for k in range(8):
                        tn.matmul(ps[j][:], yh[par][:, k * 128:(k + 1) * 128],
                                  wo[k][:, j * 512:(j + 1) * 512],
                                  start=(k == 0), stop=False)
                    tn.matmul(ps[j][:], ones1[0:1, :], bout_sb[0:1, j * 512:(j + 1) * 512],
                              start=False, stop=True).then_inc(pe, 1)
                S.emit("tensor", fmo)
                pe_now = S.bump("pe", 1)
                def fse(scl, j=j, par=par, pe_now=pe_now):
                    scl.wait_ge(pe, pe_now)
                    scl.copy(out=s1t[par][:, j * 512:(j + 1) * 512], in_=ps[j][:]).then_inc(ev, 1)
                S.emit("scalar", fse)
                s1_ev[(t, j)] = S.bump("ev", 1)
            op_pe[t] = C["pe"]
            sev = s1_ev[(t, 1)]
            stg_ = out_st.get(t - 2, 0)
            vb = C["vv"]
            def fcomb(v, t=t, par=par, sev=sev, stg_=stg_, vb=vb,
                      ldx_now=ldx_now, ldg_now=ldg_now):
                v.wait_ge(ev, sev)
                v.wait_ge(ld, max(ldx_now, ldg_now))
                if stg_ > 0:
                    v.wait_ge(st, stg_)
                v.tensor_tensor(sgt[:], s1t[par][:], g1r[par][:], ALU.mult).then_inc(vv, 1)
                v.wait_ge(vv, vb + 1)
                v.tensor_tensor(ott[par][:], x1t[par][:], sgt[:], ALU.add).then_inc(vv, 1)
            S.emit("vector", fcomb)
            comb_vv[t] = S.bump("vv", 2)
            def fsto(s, t=t, par=par, cvv=comb_vv[t]):
                s.wait_ge(vv, cvv)
                s.dma_start(out=OUT[t * 128:(t + 1) * 128, :], in_=ott[par][:]).then_inc(st, 16)
            S.emit("sync", fsto)
            out_st[t] = S.bump("st", 16)
    emit_scan()

    def emit_finale():
        PB.reset(); PF.reset()
        wo = [PB.alloc(1024) for _ in range(8)]
        we = [PB.alloc(1024) for _ in range(8)]
        wload = PB.alloc(1024)
        dr = PB.alloc(1024)
        brep = PB.alloc(1024)
        yh = [PB.alloc(2048) for _ in range(8)]
        xnh = [PB.alloc(2048) for _ in range(8)]
        s1t = PB.alloc(1024)
        g1t = PB.alloc(1024)
        sgt = PB.alloc(1024)
        x1t = [PF.alloc(1024) for _ in range(2)]
        ott = [PF.alloc(1024) for _ in range(2)]
        def f(g):
            for k in range(8):
                g.dma_start(out=wo[k][:], in_=Wout_in[k * 128:(k + 1) * 128, :]).then_inc(lg, 16)
            g.dma_start(out=brep[:], in_=bout_in[None, :].broadcast_to([128, D])).then_inc(lg, 16)
            g.dma_start(out=dr[0:64, :], in_=dis_in[:]).then_inc(lg, 16)
            g.dma_start(out=dr[64:128, :], in_=dis_in[:]).then_inc(lg, 16)
        S.emit("gpsimd", f)
        w_ld = S.bump("lg", 16 * 11)
        for bi in range(8):
            pe_g = C["pe"]
            def f(g, bi=bi, pe_g=pe_g):
                g.wait_ge(pe, pe_g)
                g.dma_start(out=wload[:], in_=wei_in[bi * 128:(bi + 1) * 128, :]).then_inc(lg, 16)
            S.emit("gpsimd", f)
            lg_now = S.bump("lg", 16)
            for bj in range(8):
                ev_need = C["ev"]
                def fmm(t, bj=bj, lg_now=lg_now, ev_need=ev_need):
                    t.wait_ge(lg, lg_now)
                    t.wait_ge(ev, ev_need)
                    t.transpose(pst[:], wload[:, bj * 128:(bj + 1) * 128], idt[:]).then_inc(pe, 1)
                S.emit("tensor", fmm)
                pe_now = S.bump("pe", 1)
                def fev(scl, bi=bi, bj=bj, pe_now=pe_now):
                    scl.wait_ge(pe, pe_now)
                    scl.copy(out=we[bj][:, bi * 128:(bi + 1) * 128], in_=pst[:]).then_inc(ev, 1)
                S.emit("scalar", fev)
                S.bump("ev", 1)
        wt_ev = C["ev"]
        def f(v, wt_ev=wt_ev, w_ld=w_ld):
            v.wait_ge(ev, wt_ev)
            v.wait_ge(lg, w_ld)
            for k in range(8):
                v.tensor_tensor(we[k][:], we[k][:], dr[:], ALU.add).then_inc(vv, 1 if k == 7 else 0)
        S.emit("vector", f)
        weff_vv = S.bump("vv", 1)
        fout_done = {}
        for half in range(2):
            pe_g = C["pe"]
            def fldh(s, half=half, pe_g=pe_g):
                s.wait_ge(pe, pe_g)
                for k in range(8):
                    s.dma_start(out=yh[k][:],
                                in_=yT_s[k * 128:(k + 1) * 128,
                                         half * 2048:(half + 1) * 2048]).then_inc(ld, 16)
                    s.dma_start(out=xnh[k][:],
                                in_=xn1T_s[k * 128:(k + 1) * 128,
                                           half * 2048:(half + 1) * 2048]).then_inc(ld, 16)
            S.emit("sync", fldh)
            ldh = S.bump("ld", 16 * 16)
            for il in range(16):
                i = half * 16 + il
                par = i % 2
                ld_now = C["ld"]
                x1_guard = fout_done.get(i - 2, 0)
                def fldx(s, i=i, par=par, x1_guard=x1_guard):
                    if x1_guard > 0:
                        s.wait_ge(vv, x1_guard)
                    s.dma_start(out=x1t[par][:], in_=x1_s[i * 128:(i + 1) * 128, :]).then_inc(ld, 16)
                S.emit("sync", fldx)
                ld_now = S.bump("ld", 16)
                for j in range(2):
                    ev_need = C["ev"]
                    vv_need = C["vv"]
                    def fmm(t, il=il, j=j, ldh=ldh, ev_need=ev_need, vv_need=vv_need, weff_vv=weff_vv):
                        t.wait_ge(ld, ldh)
                        t.wait_ge(vv, max(weff_vv, vv_need))
                        t.wait_ge(ev, ev_need)
                        for k in range(8):
                            mm = t.matmul(ps[j][:], yh[k][:, il * 128:(il + 1) * 128],
                                          wo[k][:, j * 512:(j + 1) * 512],
                                          start=(k == 0), stop=(k == 7))
                        mm.then_inc(pe, 1)
                        for k in range(8):
                            mm = t.matmul(ps[2 + j][:], xnh[k][:, il * 128:(il + 1) * 128],
                                          we[k][:, j * 512:(j + 1) * 512],
                                          start=(k == 0), stop=(k == 7))
                        mm.then_inc(pe, 1)
                    S.emit("tensor", fmm)
                    pe_now = S.bump("pe", 2)
                    vv_g2 = C["vv"]
                    def fev(v, j=j, pe_now=pe_now, vv_g2=vv_g2):
                        v.wait_ge(pe, pe_now)
                        if j == 0:
                            v.wait_ge(vv, vv_g2)
                        v.tensor_tensor(s1t[:, j * 512:(j + 1) * 512], ps[j][:],
                                        brep[:, j * 512:(j + 1) * 512], ALU.add).then_inc(vv, 1)
                    S.emit("vector", fev)
                    S.bump("vv", 1)
                    vv_g3 = C["vv"]
                    def fg(scl, j=j, pe_now=pe_now, vv_g3=vv_g3):
                        scl.wait_ge(pe, pe_now)
                        if j == 0:
                            scl.wait_ge(vv, vv_g3)
                        scl.activation(out=g1t[:, j * 512:(j + 1) * 512], in_=ps[2 + j][:],
                                       func=AF.Silu).then_inc(ev, 1)
                    S.emit("scalar", fg)
                    S.bump("ev", 1)
                mm_done_vv = C["vv"]; mm_done_ev = C["ev"]
                st_g = C["st"] - 16 if i >= 2 else 0
                vb2 = C["vv"]
                def fout(v, par=par, ld_now=ld_now, mm_done_vv=mm_done_vv,
                         mm_done_ev=mm_done_ev, st_g=st_g, vb2=vb2):
                    v.wait_ge(vv, mm_done_vv)
                    v.wait_ge(ev, mm_done_ev)
                    v.wait_ge(ld, ld_now)
                    if st_g > 0:
                        v.wait_ge(st, st_g)
                    v.tensor_tensor(sgt[:], s1t[:], g1t[:], ALU.mult).then_inc(vv, 1)
                    v.wait_ge(vv, vb2 + 1)
                    v.tensor_tensor(ott[par][:], x1t[par][:], sgt[:], ALU.add).then_inc(vv, 1)
                S.emit("vector", fout)
                vv_out = S.bump("vv", 2)
                fout_done[i] = vv_out
                def fst(s, i=i, par=par, vv_out=vv_out):
                    s.wait_ge(vv, vv_out)
                    s.dma_start(out=OUT[i * 128:(i + 1) * 128, :], in_=ott[par][:]).then_inc(st, 16)
                S.emit("sync", fst)
                S.bump("st", 16)


    final_st = C["st"]
    def f(s, final_st=final_st):
        s.wait_ge(st, final_st)
    S.emit("sync", f)

    block_cm = p.nc.Block(); block = block_cm.__enter__()
    S.install(block)
    block_cm.__exit__(None, None, None)
    return p


def kernel_fused(**inputs):
    ident = CONST["IDENT"]
    x = np.ascontiguousarray(np.asarray(inputs["L0_attn_x"], dtype=np.float32))
    nco = x.shape[0]
    p = build_mega()
    filt = np.ascontiguousarray(np.asarray(inputs["L0_attn_filter_time"],
                                           dtype=np.float32)[:, 0])
    base = {"proj_w": inputs["L0_attn_proj_w"], "proj_b": inputs["L0_attn_proj_b"],
            "filt": filt,
            "W_r": inputs["L0_ffn_W_r"], "W_k": inputs["L0_ffn_W_k"],
            "W_v": inputs["L0_ffn_W_v"], "mu_r": inputs["L0_ffn_mu_r"],
            "mu_k": inputs["L0_ffn_mu_k"], "A_log": inputs["L1_attn_A_log"],
            "Dp": inputs["L1_attn_Dp"], "W_xproj": inputs["L1_attn_W_xproj"],
            "b_xproj": inputs["L1_attn_b_xproj"], "W_out": inputs["L1_attn_W_out"],
            "b_out": inputs["L1_attn_b_out"], "weight": inputs["L1_ffn_weight"],
            "disha": inputs["L1_ffn_disha"], "IDENT": ident,
            **{k: CONST[k] for k in ["F128RE", "F128IM", "S2FWD", "S2FILT",
                                     "S1INV", "G2RE", "G2NIM", "SIGNONE",
                                     "BFLY1", "BFLY2", "BFLYH1", "BFLYH2"]}}
    maps = [{"x": x[c], **base} for c in range(nco)]
    res = run_launch(p, maps)
    return np.stack([np.asarray(res.results[c]["out"], dtype=np.float32)
                     for c in range(nco)])


def kernel(**inputs):
    return kernel_fused(**inputs)


# version 19
# speedup vs baseline: 1.2322x; 1.2322x over previous
import numpy as np
import ml_dtypes
import concourse.bass as bass
import concourse.mybir as mybir
from concourse.bass_utils import run_bass_kernel_spmd

F32 = mybir.dt.float32
BF16 = mybir.dt.bfloat16
AF = mybir.ActivationFunctionType
ALU = mybir.AluOpType
NBF = ml_dtypes.bfloat16

T, D = 4096, 1024
N1, N2 = 128, 32
K2F, K2I = 17, 16
NCORES = 8

def _fft_consts():
    n1 = np.arange(N1); k1 = np.arange(N1)
    F128 = np.exp(-2j * np.pi * np.outer(n1, k1) / N1)
    n2 = np.arange(N2); k2 = np.arange(K2F)
    W32 = np.exp(-2j * np.pi * np.outer(n2, k2) / N2)
    tw = np.exp(-2j * np.pi * np.outer(n2, np.arange(N1)) / T)
    M2 = tw.T[:, :, None] * W32[None, :, :]
    t5 = np.arange(N2)
    EI = np.exp(2j * np.pi * np.outer(np.arange(K2I), t5) / N2)
    m = np.arange(N1)
    Finv = np.exp(2j * np.pi * np.outer(np.arange(N1), m) / N1)
    twi = np.exp(2j * np.pi * np.outer(t5, np.arange(N1)) / T)
    G2 = twi[:, :, None] * Finv[None, :, :]
    return F128, M2, EI, G2

_F128, _M2, _EI, _G2 = _fft_consts()

def _pack_s2_fwd(M2, scale=1.0):
    out = np.zeros((N1, 128, 128), np.float32)
    M2re = (M2.real * scale).astype(np.float32)
    M2im = (M2.imag * scale).astype(np.float32)
    for n2 in range(N2):
        for ci in range(2):
            for s2 in range(2):
                p = n2 * 4 + ci * 2 + s2
                for k2 in range(K2F):
                    out[:, p, 0 * 64 + s2 * 32 + k2] = M2re[:, n2, k2] if ci == 0 else -M2im[:, n2, k2]
                    out[:, p, 1 * 64 + s2 * 32 + k2] = M2im[:, n2, k2] if ci == 0 else M2re[:, n2, k2]
    return out.astype(NBF)

def _pack_s1_inv(EI):
    out = np.zeros((2, 128, 128), np.float32)
    EIre = EI.real.astype(np.float32); EIim = EI.imag.astype(np.float32)
    for k2 in range(K2I):
        for ci in range(2):
            for s4 in range(4):
                p = ci * 64 + k2 * 4 + s4
                for t5 in range(N2):
                    i = s4 * 32 + t5
                    out[0, p, i] = EIre[k2, t5] if ci == 0 else -EIim[k2, t5]
                    out[1, p, i] = EIim[k2, t5] if ci == 0 else EIre[k2, t5]
    return out.astype(NBF)

def _bfly():
    b1 = np.zeros((128, 128), np.float32)
    b2 = np.zeros((128, 128), np.float32)
    for i in range(64):
        b1[i, i] = 1.0
        b1[i + 64, i] = -1.0
        b2[i, i + 64] = 1.0
        b2[i + 64, i + 64] = 1.0
    bh = np.zeros((128, 128), np.float32)
    bh2 = np.zeros((128, 128), np.float32)
    for s in range(4):
        bh[s, s] = 1.0
        bh[s + 4, s] = -1.0
        bh2[s, s + 4] = 1.0
        bh2[s + 4, s + 4] = 1.0
    return b1.astype(NBF), b2.astype(NBF), bh.astype(NBF), bh2.astype(NBF)

_B1, _B2, _BH1, _BH2 = _bfly()

CONST = {
    "BFLY1": _B1, "BFLY2": _B2, "BFLYH1": _BH1, "BFLYH2": _BH2,
    "F128RE": _F128.real.astype(np.float32).astype(NBF),
    "F128IM": _F128.imag.astype(np.float32).astype(NBF),
    "S2FWD": _pack_s2_fwd(_M2, 1.0),
    "S2FILT": _pack_s2_fwd(_M2, 2.0 / T),
    "S1INV": _pack_s1_inv(_EI),
    "G2RE": _G2.real.astype(np.float32).astype(NBF),
    "G2NIM": (-_G2.imag).astype(np.float32).astype(NBF),
    "IDENT": np.eye(128, dtype=NBF),
    "SIGNONE": np.stack([np.ones(128, np.float32),
                         -np.ones(128, np.float32)]).astype(NBF),
}

class Prog:
    def __init__(self, name):
        self.nc = bass.Bass("TRN2", target_bir_lowering=False, debug=False)
        self.name = name
        self.ins = {}
        self.outs = {}

    def inp(self, name, shape, dtype=F32):
        ap = self.nc.dram_tensor(name, list(shape), dtype, kind="ExternalInput").ap()
        self.ins[name] = ap
        return ap

    def out(self, name, shape, dtype=F32):
        ap = self.nc.dram_tensor(name, list(shape), dtype, kind="ExternalOutput").ap()
        self.outs[name] = ap
        return ap

    def sb(self, name, shape, dtype=F32):
        return self.nc.alloc_sbuf_tensor(name, list(shape), dtype).ap()

    def ps(self, name, shape, dtype=F32):
        return self.nc.alloc_psum_tensor(name, list(shape), dtype).ap()

    def sem(self, name):
        return self.nc.alloc_semaphore(name=name)


def run_launch(prog, per_core_maps, trace=False):
    maps = [{k: per_core_maps[c][k] for k in prog.ins} for c in range(len(per_core_maps))]
    return run_bass_kernel_spmd(prog.nc, maps, core_ids=list(range(len(per_core_maps))), trace=trace)


def build_ln(name):
    p = Prog(name)
    x = p.inp("x", [T, D])
    ident = p.inp("IDENT", [128, 128], BF16)
    xnT = p.out("xnT", [D, T], BF16)

    xt = p.sb("xt", [128, D], F32)
    xnb = p.sb("xnb", [128, D], BF16)
    s1 = p.sb("s1", [128, 1], F32)
    mneg = p.sb("mneg", [128, 1], F32)
    msq = p.sb("msq", [128, 1], F32)
    ssq = p.sb("ssq", [128, 1], F32)
    var = p.sb("var", [128, 1], F32)
    rstd = p.sb("rstd", [128, 1], F32)
    dummy = p.sb("scrd", [128, D], F32)
    idt = p.sb("idt", [128, 128], BF16)
    trs = p.sb("trs", [128, 128], BF16)
    pst = p.ps("pst", [128, 128], BF16)

    eps = p.sb("eps", [128, 1], F32)
    ld = p.sem("ld"); sc = p.sem("sc"); ve = p.sem("ve")
    pe = p.sem("pe"); ev = p.sem("ev"); st = p.sem("st"); ini = p.sem("ini")
    NT = T // 128; ND = D // 128

    block_cm = p.nc.Block(); block = block_cm.__enter__()

    @block.sync
    def _(sync):
        sync.dma_start(out=idt[:], in_=ident[:]).then_inc(ld, 16)
        for i in range(NT):
            if i >= 1:
                sync.wait_ge(pe, ND * i)
            sync.dma_start(out=xt[:], in_=x[i * 128:(i + 1) * 128, :]).then_inc(ld, 16)
            for j in range(ND):
                sync.wait_ge(ev, i * ND + j + 1)
                sync.dma_start(out=xnT[j * 128:(j + 1) * 128, i * 128:(i + 1) * 128],
                               in_=trs[:]).then_inc(st, 16)
        sync.wait_ge(st, 16 * ND * NT)

    @block.gpsimd
    def _(gpsimd):
        gpsimd.memset(eps[:], 1e-5).then_inc(ini, 1)

    @block.scalar
    def _(scalar):
        scalar.wait_ge(ini, 1)
        for i in range(NT):
            scalar.wait_ge(ld, 16 * (i + 2))
            if i >= 1:
                scalar.wait_ge(ve, 2 * i)
            scalar.activation(out=dummy[:], in_=xt[:], func=AF.Copy, accum_out=s1[:])
            scalar.mul(out=mneg[:], in_=s1[:], mul=-1.0 / D)
            scalar.activation(out=msq[:], in_=mneg[:], func=AF.Square)
            scalar.activation(out=dummy[:], in_=xt[:], func=AF.Square, accum_out=ssq[:]).then_inc(sc, 1)
            scalar.wait_ge(ve, 2 * i + 1)
            scalar.activation(out=rstd[:], in_=var[:], func=AF.Ln, bias=eps[:])
            scalar.activation(out=rstd[:], in_=rstd[:], func=AF.Exp, scale=-0.5).then_inc(sc, 1)

    @block.vector
    def _(vector):
        for i in range(NT):
            vector.wait_ge(sc, 2 * i + 1)
            vector.tensor_scalar(var[:], ssq[:], 1.0 / D, msq[:],
                                 ALU.mult, ALU.subtract).then_inc(ve, 1)
            vector.wait_ge(sc, 2 * i + 2)
            if i >= 1:
                vector.wait_ge(pe, ND * i)
            vector.tensor_scalar(xnb[:], xt[:], mneg[:], rstd[:], ALU.add, ALU.mult).then_inc(ve, 1)
            for j in range(ND):
                vector.wait_ge(pe, i * ND + j + 1)
                if i * ND + j >= 1:
                    vector.wait_ge(st, 16 * (i * ND + j))
                vector.tensor_copy(trs[:], pst[:]).then_inc(ev, 1)

    @block.tensor
    def _(tensor):
        for i in range(NT):
            tensor.wait_ge(ve, 2 * i + 2)
            for j in range(ND):
                if i * ND + j >= 1:
                    tensor.wait_ge(ev, i * ND + j)
                tensor.transpose(pst[:], xnb[:, j * 128:(j + 1) * 128], idt[:]).then_inc(pe, 1)

    block_cm.__exit__(None, None, None)
    return p


def build_l0proj():
    p = Prog("l0proj")
    xnT = p.inp("xnT", [D, T], BF16)
    pw = p.inp("proj_w", [D, 3072])
    pb = p.inp("proj_b", [3072])
    ptok = p.out("p_tok", [T, 3072], BF16)

    KC = D // 128
    NJ = 3072 // 512
    xsb = [p.sb(f"x{k}", [128, T], BF16) for k in range(KC)]
    wsb = [p.sb(f"w{k}", [128, 3072], BF16) for k in range(KC)]
    brep = p.sb("brep", [128, 3072], BF16)
    stg = p.sb("stg", [128, 3072], BF16)
    pss = [p.ps(f"ps{j}", [128, 512], F32) for j in range(NJ)]

    ld = p.sem("ld"); pe = p.sem("pe"); ev = p.sem("ev"); st = p.sem("st")
    NT = T // 128

    block_cm = p.nc.Block(); block = block_cm.__enter__()

    @block.gpsimd
    def _(g):
        for k in range(KC):
            g.dma_start(out=wsb[k][:], in_=pw[k * 128:(k + 1) * 128, :]).then_inc(ld, 16)
        g.dma_start(out=brep[:], in_=pb[None, :].broadcast_to([128, 3072])).then_inc(ld, 16)

    @block.sync
    def _(sync):
        for k in range(KC):
            sync.dma_start(out=xsb[k][:], in_=xnT[k * 128:(k + 1) * 128, :]).then_inc(ld, 16)
        for i in range(NT):
            sync.wait_ge(ev, i + 1)
            sync.dma_start(out=ptok[i * 128:(i + 1) * 128, :], in_=stg[:]).then_inc(st, 16)
        sync.wait_ge(st, 16 * NT)

    @block.tensor
    def _(tensor):
        tensor.wait_ge(ld, 16 * (2 * KC + 1))
        for i in range(NT):
            if i >= 1:
                tensor.wait_ge(ev, i)
            for k in range(KC):
                for j in range(NJ):
                    tensor.matmul(pss[j][:], xsb[k][:, i * 128:(i + 1) * 128],
                                  wsb[k][:, j * 512:(j + 1) * 512],
                                  start=(k == 0), stop=(k == KC - 1)).then_inc(pe, 1)

    @block.vector
    def _(vector):
        for i in range(NT):
            vector.wait_ge(pe, NJ * KC * (i + 1))
            if i >= 1:
                vector.wait_ge(st, 16 * i)
            for j in range(NJ):
                vector.tensor_tensor(stg[:, j * 512:(j + 1) * 512], pss[j][:],
                                     brep[:, j * 512:(j + 1) * 512],
                                     ALU.add).then_inc(ev, 1 if j == NJ - 1 else 0)

    block_cm.__exit__(None, None, None)
    return p


class Sched:
    def __init__(self, p):
        self.p = p
        self.ops = {"sync": [], "gpsimd": [], "scalar": [], "vector": [], "tensor": []}
        self.cnt = {}

    def sem(self, name):
        s = self.p.sem(name)
        self.cnt[name] = 0
        return s

    def emit(self, eng, fn):
        self.ops[eng].append(fn)

    def bump(self, name, by=1):
        self.cnt[name] += by
        return self.cnt[name]

    def install(self, block):
        for eng_name in ["sync", "gpsimd", "scalar", "vector", "tensor"]:
            fns = self.ops[eng_name]
            def mk(fns):
                def run(eng):
                    for f in fns:
                        f(eng)
                return run
            getattr(block, eng_name)(mk(fns))


def build_hyena(stage, phases=5):
    p = Prog(f"hyena{stage}")
    vw = 3072 if stage == 0 else D
    vsrc = p.inp("vsrc", [T, vw], BF16)
    ptok = p.inp("p_tok", [T, 3072], BF16)
    filt = p.inp("filt", [2048, D])
    f128re = p.inp("F128RE", [128, 128], BF16)
    f128im = p.inp("F128IM", [128, 128], BF16)
    s2fwd = p.inp("S2FWD", [128, 128, 128], BF16)
    s2filt = p.inp("S2FILT", [128, 128, 128], BF16)
    s1inv = p.inp("S1INV", [2, 128, 128], BF16)
    g2re = p.inp("G2RE", [32, 128, 128], BF16)
    g2nim = p.inp("G2NIM", [32, 128, 128], BF16)
    signone = p.inp("SIGNONE", [2, 128], BF16)
    vout = p.out("vout", [T, D], BF16)

    fw1 = p.nc.dram_tensor("fw1", [128, N2, 2, D], BF16, kind="Internal").ap()
    Fsp = p.nc.dram_tensor("Fsp", [128, K2F, 2, D], BF16, kind="Internal").ap()
    Zsp = p.nc.dram_tensor("Zsp", [128, K2F, 2, D], BF16, kind="Internal").ap()
    Csp = p.nc.dram_tensor("Csp", [128, K2F, 2, D], BF16, kind="Internal").ap()
    Bsp = p.nc.dram_tensor("Bsp", [N2, 2, 4, 128, 256], BF16, kind="Internal").ap()

    la = p.sb("la", [128, N2 * D], BF16)
    cre = p.sb("cre", [128, 128], BF16)
    cim = p.sb("cim", [128, 128], BF16)
    s2sb = p.sb("s2sb", [128, 128 * 32], BF16)
    stg1 = p.sb("stg1", [128, 2 * D], BF16)
    mid = p.sb("mid", [128, 32 * 512], BF16)
    stg2 = p.sb("stg2", [128, 512], BF16)
    DQ = 256
    zt = la[:, 0:K2F * 2 * DQ]
    ft = la[:, K2F * 2 * DQ:2 * K2F * 2 * DQ]
    ct = la[:, 2 * K2F * 2 * DQ:3 * K2F * 2 * DQ]
    s1sb = p.sb("s1sb", [128, 256], BF16)
    cti = la[:, 0:32 * 256]
    stg3 = p.sb("stg3", [128, 512], BF16)
    g2t = p.sb("g2t", [128, 256], BF16)
    bre = p.sb("bre", [128, D], BF16)
    bim = p.sb("bim", [128, D], BF16)
    c2k = p.sb("c2k", [1, D], BF16)
    c2kp = p.sb("c2kp", [1, D], BF16)
    c2km = p.sb("c2km", [1, D], BF16)
    ones1 = p.sb("ones1", [1, 128], BF16)
    pg = p.sb("pg", [128, D], BF16)
    ot = p.sb("ot", [128, D], BF16)
    ps = [p.ps(f"ps{i}", [128, 512], F32) for i in range(6)]

    S = Sched(p)
    ld = S.sem("ld"); pe = S.sem("pe"); ev = S.sem("ev"); st = S.sem("st"); vv = S.sem("vv")
    C = S.cnt

    def f(s):
        s.dma_start(out=cre[:], in_=f128re[:]).then_inc(ld, 16)
        s.dma_start(out=cim[:], in_=f128im[:]).then_inc(ld, 16)
        s.dma_start(out=s1sb[:, 0:128], in_=s1inv[0]).then_inc(ld, 16)
        s.dma_start(out=s1sb[:, 128:256], in_=s1inv[1]).then_inc(ld, 16)
        s.dma_start(out=ones1[:], in_=signone[0:1, :]).then_inc(ld, 16)
    S.emit("sync", f)
    S.bump("ld", 80)
    def f(g):
        g.dma_start(out=la[0:64, :], in_=filt.rearrange("(a b) d -> a (b d)", b=N2)).then_inc(ld, 16)
    S.emit("gpsimd", f)
    S.bump("ld", 16)

    def fwd(src_is_filter):
        kh = 64 if src_is_filter else 128
        outsp = Fsp if src_is_filter else Zsp
        s2c = s2filt if src_is_filter else s2fwd
        if not src_is_filter:
            pe_need = C["pe"]
            vla = vsrc[:, 0:D].rearrange("(a b) d -> a b d", b=N2) if stage == 0 \
                else vsrc.rearrange("(a b) d -> a (b d)", b=N2)
            def f(s, pe_need=pe_need, vla=vla):
                s.wait_ge(pe, pe_need)
                s.dma_start(out=la[:], in_=vla).then_inc(ld, 16)
            S.emit("sync", f)
            S.bump("ld", 16)
        ld_need = C["ld"]
        st_base = C["st"]; ev_base = C["ev"]
        for n2 in range(N2):
            ev_need = ev_base + 4 * n2
            def fmm(t, n2=n2, ld_need=ld_need, ev_need=ev_need, kh=kh):
                t.wait_ge(ld, ld_need)
                if ev_need > 0:
                    t.wait_ge(ev, ev_need)
                for dh in range(2):
                    t.matmul(ps[0 + dh][:], cre[0:kh, :],
                             la[0:kh, n2 * D + dh * 512: n2 * D + dh * 512 + 512],
                             start=True, stop=True).then_inc(pe, 1)
                    t.matmul(ps[2 + dh][:], cim[0:kh, :],
                             la[0:kh, n2 * D + dh * 512: n2 * D + dh * 512 + 512],
                             start=True, stop=True).then_inc(pe, 1)
            S.emit("tensor", fmm)
            pe_now = S.bump("pe", 4)
            st_need = st_base + 16 * n2
            def fev(sc, pe_now=pe_now, st_need=st_need):
                sc.wait_ge(pe, pe_now)
                if st_need > 0:
                    sc.wait_ge(st, st_need)
                for dh in range(2):
                    sc.copy(out=stg1[:, 0 * D + dh * 512: 0 * D + dh * 512 + 512], in_=ps[0 + dh][:])
                c1 = sc.copy(out=stg1[:, 1 * D + 0 * 512: 1 * D + 0 * 512 + 512], in_=ps[2][:])
                c2 = sc.copy(out=stg1[:, 1 * D + 1 * 512: 1 * D + 1 * 512 + 512], in_=ps[3][:])
                c2.then_inc(ev, 4)
            S.emit("scalar", fev)
            ev_now = S.bump("ev", 4)
            def fst(s, n2=n2, ev_now=ev_now):
                s.wait_ge(ev, ev_now)
                s.dma_start(out=fw1[:, n2, :, :].rearrange("k c d -> k (c d)"),
                            in_=stg1[:]).then_inc(st, 16)
            S.emit("sync", fst)
            S.bump("st", 16)
        s1_done_st = C["st"]
        for kc in range(4):
            ld_mid = C["ld"]
            def fldc(s, kc=kc, s1_done_st=s1_done_st, s2c=s2c):
                s.wait_ge(st, s1_done_st)
                s.dma_start(out=s2sb[:],
                            in_=s2c[kc * 32:(kc + 1) * 32, :, :]
                            .rearrange("k p i -> p k i")).then_inc(ld, 16)
                s.dma_start(out=mid[:],
                            in_=fw1.rearrange("k n c (s e) -> (n c s) k e", s=2)
                            [:, kc * 32:(kc + 1) * 32, :]).then_inc(ld, 16)
            S.emit("sync", fldc)
            ldc_now = S.bump("ld", 32)
            for k1i in range(32):
                k1 = kc * 32 + k1i
                ev_need = C["ev"]
                def fmm(t, k1i=k1i, ldc_now=ldc_now, ev_need=ev_need):
                    t.wait_ge(ld, ldc_now)
                    t.wait_ge(ev, ev_need)
                    t.matmul(ps[4][:], s2sb[:, k1i * 128:(k1i + 1) * 128],
                             mid[:, k1i * 512:(k1i + 1) * 512],
                             start=True, stop=True).then_inc(pe, 1)
                S.emit("tensor", fmm)
                pe_now = S.bump("pe", 1)
                st_need = C["st"]
                def fev(sc, pe_now=pe_now, st_need=st_need):
                    sc.wait_ge(pe, pe_now)
                    sc.wait_ge(st, st_need)
                    sc.copy(out=stg2[:], in_=ps[4][:]).then_inc(ev, 1)
                S.emit("scalar", fev)
                ev_now = S.bump("ev", 1)
                def fst(s, k1=k1, ev_now=ev_now, outsp=outsp):
                    s.wait_ge(ev, ev_now)
                    s.dma_start(out=outsp[k1, :, 0, :].rearrange("k (s e) -> k s e", s=2),
                                in_=stg2[0:34, :]).then_inc(st, 16)
                    s.dma_start(out=outsp[k1, :, 1, :].rearrange("k (s e) -> k s e", s=2),
                                in_=stg2[64:98, :]).then_inc(st, 16)
                S.emit("sync", fst)
                S.bump("st", 32)
            pe_need2 = C["pe"]
            def fbar(s, pe_need2=pe_need2):
                s.wait_ge(pe, pe_need2)
            S.emit("sync", fbar)
        return outsp

    fwd(True)
    if phases >= 2:
        fwd(False)
    if phases < 3:
        final_st = C["st"]
        def f(s, final_st=final_st):
            s.wait_ge(st, final_st)
        S.emit("sync", f)
        block_cm = p.nc.Block(); block = block_cm.__enter__()
        S.install(block)
        block_cm.__exit__(None, None, None)
        return p

    pw_st_need = C["st"]
    for dh in range(4):
        ld_now = C["ld"]
        vv_prev = C["vv"]
        def fld(s, dh=dh, pw_st_need=pw_st_need, vv_prev=vv_prev):
            s.wait_ge(st, pw_st_need)
            if dh >= 1:
                s.wait_ge(vv, vv_prev)
            s.dma_start(out=zt[:], in_=Zsp[:, :, :, dh * 256:(dh + 1) * 256]).then_inc(ld, 16)
            s.dma_start(out=ft[:], in_=Fsp[:, :, :, dh * 256:(dh + 1) * 256]).then_inc(ld, 16)
        S.emit("sync", fld)
        ld_now = S.bump("ld", 32)
        z3 = zt.rearrange("k (a c e) -> k a c e", a=K2F, c=2)
        f3 = ft.rearrange("k (a c e) -> k a c e", a=K2F, c=2)
        c3 = ct.rearrange("k (a c e) -> k a c e", a=K2F, c=2)
        vv_base = C["vv"]
        def fpw(v, ld_now=ld_now, z3=z3, f3=f3, c3=c3, dh=dh, vv_base=vv_base):
            v.wait_ge(ld, ld_now)
            v.tensor_tensor(c3[:, :, 0, :], z3[:, :, 0, :], f3[:, :, 0, :], ALU.mult).then_inc(vv, 1)
            v.tensor_tensor(c3[:, :, 1, :], z3[:, :, 1, :], f3[:, :, 1, :], ALU.mult).then_inc(vv, 1)
            v.wait_ge(vv, vv_base + 2)
            v.tensor_tensor(c3[:, :, 0, :], c3[:, :, 0, :], c3[:, :, 1, :], ALU.subtract).then_inc(vv, 1)
            v.tensor_tensor(z3[:, :, 0, :], z3[:, :, 0, :], f3[:, :, 1, :], ALU.mult).then_inc(vv, 1)
            v.tensor_tensor(z3[:, :, 1, :], z3[:, :, 1, :], f3[:, :, 0, :], ALU.mult).then_inc(vv, 1)
            v.wait_ge(vv, vv_base + 5)
            v.tensor_tensor(c3[:, :, 1, :], z3[:, :, 0, :], z3[:, :, 1, :], ALU.add).then_inc(vv, 1)
            v.wait_ge(vv, vv_base + 6)
            v.tensor_scalar(c3[0:1, 0, :, :].rearrange("k c e -> k (c e)"),
                            c3[0:1, 0, :, :].rearrange("k c e -> k (c e)"),
                            0.5, None, ALU.mult).then_inc(vv, 1)
        S.emit("vector", fpw)
        vv_now = S.bump("vv", 7)
        def fstc(s, dh=dh, vv_now=vv_now):
            s.wait_ge(vv, vv_now)
            s.dma_start(out=Csp[:, :, :, dh * 256:(dh + 1) * 256], in_=ct[:]).then_inc(st, 16)
        S.emit("sync", fstc)
        S.bump("st", 16)
    pw_done_st = C["st"]
    if phases < 4:
        def f(s, pw_done_st=pw_done_st):
            s.wait_ge(st, pw_done_st)
        S.emit("sync", f)
        block_cm = p.nc.Block(); block = block_cm.__enter__()
        S.install(block)
        block_cm.__exit__(None, None, None)
        return p

    def f(s, pw_done_st=pw_done_st):
        s.wait_ge(st, pw_done_st)
        s.dma_start(out=c2k[:], in_=Csp[0:1, 16, 0, :]).then_inc(ld, 16)
    S.emit("sync", f)
    c2k_ld = S.bump("ld", 16)
    def f(sc, c2k_ld=c2k_ld):
        sc.wait_ge(ld, c2k_ld)
        sc.mul(out=c2kp[:], in_=c2k[:], mul=0.5)
        sc.mul(out=c2km[:], in_=c2k[:], mul=-0.5).then_inc(ev, 1)
    S.emit("scalar", f)
    c2k_ev = S.bump("ev", 1)

    for kc in range(4):
        pe_guard = C["pe"]
        ld_now = C["ld"]
        def fld(s, kc=kc, pw_done_st=pw_done_st, pe_guard=pe_guard):
            s.wait_ge(st, pw_done_st)
            s.wait_ge(pe, pe_guard)
            s.dma_start(out=cti[:],
                        in_=Csp.rearrange("k a c (s e) -> (a c s) k e", s=4)
                        [0:128, kc * 32:(kc + 1) * 32, :]).then_inc(ld, 16)
        S.emit("sync", fld)
        ld_now = S.bump("ld", 16)
        for co in range(2):
            for js in range(16):
                ev_need = C["ev"]
                def fmm(t, co=co, js=js, ld_now=ld_now, ev_need=ev_need):
                    t.wait_ge(ld, ld_now)
                    t.wait_ge(ev, ev_need)
                    t.matmul(ps[5][:], s1sb[:, co * 128:(co + 1) * 128],
                             cti[:, js * 512:(js + 1) * 512],
                             start=True, stop=True).then_inc(pe, 1)
                S.emit("tensor", fmm)
                pe_now = S.bump("pe", 1)
                st_need = C["st"]
                def fev(sc, pe_now=pe_now, st_need=st_need):
                    sc.wait_ge(pe, pe_now)
                    sc.wait_ge(st, st_need)
                    sc.copy(out=stg3[:], in_=ps[5][:]).then_inc(ev, 1)
                S.emit("scalar", fev)
                ev_now = S.bump("ev", 1)
                k1b = kc * 32 + js * 2
                def fst(s, co=co, k1b=k1b, ev_now=ev_now):
                    s.wait_ge(ev, ev_now)
                    s.dma_start(out=Bsp[:, co, :, k1b:k1b + 2, :]
                                .rearrange("a s k e -> a s (k e)"),
                                in_=stg3[:]).then_inc(st, 16)
                S.emit("sync", fst)
                S.bump("st", 16)
    inv1_done_st = C["st"]
    if phases < 5:
        def f(s, inv1_done_st=inv1_done_st):
            s.wait_ge(st, inv1_done_st)
            s.wait_ge(ev, C_ev_final)
        C_ev_final = C["ev"]
        S.emit("sync", f)
        block_cm = p.nc.Block(); block = block_cm.__enter__()
        S.install(block)
        block_cm.__exit__(None, None, None)
        return p

    for t5 in range(N2):
        ld_now = C["ld"]
        pe_guard = C["pe"]; vv_guard = C["vv"]
        def fld(s, t5=t5, inv1_done_st=inv1_done_st, pe_guard=pe_guard, vv_guard=vv_guard):
            s.wait_ge(st, inv1_done_st)
            s.wait_ge(pe, pe_guard)
            s.wait_ge(vv, vv_guard)
            s.dma_start(out=g2t[:, 0:128], in_=g2re[t5]).then_inc(ld, 16)
            s.dma_start(out=g2t[:, 128:256], in_=g2nim[t5]).then_inc(ld, 16)
            s.dma_start(out=bre[:], in_=Bsp[t5, 0].rearrange("s k e -> k s e")).then_inc(ld, 16)
            s.dma_start(out=bim[:], in_=Bsp[t5, 1].rearrange("s k e -> k s e")).then_inc(ld, 16)
            s.dma_start(out=pg[:], in_=ptok.rearrange("(m b) e -> m (b e)", b=N2)
                        [:, t5 * 3072 + (stage + 1) * D: t5 * 3072 + (stage + 2) * D]).then_inc(ld, 16)
        S.emit("sync", fld)
        ld_now = S.bump("ld", 80)
        for dh in range(2):
            vv_need = C["vv"] - (1 + dh) if t5 > 0 or dh > 0 else 0
            vv_need = max(0, C["vv"] - 2 + dh) if (t5 > 0) else (C["vv"] if dh == 1 else 0)
            vv_need = C["vv"]
            def fmm(t, dh=dh, t5=t5, ld_now=ld_now, vv_need=vv_need, c2k_ev=c2k_ev):
                t.wait_ge(ld, ld_now)
                t.wait_ge(ev, c2k_ev)
                t.wait_ge(vv, vv_need)
                t.matmul(ps[4 + dh][:], g2t[:, 0:128], bre[:, dh * 512:(dh + 1) * 512],
                         start=True, stop=False)
                t.matmul(ps[4 + dh][:], g2t[:, 128:256], bim[:, dh * 512:(dh + 1) * 512],
                         start=False, stop=False)
                t.matmul(ps[4 + dh][:], ones1[:],
                         (c2kp if t5 % 2 == 0 else c2km)[:, dh * 512:(dh + 1) * 512],
                         start=False, stop=True).then_inc(pe, 1)
            S.emit("tensor", fmm)
            pe_now = S.bump("pe", 1)
            def fgate(v, dh=dh, pe_now=pe_now):
                v.wait_ge(pe, pe_now)
                v.tensor_tensor(ot[:, dh * 512:(dh + 1) * 512], ps[4 + dh][:],
                                pg[:, dh * 512:(dh + 1) * 512], ALU.mult).then_inc(vv, 1)
            S.emit("vector", fgate)
            S.bump("vv", 1)
        vv_now = C["vv"]
        def fst(s, t5=t5, vv_now=vv_now):
            s.wait_ge(vv, vv_now)
            s.dma_start(out=vout.rearrange("(m b) d -> m (b d)", b=N2)
                        [:, t5 * D:(t5 + 1) * D], in_=ot[:]).then_inc(st, 16)
        S.emit("sync", fst)
        S.bump("st", 16)
    final_st = C["st"]
    def f(s, final_st=final_st):
        s.wait_ge(st, final_st)
    S.emit("sync", f)

    block_cm = p.nc.Block(); block = block_cm.__enter__()
    S.install(block)
    block_cm.__exit__(None, None, None)
    return p


def build_l0ffn(dbg=False):
    p = Prog("l0ffn")
    xnT = p.inp("xnT", [D, T], BF16)
    Wr = p.inp("W_r", [D, D]); Wk = p.inp("W_k", [D, D]); Wv = p.inp("W_v", [D, D])
    mur = p.inp("mu_r", [D]); muk = p.inp("mu_k", [D])
    ident = p.inp("IDENT", [128, 128], BF16)
    gtok = p.out("g_tok", [T, D], BF16)
    if dbg:
        dwt = p.out("dwt", [D, D], BF16)
        dxr = p.out("dxr", [D, 1024], BF16)
        drt = p.out("drt", [D, 1024], BF16)
        dka = p.out("dka", [D, 1024], BF16)

    KC = 8
    wload = p.sb("wload", [128, D], BF16)
    wT = {n: p.sb(f"wT{n}", [128, KC * D], BF16) for n in ["r", "k", "v"]}
    mu = p.sb("mu", [128, 16], F32)
    idt = p.sb("idt", [128, 128], BF16)
    xnc = p.sb("xnc", [128, KC * 1025], BF16)
    tmp = p.sb("tmpv", [128, 1024], F32)
    xr = p.sb("xr", [128, KC * 1024], BF16)
    xk = p.sb("xk", [128, KC * 1024], BF16)
    rt = p.sb("rt", [128, KC * 1024], BF16)
    kact = p.sb("kact", [128, KC * 1024], BF16)
    sq = p.sb("sq", [128, 512], F32)
    gt = p.sb("gt", [128, KC * 1024], BF16)
    trs = p.sb("trs2", [128, 128], BF16)
    pst = p.ps("pst2", [128, 128], BF16)
    pss = [p.ps(f"psf{j}", [128, 512], F32) for j in range(2)]

    S = Sched(p)
    ld = S.sem("ld"); pe = S.sem("pe"); ev = S.sem("ev"); st = S.sem("st"); vv = S.sem("vv")
    C = S.cnt

    def f(s):
        s.dma_start(out=idt[:], in_=ident[:]).then_inc(ld, 16)
    S.emit("sync", f)
    S.bump("ld", 16)
    def f(g):
        g.dma_start(out=mu[:, 0:8], in_=mur.rearrange("(g p) -> p g", p=128), allow_slow_non_contiguous=True).then_inc(ld, 16)
        g.dma_start(out=mu[:, 8:16], in_=muk.rearrange("(g p) -> p g", p=128), allow_slow_non_contiguous=True).then_inc(ld, 16)
    S.emit("gpsimd", f)
    S.bump("ld", 32)

    for n, W in [("r", Wr), ("k", Wk), ("v", Wv)]:
        for bi in range(8):
            ld_now = C["ld"]
            pe_g = C["pe"]
            def f(g, W=W, bi=bi, pe_g=pe_g):
                g.wait_ge(pe, pe_g)
                g.dma_start(out=wload[:], in_=W[bi * 128:(bi + 1) * 128, :]).then_inc(ld, 16)
            S.emit("gpsimd", f)
            ld_now = S.bump("ld", 16)
            for bj in range(8):
                ev_need = C["ev"]
                def fmm(t, bj=bj, ld_now=ld_now, ev_need=ev_need):
                    t.wait_ge(ld, ld_now)
                    t.wait_ge(ev, ev_need)
                    t.transpose(pst[:], wload[:, bj * 128:(bj + 1) * 128], idt[:]).then_inc(pe, 1)
                S.emit("tensor", fmm)
                pe_now = S.bump("pe", 1)
                def fev(sc, n=n, bi=bi, bj=bj, pe_now=pe_now):
                    sc.wait_ge(pe, pe_now)
                    sc.copy(out=wT[n][:, bj * 1024 + bi * 128: bj * 1024 + (bi + 1) * 128],
                            in_=pst[:]).then_inc(ev, 1)
                S.emit("scalar", fev)
                S.bump("ev", 1)
    wprep_ev = C["ev"]
    if dbg:
        def fdw(s, wprep_ev=wprep_ev):
            s.wait_ge(ev, wprep_ev)
            for bj in range(8):
                s.dma_start(out=dwt[bj * 128:(bj + 1) * 128, :],
                            in_=wT["r"][:, bj * 1024:(bj + 1) * 1024]).then_inc(st, 16)
        S.emit("sync", fdw)
        S.bump("st", 16 * 8)

    NCH = 4; TC = 1024
    for ch in range(NCH):
        t0 = ch * TC
        ld_now = C["ld"]
        vv_g = C["vv"]
        pe_gd = C["pe"]
        def f(s, ch=ch, t0=t0, vv_g=vv_g, pe_gd=pe_gd):
            s.wait_ge(vv, vv_g)
            s.wait_ge(pe, pe_gd)
            for g in range(KC):
                if ch == 0:
                    s.dma_start(out=xnc[:, g * 1025 + 1: (g + 1) * 1025],
                                in_=xnT[g * 128:(g + 1) * 128, 0:TC]).then_inc(ld, 16)
                else:
                    s.dma_start(out=xnc[:, g * 1025: (g + 1) * 1025],
                                in_=xnT[g * 128:(g + 1) * 128, t0 - 1: t0 + TC]).then_inc(ld, 16)
        S.emit("sync", f)
        ld_now = S.bump("ld", 16 * KC)
        if ch == 0:
            def f(v, ld_now=ld_now):
                v.wait_ge(ld, ld_now)
                for g in range(KC):
                    v.memset(xnc[:, g * 1025: g * 1025 + 1], 0.0)
                v.engine_nop().then_inc(vv, 1)
            S.emit("vector", f)
            S.bump("vv", 1)
        for g in range(KC):
            vv_base = C["vv"]
            ev_g = C["ev"]
            def f(v, g=g, ld_now=ld_now, vv_base=vv_base, ev_g=ev_g):
                v.wait_ge(ld, ld_now)
                if g == 0:
                    v.wait_ge(vv, vv_base)
                    v.wait_ge(ev, ev_g)
                xp = xnc[:, g * 1025: g * 1025 + 1024]
                xc = xnc[:, g * 1025 + 1: g * 1025 + 1025]
                v.tensor_tensor(tmp[:], xc, xp, ALU.subtract).then_inc(vv, 1)
                v.wait_ge(vv, vv_base + 1)
                v.scalar_tensor_tensor(xr[:, g * 1024:(g + 1) * 1024], tmp[:],
                                       mu[:, g:g + 1], xp, ALU.mult, ALU.add).then_inc(vv, 1)
                v.scalar_tensor_tensor(xk[:, g * 1024:(g + 1) * 1024], tmp[:],
                                       mu[:, 8 + g:9 + g], xp, ALU.mult, ALU.add).then_inc(vv, 1)
            S.emit("vector", f)
            S.bump("vv", 3)
        mix_vv = C["vv"]

        for (nm, src, dstname) in [("r", xr, "rt"), ("k", xk, "kact")]:
            for bi in range(8):
                for j in range(2):
                    ev_need = C["ev"]
                    def fmm(t, nm=nm, src=src, bi=bi, j=j, mix_vv=mix_vv, ev_need=ev_need,
                            wprep_ev=wprep_ev):
                        t.wait_ge(vv, mix_vv)
                        t.wait_ge(ev, max(ev_need, wprep_ev))
                        for kb in range(8):
                            mm = t.matmul(pss[j][:],
                                     wT[nm][:, kb * 1024 + bi * 128: kb * 1024 + (bi + 1) * 128],
                                     src[:, kb * 1024 + j * 512: kb * 1024 + j * 512 + 512],
                                     start=(kb == 0), stop=(kb == 7))
                        mm.then_inc(pe, 1)
                    S.emit("tensor", fmm)
                    pe_now = S.bump("pe", 1)
                    vv_need = C["vv"]
                    def fev(sc, nm=nm, dstname=dstname, bi=bi, j=j, pe_now=pe_now, vv_need=vv_need):
                        sc.wait_ge(pe, pe_now)
                        sc.wait_ge(vv, vv_need)
                        dst = (rt if dstname == "rt" else kact)
                        piece = dst[:, bi * 1024 + j * 512: bi * 1024 + j * 512 + 512]
                        if nm == "r":
                            sc.activation(out=piece, in_=pss[j][:], func=AF.Sigmoid).then_inc(ev, 1)
                        else:
                            sc.activation(out=sq[:], in_=pss[j][:], func=AF.Relu)
                            sc.activation(out=piece, in_=sq[:], func=AF.Square).then_inc(ev, 1)
                    S.emit("scalar", fev)
                    S.bump("ev", 1)
        rk_ev = C["ev"]
        for bi in range(8):
            for j in range(2):
                ev_need = C["ev"]
                def fmm(t, bi=bi, j=j, rk_ev=rk_ev, ev_need=ev_need):
                    t.wait_ge(ev, max(rk_ev, ev_need))
                    for kb in range(8):
                        mm = t.matmul(pss[j][:],
                                 wT["v"][:, kb * 1024 + bi * 128: kb * 1024 + (bi + 1) * 128],
                                 kact[:, kb * 1024 + j * 512: kb * 1024 + j * 512 + 512],
                                 start=(kb == 0), stop=(kb == 7))
                    mm.then_inc(pe, 1)
                S.emit("tensor", fmm)
                pe_now = S.bump("pe", 1)
                ev_need2 = C["ev"]
                def fgv(v, bi=bi, j=j, pe_now=pe_now, ev_need2=ev_need2):
                    v.wait_ge(pe, pe_now)
                    v.wait_ge(ev, ev_need2)
                    piece = rt[:, bi * 1024 + j * 512: bi * 1024 + j * 512 + 512]
                    v.tensor_tensor(tmp[:, 0:512], pss[j][:], piece, ALU.mult).then_inc(vv, 1)
                S.emit("vector", fgv)
                vv_now = S.bump("vv", 1)
                pe_g2 = C["pe"]
                def fge(sc, bi=bi, j=j, vv_now=vv_now, pe_g2=pe_g2):
                    sc.wait_ge(vv, vv_now)
                    sc.wait_ge(pe, pe_g2)
                    sc.activation(out=gt[:, bi * 1024 + j * 512: bi * 1024 + j * 512 + 512],
                                  in_=tmp[:, 0:512], func=AF.Silu).then_inc(ev, 1)
                S.emit("scalar", fge)
                S.bump("ev", 1)
        g_ev = C["ev"]
        if dbg and ch == 0:
            def fdbg(s, g_ev=g_ev, mix_vv=mix_vv):
                s.wait_ge(ev, g_ev)
                s.wait_ge(vv, mix_vv)
                for g in range(KC):
                    s.dma_start(out=dxr[g * 128:(g + 1) * 128, :],
                                in_=xr[:, g * 1024:(g + 1) * 1024]).then_inc(st, 16)
                    s.dma_start(out=drt[g * 128:(g + 1) * 128, :],
                                in_=rt[:, g * 1024:(g + 1) * 1024]).then_inc(st, 16)
                    s.dma_start(out=dka[g * 128:(g + 1) * 128, :],
                                in_=kact[:, g * 1024:(g + 1) * 1024]).then_inc(st, 16)
            S.emit("sync", fdbg)
            S.bump("st", 16 * 24)
        for it in range(8):
            for jd in range(8):
                ev_need = C["ev"]
                def fmm(t, it=it, jd=jd, g_ev=g_ev, ev_need=ev_need):
                    t.wait_ge(ev, max(g_ev, ev_need))
                    t.transpose(pst[:], gt[:, jd * 1024 + it * 128: jd * 1024 + (it + 1) * 128],
                                idt[:]).then_inc(pe, 1)
                S.emit("tensor", fmm)
                pe_now = S.bump("pe", 1)
                st_need = C["st"]
                def fev(sc, pe_now=pe_now, st_need=st_need):
                    sc.wait_ge(pe, pe_now)
                    sc.wait_ge(st, st_need)
                    sc.copy(out=trs[:], in_=pst[:]).then_inc(ev, 1)
                S.emit("scalar", fev)
                ev_now = S.bump("ev", 1)
                def fst(s, it=it, jd=jd, t0=t0, ev_now=ev_now):
                    s.wait_ge(ev, ev_now)
                    s.dma_start(out=gtok[t0 + it * 128: t0 + (it + 1) * 128,
                                         jd * 128:(jd + 1) * 128], in_=trs[:]).then_inc(st, 16)
                S.emit("sync", fst)
                S.bump("st", 16)
    final_st = C["st"]
    def f(s, final_st=final_st):
        s.wait_ge(st, final_st)
    S.emit("sync", f)

    block_cm = p.nc.Block(); block = block_cm.__enter__()
    S.install(block)
    block_cm.__exit__(None, None, None)
    return p


def build_resid_ln():
    p = Prog("residln")
    x = p.inp("x", [T, D])
    stok = p.inp("s_tok", [T, D], BF16)
    gtok = p.inp("g_tok", [T, D], BF16)
    ident = p.inp("IDENT", [128, 128], BF16)
    x1o = p.out("x1", [T, D])
    xnT = p.out("xn1T", [D, T], BF16)

    xt = p.sb("xt", [128, D], F32)
    stt = p.sb("stt", [128, D], BF16)
    gtt = p.sb("gtt", [128, D], BF16)
    sg = p.sb("sg", [128, D], BF16)
    x1t = p.sb("x1t", [128, D], F32)
    xnb = p.sb("xnb", [128, D], BF16)
    s1 = p.sb("s1", [128, 1], F32)
    mneg = p.sb("mneg", [128, 1], F32)
    msq = p.sb("msq", [128, 1], F32)
    ssq = p.sb("ssq", [128, 1], F32)
    var = p.sb("var", [128, 1], F32)
    rstd = p.sb("rstd", [128, 1], F32)
    dummy = p.sb("scrd", [128, D], F32)
    eps = p.sb("eps", [128, 1], F32)
    idt = p.sb("idt", [128, 128], BF16)
    trs = p.sb("trs", [128, 128], BF16)
    pst = p.ps("pst", [128, 128], BF16)

    S = Sched(p)
    ld = S.sem("ld"); sc = S.sem("sc"); ve = S.sem("ve")
    pe = S.sem("pe"); ev = S.sem("ev"); st = S.sem("st")
    C = S.cnt
    NT = T // 128; ND = D // 128

    def f(s):
        s.dma_start(out=idt[:], in_=ident[:]).then_inc(ld, 16)
    S.emit("sync", f)
    S.bump("ld", 16)
    def f(g):
        g.memset(eps[:], 1e-5).then_inc(ve, 1)
    S.emit("gpsimd", f)
    S.bump("ve", 1)

    for i in range(NT):
        ld_now = C["ld"]
        ve_g = C["ve"]; pe_g = C["pe"]; sc_gl = C["sc"]
        def f(s, i=i, ve_g=ve_g, pe_g=pe_g, sc_gl=sc_gl):
            s.wait_ge(ve, ve_g)
            s.wait_ge(pe, pe_g)
            s.wait_ge(sc, sc_gl)
            s.dma_start(out=xt[:], in_=x[i * 128:(i + 1) * 128, :]).then_inc(ld, 16)
            s.dma_start(out=stt[:], in_=stok[i * 128:(i + 1) * 128, :]).then_inc(ld, 16)
            s.dma_start(out=gtt[:], in_=gtok[i * 128:(i + 1) * 128, :]).then_inc(ld, 16)
        S.emit("sync", f)
        ld_now = S.bump("ld", 48)
        st_g = C["st"]
        vbase = C["ve"]
        def fv(v, ld_now=ld_now, vbase=vbase, st_g=st_g):
            v.wait_ge(ld, ld_now)
            v.wait_ge(st, st_g)
            v.tensor_tensor(sg[:], stt[:], gtt[:], ALU.mult).then_inc(ve, 1)
            v.wait_ge(ve, vbase + 1)
            v.tensor_tensor(x1t[:], xt[:], sg[:], ALU.add).then_inc(ve, 1)
        S.emit("vector", fv)
        ve_now = S.bump("ve", 2)
        def fst1(s, i=i, ve_now=ve_now):
            s.wait_ge(ve, ve_now)
            s.dma_start(out=x1o[i * 128:(i + 1) * 128, :], in_=x1t[:]).then_inc(st, 16)
        S.emit("sync", fst1)
        S.bump("st", 16)
        sc_b = C["sc"]
        def fsc1(scl, ve_now=ve_now, sc_b=sc_b):
            scl.wait_ge(ve, ve_now)
            scl.activation(out=dummy[:], in_=x1t[:], func=AF.Copy, accum_out=s1[:])
            scl.mul(out=mneg[:], in_=s1[:], mul=-1.0 / D)
            scl.activation(out=msq[:], in_=mneg[:], func=AF.Square)
            scl.activation(out=dummy[:], in_=x1t[:], func=AF.Square, accum_out=ssq[:]).then_inc(sc, 1)
        S.emit("scalar", fsc1)
        sc_now = S.bump("sc", 1)
        def fv2(v, sc_now=sc_now):
            v.wait_ge(sc, sc_now)
            v.tensor_scalar(var[:], ssq[:], 1.0 / D, msq[:], ALU.mult, ALU.subtract).then_inc(ve, 1)
        S.emit("vector", fv2)
        ve2 = S.bump("ve", 1)
        def fsc2(scl, ve2=ve2):
            scl.wait_ge(ve, ve2)
            scl.activation(out=rstd[:], in_=var[:], func=AF.Ln, bias=eps[:])
            scl.activation(out=rstd[:], in_=rstd[:], func=AF.Exp, scale=-0.5).then_inc(sc, 1)
        S.emit("scalar", fsc2)
        sc2 = S.bump("sc", 1)
        pe_g2 = C["pe"]
        def fv3(v, sc2=sc2, pe_g2=pe_g2):
            v.wait_ge(sc, sc2)
            v.wait_ge(pe, pe_g2)
            v.tensor_scalar(xnb[:], x1t[:], mneg[:], rstd[:], ALU.add, ALU.mult).then_inc(ve, 1)
        S.emit("vector", fv3)
        ve3 = S.bump("ve", 1)
        for j in range(ND):
            ev_need = C["ev"]
            def fmm(t, j=j, ve3=ve3, ev_need=ev_need):
                t.wait_ge(ve, ve3)
                t.wait_ge(ev, ev_need)
                t.transpose(pst[:], xnb[:, j * 128:(j + 1) * 128], idt[:]).then_inc(pe, 1)
            S.emit("tensor", fmm)
            pe_now = S.bump("pe", 1)
            st_need = C["st"]
            def fev(scl, pe_now=pe_now, st_need=st_need):
                scl.wait_ge(pe, pe_now)
                scl.wait_ge(st, st_need)
                scl.copy(out=trs[:], in_=pst[:]).then_inc(ev, 1)
            S.emit("scalar", fev)
            ev_now = S.bump("ev", 1)
            def fst(s, i=i, j=j, ev_now=ev_now):
                s.wait_ge(ev, ev_now)
                s.dma_start(out=xnT[j * 128:(j + 1) * 128, i * 128:(i + 1) * 128],
                            in_=trs[:]).then_inc(st, 16)
            S.emit("sync", fst)
            S.bump("st", 16)
    final_st = C["st"]
    def f(s, final_st=final_st):
        s.wait_ge(st, final_st)
    S.emit("sync", f)
    block_cm = p.nc.Block(); block = block_cm.__enter__()
    S.install(block)
    block_cm.__exit__(None, None, None)
    return p


def build_l1proj():
    p = Prog("l1proj")
    xnT = p.inp("xn1T", [D, T], BF16)
    W = p.inp("W_xproj", [D, 1056])
    b = p.inp("b_xproj", [1056])
    dT = p.out("deltaT", [D, T])
    bc = p.out("bc", [32, T])

    KC = 8
    xsb = [p.sb(f"x{k}", [128, T], BF16) for k in range(KC)]
    wsb = [p.sb(f"w{k}", [128, 1056], BF16) for k in range(KC)]
    bcol = p.sb("bcol", [128, 9], F32)
    t1 = p.sb("t1", [128, 512], F32)
    t2 = p.sb("t2", [128, 512], F32)
    stg = p.sb("stg", [128, 512], F32)
    pss = [p.ps(f"ps{j}", [128, 512], F32) for j in range(2)]

    S = Sched(p)
    ld = S.sem("ld"); pe = S.sem("pe"); ev = S.sem("ev"); st = S.sem("st"); vv = S.sem("vv")
    C = S.cnt
    def f(g):
        for k in range(KC):
            g.dma_start(out=wsb[k][:], in_=W[k * 128:(k + 1) * 128, :]).then_inc(ld, 16)
        g.dma_start(out=bcol[:, 0:8], in_=b[0:1024].rearrange("(g p) -> p g", p=128),
                    allow_slow_non_contiguous=True).then_inc(ld, 16)
        g.dma_start(out=bcol[0:32, 8:9], in_=b[1024:1056].unsqueeze(1),
                    allow_slow_non_contiguous=True).then_inc(ld, 16)
    S.emit("gpsimd", f)
    S.bump("ld", 16 * (KC + 2))
    def f(s):
        for k in range(KC):
            s.dma_start(out=xsb[k][:], in_=xnT[k * 128:(k + 1) * 128, :]).then_inc(ld, 16)
    S.emit("sync", f)
    ld_all = S.bump("ld", 16 * KC)

    NJ = T // 512
    for i in range(9):
        hi = 128 if i < 8 else 32
        for j in range(NJ):
            jb = j % 2
            ev_need = C["ev"]
            vv_need = C["vv"]
            def fmm(t, i=i, j=j, jb=jb, hi=hi, ld_all=ld_all, ev_need=ev_need, vv_need=vv_need):
                t.wait_ge(ld, ld_all)
                t.wait_ge(ev, ev_need)
                t.wait_ge(vv, vv_need)
                for k in range(KC):
                    mm = t.matmul(pss[jb][0:hi, :], wsb[k][:, i * 128:i * 128 + hi],
                                  xsb[k][:, j * 512:(j + 1) * 512],
                                  start=(k == 0), stop=(k == KC - 1))
                mm.then_inc(pe, 1)
            S.emit("tensor", fmm)
            pe_now = S.bump("pe", 1)
            if i < 8:
                st_need = C["st"]
                def fev(scl, i=i, jb=jb, pe_now=pe_now, st_need=st_need):
                    scl.wait_ge(pe, pe_now)
                    scl.wait_ge(st, st_need)
                    scl.activation(out=t1[:], in_=pss[jb][:], func=AF.Abs, bias=bcol[:, i:i + 1])
                    scl.activation(out=t1[:], in_=t1[:], func=AF.Exp, scale=-1.0)
                    scl.activation(out=t1[:], in_=t1[:], func=AF.Ln, bias=1.0)
                    scl.activation(out=t2[:], in_=pss[jb][:], func=AF.Relu,
                                   bias=bcol[:, i:i + 1]).then_inc(ev, 1)
                S.emit("scalar", fev)
                ev_now = S.bump("ev", 1)
                def fv(v, ev_now=ev_now):
                    v.wait_ge(ev, ev_now)
                    v.tensor_tensor(stg[:], t1[:], t2[:], ALU.add).then_inc(vv, 1)
                S.emit("vector", fv)
                vv_now = S.bump("vv", 1)
                def fst(s, i=i, j=j, vv_now=vv_now):
                    s.wait_ge(vv, vv_now)
                    s.dma_start(out=dT[i * 128:(i + 1) * 128, j * 512:(j + 1) * 512],
                                in_=stg[:]).then_inc(st, 16)
                S.emit("sync", fst)
                S.bump("st", 16)
            else:
                st_need = C["st"]
                def fev(v, jb=jb, pe_now=pe_now, st_need=st_need):
                    v.wait_ge(pe, pe_now)
                    v.wait_ge(st, st_need)
                    v.tensor_scalar(stg[0:32, :], pss[jb][0:32, :], bcol[0:32, 8:9], None,
                                    ALU.add).then_inc(vv, 1)
                S.emit("vector", fev)
                vv_now = S.bump("vv", 1)
                def fst(s, j=j, vv_now=vv_now):
                    s.wait_ge(vv, vv_now)
                    s.dma_start(out=bc[:, j * 512:(j + 1) * 512], in_=stg[0:32, :]).then_inc(st, 16)
                S.emit("sync", fst)
                S.bump("st", 16)
    final_st = C["st"]
    def f(s, final_st=final_st):
        s.wait_ge(st, final_st)
    S.emit("sync", f)
    block_cm = p.nc.Block(); block = block_cm.__enter__()
    S.install(block)
    block_cm.__exit__(None, None, None)
    return p


def build_scan():
    p = Prog("scan")
    dT = p.inp("deltaT", [D, T])
    xnT = p.inp("xn1T", [D, T], BF16)
    bc = p.inp("bc", [32, T])
    Alog = p.inp("A_log", [D, 16])
    Dp = p.inp("Dp", [D])
    yT = p.out("yT", [D, T], BF16)

    at = p.sb("at", [128, 16 * 8], F32)
    dpc = p.sb("dpc", [128, 8], F32)
    dg = p.sb("dg", [128, T], F32)
    xg = p.sb("xg", [128, T], BF16)
    dx = p.sb("dx", [128, T], BF16)
    brep = p.sb("brep", [128, T], BF16)
    crep = p.sb("crep", [128, T], BF16)
    atile = p.sb("atile", [128, T], F32)
    util = p.sb("util", [128, T], BF16)
    htile = p.sb("htile", [128, T], BF16)
    yacc = p.sb("yacc", [128, T], F32)
    yout = p.sb("yout", [128, T], BF16)

    S = Sched(p)
    ld = S.sem("ld"); sc = S.sem("sc"); vv = S.sem("vv"); st = S.sem("st")
    C = S.cnt

    def f(g):
        g.dma_start(out=at[:], in_=Alog.rearrange("(g p) n -> p g n", p=128),
                    allow_slow_non_contiguous=True).then_inc(ld, 16)
        g.dma_start(out=dpc[:], in_=Dp.rearrange("(g p) -> p g", p=128),
                    allow_slow_non_contiguous=True).then_inc(ld, 16)
    S.emit("gpsimd", f)
    S.bump("ld", 32)
    def f(scl):
        scl.wait_ge(ld, 32)
        scl.activation(out=at[:], in_=at[:], func=AF.Exp, scale=1.0).then_inc(sc, 1)
    S.emit("scalar", f)
    S.bump("sc", 1)
    def f(v):
        v.wait_ge(sc, 1)
        v.tensor_scalar(at[:], at[:], -1.0, None, ALU.mult).then_inc(vv, 1)
    S.emit("vector", f)
    S.bump("vv", 1)
    aprep_vv = C["vv"]

    for g in range(8):
        ld_now = C["ld"]
        vv_g = C["vv"]; sc_g = C["sc"]
        def f(s, g=g, vv_g=vv_g, sc_g=sc_g):
            s.wait_ge(vv, vv_g)
            s.wait_ge(sc, sc_g)
            s.dma_start(out=dg[:], in_=dT[g * 128:(g + 1) * 128, :]).then_inc(ld, 16)
            s.dma_start(out=xg[:], in_=xnT[g * 128:(g + 1) * 128, :]).then_inc(ld, 16)
        S.emit("sync", f)
        ld_now = S.bump("ld", 32)
        vbase = C["vv"]
        def f(v, ld_now=ld_now, vbase=vbase):
            v.wait_ge(ld, ld_now)
            v.tensor_tensor(dx[:], dg[:], xg[:], ALU.mult).then_inc(vv, 1)
        S.emit("vector", f)
        S.bump("vv", 1)
        for n in range(16):
            ld_bc = C["ld"]
            vv_bc = C["vv"]
            def f(gp, n=n, vv_bc=vv_bc):
                gp.wait_ge(vv, vv_bc)
                gp.dma_start(out=brep[:], in_=bc[n:n + 1, :].broadcast_to([128, T])).then_inc(ld, 16)
                gp.dma_start(out=crep[:], in_=bc[16 + n:17 + n, :].broadcast_to([128, T])).then_inc(ld, 16)
            S.emit("gpsimd", f)
            ld_bc = S.bump("ld", 32)
            sc_a = C["sc"]
            vv_a = C["vv"]
            def f(scl, g=g, n=n, ld_now=ld_now, vv_a=vv_a):
                scl.wait_ge(ld, ld_now)
                scl.wait_ge(vv, max(vv_a, aprep_vv))
                scl.activation(out=atile[:], in_=dg[:], func=AF.Exp,
                               scale=at[:, g * 16 + n: g * 16 + n + 1]).then_inc(sc, 1)
            S.emit("scalar", f)
            sc_now = S.bump("sc", 1)
            vb = C["vv"]
            def f(v, g=g, n=n, ld_bc=ld_bc, sc_now=sc_now, vb=vb):
                v.wait_ge(ld, ld_bc)
                v.wait_ge(sc, sc_now)
                v.tensor_tensor(util[:], dx[:], brep[:], ALU.mult).then_inc(vv, 1)
                v.wait_ge(vv, vb + 1)
                v.tensor_tensor_scan(htile[:], atile[:], util[:], 0.0,
                                     ALU.mult, ALU.add).then_inc(vv, 1)
                v.wait_ge(vv, vb + 2)
                if n == 0:
                    v.tensor_tensor(yacc[:], htile[:], crep[:], ALU.mult).then_inc(vv, 1)
                else:
                    v.tensor_tensor(util[:], htile[:], crep[:], ALU.mult).then_inc(vv, 1)
                    v.wait_ge(vv, vb + 3)
                    v.tensor_tensor(yacc[:], yacc[:], util[:], ALU.add).then_inc(vv, 1)
            S.emit("vector", f)
            S.bump("vv", 3 if n == 0 else 4)
        st_g = C["st"]
        vb2 = C["vv"]
        def f(v, g=g, st_g=st_g, vb2=vb2):
            v.wait_ge(st, st_g)
            v.wait_ge(vv, vb2)
            v.scalar_tensor_tensor(yout[:], xg[:], dpc[:, g:g + 1], yacc[:],
                                   ALU.mult, ALU.add).then_inc(vv, 1)
        S.emit("vector", f)
        vv_y = S.bump("vv", 1)
        def f(s, g=g, vv_y=vv_y):
            s.wait_ge(vv, vv_y)
            s.dma_start(out=yT[g * 128:(g + 1) * 128, :], in_=yout[:]).then_inc(st, 16)
        S.emit("sync", f)
        S.bump("st", 16)
    final_st = C["st"]
    def f(s, final_st=final_st):
        s.wait_ge(st, final_st)
    S.emit("sync", f)
    block_cm = p.nc.Block(); block = block_cm.__enter__()
    S.install(block)
    block_cm.__exit__(None, None, None)
    return p


def build_finale():
    p = Prog("finale")
    x1 = p.inp("x1", [T, D])
    xnT = p.inp("xn1T", [D, T], BF16)
    yT = p.inp("yT", [D, T], BF16)
    Wout = p.inp("W_out", [D, D])
    bout = p.inp("b_out", [D])
    weight = p.inp("weight", [D, D])
    disha = p.inp("disha", [64, D])
    ident = p.inp("IDENT", [128, 128], BF16)
    OUT = p.out("out", [T, D])

    KC = 8
    wo = [p.sb(f"wo{k}", [128, D], BF16) for k in range(KC)]
    we = [p.sb(f"we{k}", [128, D], BF16) for k in range(KC)]
    wload = p.sb("wload", [128, D], BF16)
    dr = p.sb("dr", [128, D], BF16)
    brep = p.sb("brep", [128, D], BF16)
    idt = p.sb("idt", [128, 128], BF16)
    lhs_y = p.sb("lhs_y", [128, D], BF16)
    lhs_x = p.sb("lhs_x", [128, D], BF16)
    s1t = p.sb("s1t", [128, D], BF16)
    g1t = p.sb("g1t", [128, D], BF16)
    x1t = p.sb("x1t", [128, D], F32)
    sgt = p.sb("sgt", [128, D], BF16)
    ott = p.sb("ott", [128, D], F32)
    pst = p.ps("pst", [128, 128], BF16)
    pss = [p.ps(f"ps{j}", [128, 512], F32) for j in range(4)]

    S = Sched(p)
    ld = S.sem("ld"); pe = S.sem("pe"); ev = S.sem("ev"); st = S.sem("st"); vv = S.sem("vv")
    C = S.cnt

    def f(s):
        s.dma_start(out=idt[:], in_=ident[:]).then_inc(ld, 16)
    S.emit("sync", f)
    S.bump("ld", 16)
    def f(g):
        for k in range(KC):
            g.dma_start(out=wo[k][:], in_=Wout[k * 128:(k + 1) * 128, :]).then_inc(ld, 16)
        g.dma_start(out=brep[:], in_=bout[None, :].broadcast_to([128, D])).then_inc(ld, 16)
        g.dma_start(out=dr[0:64, :], in_=disha[:]).then_inc(ld, 16)
        g.dma_start(out=dr[64:128, :], in_=disha[:]).then_inc(ld, 16)
    S.emit("gpsimd", f)
    S.bump("ld", 16 * (KC + 3))

    for bi in range(8):
        ld_now = C["ld"]
        pe_g = C["pe"]
        def f(g, bi=bi, pe_g=pe_g):
            g.wait_ge(pe, pe_g)
            g.dma_start(out=wload[:], in_=weight[bi * 128:(bi + 1) * 128, :]).then_inc(ld, 16)
        S.emit("gpsimd", f)
        ld_now = S.bump("ld", 16)
        for bj in range(8):
            ev_need = C["ev"]
            def fmm(t, bj=bj, ld_now=ld_now, ev_need=ev_need):
                t.wait_ge(ld, ld_now)
                t.wait_ge(ev, ev_need)
                t.transpose(pst[:], wload[:, bj * 128:(bj + 1) * 128], idt[:]).then_inc(pe, 1)
            S.emit("tensor", fmm)
            pe_now = S.bump("pe", 1)
            def fev(scl, bi=bi, bj=bj, pe_now=pe_now):
                scl.wait_ge(pe, pe_now)
                scl.copy(out=we[bj][:, bi * 128:(bi + 1) * 128], in_=pst[:]).then_inc(ev, 1)
            S.emit("scalar", fev)
            S.bump("ev", 1)
    wt_ev = C["ev"]
    def f(v, wt_ev=wt_ev):
        v.wait_ge(ev, wt_ev)
        v.wait_ge(ld, 16 * (1 + KC + 3))
        for k in range(KC):
            v.tensor_tensor(we[k][:], we[k][:], dr[:], ALU.add)
        v.engine_nop().then_inc(vv, 1)
    S.emit("vector", f)
    weff_vv = S.bump("vv", 1)

    NT = T // 128
    for i in range(NT):
        ld_now = C["ld"]
        pe_g = C["pe"]; vv_g = C["vv"]
        def f(s, i=i, pe_g=pe_g, vv_g=vv_g):
            s.wait_ge(pe, pe_g)
            s.wait_ge(vv, vv_g)
            for k in range(KC):
                s.dma_start(out=lhs_y[:, k * 128:(k + 1) * 128],
                            in_=yT[k * 128:(k + 1) * 128, i * 128:(i + 1) * 128]).then_inc(ld, 16)
                s.dma_start(out=lhs_x[:, k * 128:(k + 1) * 128],
                            in_=xnT[k * 128:(k + 1) * 128, i * 128:(i + 1) * 128]).then_inc(ld, 16)
            s.dma_start(out=x1t[:], in_=x1[i * 128:(i + 1) * 128, :]).then_inc(ld, 16)
        S.emit("sync", f)
        ld_now = S.bump("ld", 16 * (2 * KC + 1))
        for j in range(2):
            ev_need = C["ev"]
            def fmm(t, i=i, j=j, ld_now=ld_now, ev_need=ev_need, weff_vv=weff_vv):
                t.wait_ge(ld, ld_now)
                t.wait_ge(vv, weff_vv)
                t.wait_ge(ev, ev_need)
                for k in range(KC):
                    mm = t.matmul(pss[j][:], lhs_y[:, k * 128:(k + 1) * 128],
                                  wo[k][:, j * 512:(j + 1) * 512],
                                  start=(k == 0), stop=(k == KC - 1))
                mm.then_inc(pe, 1)
                for k in range(KC):
                    mm = t.matmul(pss[2 + j][:], lhs_x[:, k * 128:(k + 1) * 128],
                                  we[k][:, j * 512:(j + 1) * 512],
                                  start=(k == 0), stop=(k == KC - 1))
                mm.then_inc(pe, 1)
            S.emit("tensor", fmm)
            pe_now = S.bump("pe", 2)
            vb = C["vv"]
            def fev(v, j=j, pe_now=pe_now, vb=vb):
                v.wait_ge(pe, pe_now)
                v.tensor_tensor(s1t[:, j * 512:(j + 1) * 512], pss[j][:],
                                brep[:, j * 512:(j + 1) * 512], ALU.add).then_inc(vv, 1)
            S.emit("vector", fev)
            S.bump("vv", 1)
            def fg(scl, j=j, pe_now=pe_now):
                scl.wait_ge(pe, pe_now)
                scl.activation(out=g1t[:, j * 512:(j + 1) * 512], in_=pss[2 + j][:],
                               func=AF.Silu).then_inc(ev, 1)
            S.emit("scalar", fg)
            S.bump("ev", 1)
        mm_done_vv = C["vv"]; mm_done_ev = C["ev"]
        st_g = C["st"]
        vb2 = C["vv"]
        def fout(v, mm_done_vv=mm_done_vv, mm_done_ev=mm_done_ev, st_g=st_g, vb2=vb2):
            v.wait_ge(vv, mm_done_vv)
            v.wait_ge(ev, mm_done_ev)
            v.wait_ge(st, st_g)
            v.tensor_tensor(sgt[:], s1t[:], g1t[:], ALU.mult).then_inc(vv, 1)
            v.wait_ge(vv, vb2 + 1)
            v.tensor_tensor(ott[:], x1t[:], sgt[:], ALU.add).then_inc(vv, 1)
        S.emit("vector", fout)
        vv_out = S.bump("vv", 2)
        def fst(s, i=i, vv_out=vv_out):
            s.wait_ge(vv, vv_out)
            s.dma_start(out=OUT[i * 128:(i + 1) * 128, :], in_=ott[:]).then_inc(st, 16)
        S.emit("sync", fst)
        S.bump("st", 16)
    final_st = C["st"]
    def f(s, final_st=final_st):
        s.wait_ge(st, final_st)
    S.emit("sync", f)
    block_cm = p.nc.Block(); block = block_cm.__enter__()
    S.install(block)
    block_cm.__exit__(None, None, None)
    return p


def kernel_multilaunch(**inputs):
    ident = CONST["IDENT"]
    x = np.ascontiguousarray(np.asarray(inputs["L0_attn_x"], dtype=np.float32))
    nco = x.shape[0]
    cores = list(range(nco))

    def rl(prog, maps):
        return run_launch(prog, maps)

    p_ln = build_ln("ln0")
    r = rl(p_ln, [{"x": x[c], "IDENT": ident} for c in cores])
    xnT = [np.asarray(r.results[c]["xnT"]) for c in cores]

    p_pr = build_l0proj()
    r = rl(p_pr, [{"xnT": xnT[c], "proj_w": inputs["L0_attn_proj_w"],
                   "proj_b": inputs["L0_attn_proj_b"]} for c in cores])
    ptok = [np.asarray(r.results[c]["p_tok"]) for c in cores]

    consts_h = {k: CONST[k] for k in ["F128RE", "F128IM", "S2FWD", "S2FILT",
                                      "S1INV", "G2RE", "G2NIM", "SIGNONE"]}
    filt = np.asarray(inputs["L0_attn_filter_time"], dtype=np.float32)

    p_h0 = build_hyena(0)
    r = rl(p_h0, [{"vsrc": ptok[c], "p_tok": ptok[c], "filt": filt[0, 0], **consts_h}
                  for c in cores])
    v2 = [np.asarray(r.results[c]["vout"]) for c in cores]

    p_h1 = build_hyena(1)
    r = rl(p_h1, [{"vsrc": v2[c], "p_tok": ptok[c], "filt": filt[1, 0], **consts_h}
                  for c in cores])
    stok = [np.asarray(r.results[c]["vout"]) for c in cores]

    p_ff = build_l0ffn()
    r = rl(p_ff, [{"xnT": xnT[c], "W_r": inputs["L0_ffn_W_r"], "W_k": inputs["L0_ffn_W_k"],
                   "W_v": inputs["L0_ffn_W_v"], "mu_r": inputs["L0_ffn_mu_r"],
                   "mu_k": inputs["L0_ffn_mu_k"], "IDENT": ident} for c in cores])
    gtok = [np.asarray(r.results[c]["g_tok"]) for c in cores]

    p_rl = build_resid_ln()
    r = rl(p_rl, [{"x": x[c], "s_tok": stok[c], "g_tok": gtok[c], "IDENT": ident}
                  for c in cores])
    x1 = [np.asarray(r.results[c]["x1"]) for c in cores]
    xn1T = [np.asarray(r.results[c]["xn1T"]) for c in cores]

    p_l1 = build_l1proj()
    r = rl(p_l1, [{"xn1T": xn1T[c], "W_xproj": inputs["L1_attn_W_xproj"],
                   "b_xproj": inputs["L1_attn_b_xproj"]} for c in cores])
    deltaT = [np.asarray(r.results[c]["deltaT"]) for c in cores]
    bc = [np.asarray(r.results[c]["bc"]) for c in cores]

    p_sc = build_scan()
    r = rl(p_sc, [{"deltaT": deltaT[c], "xn1T": xn1T[c], "bc": bc[c],
                   "A_log": inputs["L1_attn_A_log"], "Dp": inputs["L1_attn_Dp"]}
                  for c in cores])
    yT = [np.asarray(r.results[c]["yT"]) for c in cores]

    p_fi = build_finale()
    r = rl(p_fi, [{"x1": x1[c], "xn1T": xn1T[c], "yT": yT[c],
                   "W_out": inputs["L1_attn_W_out"], "b_out": inputs["L1_attn_b_out"],
                   "weight": inputs["L1_ffn_weight"], "disha": inputs["L1_ffn_disha"],
                   "IDENT": ident} for c in cores])
    out = np.stack([np.asarray(r.results[c]["out"], dtype=np.float32) for c in cores])
    return out


class Pool:
    def __init__(self, p, name, cols, dtype):
        self.ap = p.sb(name, [128, cols], dtype)
        self.cols = cols
        self.off = 0

    def alloc(self, n):
        assert self.off + n <= self.cols, (self.off, n, self.cols)
        ap = self.ap[:, self.off:self.off + n]
        self.off += n
        return ap

    def reset(self):
        self.off = 0

    def reset_to(self, off):
        self.off = off


def S_barrier(S, sems):
    snap = {k: S.cnt[k] for k in S.cnt}
    for eng in ["sync", "gpsimd", "scalar", "vector", "tensor"]:
        def f(e, snap=snap, sems=sems):
            for k, v in snap.items():
                if v > 0:
                    e.wait_ge(sems[k], v)
        S.emit(eng, f)


def build_mega(dbg=False):
    p = Prog("mega")
    x_in = p.inp("x", [T, D])
    pw = p.inp("proj_w", [D, 3072])
    pb = p.inp("proj_b", [3072])
    filt_in = p.inp("filt", [2, 2048, D])
    WrT_in = p.inp("W_rT", [D, D]); WkT_in = p.inp("W_kT", [D, D]); WvT_in = p.inp("W_vT", [D, D])
    mur_in = p.inp("mu_r", [D]); muk_in = p.inp("mu_k", [D])
    Alog_in = p.inp("A_log", [D, 16]); Dp_in = p.inp("Dp", [D])
    Wxp_in = p.inp("W_xproj", [D, 1056]); bxp_in = p.inp("b_xproj", [1056])
    Wout_in = p.inp("W_out", [D, D]); bout_in = p.inp("b_out", [D])
    weiT_in = p.inp("weightT", [D, D]); dis_in = p.inp("disha", [64, D])
    ident_in = p.inp("IDENT", [128, 128], BF16)
    f128re_in = p.inp("F128RE", [128, 128], BF16)
    f128im_in = p.inp("F128IM", [128, 128], BF16)
    s2fwd_in = p.inp("S2FWD", [128, 128, 128], BF16)
    s2filt_in = p.inp("S2FILT", [128, 128, 128], BF16)
    s1inv_in = p.inp("S1INV", [2, 128, 128], BF16)
    g2re_in = p.inp("G2RE", [32, 128, 128], BF16)
    g2nim_in = p.inp("G2NIM", [32, 128, 128], BF16)
    signone_in = p.inp("SIGNONE", [2, 128], BF16)
    bfly1_in = p.inp("BFLY1", [128, 128], BF16)
    bfly2_in = p.inp("BFLY2", [128, 128], BF16)
    bflyh1_in = p.inp("BFLYH1", [128, 128], BF16)
    bflyh2_in = p.inp("BFLYH2", [128, 128], BF16)
    OUT = p.out("out", [T, D])

    dram = lambda n, sh, dt=BF16: p.nc.dram_tensor(n, list(sh), dt, kind="Internal").ap()
    xnT_s = dram("xnT_s", [D, T])
    ptok_s = dram("ptok_s", [T, 3072])
    fw1 = dram("fw1", [128, 128, 512])
    Fsp = dram("Fsp_s", [136, 128, 256])
    Zsp = dram("Zsp_s", [136, 128, 256])
    Csp = dram("Csp_s", [136, 128, 256])
    Bsp = dram("Bsp_s", [N2, 2, 128, D])
    v2_s = dram("v2_s", [T, D])
    stok_s = dram("stok_s", [T, D])
    gtok_s = dram("gtok_s", [T, D])
    x1_s = dram("x1_s", [T, D], F32)
    xn1T_s = dram("xn1T_s", [D, T])
    g1_s = dram("g1_s", [T, D])
    if dbg:
        dT_s = p.out("dT_s", [D, T])
        bc_s = p.out("bc_s", [32, T])
        yT_s = p.out("yT_s", [D, T], BF16)
        xn1T_d = p.out("xn1T_d", [D, T], BF16)
    else:
        dT_s = dram("dT_s", [D, T], F32)
        bc_s = dram("bc_s", [32, T], F32)
        yT_s = dram("yT_s", [D, T])
    c2_s = dram("c2_s", [4, 256])

    PB = Pool(p, "PB", 59392, BF16)
    PF = Pool(p, "PF", 16640, F32)
    idt = p.sb("idt", [128, 128], BF16)
    sm = p.sb("smalls", [128, 256], F32)
    sm2 = p.sb("smalls2", [128, 160], F32)
    c2row = p.sb("c2row", [1, 3 * D], BF16)
    ones1 = p.sb("ones1", [1, 128], BF16)
    pst = p.ps("pst", [128, 128], BF16)
    pst2 = p.ps("pst2", [128, 128], BF16)
    ps = [p.ps(f"ps{i}", [128, 512], F32) for i in range(6)]

    S = Sched(p)
    sems = {}
    for nsem in ["ld", "pe", "ev", "st", "vv", "sc", "lg", "st2", "sg"]:
        sems[nsem] = S.sem(nsem)
    ld = sems["ld"]; pe = sems["pe"]; ev = sems["ev"]
    st = sems["st"]; vv = sems["vv"]; sc = sems["sc"]
    lg = sems["lg"]; st2 = sems["st2"]; sg = sems["sg"]
    C = S.cnt

    def bar():
        S_barrier(S, sems)

    def f(s):
        s.dma_start(out=idt[:], in_=ident_in[:]).then_inc(ld, 16)
        s.dma_start(out=ones1[:], in_=signone_in[0:1, :]).then_inc(ld, 16)
    S.emit("sync", f)
    S.bump("ld", 32)
    eps = sm[:, 0:1]
    def f(g):
        g.memset(eps, 1e-5).then_inc(vv, 1)
    S.emit("gpsimd", f)
    S.bump("vv", 1)

    def emit_ln(x_dram, xnT_dram, resid=None):
        PB.reset(); PF.reset()
        res = [PB.alloc(T) for _ in range(8)]
        xnb = [PB.alloc(1024) for _ in range(2)]
        stt = [PB.alloc(1024) for _ in range(2)] if resid else None
        gtt = [PB.alloc(1024) for _ in range(2)] if resid else None
        sgt = PB.alloc(1024) if resid else None
        xt = [PF.alloc(1024) for _ in range(2)]
        x1t = [PF.alloc(1024) for _ in range(2)] if resid else xt
        scrd = PF.alloc(1024)
        stats = [sm[:, 1 + 8 * q: 9 + 8 * q] for q in range(2)]
        NT = T // 128; ND = D // 128
        ev_done = {}
        vv_done = {}
        for i in range(NT):
            par = i % 2
            s1 = stats[par][:, 0:1]; mneg = stats[par][:, 1:2]; msq = stats[par][:, 2:3]
            ssq = stats[par][:, 3:4]; var = stats[par][:, 4:5]; rstd = stats[par][:, 5:6]
            ld_now = C["ld"]
            vv_g = vv_done.get(i - 2, 0)
            sc_g = C["sc"] - 4 if i >= 2 else 0
            def f(s, i=i, par=par, vv_g=vv_g, sc_g=sc_g, x_dram=x_dram, resid=resid):
                if vv_g > 0:
                    s.wait_ge(vv, vv_g)
                if sc_g > 0:
                    s.wait_ge(sc, sc_g)
                s.dma_start(out=xt[par][:], in_=x_dram[i * 128:(i + 1) * 128, :]).then_inc(ld, 16)
                if resid:
                    s.dma_start(out=stt[par][:], in_=resid[0][i * 128:(i + 1) * 128, :]).then_inc(ld, 16)
                    s.dma_start(out=gtt[par][:], in_=resid[1][i * 128:(i + 1) * 128, :]).then_inc(ld, 16)
            S.emit("sync", f)
            ld_now = S.bump("ld", 48 if resid else 16)
            if resid:
                st_g = C["st"] - 16 if i >= 2 else 0
                vb = C["vv"]
                def fv(v, par=par, ld_now=ld_now, vb=vb, st_g=st_g):
                    v.wait_ge(ld, ld_now)
                    if st_g > 0:
                        v.wait_ge(st, st_g)
                    v.tensor_tensor(sgt[:], stt[par][:], gtt[par][:], ALU.mult).then_inc(vv, 1)
                    v.wait_ge(vv, vb + 1)
                    v.tensor_tensor(x1t[par][:], xt[par][:], sgt[:], ALU.add).then_inc(vv, 1)
                S.emit("vector", fv)
                ve_now = S.bump("vv", 2)
                def fst1(s, i=i, par=par, ve_now=ve_now, resid=resid):
                    s.wait_ge(vv, ve_now)
                    s.dma_start(out=resid[2][i * 128:(i + 1) * 128, :],
                                in_=x1t[par][:]).then_inc(st, 16)
                S.emit("sync", fst1)
                S.bump("st", 16)
                ready = ve_now
                rsem = vv
            else:
                ready = ld_now
                rsem = ld
            def fsc1(scl, par=par, ready=ready, rsem=rsem, s1=s1, mneg=mneg, msq=msq, ssq=ssq):
                scl.wait_ge(rsem, ready)
                scl.activation(out=scrd[:], in_=x1t[par][:], func=AF.Copy, accum_out=s1)
                scl.mul(out=mneg, in_=s1, mul=-1.0 / D)
                scl.activation(out=msq, in_=mneg, func=AF.Square)
                scl.activation(out=scrd[:], in_=x1t[par][:], func=AF.Square, accum_out=ssq).then_inc(sc, 1)
            S.emit("scalar", fsc1)
            sc_now = S.bump("sc", 1)
            def fv2(v, sc_now=sc_now, var=var, ssq=ssq, msq=msq):
                v.wait_ge(sc, sc_now)
                v.tensor_scalar(var, ssq, 1.0 / D, msq, ALU.mult, ALU.subtract).then_inc(vv, 1)
            S.emit("vector", fv2)
            ve2 = S.bump("vv", 1)
            def fsc2(scl, ve2=ve2, var=var, rstd=rstd):
                scl.wait_ge(vv, ve2)
                scl.activation(out=rstd, in_=var, func=AF.Ln, bias=eps)
                scl.activation(out=rstd, in_=rstd, func=AF.Exp, scale=-0.5).then_inc(sc, 1)
            S.emit("scalar", fsc2)
            sc2 = S.bump("sc", 1)
            pe_g2 = C["pe"] - ND if i >= 2 else 0
            def fv3(v, par=par, sc2=sc2, pe_g2=pe_g2, mneg=mneg, rstd=rstd):
                v.wait_ge(sc, sc2)
                if pe_g2 > 0:
                    v.wait_ge(pe, pe_g2)
                v.tensor_scalar(xnb[par][:], x1t[par][:], mneg, rstd, ALU.add, ALU.mult).then_inc(vv, 1)
            S.emit("vector", f_ := fv3)
            ve3 = S.bump("vv", 1)
            vv_done[i] = ve3
            for j in range(ND):
                pp = (i * ND + j) % 2
                pstj = pst if pp == 0 else pst2
                ev_need = ev_done.get(i * ND + j - 2, 0)
                def fmm(t, j=j, pstj=pstj, ve3=ve3, ev_need=ev_need, par=par):
                    t.wait_ge(vv, ve3)
                    if ev_need > 0:
                        t.wait_ge(ev, ev_need)
                    t.transpose(pstj[:], xnb[par][:, j * 128:(j + 1) * 128], idt[:]).then_inc(pe, 1)
                S.emit("tensor", fmm)
                pe_now = S.bump("pe", 1)
                def fev(scl, i=i, j=j, pstj=pstj, pe_now=pe_now):
                    scl.wait_ge(pe, pe_now)
                    scl.copy(out=res[j][:, i * 128:(i + 1) * 128], in_=pstj[:]).then_inc(ev, 1)
                S.emit("scalar", fev)
                ev_done[i * ND + j] = S.bump("ev", 1)
        fin_ev = C["ev"]
        def fstall(s, fin_ev=fin_ev, xnT_dram=xnT_dram):
            s.wait_ge(ev, fin_ev)
            for j in range(8):
                s.dma_start(out=xnT_dram[j * 128:(j + 1) * 128, :], in_=res[j][:]).then_inc(st, 16)
        S.emit("sync", fstall)
        S.bump("st", 16 * 8)
        return res

    xn_res = emit_ln(x_in, xnT_s)
    bar()

    def emit_proj():
        PB.reset_to(8 * T)
        wsb = [PB.alloc(1536) for _ in range(8)]
        brep = PB.alloc(3072)
        stg = [PB.alloc(1536) for _ in range(2)]
        NJ = 3
        def f(g):
            g.dma_start(out=brep[:], in_=pb[None, :].broadcast_to([128, 3072])).then_inc(lg, 16)
        S.emit("gpsimd", f)
        b_lg = S.bump("lg", 16)
        for douth in range(2):
            pe_gw = C["pe"]
            def fw(g, douth=douth, pe_gw=pe_gw):
                g.wait_ge(pe, pe_gw)
                for k in range(8):
                    g.dma_start(out=wsb[k][:],
                                in_=pw[k * 128:(k + 1) * 128,
                                       douth * 1536:(douth + 1) * 1536]).then_inc(lg, 16)
            S.emit("gpsimd", fw)
            w_lg = S.bump("lg", 16 * 8)
            for i in range(T // 128):
                vv_need = C["vv"]
                def fmm(t, i=i, w_lg=w_lg, vv_need=vv_need):
                    t.wait_ge(lg, w_lg)
                    t.wait_ge(vv, vv_need)
                    for k in range(8):
                        for j in range(NJ):
                            t.matmul(ps[j][:], xn_res[k][:, i * 128:(i + 1) * 128],
                                     wsb[k][:, j * 512:(j + 1) * 512],
                                     start=(k == 0), stop=(k == 7)).then_inc(pe, 1)
                S.emit("tensor", fmm)
                pe_now = S.bump("pe", 8 * NJ)
                par = i % 2
                st_need = C["st"] - 16 if i >= 2 else 0
                def fevv(v, par=par, douth=douth, pe_now=pe_now, st_need=st_need, b_lg=b_lg):
                    v.wait_ge(pe, pe_now)
                    v.wait_ge(lg, b_lg)
                    if st_need > 0:
                        v.wait_ge(st, st_need)
                    for j in range(NJ):
                        v.tensor_tensor(stg[par][:, j * 512:(j + 1) * 512], ps[j][:],
                                        brep[:, douth * 1536 + j * 512: douth * 1536 + (j + 1) * 512],
                                        ALU.add).then_inc(vv, 1 if j == NJ - 1 else 0)
                S.emit("vector", fevv)
                vv_now = S.bump("vv", 1)
                def fst(s, i=i, par=par, douth=douth, vv_now=vv_now):
                    s.wait_ge(vv, vv_now)
                    s.dma_start(out=ptok_s[i * 128:(i + 1) * 128,
                                           douth * 1536:(douth + 1) * 1536],
                                in_=stg[par][:]).then_inc(st, 16)
                S.emit("sync", fst)
                S.bump("st", 16)
    emit_proj()


    bar()

    def emit_hyena(stage):
        PB.reset(); PF.reset()
        la = PB.alloc(N2 * 1024)
        mid = [la[:, 0:8192], la[:, 8192:16384]]
        s2sb = [la[:, 16384:18432], la[:, 18432:20480]]
        stg2 = [la[:, 20480:24576], la[:, 24576:28672]]
        stg1 = [PB.alloc(2 * 1024) for _ in range(2)]
        s1sb = PB.alloc(256)
        g2t = [PB.alloc(256) for _ in range(4)]
        bre = [PB.alloc(1024) for _ in range(4)]
        bim = [PB.alloc(1024) for _ in range(4)]
        pg = [PB.alloc(1024) for _ in range(4)]
        ot = [PB.alloc(1024) for _ in range(4)]
        cre = PB.alloc(128)
        cim = PB.alloc(128)
        DQ = 256
        zt = la[:, 0:K2F * 2 * DQ]
        ft_ = la[:, K2F * 2 * DQ:2 * K2F * 2 * DQ]
        ct_ = la[:, 2 * K2F * 2 * DQ:3 * K2F * 2 * DQ]
        cti = [la[:, 0:32 * 256], la[:, 32 * 256:64 * 256]]
        stg3 = [la[:, 64 * 256:64 * 256 + 2048], la[:, 64 * 256 + 2048:64 * 256 + 4096]]
        c2k = c2row[:, 0:D]; c2kp = c2row[:, D:2 * D]; c2km = c2row[:, 2 * D:3 * D]
        filt = filt_in[stage]
        vsrc = ptok_s if stage == 0 else v2_s
        vout = v2_s if stage == 0 else stok_s

        def f(s):
            s.dma_start(out=cre[:], in_=f128re_in[:]).then_inc(ld, 16)
            s.dma_start(out=cim[:], in_=f128im_in[:]).then_inc(ld, 16)
            s.dma_start(out=s1sb[:, 0:128], in_=s1inv_in[0]).then_inc(ld, 16)
            s.dma_start(out=s1sb[:, 128:256], in_=s1inv_in[1]).then_inc(ld, 16)
        S.emit("sync", f)
        S.bump("ld", 64)
        def f(g):
            g.dma_start(out=la[0:64, :], in_=filt.rearrange("(a b) d -> a (b d)", b=N2)).then_inc(lg, 16)
        S.emit("gpsimd", f)
        S.bump("lg", 16)

        def fwd(src_is_filter):
            kh = 64 if src_is_filter else 128
            outsp = Fsp if src_is_filter else Zsp
            s2c = s2filt_in if src_is_filter else s2fwd_in
            if not src_is_filter:
                pe_need = C["pe"]
                st_need0 = C["st2"]
                sg_need0 = C["sg"]
                vla = vsrc[:, 0:D].rearrange("(a b) d -> a b d", b=N2) if stage == 0 \
                    else vsrc.rearrange("(a b) d -> a (b d)", b=N2)
                def f(s, pe_need=pe_need, st_need0=st_need0, sg_need0=sg_need0, vla=vla):
                    s.wait_ge(pe, pe_need)
                    s.wait_ge(st2, st_need0)
                    if sg_need0 > 0:
                        s.wait_ge(sg, sg_need0)
                    s.dma_start(out=la[:], in_=vla).then_inc(ld, 16)
                S.emit("sync", f)
                S.bump("ld", 16)
            ld_need = C["ld"]; lg_need = C["lg"]
            ev_base = C["ev"]; st_base = C["st2"]
            for n2 in range(N2):
                par = n2 % 2
                ev_need = ev_base + 4 * n2
                def fmm(t, n2=n2, ld_need=ld_need, lg_need=lg_need, ev_need=ev_need, kh=kh):
                    t.wait_ge(ld, ld_need)
                    t.wait_ge(lg, lg_need)
                    if ev_need > 0:
                        t.wait_ge(ev, ev_need)
                    for dh in range(2):
                        t.matmul(ps[0 + dh][:], cre[0:kh, :],
                                 la[0:kh, n2 * D + dh * 512: n2 * D + dh * 512 + 512],
                                 start=True, stop=True).then_inc(pe, 1)
                        t.matmul(ps[2 + dh][:], cim[0:kh, :],
                                 la[0:kh, n2 * D + dh * 512: n2 * D + dh * 512 + 512],
                                 start=True, stop=True).then_inc(pe, 1)
                S.emit("tensor", fmm)
                pe_now = S.bump("pe", 4)
                st_need = st_base + 16 * (n2 - 1) if n2 >= 2 else 0
                def fev(v, par=par, pe_now=pe_now, st_need=st_need):
                    v.wait_ge(pe, pe_now)
                    if st_need > 0:
                        v.wait_ge(st2, st_need)
                    v.tensor_copy(stg1[par][:, 0:512], ps[0][:])
                    v.tensor_copy(stg1[par][:, 512:1024], ps[1][:])
                    v.tensor_copy(stg1[par][:, 1024:1536], ps[2][:])
                    v.tensor_copy(stg1[par][:, 1536:2048], ps[3][:]).then_inc(ev, 4)
                S.emit("vector", fev)
                ev_now = S.bump("ev", 4)
                def fst(g, n2=n2, par=par, ev_now=ev_now):
                    g.wait_ge(ev, ev_now)
                    for c_ in range(2):
                        g.dma_start(out=fw1[n2 * 4 + c_ * 2: n2 * 4 + c_ * 2 + 2, :, :]
                                    .rearrange("s k e -> k s e"),
                                    in_=stg1[par][:, c_ * 1024:(c_ + 1) * 1024]
                                    .rearrange("k (s e) -> k s e", s=2)).then_inc(st2, 16)
                S.emit("scalar", fst)
                S.bump("st2", 32)
            s1_done_st = C["st2"]
            for kc in range(8):
                par = kc % 2
                pe_guard = C["pe"] - 16 if kc >= 2 else 0
                def fldc(s, kc=kc, par=par, s1_done_st=s1_done_st, s2c=s2c, pe_guard=pe_guard):
                    s.wait_ge(st2, s1_done_st)
                    if pe_guard > 0:
                        s.wait_ge(pe, pe_guard)
                    s.dma_start(out=s2sb[par][:],
                                in_=s2c[kc * 16:(kc + 1) * 16, :, :]
                                .rearrange("k p i -> p k i")).then_inc(ld, 16)
                    s.dma_start(out=mid[par][:],
                                in_=fw1[:, kc * 16:(kc + 1) * 16, :]).then_inc(ld, 16)
                S.emit("sync", fldc)
                ldc_now = S.bump("ld", 32)
                for half in range(2):
                    spar = (kc * 2 + half) % 2
                    stb = stg2[spar]
                    qname = "st2"
                    qsem = st2
                    for k1h in range(8):
                        k1i = half * 8 + k1h
                        k1 = kc * 16 + k1i
                        bank = 2 + (k1i % 4)
                        ev_need = C["ev"] if (kc == 0 and half == 0 and k1h <= 3) else C["ev"] - 3
                        def fmm(t, par=par, k1i=k1i, bank=bank, ldc_now=ldc_now, ev_need=ev_need):
                            t.wait_ge(ld, ldc_now)
                            if ev_need > 0:
                                t.wait_ge(ev, ev_need)
                            t.matmul(ps[bank][:], s2sb[par][:, k1i * 128:(k1i + 1) * 128],
                                     mid[par][:, k1i * 512:(k1i + 1) * 512],
                                     start=True, stop=True).then_inc(pe, 1)
                        S.emit("tensor", fmm)
                        pe_now = S.bump("pe", 1)
                        st_need = C["st2"] - 256 if (kc * 2 + half) >= 2 else 0
                        def fev(v, stb=stb, k1h=k1h, bank=bank, pe_now=pe_now,
                                st_need=st_need, qsem=qsem):
                            v.wait_ge(pe, pe_now)
                            if st_need > 0 and k1h == 0:
                                v.wait_ge(qsem, st_need)
                            v.tensor_copy(stb[:, k1h * 512:(k1h + 1) * 512], ps[bank][:]).then_inc(ev, 1)
                        S.emit("vector", fev)
                        ev_now = S.bump("ev", 1)
                    k1b = kc * 16 + half * 8
                    ev_now = C["ev"]
                    def fst(scl, k1b=k1b, stb=stb, ev_now=ev_now, outsp=outsp, qsem=qsem):
                        scl.wait_ge(ev, ev_now)
                        for co in range(2):
                            for s2 in range(2):
                                for eh in range(2):
                                    scl.dma_start(
                                        out=outsp.rearrange("r k e -> r (k e)")
                                        [co * 64 + s2 * 2 + eh: co * 64 + s2 * 2 + eh + 61: 4,
                                         k1b * 256:(k1b + 8) * 256]
                                        .rearrange("r (q e) -> r q e", q=8),
                                        in_=stb[co * 64 + s2 * 32: co * 64 + s2 * 32 + 16, :]
                                        .rearrange("p (q h e) -> p q h e", q=8, h=2)
                                        [:, :, eh, :]).then_inc(st2, 16)
                                    scl.dma_start(
                                        out=outsp[128 + co * 4 + s2 * 2 + eh,
                                                  k1b:k1b + 8, :].unsqueeze(0),
                                        in_=stb[co * 64 + s2 * 32 + 16: co * 64 + s2 * 32 + 17, :]
                                        .rearrange("p (q h e) -> p q h e", q=8, h=2)
                                        [:, :, eh, :]).then_inc(qsem, 16)
                    S.emit("scalar", fst)
                    S.bump(qname, 256)

        fwd(True)
        fwd(False)

        pwz = la[:, 0:4096]
        pwf = la[:, 4096:8192]
        pwfs = la[:, 8192:12288]
        pwt1 = la[:, 12288:16384]
        pwt2 = la[:, 16384:20480]
        pwct = la[:, 20480:24576]
        zhi = la[:, 24576:24832]
        fhi = la[:, 24832:25088]
        fshi = la[:, 25088:25344]
        thi1 = la[:, 25344:25600]
        thi2 = la[:, 25600:25856]
        c2st = la[:, 25856:26112]
        bfly = PB.alloc(512)
        pw_st_need = C["st2"]
        pw_sg_need = C["sg"]
        def fldb(s):
            s.dma_start(out=bfly[:, 0:128], in_=bfly1_in[:]).then_inc(ld, 16)
            s.dma_start(out=bfly[:, 128:256], in_=bfly2_in[:]).then_inc(ld, 16)
            s.dma_start(out=bfly[:, 256:384], in_=bflyh1_in[:]).then_inc(ld, 16)
            s.dma_start(out=bfly[:, 384:512], in_=bflyh2_in[:]).then_inc(ld, 16)
        S.emit("sync", fldb)
        bf_ld = S.bump("ld", 64)
        Zv = Zsp.rearrange("r k e -> r (k e)")
        Fv = Fsp.rearrange("r k e -> r (k e)")
        Cv = Csp.rearrange("r k e -> r (k e)")
        for kc in range(8):
            c0 = kc * 16 * 256
            vv_prev = C["vv"]
            pe_prev = C["pe"]
            def fld(s, kc=kc, c0=c0, pw_st_need=pw_st_need, pw_sg_need=pw_sg_need,
                    vv_prev=vv_prev, pe_prev=pe_prev):
                s.wait_ge(st2, pw_st_need)
                if pw_sg_need > 0:
                    s.wait_ge(sg, pw_sg_need)
                if kc >= 1:
                    s.wait_ge(vv, vv_prev)
                    s.wait_ge(pe, pe_prev)
                s.dma_start(out=pwz[:], in_=Zv[0:128, c0:c0 + 4096]).then_inc(ld, 16)
                s.dma_start(out=pwf[:], in_=Fv[0:128, c0:c0 + 4096]).then_inc(ld, 16)
                s.dma_start(out=pwfs[0:64, :], in_=Fv[64:128, c0:c0 + 4096]).then_inc(ld, 16)
                s.dma_start(out=pwfs[64:128, :], in_=Fv[0:64, c0:c0 + 4096]).then_inc(ld, 16)
            S.emit("sync", fld)
            ld_now = S.bump("ld", 64)
            vb = C["vv"]
            def fmul(v, ld_now=ld_now, vb=vb):
                v.wait_ge(ld, ld_now)
                v.tensor_tensor(pwt1[:], pwz[:], pwf[:], ALU.mult).then_inc(vv, 1)
                v.tensor_tensor(pwt2[:], pwz[:], pwfs[:], ALU.mult).then_inc(vv, 1)
            S.emit("vector", fmul)
            mul_vv = S.bump("vv", 2)
            for js in range(8):
                bank = 4 + (js % 2)
                ev_need = C["ev"]
                def fbf(t, js=js, bank=bank, mul_vv=mul_vv, ev_need=ev_need, bf_ld=bf_ld):
                    t.wait_ge(vv, mul_vv)
                    t.wait_ge(ld, bf_ld)
                    t.wait_ge(ev, ev_need)
                    t.matmul(ps[bank][:], bfly[:, 0:128], pwt1[:, js * 512:(js + 1) * 512],
                             start=True, stop=False)
                    t.matmul(ps[bank][:], bfly[:, 128:256], pwt2[:, js * 512:(js + 1) * 512],
                             start=False, stop=True).then_inc(pe, 1)
                S.emit("tensor", fbf)
                pe_now = S.bump("pe", 1)
                st_need2 = C["st"]
                def fev(scl, js=js, bank=bank, pe_now=pe_now, st_need2=st_need2):
                    scl.wait_ge(pe, pe_now)
                    if js == 0:
                        scl.wait_ge(st, st_need2)
                    scl.copy(out=pwct[:, js * 512:(js + 1) * 512], in_=ps[bank][:]).then_inc(ev, 1)
                S.emit("scalar", fev)
                S.bump("ev", 1)
            ev_h = C["ev"]
            if kc == 0:
                def fhv(v, ev_h=ev_h):
                    v.wait_ge(ev, ev_h)
                    v.tensor_scalar(pwct[0:4, 0:256], pwct[0:4, 0:256], 0.5, None, ALU.mult).then_inc(vv, 1)
                    v.tensor_scalar(pwct[64:68, 0:256], pwct[64:68, 0:256], 0.5, None, ALU.mult).then_inc(vv, 1)
                S.emit("vector", fhv)
                S.bump("vv", 2)
            vv_now = C["vv"]; ev_now = C["ev"]
            def fstc(s, kc=kc, c0=c0, vv_now=vv_now, ev_now=ev_now):
                s.wait_ge(vv, vv_now)
                s.wait_ge(ev, ev_now)
                s.dma_start(out=Cv[0:128, c0:c0 + 4096], in_=pwct[:]).then_inc(st, 16)
            S.emit("sync", fstc)
            S.bump("st", 16)
        def fldh(s, pw_st_need=pw_st_need, pw_sg_need=pw_sg_need):
            s.wait_ge(st2, pw_st_need)
            if pw_sg_need > 0:
                s.wait_ge(sg, pw_sg_need)
            s.dma_start(out=zhi[0:8, :], in_=Zsp[128:136, 0, :]).then_inc(ld, 16)
            s.dma_start(out=fhi[0:8, :], in_=Fsp[128:136, 0, :]).then_inc(ld, 16)
            s.dma_start(out=fshi[0:4, :], in_=Fsp[132:136, 0, :]).then_inc(ld, 16)
            s.dma_start(out=fshi[4:8, :], in_=Fsp[128:132, 0, :]).then_inc(ld, 16)
        S.emit("sync", fldh)
        hi_ld = S.bump("ld", 64)
        vbh = C["vv"]
        def fmh(v, hi_ld=hi_ld, vbh=vbh):
            v.wait_ge(ld, hi_ld)
            v.tensor_tensor(thi1[0:8, :], zhi[0:8, :], fhi[0:8, :], ALU.mult).then_inc(vv, 1)
            v.tensor_tensor(thi2[0:8, :], zhi[0:8, :], fshi[0:8, :], ALU.mult).then_inc(vv, 1)
        S.emit("vector", fmh)
        mh_vv = S.bump("vv", 2)
        ev_need = C["ev"]
        def fbh(t, mh_vv=mh_vv, ev_need=ev_need, bf_ld=bf_ld):
            t.wait_ge(vv, mh_vv)
            t.wait_ge(ev, ev_need)
            t.matmul(ps[4][0:8, 0:256], bfly[0:8, 256:264], thi1[0:8, :],
                     start=True, stop=False)
            t.matmul(ps[4][0:8, 0:256], bfly[0:8, 384:392], thi2[0:8, :],
                     start=False, stop=True).then_inc(pe, 1)
        S.emit("tensor", fbh)
        peh = S.bump("pe", 1)
        def fevh(scl, peh=peh):
            scl.wait_ge(pe, peh)
            scl.copy(out=c2st[0:8, :], in_=ps[4][0:8, 0:256]).then_inc(ev, 1)
        S.emit("scalar", fevh)
        evh = S.bump("ev", 1)
        def fgat(s, evh=evh):
            s.wait_ge(ev, evh)
            s.dma_start(out=c2_s[:], in_=c2st[0:4, :]).then_inc(ld, 16)
        S.emit("sync", fgat)
        gat_ld = S.bump("ld", 16)
        def fgat2(s, gat_ld=gat_ld):
            s.wait_ge(ld, gat_ld)
            s.dma_start(out=c2k[:], in_=c2_s.rearrange("s e -> (s e)").unsqueeze(0)).then_inc(ld, 16)
        S.emit("sync", fgat2)
        c2k_ld = S.bump("ld", 16)
        def f(scl, c2k_ld=c2k_ld):
            scl.wait_ge(ld, c2k_ld)
            scl.mul(out=c2kp[:], in_=c2k[:], mul=0.5)
            scl.mul(out=c2km[:], in_=c2k[:], mul=-0.5).then_inc(ev, 1)
        S.emit("scalar", f)
        c2k_ev = S.bump("ev", 1)
        pw_done_st = C["st"]

        for kc in range(4):
            par = kc % 2
            pe_guard = C["pe"] - 32 if kc >= 2 else 0
            def fld(s, kc=kc, par=par, pw_done_st=pw_done_st, pe_guard=pe_guard):
                s.wait_ge(st, pw_done_st)
                if pe_guard > 0:
                    s.wait_ge(pe, pe_guard)
                s.dma_start(out=cti[par][:],
                            in_=Csp[0:128, kc * 32:(kc + 1) * 32, :]).then_inc(ld, 16)
            S.emit("sync", fld)
            ld_now = S.bump("ld", 16)
            for co in range(2):
                for jg in range(4):
                    m = kc * 8 + co * 4 + jg
                    spar = m % 2
                    stb = stg3[spar]
                    qname = "st2"
                    qsem = st2
                    for jh in range(4):
                        js = jg * 4 + jh
                        bank = 2 + (js % 4)
                        ev_need = C["ev"] if (kc == 0 and co == 0 and js <= 3) else C["ev"] - 3
                        def fmm(t, par=par, co=co, js=js, bank=bank, ld_now=ld_now, ev_need=ev_need):
                            t.wait_ge(ld, ld_now)
                            if ev_need > 0:
                                t.wait_ge(ev, ev_need)
                            t.matmul(ps[bank][:], s1sb[:, co * 128:(co + 1) * 128],
                                     cti[par][:, js * 512:(js + 1) * 512],
                                     start=True, stop=True).then_inc(pe, 1)
                        S.emit("tensor", fmm)
                        pe_now = S.bump("pe", 1)
                        st_need = C["st2"] - 64 if m >= 2 else 0
                        def fev(v, stb=stb, jh=jh, bank=bank, pe_now=pe_now,
                                st_need=st_need, qsem=qsem):
                            v.wait_ge(pe, pe_now)
                            if st_need > 0 and jh == 0:
                                v.wait_ge(qsem, st_need)
                            v.tensor_copy(stb[:, jh * 512:(jh + 1) * 512], ps[bank][:]).then_inc(ev, 1)
                        S.emit("vector", fev)
                        S.bump("ev", 1)
                    ev_now = C["ev"]
                    k1b = kc * 32 + jg * 8
                    def fst(scl, co=co, k1b=k1b, stb=stb, ev_now=ev_now, qsem=qsem):
                        scl.wait_ge(ev, ev_now)
                        for s4 in range(4):
                            scl.dma_start(out=Bsp[:, co, k1b:k1b + 8, s4 * 256:(s4 + 1) * 256],
                                          in_=stb[s4 * 32:(s4 + 1) * 32, :]
                                          .rearrange("a (k e) -> a k e", k=8)).then_inc(qsem, 16)
                    S.emit("scalar", fst)
                    S.bump(qname, 64)
        inv1_done_st = C["st2"]
        inv1_done_sg = C["sg"]

        gate_vv = {}
        mm_pe = {}
        st_done = {}
        inv1_ev = C["ev"]
        for t5 in range(N2):
            par = t5 % 4
            pe_guard = mm_pe.get(t5 - 4, 0)
            vv_guard = gate_vv.get((t5 - 4, 1), 0)
            def fld(s, t5=t5, par=par, inv1_done_st=inv1_done_st,
                    pe_guard=pe_guard, vv_guard=vv_guard):
                s.wait_ge(st2, inv1_done_st)
                if pe_guard > 0:
                    s.wait_ge(pe, pe_guard)
                if vv_guard > 0:
                    s.wait_ge(vv, vv_guard)
                s.dma_start(out=g2t[par][:, 0:128], in_=g2re_in[t5]).then_inc(ld, 16)
                s.dma_start(out=g2t[par][:, 128:256], in_=g2nim_in[t5]).then_inc(ld, 16)
                s.dma_start(out=bre[par][:], in_=Bsp[t5, 0]).then_inc(ld, 16)
                s.dma_start(out=bim[par][:], in_=Bsp[t5, 1]).then_inc(ld, 16)
                s.dma_start(out=pg[par][:], in_=ptok_s.rearrange("(m b) e -> m (b e)", b=N2)
                            [:, t5 * 3072 + (stage + 1) * D: t5 * 3072 + (stage + 2) * D]).then_inc(ld, 16)
            S.emit("sync", fld)
            ld_now = S.bump("ld", 80)
            for dh in range(2):
                bank = 2 + ((t5 * 2 + dh) % 4)
                vv_need = gate_vv.get((t5 - 2, dh), 0)
                def fmm(t, dh=dh, t5=t5, par=par, bank=bank, ld_now=ld_now, vv_need=vv_need,
                        c2k_ev=c2k_ev, inv1_ev=inv1_ev):
                    t.wait_ge(ld, ld_now)
                    t.wait_ge(ev, max(c2k_ev, inv1_ev))
                    if vv_need > 0:
                        t.wait_ge(vv, vv_need)
                    t.matmul(ps[bank][:], g2t[par][:, 0:128], bre[par][:, dh * 512:(dh + 1) * 512],
                             start=True, stop=False)
                    t.matmul(ps[bank][:], g2t[par][:, 128:256], bim[par][:, dh * 512:(dh + 1) * 512],
                             start=False, stop=False)
                    t.matmul(ps[bank][:], ones1[:],
                             (c2kp if t5 % 2 == 0 else c2km)[:, dh * 512:(dh + 1) * 512],
                             start=False, stop=True).then_inc(pe, 1)
                S.emit("tensor", fmm)
                pe_now = S.bump("pe", 1)
                st_g = st_done.get(t5 - 4, 0)
                def fgate(v, dh=dh, par=par, bank=bank, pe_now=pe_now, st_g=st_g):
                    v.wait_ge(pe, pe_now)
                    if st_g > 0 and dh == 0:
                        v.wait_ge(st, st_g)
                    v.tensor_tensor(ot[par][:, dh * 512:(dh + 1) * 512], ps[bank][:],
                                    pg[par][:, dh * 512:(dh + 1) * 512], ALU.mult).then_inc(vv, 1)
                S.emit("vector", fgate)
                gate_vv[(t5, dh)] = S.bump("vv", 1)
            mm_pe[t5] = C["pe"]
            vv_now = C["vv"]
            def fst(s, t5=t5, par=par, vv_now=vv_now, vout=vout):
                s.wait_ge(vv, vv_now)
                s.dma_start(out=vout.rearrange("(m b) d -> m (b d)", b=N2)
                            [:, t5 * D:(t5 + 1) * D], in_=ot[par][:]).then_inc(st, 16)
            S.emit("sync", fst)
            st_done[t5] = S.bump("st", 16)

    emit_hyena(0)
    bar()
    emit_hyena(1)
    bar()

    def emit_l0ffn():
        PB.reset(); PF.reset()
        wTr = PB.alloc(8 * 1024)
        wTk = PB.alloc(8 * 1024)
        wTv = PB.alloc(8 * 1024)
        wT = {"r": wTr, "k": wTk, "v": wTv}
        wload = PB.alloc(1024)
        xnc = PB.alloc(8 * 1025)
        xr = PB.alloc(8 * 1024)
        xk = PB.alloc(8 * 1024)
        rt = PB.alloc(8 * 1024)
        kact = xr
        gt = xnc
        trs = PB.alloc(256)
        tmp = PF.alloc(1024)
        sq = PF.alloc(512)
        mu = sm2[:, 0:16]
        def f(g):
            g.dma_start(out=mu[:, 0:8], in_=mur_in.rearrange("(g p) -> p g", p=128),
                        allow_slow_non_contiguous=True).then_inc(lg, 16)
            g.dma_start(out=mu[:, 8:16], in_=muk_in.rearrange("(g p) -> p g", p=128),
                        allow_slow_non_contiguous=True).then_inc(lg, 16)
        S.emit("gpsimd", f)
        mu_lg = S.bump("lg", 32)
        def fw(g):
            for n, W in [("r", WrT_in), ("k", WkT_in), ("v", WvT_in)]:
                for bj in range(8):
                    g.dma_start(out=wT[n][:, bj * 1024:(bj + 1) * 1024],
                                in_=W[bj * 128:(bj + 1) * 128, :]).then_inc(lg, 16)
        S.emit("gpsimd", fw)
        wprep_lg = S.bump("lg", 16 * 24)
        NCH = 4; TC = 1024
        for ch in range(NCH):
            t0 = ch * TC
            vv_g = C["vv"]; pe_gd = C["pe"]; ev_g0 = C["ev"]
            def f(s, ch=ch, t0=t0, vv_g=vv_g, pe_gd=pe_gd, ev_g0=ev_g0):
                s.wait_ge(vv, vv_g)
                s.wait_ge(pe, pe_gd)
                s.wait_ge(ev, ev_g0)
                for g in range(8):
                    if ch == 0:
                        s.dma_start(out=xnc[:, g * 1025 + 1: (g + 1) * 1025],
                                    in_=xnT_s[g * 128:(g + 1) * 128, 0:TC]).then_inc(ld, 16)
                    else:
                        s.dma_start(out=xnc[:, g * 1025: (g + 1) * 1025],
                                    in_=xnT_s[g * 128:(g + 1) * 128, t0 - 1: t0 + TC]).then_inc(ld, 16)
            S.emit("sync", f)
            ld_now = S.bump("ld", 16 * 8)
            if ch == 0:
                def f(v, ld_now=ld_now):
                    v.wait_ge(ld, ld_now)
                    for g in range(8):
                        v.memset(xnc[:, g * 1025: g * 1025 + 1], 0.0)
                    v.engine_nop().then_inc(vv, 1)
                S.emit("vector", f)
                S.bump("vv", 1)
            for g in range(8):
                vv_base = C["vv"]
                ev_g = C["ev"]
                def f(v, g=g, ld_now=ld_now, vv_base=vv_base, ev_g=ev_g):
                    v.wait_ge(ld, ld_now)
                    if g == 0:
                        v.wait_ge(vv, vv_base)
                        v.wait_ge(ev, ev_g)
                        v.wait_ge(lg, mu_lg)
                    xp = xnc[:, g * 1025: g * 1025 + 1024]
                    xc = xnc[:, g * 1025 + 1: g * 1025 + 1025]
                    v.tensor_tensor(tmp[:], xc, xp, ALU.subtract).then_inc(vv, 1)
                    v.wait_ge(vv, vv_base + 1)
                    v.scalar_tensor_tensor(xr[:, g * 1024:(g + 1) * 1024], tmp[:],
                                           mu[:, g:g + 1], xp, ALU.mult, ALU.add).then_inc(vv, 1)
                    v.scalar_tensor_tensor(xk[:, g * 1024:(g + 1) * 1024], tmp[:],
                                           mu[:, 8 + g:9 + g], xp, ALU.mult, ALU.add).then_inc(vv, 1)
                S.emit("vector", f)
                S.bump("vv", 3)
            mix_vv = C["vv"]
            for (nm, srcb, dstb) in [("r", xr, rt), ("k", xk, kact)]:
                for bi in range(8):
                    for j in range(2):
                        ev_need = C["ev"]
                        def fmm(t, nm=nm, srcb=srcb, bi=bi, j=j, mix_vv=mix_vv,
                                ev_need=ev_need, wprep_lg=wprep_lg):
                            t.wait_ge(vv, mix_vv)
                            t.wait_ge(lg, wprep_lg)
                            if ev_need > 0:
                                t.wait_ge(ev, ev_need)
                            for kb in range(8):
                                mm = t.matmul(ps[j][:],
                                              wT[nm][:, kb * 1024 + bi * 128: kb * 1024 + (bi + 1) * 128],
                                              srcb[:, kb * 1024 + j * 512: kb * 1024 + j * 512 + 512],
                                              start=(kb == 0), stop=(kb == 7))
                            mm.then_inc(pe, 1)
                        S.emit("tensor", fmm)
                        pe_now = S.bump("pe", 1)
                        vv_need = C["vv"]
                        def fev(scl, nm=nm, dstb=dstb, bi=bi, j=j, pe_now=pe_now, vv_need=vv_need):
                            scl.wait_ge(pe, pe_now)
                            scl.wait_ge(vv, vv_need)
                            piece = dstb[:, bi * 1024 + j * 512: bi * 1024 + j * 512 + 512]
                            if nm == "r":
                                scl.activation(out=piece, in_=ps[j][:], func=AF.Sigmoid).then_inc(ev, 1)
                            else:
                                scl.activation(out=sq[:], in_=ps[j][:], func=AF.Relu)
                                scl.activation(out=piece, in_=sq[:], func=AF.Square).then_inc(ev, 1)
                        S.emit("scalar", fev)
                        S.bump("ev", 1)
            rk_ev = C["ev"]
            for bi in range(8):
                for j in range(2):
                    ev_need = C["ev"]
                    def fmm(t, bi=bi, j=j, rk_ev=rk_ev, ev_need=ev_need):
                        t.wait_ge(ev, max(rk_ev, ev_need))
                        for kb in range(8):
                            mm = t.matmul(ps[j][:],
                                          wT["v"][:, kb * 1024 + bi * 128: kb * 1024 + (bi + 1) * 128],
                                          kact[:, kb * 1024 + j * 512: kb * 1024 + j * 512 + 512],
                                          start=(kb == 0), stop=(kb == 7))
                        mm.then_inc(pe, 1)
                    S.emit("tensor", fmm)
                    pe_now = S.bump("pe", 1)
                    ev_need2 = C["ev"]
                    def fgv(v, bi=bi, j=j, pe_now=pe_now, ev_need2=ev_need2):
                        v.wait_ge(pe, pe_now)
                        v.wait_ge(ev, ev_need2)
                        piece = rt[:, bi * 1024 + j * 512: bi * 1024 + j * 512 + 512]
                        v.tensor_tensor(tmp[:, 0:512], ps[j][:], piece, ALU.mult).then_inc(vv, 1)
                    S.emit("vector", fgv)
                    vv_now = S.bump("vv", 1)
                    pe_g2 = C["pe"]
                    def fge(scl, bi=bi, j=j, vv_now=vv_now, pe_g2=pe_g2):
                        scl.wait_ge(vv, vv_now)
                        scl.wait_ge(pe, pe_g2)
                        scl.activation(out=gt[:, bi * 1025 + j * 512: bi * 1025 + j * 512 + 512],
                                       in_=tmp[:, 0:512], func=AF.Silu).then_inc(ev, 1)
                    S.emit("scalar", fge)
                    S.bump("ev", 1)
            g_ev = C["ev"]
            for it in range(8):
                for jd in range(8):
                    idx2 = it * 8 + jd
                    pstj = pst if idx2 % 2 == 0 else pst2
                    ev_need = C["ev"] if idx2 <= 1 else C["ev"] - 1
                    def fmm(t, it=it, jd=jd, pstj=pstj, g_ev=g_ev, ev_need=ev_need):
                        t.wait_ge(ev, max(g_ev, ev_need))
                        t.transpose(pstj[:], gt[:, jd * 1025 + it * 128: jd * 1025 + (it + 1) * 128],
                                    idt[:]).then_inc(pe, 1)
                    S.emit("tensor", fmm)
                    pe_now = S.bump("pe", 1)
                    st_need = C["st"] - 16 if idx2 >= 2 else 0
                    trsj = trs[:, 0:128] if idx2 % 2 == 0 else trs[:, 128:256]
                    def fev(scl, pstj=pstj, trsj=trsj, pe_now=pe_now, st_need=st_need):
                        scl.wait_ge(pe, pe_now)
                        if st_need > 0:
                            scl.wait_ge(st, st_need)
                        scl.copy(out=trsj, in_=pstj[:]).then_inc(ev, 1)
                    S.emit("scalar", fev)
                    ev_now = S.bump("ev", 1)
                    def fst(s, it=it, jd=jd, t0=t0, trsj=trsj, ev_now=ev_now):
                        s.wait_ge(ev, ev_now)
                        s.dma_start(out=gtok_s[t0 + it * 128: t0 + (it + 1) * 128,
                                               jd * 128:(jd + 1) * 128], in_=trsj).then_inc(st, 16)
                    S.emit("sync", fst)
                    S.bump("st", 16)
    emit_l0ffn()
    bar()

    xn1_res = emit_ln(x_in, xn1T_s, resid=(stok_s, gtok_s, x1_s))
    bar()
    if dbg:
        def fdbgx(s):
            for jj in range(8):
                s.dma_start(out=xn1T_d[jj * 128:(jj + 1) * 128, :],
                            in_=xn1T_s[jj * 128:(jj + 1) * 128, :]).then_inc(st, 16)
        S.emit("sync", fdbgx)
        S.bump("st", 16 * 8)
        bar()

    def emit_l1proj():
        PB.reset_to(8 * T); PF.reset()
        wsb = [PB.alloc(1056) for _ in range(8)]
        t1 = PF.alloc(512)
        t2 = PF.alloc(512)
        stg = PF.alloc(512)
        bcol = sm2[:, 16:25]
        def f(g):
            for k in range(8):
                g.dma_start(out=wsb[k][:], in_=Wxp_in[k * 128:(k + 1) * 128, :]).then_inc(lg, 16)
            g.dma_start(out=bcol[:, 0:8], in_=bxp_in[0:1024].rearrange("(g p) -> p g", p=128),
                        allow_slow_non_contiguous=True).then_inc(lg, 16)
            g.dma_start(out=bcol[0:32, 8:9], in_=bxp_in[1024:1056].unsqueeze(1),
                        allow_slow_non_contiguous=True).then_inc(lg, 16)
        S.emit("gpsimd", f)
        w_ld = S.bump("lg", 16 * 10)
        for j in range(T // 512):
            ld_now = C["ld"]
            for i in range(9):
                hi = 128 if i < 8 else 32
                jb = i % 2
                ev_need = C["ev"]; vv_need = C["vv"]
                def fmm(t, i=i, j=j, jb=jb, hi=hi, w_ld=w_ld,
                        ev_need=ev_need, vv_need=vv_need):
                    t.wait_ge(lg, w_ld)
                    t.wait_ge(ev, ev_need)
                    t.wait_ge(vv, vv_need)
                    for k in range(8):
                        mm = t.matmul(ps[jb][0:hi, :], wsb[k][:, i * 128:i * 128 + hi],
                                      xn1_res[k][:, j * 512:(j + 1) * 512],
                                      start=(k == 0), stop=(k == 7))
                    mm.then_inc(pe, 1)
                S.emit("tensor", fmm)
                pe_now = S.bump("pe", 1)
                if i < 8:
                    st_need = C["st"]
                    def fev(scl, i=i, jb=jb, pe_now=pe_now, st_need=st_need):
                        scl.wait_ge(pe, pe_now)
                        scl.wait_ge(st, st_need)
                        scl.activation(out=t1[:], in_=ps[jb][:], func=AF.Abs, bias=bcol[:, i:i + 1])
                        scl.activation(out=t1[:], in_=t1[:], func=AF.Exp, scale=-1.0)
                        scl.activation(out=t1[:], in_=t1[:], func=AF.Ln, bias=1.0)
                        scl.activation(out=t2[:], in_=ps[jb][:], func=AF.Relu,
                                       bias=bcol[:, i:i + 1]).then_inc(ev, 1)
                    S.emit("scalar", fev)
                    ev_now = S.bump("ev", 1)
                    def fv(v, ev_now=ev_now):
                        v.wait_ge(ev, ev_now)
                        v.tensor_tensor(stg[:], t1[:], t2[:], ALU.add).then_inc(vv, 1)
                    S.emit("vector", fv)
                    vv_now = S.bump("vv", 1)
                    def fst(s, i=i, j=j, vv_now=vv_now):
                        s.wait_ge(vv, vv_now)
                        s.dma_start(out=dT_s[i * 128:(i + 1) * 128, j * 512:(j + 1) * 512],
                                    in_=stg[:]).then_inc(st, 16)
                    S.emit("sync", fst)
                    S.bump("st", 16)
                else:
                    st_need = C["st"]
                    def fev(v, jb=jb, pe_now=pe_now, st_need=st_need):
                        v.wait_ge(pe, pe_now)
                        v.wait_ge(st, st_need)
                        v.tensor_scalar(stg[0:32, :], ps[jb][0:32, :], bcol[0:32, 8:9], None,
                                        ALU.add).then_inc(vv, 1)
                    S.emit("vector", fev)
                    vv_now = S.bump("vv", 1)
                    def fst(s, j=j, vv_now=vv_now):
                        s.wait_ge(vv, vv_now)
                        s.dma_start(out=bc_s[:, j * 512:(j + 1) * 512],
                                    in_=stg[0:32, :]).then_inc(st, 16)
                    S.emit("sync", fst)
                    S.bump("st", 16)
    emit_l1proj()
    bar()

    def emit_scan():
        PB.reset(); PF.reset()
        at_ = PB.alloc(256).bitcast(F32)
        xg = PB.alloc(T)
        dx = PB.alloc(T)
        brepb = [PB.alloc(T), PB.alloc(T)]
        crepb = [PB.alloc(T), PB.alloc(T)]
        util = PB.alloc(T)
        htile = PB.alloc(T)
        yout = PB.alloc(T)
        dpc8 = PB.alloc(16).bitcast(F32)
        PF.reset()
        dg = PF.alloc(4096)
        atileb = [PF.alloc(4096), PF.alloc(4096)]
        yacc = PF.alloc(4096)
        we = [PB.alloc(1024) for _ in range(8)]
        dr = PB.alloc(1024)
        bout_sb = PB.alloc(1024)
        xnh = [PB.alloc(1024) for _ in range(2)]
        g1t = [PB.alloc(1024) for _ in range(2)]
        wo = [brepb[k // 4][:, (k % 4) * 1024:(k % 4 + 1) * 1024] for k in range(8)]
        yh = [crepb[0][:, 0:1024], crepb[0][:, 1024:2048]]
        s1t = [crepb[0][:, 2048:3072], crepb[0][:, 3072:4096]]
        x1f = crepb[1].bitcast(F32)
        x1t = [x1f[:, 0:1024], x1f[:, 1024:2048]]
        otf = xg.bitcast(F32)
        ott = [otf[:, 0:1024], otf[:, 1024:2048]]
        g1r = [dx[:, 0:1024], dx[:, 1024:2048]]
        sgt = dx[:, 2048:3072]
        ffn_state = {"mm_pe": {}, "silu_ev": {}, "g1_st": {}, "weadd_vv": 0}

        def emit_weight_prep():
            def f(g):
                g.dma_start(out=dr[0:64, :], in_=dis_in[:]).then_inc(lg, 16)
                g.dma_start(out=dr[64:128, :], in_=dis_in[:]).then_inc(lg, 16)
                g.dma_start(out=bout_sb[0:1, :], in_=bout_in[None, :]).then_inc(lg, 16)
                for bj in range(8):
                    g.dma_start(out=we[bj][:],
                                in_=weiT_in[bj * 128:(bj + 1) * 128, :]).then_inc(lg, 16)
            S.emit("gpsimd", f)
            ffn_state["dr_lg"] = S.bump("lg", 16 * 11)

        def emit_weadd():
            dr_lg = ffn_state["dr_lg"]
            def f(v, dr_lg=dr_lg):
                v.wait_ge(lg, dr_lg)
                for k in range(8):
                    v.tensor_tensor(we[k][:], we[k][:], dr[:], ALU.add).then_inc(vv, 1 if k == 7 else 0)
            S.emit("vector", f)
            ffn_state["weadd_vv"] = S.bump("vv", 1)

        def emit_ffn_tile(t):
            par = t % 2
            pe_guard = ffn_state["mm_pe"].get(t - 2, 0)
            def fldx(s, t=t, par=par, pe_guard=pe_guard):
                if pe_guard > 0:
                    s.wait_ge(pe, pe_guard)
                for k in range(8):
                    s.dma_start(out=xnh[par][:, k * 128:(k + 1) * 128],
                                in_=xn1T_s[k * 128:(k + 1) * 128,
                                           t * 128:(t + 1) * 128]).then_inc(sg, 16)
            S.emit("scalar", fldx)
            sg_now = S.bump("sg", 16 * 8)
            for j in range(2):
                ev_need = ffn_state["silu_ev"].get((t - 2, j), 0)
                weadd_vv = ffn_state["weadd_vv"]
                def fmm(tn, t=t, j=j, par=par, sg_now=sg_now, ev_need=ev_need, weadd_vv=weadd_vv):
                    tn.wait_ge(sg, sg_now)
                    tn.wait_ge(vv, weadd_vv)
                    if ev_need > 0:
                        tn.wait_ge(ev, ev_need)
                    for k in range(8):
                        mm = tn.matmul(ps[2 + j][:], xnh[par][:, k * 128:(k + 1) * 128],
                                       we[k][:, j * 512:(j + 1) * 512],
                                       start=(k == 0), stop=(k == 7))
                    mm.then_inc(pe, 1)
                S.emit("tensor", fmm)
                pe_now = S.bump("pe", 1)
                st_guard = ffn_state["g1_st"].get(t - 2, 0)
                def fsil(scl, t=t, j=j, par=par, pe_now=pe_now, st_guard=st_guard):
                    scl.wait_ge(pe, pe_now)
                    if st_guard > 0 and j == 0:
                        scl.wait_ge(st2, st_guard)
                    scl.activation(out=g1t[par][:, j * 512:(j + 1) * 512],
                                   in_=ps[2 + j][:], func=AF.Silu).then_inc(ev, 1)
                S.emit("scalar", fsil)
                ffn_state["silu_ev"][(t, j)] = S.bump("ev", 1)
            ffn_state["mm_pe"][t] = C["pe"]
            ev_now = C["ev"]
            def fstg(scl, t=t, par=par, ev_now=ev_now):
                scl.wait_ge(ev, ev_now)
                scl.dma_start(out=g1_s[t * 128:(t + 1) * 128, :], in_=g1t[par][:]).then_inc(st2, 16)
            S.emit("scalar", fstg)
            ffn_state["g1_st"][t] = S.bump("st2", 16)
        def f(g):
            g.dma_start(out=at_[:], in_=Alog_in.rearrange("(g p) n -> p g n", p=128),
                        allow_slow_non_contiguous=True).then_inc(lg, 16)
            g.dma_start(out=dpc8[:], in_=Dp_in.rearrange("(g p) -> p g", p=128),
                        allow_slow_non_contiguous=True).then_inc(lg, 16)
        S.emit("gpsimd", f)
        a_ld = S.bump("lg", 32)
        def f(scl, a_ld=a_ld):
            scl.wait_ge(lg, a_ld)
            scl.activation(out=at_[:], in_=at_[:], func=AF.Exp, scale=1.0).then_inc(sc, 1)
        S.emit("scalar", f)
        sc_now = S.bump("sc", 1)
        def f(v, sc_now=sc_now):
            v.wait_ge(sc, sc_now)
            v.tensor_scalar(at_[:], at_[:], -1.0, None, ALU.mult).then_inc(vv, 1)
        S.emit("vector", f)
        S.bump("vv", 1)
        aprep_vv = C["vv"]
        bc_done = {}
        scan_done = {}
        for g in range(8):
            vv_g = C["vv"]; sc_g = C["sc"]
            def f(s, g=g, vv_g=vv_g, sc_g=sc_g):
                s.wait_ge(vv, vv_g)
                s.wait_ge(sc, sc_g)
                s.dma_start(out=dg[:], in_=dT_s[g * 128:(g + 1) * 128, :]).then_inc(ld, 16)
                s.dma_start(out=xg[:], in_=xn1T_s[g * 128:(g + 1) * 128, :]).then_inc(ld, 16)
            S.emit("sync", f)
            ld_now = S.bump("ld", 32)
            def f(v, ld_now=ld_now):
                v.wait_ge(ld, ld_now)
                v.tensor_tensor(dx[:], dg[:], xg[:], ALU.mult).then_inc(vv, 1)
            S.emit("vector", f)
            S.bump("vv", 1)
            for n in range(16):
                par = (g * 16 + n) % 2
                brep = brepb[par]; crep = crepb[par]
                i2 = g * 16 + n - 2
                vv_bc = bc_done.get(i2, 0)
                def f(gp, n=n, par=par, vv_bc=vv_bc):
                    if vv_bc > 0:
                        gp.wait_ge(vv, vv_bc)
                    gp.dma_start(out=brepb[par][:], in_=bc_s[n:n + 1, :].broadcast_to([128, T])).then_inc(lg, 16)
                    gp.dma_start(out=crepb[par][:], in_=bc_s[16 + n:17 + n, :].broadcast_to([128, T])).then_inc(lg, 16)
                S.emit("gpsimd", f)
                ld_bc = S.bump("lg", 32)
                atile = atileb[par]
                vv_a = max(scan_done.get(g * 16 + n - 2, 0), aprep_vv)
                def f(scl, g=g, n=n, ld_now=ld_now, vv_a=vv_a, atile=atile):
                    scl.wait_ge(ld, ld_now)
                    scl.wait_ge(vv, vv_a)
                    scl.activation(out=atile[:], in_=dg[:], func=AF.Exp,
                                   scale=at_[:, g * 16 + n: g * 16 + n + 1]).then_inc(sc, 1)
                S.emit("scalar", f)
                sc_now = S.bump("sc", 1)
                vb = C["vv"]
                def f(v, g=g, n=n, ld_bc=ld_bc, sc_now=sc_now, vb=vb, brep=brep, crep=crep, atile=atile):
                    v.wait_ge(lg, ld_bc)
                    v.tensor_tensor(util[:], dx[:], brep[:], ALU.mult).then_inc(vv, 1)
                    v.wait_ge(vv, vb + 1)
                    v.wait_ge(sc, sc_now)
                    v.tensor_tensor_scan(htile[:], atile[:], util[:], 0.0,
                                         ALU.mult, ALU.add).then_inc(vv, 1)
                    v.wait_ge(vv, vb + 2)
                    if n == 0:
                        v.tensor_tensor(yacc[:], htile[:], crep[:], ALU.mult).then_inc(vv, 1)
                    else:
                        v.tensor_tensor(util[:], htile[:], crep[:], ALU.mult).then_inc(vv, 1)
                        v.wait_ge(vv, vb + 3)
                        v.tensor_tensor(yacc[:], yacc[:], util[:], ALU.add).then_inc(vv, 1)
                S.emit("vector", f)
                scan_done[g * 16 + n] = vb + 2
                bc_done[g * 16 + n] = S.bump("vv", 3 if n == 0 else 4)
            st_g = C["st"]
            vb2 = C["vv"]
            def f(v, g=g, st_g=st_g, vb2=vb2):
                if st_g > 0:
                    v.wait_ge(st, st_g)
                v.wait_ge(vv, vb2)
                v.scalar_tensor_tensor(yout[:], xg[:], dpc8[:, g:g + 1], yacc[:],
                                       ALU.mult, ALU.add).then_inc(vv, 1)
            S.emit("vector", f)
            vv_y = S.bump("vv", 1)
            def f(s, g=g, vv_y=vv_y):
                s.wait_ge(vv, vv_y)
                s.dma_start(out=yT_s[g * 128:(g + 1) * 128, :], in_=yout[:]).then_inc(st, 16)
            S.emit("sync", f)
            S.bump("st", 16)
            if g == 0:
                emit_weight_prep()
                emit_weadd()
            else:
                for t_ in range((g - 1) * 5, min(32, g * 5)):
                    emit_ffn_tile(t_)

        scan_st = C["st"]
        def fwo(g, scan_st=scan_st):
            g.wait_ge(st, scan_st)
            for k in range(8):
                g.dma_start(out=wo[k][:], in_=Wout_in[k * 128:(k + 1) * 128, :]).then_inc(lg, 16)
        S.emit("gpsimd", fwo)
        wo_lg = S.bump("lg", 16 * 8)
        op_pe = {}
        s1_ev = {}
        comb_vv = {}
        out_st = {}
        for t in range(32):
            par = t % 2
            pe_g = op_pe.get(t - 2, 0)
            def fldy(s, t=t, par=par, scan_st=scan_st, pe_g=pe_g):
                s.wait_ge(st, scan_st)
                if pe_g > 0:
                    s.wait_ge(pe, pe_g)
                for k in range(8):
                    s.dma_start(out=yh[par][:, k * 128:(k + 1) * 128],
                                in_=yT_s[k * 128:(k + 1) * 128,
                                         t * 128:(t + 1) * 128]).then_inc(ld, 16)
            S.emit("sync", fldy)
            ldy_now = S.bump("ld", 16 * 8)
            cg = comb_vv.get(t - 2, 0)
            def fldx1(s, t=t, par=par, cg=cg):
                if cg > 0:
                    s.wait_ge(vv, cg)
                s.dma_start(out=x1t[par][:], in_=x1_s[t * 128:(t + 1) * 128, :]).then_inc(ld, 16)
            S.emit("sync", fldx1)
            ldx_now = S.bump("ld", 16)
            g1_need = ffn_state["g1_st"][t]
            def fldg1(scl, t=t, par=par, cg=cg, g1_need=g1_need):
                scl.wait_ge(st2, g1_need)
                if cg > 0:
                    scl.wait_ge(vv, cg)
                scl.dma_start(out=g1r[par][:], in_=g1_s[t * 128:(t + 1) * 128, :]).then_inc(ld, 16)
            S.emit("sync", fldg1)
            ldg_now = S.bump("ld", 16)
            for j in range(2):
                ev_need = s1_ev.get((t - 2, j), 0)
                def fmo(tn, t=t, j=j, par=par, ldy_now=ldy_now, ev_need=ev_need, wo_lg=wo_lg):
                    tn.wait_ge(ld, ldy_now)
                    tn.wait_ge(lg, wo_lg)
                    if ev_need > 0:
                        tn.wait_ge(ev, ev_need)
                    for k in range(8):
                        tn.matmul(ps[j][:], yh[par][:, k * 128:(k + 1) * 128],
                                  wo[k][:, j * 512:(j + 1) * 512],
                                  start=(k == 0), stop=False)
                    tn.matmul(ps[j][:], ones1[0:1, :], bout_sb[0:1, j * 512:(j + 1) * 512],
                              start=False, stop=True).then_inc(pe, 1)
                S.emit("tensor", fmo)
                pe_now = S.bump("pe", 1)
                def fse(scl, j=j, par=par, pe_now=pe_now):
                    scl.wait_ge(pe, pe_now)
                    scl.copy(out=s1t[par][:, j * 512:(j + 1) * 512], in_=ps[j][:]).then_inc(ev, 1)
                S.emit("scalar", fse)
                s1_ev[(t, j)] = S.bump("ev", 1)
            op_pe[t] = C["pe"]
            sev = s1_ev[(t, 1)]
            stg_ = out_st.get(t - 2, 0)
            vb = C["vv"]
            def fcomb(v, t=t, par=par, sev=sev, stg_=stg_, vb=vb,
                      ldx_now=ldx_now, ldg_now=ldg_now):
                v.wait_ge(ev, sev)
                v.wait_ge(ld, max(ldx_now, ldg_now))
                if stg_ > 0:
                    v.wait_ge(st, stg_)
                v.tensor_tensor(sgt[:], s1t[par][:], g1r[par][:], ALU.mult).then_inc(vv, 1)
                v.wait_ge(vv, vb + 1)
                v.tensor_tensor(ott[par][:], x1t[par][:], sgt[:], ALU.add).then_inc(vv, 1)
            S.emit("vector", fcomb)
            comb_vv[t] = S.bump("vv", 2)
            def fsto(s, t=t, par=par, cvv=comb_vv[t]):
                s.wait_ge(vv, cvv)
                s.dma_start(out=OUT[t * 128:(t + 1) * 128, :], in_=ott[par][:]).then_inc(st, 16)
            S.emit("sync", fsto)
            out_st[t] = S.bump("st", 16)
    emit_scan()

    def emit_finale():
        PB.reset(); PF.reset()
        wo = [PB.alloc(1024) for _ in range(8)]
        we = [PB.alloc(1024) for _ in range(8)]
        wload = PB.alloc(1024)
        dr = PB.alloc(1024)
        brep = PB.alloc(1024)
        yh = [PB.alloc(2048) for _ in range(8)]
        xnh = [PB.alloc(2048) for _ in range(8)]
        s1t = PB.alloc(1024)
        g1t = PB.alloc(1024)
        sgt = PB.alloc(1024)
        x1t = [PF.alloc(1024) for _ in range(2)]
        ott = [PF.alloc(1024) for _ in range(2)]
        def f(g):
            for k in range(8):
                g.dma_start(out=wo[k][:], in_=Wout_in[k * 128:(k + 1) * 128, :]).then_inc(lg, 16)
            g.dma_start(out=brep[:], in_=bout_in[None, :].broadcast_to([128, D])).then_inc(lg, 16)
            g.dma_start(out=dr[0:64, :], in_=dis_in[:]).then_inc(lg, 16)
            g.dma_start(out=dr[64:128, :], in_=dis_in[:]).then_inc(lg, 16)
        S.emit("gpsimd", f)
        w_ld = S.bump("lg", 16 * 11)
        for bi in range(8):
            pe_g = C["pe"]
            def f(g, bi=bi, pe_g=pe_g):
                g.wait_ge(pe, pe_g)
                g.dma_start(out=wload[:], in_=wei_in[bi * 128:(bi + 1) * 128, :]).then_inc(lg, 16)
            S.emit("gpsimd", f)
            lg_now = S.bump("lg", 16)
            for bj in range(8):
                ev_need = C["ev"]
                def fmm(t, bj=bj, lg_now=lg_now, ev_need=ev_need):
                    t.wait_ge(lg, lg_now)
                    t.wait_ge(ev, ev_need)
                    t.transpose(pst[:], wload[:, bj * 128:(bj + 1) * 128], idt[:]).then_inc(pe, 1)
                S.emit("tensor", fmm)
                pe_now = S.bump("pe", 1)
                def fev(scl, bi=bi, bj=bj, pe_now=pe_now):
                    scl.wait_ge(pe, pe_now)
                    scl.copy(out=we[bj][:, bi * 128:(bi + 1) * 128], in_=pst[:]).then_inc(ev, 1)
                S.emit("scalar", fev)
                S.bump("ev", 1)
        wt_ev = C["ev"]
        def f(v, wt_ev=wt_ev, w_ld=w_ld):
            v.wait_ge(ev, wt_ev)
            v.wait_ge(lg, w_ld)
            for k in range(8):
                v.tensor_tensor(we[k][:], we[k][:], dr[:], ALU.add).then_inc(vv, 1 if k == 7 else 0)
        S.emit("vector", f)
        weff_vv = S.bump("vv", 1)
        fout_done = {}
        for half in range(2):
            pe_g = C["pe"]
            def fldh(s, half=half, pe_g=pe_g):
                s.wait_ge(pe, pe_g)
                for k in range(8):
                    s.dma_start(out=yh[k][:],
                                in_=yT_s[k * 128:(k + 1) * 128,
                                         half * 2048:(half + 1) * 2048]).then_inc(ld, 16)
                    s.dma_start(out=xnh[k][:],
                                in_=xn1T_s[k * 128:(k + 1) * 128,
                                           half * 2048:(half + 1) * 2048]).then_inc(ld, 16)
            S.emit("sync", fldh)
            ldh = S.bump("ld", 16 * 16)
            for il in range(16):
                i = half * 16 + il
                par = i % 2
                ld_now = C["ld"]
                x1_guard = fout_done.get(i - 2, 0)
                def fldx(s, i=i, par=par, x1_guard=x1_guard):
                    if x1_guard > 0:
                        s.wait_ge(vv, x1_guard)
                    s.dma_start(out=x1t[par][:], in_=x1_s[i * 128:(i + 1) * 128, :]).then_inc(ld, 16)
                S.emit("sync", fldx)
                ld_now = S.bump("ld", 16)
                for j in range(2):
                    ev_need = C["ev"]
                    vv_need = C["vv"]
                    def fmm(t, il=il, j=j, ldh=ldh, ev_need=ev_need, vv_need=vv_need, weff_vv=weff_vv):
                        t.wait_ge(ld, ldh)
                        t.wait_ge(vv, max(weff_vv, vv_need))
                        t.wait_ge(ev, ev_need)
                        for k in range(8):
                            mm = t.matmul(ps[j][:], yh[k][:, il * 128:(il + 1) * 128],
                                          wo[k][:, j * 512:(j + 1) * 512],
                                          start=(k == 0), stop=(k == 7))
                        mm.then_inc(pe, 1)
                        for k in range(8):
                            mm = t.matmul(ps[2 + j][:], xnh[k][:, il * 128:(il + 1) * 128],
                                          we[k][:, j * 512:(j + 1) * 512],
                                          start=(k == 0), stop=(k == 7))
                        mm.then_inc(pe, 1)
                    S.emit("tensor", fmm)
                    pe_now = S.bump("pe", 2)
                    vv_g2 = C["vv"]
                    def fev(v, j=j, pe_now=pe_now, vv_g2=vv_g2):
                        v.wait_ge(pe, pe_now)
                        if j == 0:
                            v.wait_ge(vv, vv_g2)
                        v.tensor_tensor(s1t[:, j * 512:(j + 1) * 512], ps[j][:],
                                        brep[:, j * 512:(j + 1) * 512], ALU.add).then_inc(vv, 1)
                    S.emit("vector", fev)
                    S.bump("vv", 1)
                    vv_g3 = C["vv"]
                    def fg(scl, j=j, pe_now=pe_now, vv_g3=vv_g3):
                        scl.wait_ge(pe, pe_now)
                        if j == 0:
                            scl.wait_ge(vv, vv_g3)
                        scl.activation(out=g1t[:, j * 512:(j + 1) * 512], in_=ps[2 + j][:],
                                       func=AF.Silu).then_inc(ev, 1)
                    S.emit("scalar", fg)
                    S.bump("ev", 1)
                mm_done_vv = C["vv"]; mm_done_ev = C["ev"]
                st_g = C["st"] - 16 if i >= 2 else 0
                vb2 = C["vv"]
                def fout(v, par=par, ld_now=ld_now, mm_done_vv=mm_done_vv,
                         mm_done_ev=mm_done_ev, st_g=st_g, vb2=vb2):
                    v.wait_ge(vv, mm_done_vv)
                    v.wait_ge(ev, mm_done_ev)
                    v.wait_ge(ld, ld_now)
                    if st_g > 0:
                        v.wait_ge(st, st_g)
                    v.tensor_tensor(sgt[:], s1t[:], g1t[:], ALU.mult).then_inc(vv, 1)
                    v.wait_ge(vv, vb2 + 1)
                    v.tensor_tensor(ott[par][:], x1t[par][:], sgt[:], ALU.add).then_inc(vv, 1)
                S.emit("vector", fout)
                vv_out = S.bump("vv", 2)
                fout_done[i] = vv_out
                def fst(s, i=i, par=par, vv_out=vv_out):
                    s.wait_ge(vv, vv_out)
                    s.dma_start(out=OUT[i * 128:(i + 1) * 128, :], in_=ott[par][:]).then_inc(st, 16)
                S.emit("sync", fst)
                S.bump("st", 16)


    final_st = C["st"]
    def f(s, final_st=final_st):
        s.wait_ge(st, final_st)
    S.emit("sync", f)

    block_cm = p.nc.Block(); block = block_cm.__enter__()
    S.install(block)
    block_cm.__exit__(None, None, None)
    return p


def kernel_fused(**inputs):
    ident = CONST["IDENT"]
    x = np.ascontiguousarray(np.asarray(inputs["L0_attn_x"], dtype=np.float32))
    nco = x.shape[0]
    p = build_mega()
    filt = np.ascontiguousarray(np.asarray(inputs["L0_attn_filter_time"],
                                           dtype=np.float32)[:, 0])
    tr = lambda a: np.ascontiguousarray(np.asarray(a, dtype=np.float32).T)
    base = {"proj_w": inputs["L0_attn_proj_w"], "proj_b": inputs["L0_attn_proj_b"],
            "filt": filt,
            "W_rT": tr(inputs["L0_ffn_W_r"]), "W_kT": tr(inputs["L0_ffn_W_k"]),
            "W_vT": tr(inputs["L0_ffn_W_v"]), "mu_r": inputs["L0_ffn_mu_r"],
            "mu_k": inputs["L0_ffn_mu_k"], "A_log": inputs["L1_attn_A_log"],
            "Dp": inputs["L1_attn_Dp"], "W_xproj": inputs["L1_attn_W_xproj"],
            "b_xproj": inputs["L1_attn_b_xproj"], "W_out": inputs["L1_attn_W_out"],
            "b_out": inputs["L1_attn_b_out"], "weightT": tr(inputs["L1_ffn_weight"]),
            "disha": inputs["L1_ffn_disha"], "IDENT": ident,
            **{k: CONST[k] for k in ["F128RE", "F128IM", "S2FWD", "S2FILT",
                                     "S1INV", "G2RE", "G2NIM", "SIGNONE",
                                     "BFLY1", "BFLY2", "BFLYH1", "BFLYH2"]}}
    maps = [{"x": x[c], **base} for c in range(nco)]
    res = run_launch(p, maps)
    return np.stack([np.asarray(res.results[c]["out"], dtype=np.float32)
                     for c in range(nco)])


def kernel(**inputs):
    return kernel_fused(**inputs)


# version 24
# speedup vs baseline: 1.2361x; 1.0032x over previous
import numpy as np
import ml_dtypes
import concourse.bass as bass
import concourse.mybir as mybir
from concourse.bass_utils import run_bass_kernel_spmd

F32 = mybir.dt.float32
BF16 = mybir.dt.bfloat16
AF = mybir.ActivationFunctionType
ALU = mybir.AluOpType
NBF = ml_dtypes.bfloat16

T, D = 4096, 1024
N1, N2 = 128, 32
K2F, K2I = 17, 16
NCORES = 8

def _fft_consts():
    n1 = np.arange(N1); k1 = np.arange(N1)
    F128 = np.exp(-2j * np.pi * np.outer(n1, k1) / N1)
    n2 = np.arange(N2); k2 = np.arange(K2F)
    W32 = np.exp(-2j * np.pi * np.outer(n2, k2) / N2)
    tw = np.exp(-2j * np.pi * np.outer(n2, np.arange(N1)) / T)
    M2 = tw.T[:, :, None] * W32[None, :, :]
    t5 = np.arange(N2)
    EI = np.exp(2j * np.pi * np.outer(np.arange(K2I), t5) / N2)
    m = np.arange(N1)
    Finv = np.exp(2j * np.pi * np.outer(np.arange(N1), m) / N1)
    twi = np.exp(2j * np.pi * np.outer(t5, np.arange(N1)) / T)
    G2 = twi[:, :, None] * Finv[None, :, :]
    return F128, M2, EI, G2

_F128, _M2, _EI, _G2 = _fft_consts()

def _pack_s2_fwd(M2, scale=1.0):
    out = np.zeros((N1, 128, 128), np.float32)
    M2re = (M2.real * scale).astype(np.float32)
    M2im = (M2.imag * scale).astype(np.float32)
    for n2 in range(N2):
        for ci in range(2):
            for s2 in range(2):
                p = n2 * 4 + ci * 2 + s2
                for k2 in range(K2F):
                    out[:, p, 0 * 64 + s2 * 32 + k2] = M2re[:, n2, k2] if ci == 0 else -M2im[:, n2, k2]
                    out[:, p, 1 * 64 + s2 * 32 + k2] = M2im[:, n2, k2] if ci == 0 else M2re[:, n2, k2]
    return out.astype(NBF)

def _pack_s1_inv(EI):
    out = np.zeros((2, 128, 128), np.float32)
    EIre = EI.real.astype(np.float32); EIim = EI.imag.astype(np.float32)
    for k2 in range(K2I):
        for ci in range(2):
            for s4 in range(4):
                p = ci * 64 + k2 * 4 + s4
                for t5 in range(N2):
                    i = s4 * 32 + t5
                    out[0, p, i] = EIre[k2, t5] if ci == 0 else -EIim[k2, t5]
                    out[1, p, i] = EIim[k2, t5] if ci == 0 else EIre[k2, t5]
    return out.astype(NBF)

def _bfly():
    b1 = np.zeros((128, 128), np.float32)
    b2 = np.zeros((128, 128), np.float32)
    for i in range(64):
        b1[i, i] = 1.0
        b1[i + 64, i] = -1.0
        b2[i, i + 64] = 1.0
        b2[i + 64, i + 64] = 1.0
    bh = np.zeros((128, 128), np.float32)
    bh2 = np.zeros((128, 128), np.float32)
    for s in range(4):
        bh[s, s] = 1.0
        bh[s + 4, s] = -1.0
        bh2[s, s + 4] = 1.0
        bh2[s + 4, s + 4] = 1.0
    return b1.astype(NBF), b2.astype(NBF), bh.astype(NBF), bh2.astype(NBF)

_B1, _B2, _BH1, _BH2 = _bfly()

CONST = {
    "BFLY1": _B1, "BFLY2": _B2, "BFLYH1": _BH1, "BFLYH2": _BH2,
    "F128RE": _F128.real.astype(np.float32).astype(NBF),
    "F128IM": _F128.imag.astype(np.float32).astype(NBF),
    "S2FWD": _pack_s2_fwd(_M2, 1.0),
    "S2FILT": _pack_s2_fwd(_M2, 2.0 / T),
    "S1INV": _pack_s1_inv(_EI),
    "G2RE": _G2.real.astype(np.float32).astype(NBF),
    "G2NIM": (-_G2.imag).astype(np.float32).astype(NBF),
    "IDENT": np.eye(128, dtype=NBF),
    "SIGNONE": np.stack([np.ones(128, np.float32),
                         -np.ones(128, np.float32)]).astype(NBF),
}

class Prog:
    def __init__(self, name):
        self.nc = bass.Bass("TRN2", target_bir_lowering=False, debug=False)
        self.name = name
        self.ins = {}
        self.outs = {}

    def inp(self, name, shape, dtype=F32):
        ap = self.nc.dram_tensor(name, list(shape), dtype, kind="ExternalInput").ap()
        self.ins[name] = ap
        return ap

    def out(self, name, shape, dtype=F32):
        ap = self.nc.dram_tensor(name, list(shape), dtype, kind="ExternalOutput").ap()
        self.outs[name] = ap
        return ap

    def sb(self, name, shape, dtype=F32):
        return self.nc.alloc_sbuf_tensor(name, list(shape), dtype).ap()

    def ps(self, name, shape, dtype=F32):
        return self.nc.alloc_psum_tensor(name, list(shape), dtype).ap()

    def sem(self, name):
        return self.nc.alloc_semaphore(name=name)


def run_launch(prog, per_core_maps, trace=False):
    maps = [{k: per_core_maps[c][k] for k in prog.ins} for c in range(len(per_core_maps))]
    return run_bass_kernel_spmd(prog.nc, maps, core_ids=list(range(len(per_core_maps))), trace=trace)


def build_ln(name):
    p = Prog(name)
    x = p.inp("x", [T, D])
    ident = p.inp("IDENT", [128, 128], BF16)
    xnT = p.out("xnT", [D, T], BF16)

    xt = p.sb("xt", [128, D], F32)
    xnb = p.sb("xnb", [128, D], BF16)
    s1 = p.sb("s1", [128, 1], F32)
    mneg = p.sb("mneg", [128, 1], F32)
    msq = p.sb("msq", [128, 1], F32)
    ssq = p.sb("ssq", [128, 1], F32)
    var = p.sb("var", [128, 1], F32)
    rstd = p.sb("rstd", [128, 1], F32)
    dummy = p.sb("scrd", [128, D], F32)
    idt = p.sb("idt", [128, 128], BF16)
    trs = p.sb("trs", [128, 128], BF16)
    pst = p.ps("pst", [128, 128], BF16)

    eps = p.sb("eps", [128, 1], F32)
    ld = p.sem("ld"); sc = p.sem("sc"); ve = p.sem("ve")
    pe = p.sem("pe"); ev = p.sem("ev"); st = p.sem("st"); ini = p.sem("ini")
    NT = T // 128; ND = D // 128

    block_cm = p.nc.Block(); block = block_cm.__enter__()

    @block.sync
    def _(sync):
        sync.dma_start(out=idt[:], in_=ident[:]).then_inc(ld, 16)
        for i in range(NT):
            if i >= 1:
                sync.wait_ge(pe, ND * i)
            sync.dma_start(out=xt[:], in_=x[i * 128:(i + 1) * 128, :]).then_inc(ld, 16)
            for j in range(ND):
                sync.wait_ge(ev, i * ND + j + 1)
                sync.dma_start(out=xnT[j * 128:(j + 1) * 128, i * 128:(i + 1) * 128],
                               in_=trs[:]).then_inc(st, 16)
        sync.wait_ge(st, 16 * ND * NT)

    @block.gpsimd
    def _(gpsimd):
        gpsimd.memset(eps[:], 1e-5).then_inc(ini, 1)

    @block.scalar
    def _(scalar):
        scalar.wait_ge(ini, 1)
        for i in range(NT):
            scalar.wait_ge(ld, 16 * (i + 2))
            if i >= 1:
                scalar.wait_ge(ve, 2 * i)
            scalar.activation(out=dummy[:], in_=xt[:], func=AF.Copy, accum_out=s1[:])
            scalar.mul(out=mneg[:], in_=s1[:], mul=-1.0 / D)
            scalar.activation(out=msq[:], in_=mneg[:], func=AF.Square)
            scalar.activation(out=dummy[:], in_=xt[:], func=AF.Square, accum_out=ssq[:]).then_inc(sc, 1)
            scalar.wait_ge(ve, 2 * i + 1)
            scalar.activation(out=rstd[:], in_=var[:], func=AF.Ln, bias=eps[:])
            scalar.activation(out=rstd[:], in_=rstd[:], func=AF.Exp, scale=-0.5).then_inc(sc, 1)

    @block.vector
    def _(vector):
        for i in range(NT):
            vector.wait_ge(sc, 2 * i + 1)
            vector.tensor_scalar(var[:], ssq[:], 1.0 / D, msq[:],
                                 ALU.mult, ALU.subtract).then_inc(ve, 1)
            vector.wait_ge(sc, 2 * i + 2)
            if i >= 1:
                vector.wait_ge(pe, ND * i)
            vector.tensor_scalar(xnb[:], xt[:], mneg[:], rstd[:], ALU.add, ALU.mult).then_inc(ve, 1)
            for j in range(ND):
                vector.wait_ge(pe, i * ND + j + 1)
                if i * ND + j >= 1:
                    vector.wait_ge(st, 16 * (i * ND + j))
                vector.tensor_copy(trs[:], pst[:]).then_inc(ev, 1)

    @block.tensor
    def _(tensor):
        for i in range(NT):
            tensor.wait_ge(ve, 2 * i + 2)
            for j in range(ND):
                if i * ND + j >= 1:
                    tensor.wait_ge(ev, i * ND + j)
                tensor.transpose(pst[:], xnb[:, j * 128:(j + 1) * 128], idt[:]).then_inc(pe, 1)

    block_cm.__exit__(None, None, None)
    return p


def build_l0proj():
    p = Prog("l0proj")
    xnT = p.inp("xnT", [D, T], BF16)
    pw = p.inp("proj_w", [D, 3072])
    pb = p.inp("proj_b", [3072])
    ptok = p.out("p_tok", [T, 3072], BF16)

    KC = D // 128
    NJ = 3072 // 512
    xsb = [p.sb(f"x{k}", [128, T], BF16) for k in range(KC)]
    wsb = [p.sb(f"w{k}", [128, 3072], BF16) for k in range(KC)]
    brep = p.sb("brep", [128, 3072], BF16)
    stg = p.sb("stg", [128, 3072], BF16)
    pss = [p.ps(f"ps{j}", [128, 512], F32) for j in range(NJ)]

    ld = p.sem("ld"); pe = p.sem("pe"); ev = p.sem("ev"); st = p.sem("st")
    NT = T // 128

    block_cm = p.nc.Block(); block = block_cm.__enter__()

    @block.gpsimd
    def _(g):
        for k in range(KC):
            g.dma_start(out=wsb[k][:], in_=pw[k * 128:(k + 1) * 128, :]).then_inc(ld, 16)
        g.dma_start(out=brep[:], in_=pb[None, :].broadcast_to([128, 3072])).then_inc(ld, 16)

    @block.sync
    def _(sync):
        for k in range(KC):
            sync.dma_start(out=xsb[k][:], in_=xnT[k * 128:(k + 1) * 128, :]).then_inc(ld, 16)
        for i in range(NT):
            sync.wait_ge(ev, i + 1)
            sync.dma_start(out=ptok[i * 128:(i + 1) * 128, :], in_=stg[:]).then_inc(st, 16)
        sync.wait_ge(st, 16 * NT)

    @block.tensor
    def _(tensor):
        tensor.wait_ge(ld, 16 * (2 * KC + 1))
        for i in range(NT):
            if i >= 1:
                tensor.wait_ge(ev, i)
            for k in range(KC):
                for j in range(NJ):
                    tensor.matmul(pss[j][:], xsb[k][:, i * 128:(i + 1) * 128],
                                  wsb[k][:, j * 512:(j + 1) * 512],
                                  start=(k == 0), stop=(k == KC - 1)).then_inc(pe, 1)

    @block.vector
    def _(vector):
        for i in range(NT):
            vector.wait_ge(pe, NJ * KC * (i + 1))
            if i >= 1:
                vector.wait_ge(st, 16 * i)
            for j in range(NJ):
                vector.tensor_tensor(stg[:, j * 512:(j + 1) * 512], pss[j][:],
                                     brep[:, j * 512:(j + 1) * 512],
                                     ALU.add).then_inc(ev, 1 if j == NJ - 1 else 0)

    block_cm.__exit__(None, None, None)
    return p


class Sched:
    def __init__(self, p):
        self.p = p
        self.ops = {"sync": [], "gpsimd": [], "scalar": [], "vector": [], "tensor": []}
        self.cnt = {}

    def sem(self, name):
        s = self.p.sem(name)
        self.cnt[name] = 0
        return s

    def emit(self, eng, fn):
        self.ops[eng].append(fn)

    def bump(self, name, by=1):
        self.cnt[name] += by
        return self.cnt[name]

    def install(self, block):
        for eng_name in ["sync", "gpsimd", "scalar", "vector", "tensor"]:
            fns = self.ops[eng_name]
            def mk(fns):
                def run(eng):
                    for f in fns:
                        f(eng)
                return run
            getattr(block, eng_name)(mk(fns))


def build_hyena(stage, phases=5):
    p = Prog(f"hyena{stage}")
    vw = 3072 if stage == 0 else D
    vsrc = p.inp("vsrc", [T, vw], BF16)
    ptok = p.inp("p_tok", [T, 3072], BF16)
    filt = p.inp("filt", [2048, D])
    f128re = p.inp("F128RE", [128, 128], BF16)
    f128im = p.inp("F128IM", [128, 128], BF16)
    s2fwd = p.inp("S2FWD", [128, 128, 128], BF16)
    s2filt = p.inp("S2FILT", [128, 128, 128], BF16)
    s1inv = p.inp("S1INV", [2, 128, 128], BF16)
    g2re = p.inp("G2RE", [32, 128, 128], BF16)
    g2nim = p.inp("G2NIM", [32, 128, 128], BF16)
    signone = p.inp("SIGNONE", [2, 128], BF16)
    vout = p.out("vout", [T, D], BF16)

    fw1 = p.nc.dram_tensor("fw1", [128, N2, 2, D], BF16, kind="Internal").ap()
    Fsp = p.nc.dram_tensor("Fsp", [128, K2F, 2, D], BF16, kind="Internal").ap()
    Zsp = p.nc.dram_tensor("Zsp", [128, K2F, 2, D], BF16, kind="Internal").ap()
    Csp = p.nc.dram_tensor("Csp", [128, K2F, 2, D], BF16, kind="Internal").ap()
    Bsp = p.nc.dram_tensor("Bsp", [N2, 2, 4, 128, 256], BF16, kind="Internal").ap()

    la = p.sb("la", [128, N2 * D], BF16)
    cre = p.sb("cre", [128, 128], BF16)
    cim = p.sb("cim", [128, 128], BF16)
    s2sb = p.sb("s2sb", [128, 128 * 32], BF16)
    stg1 = p.sb("stg1", [128, 2 * D], BF16)
    mid = p.sb("mid", [128, 32 * 512], BF16)
    stg2 = p.sb("stg2", [128, 512], BF16)
    DQ = 256
    zt = la[:, 0:K2F * 2 * DQ]
    ft = la[:, K2F * 2 * DQ:2 * K2F * 2 * DQ]
    ct = la[:, 2 * K2F * 2 * DQ:3 * K2F * 2 * DQ]
    s1sb = p.sb("s1sb", [128, 256], BF16)
    cti = la[:, 0:32 * 256]
    stg3 = p.sb("stg3", [128, 512], BF16)
    g2t = p.sb("g2t", [128, 256], BF16)
    bre = p.sb("bre", [128, D], BF16)
    bim = p.sb("bim", [128, D], BF16)
    c2k = p.sb("c2k", [1, D], BF16)
    c2kp = p.sb("c2kp", [1, D], BF16)
    c2km = p.sb("c2km", [1, D], BF16)
    ones1 = p.sb("ones1", [1, 128], BF16)
    pg = p.sb("pg", [128, D], BF16)
    ot = p.sb("ot", [128, D], BF16)
    ps = [p.ps(f"ps{i}", [128, 512], F32) for i in range(6)]

    S = Sched(p)
    ld = S.sem("ld"); pe = S.sem("pe"); ev = S.sem("ev"); st = S.sem("st"); vv = S.sem("vv")
    C = S.cnt

    def f(s):
        s.dma_start(out=cre[:], in_=f128re[:]).then_inc(ld, 16)
        s.dma_start(out=cim[:], in_=f128im[:]).then_inc(ld, 16)
        s.dma_start(out=s1sb[:, 0:128], in_=s1inv[0]).then_inc(ld, 16)
        s.dma_start(out=s1sb[:, 128:256], in_=s1inv[1]).then_inc(ld, 16)
        s.dma_start(out=ones1[:], in_=signone[0:1, :]).then_inc(ld, 16)
    S.emit("sync", f)
    S.bump("ld", 80)
    def f(g):
        g.dma_start(out=la[0:64, :], in_=filt.rearrange("(a b) d -> a (b d)", b=N2)).then_inc(ld, 16)
    S.emit("gpsimd", f)
    S.bump("ld", 16)

    def fwd(src_is_filter):
        kh = 64 if src_is_filter else 128
        outsp = Fsp if src_is_filter else Zsp
        s2c = s2filt if src_is_filter else s2fwd
        if not src_is_filter:
            pe_need = C["pe"]
            vla = vsrc[:, 0:D].rearrange("(a b) d -> a b d", b=N2) if stage == 0 \
                else vsrc.rearrange("(a b) d -> a (b d)", b=N2)
            def f(s, pe_need=pe_need, vla=vla):
                s.wait_ge(pe, pe_need)
                s.dma_start(out=la[:], in_=vla).then_inc(ld, 16)
            S.emit("sync", f)
            S.bump("ld", 16)
        ld_need = C["ld"]
        st_base = C["st"]; ev_base = C["ev"]
        for n2 in range(N2):
            ev_need = ev_base + 4 * n2
            def fmm(t, n2=n2, ld_need=ld_need, ev_need=ev_need, kh=kh):
                t.wait_ge(ld, ld_need)
                if ev_need > 0:
                    t.wait_ge(ev, ev_need)
                for dh in range(2):
                    t.matmul(ps[0 + dh][:], cre[0:kh, :],
                             la[0:kh, n2 * D + dh * 512: n2 * D + dh * 512 + 512],
                             start=True, stop=True).then_inc(pe, 1)
                    t.matmul(ps[2 + dh][:], cim[0:kh, :],
                             la[0:kh, n2 * D + dh * 512: n2 * D + dh * 512 + 512],
                             start=True, stop=True).then_inc(pe, 1)
            S.emit("tensor", fmm)
            pe_now = S.bump("pe", 4)
            st_need = st_base + 16 * n2
            def fev(sc, pe_now=pe_now, st_need=st_need):
                sc.wait_ge(pe, pe_now)
                if st_need > 0:
                    sc.wait_ge(st, st_need)
                for dh in range(2):
                    sc.copy(out=stg1[:, 0 * D + dh * 512: 0 * D + dh * 512 + 512], in_=ps[0 + dh][:])
                c1 = sc.copy(out=stg1[:, 1 * D + 0 * 512: 1 * D + 0 * 512 + 512], in_=ps[2][:])
                c2 = sc.copy(out=stg1[:, 1 * D + 1 * 512: 1 * D + 1 * 512 + 512], in_=ps[3][:])
                c2.then_inc(ev, 4)
            S.emit("scalar", fev)
            ev_now = S.bump("ev", 4)
            def fst(s, n2=n2, ev_now=ev_now):
                s.wait_ge(ev, ev_now)
                s.dma_start(out=fw1[:, n2, :, :].rearrange("k c d -> k (c d)"),
                            in_=stg1[:]).then_inc(st, 16)
            S.emit("sync", fst)
            S.bump("st", 16)
        s1_done_st = C["st"]
        for kc in range(4):
            ld_mid = C["ld"]
            def fldc(s, kc=kc, s1_done_st=s1_done_st, s2c=s2c):
                s.wait_ge(st, s1_done_st)
                s.dma_start(out=s2sb[:],
                            in_=s2c[kc * 32:(kc + 1) * 32, :, :]
                            .rearrange("k p i -> p k i")).then_inc(ld, 16)
                s.dma_start(out=mid[:],
                            in_=fw1.rearrange("k n c (s e) -> (n c s) k e", s=2)
                            [:, kc * 32:(kc + 1) * 32, :]).then_inc(ld, 16)
            S.emit("sync", fldc)
            ldc_now = S.bump("ld", 32)
            for k1i in range(32):
                k1 = kc * 32 + k1i
                ev_need = C["ev"]
                def fmm(t, k1i=k1i, ldc_now=ldc_now, ev_need=ev_need):
                    t.wait_ge(ld, ldc_now)
                    t.wait_ge(ev, ev_need)
                    t.matmul(ps[4][:], s2sb[:, k1i * 128:(k1i + 1) * 128],
                             mid[:, k1i * 512:(k1i + 1) * 512],
                             start=True, stop=True).then_inc(pe, 1)
                S.emit("tensor", fmm)
                pe_now = S.bump("pe", 1)
                st_need = C["st"]
                def fev(sc, pe_now=pe_now, st_need=st_need):
                    sc.wait_ge(pe, pe_now)
                    sc.wait_ge(st, st_need)
                    sc.copy(out=stg2[:], in_=ps[4][:]).then_inc(ev, 1)
                S.emit("scalar", fev)
                ev_now = S.bump("ev", 1)
                def fst(s, k1=k1, ev_now=ev_now, outsp=outsp):
                    s.wait_ge(ev, ev_now)
                    s.dma_start(out=outsp[k1, :, 0, :].rearrange("k (s e) -> k s e", s=2),
                                in_=stg2[0:34, :]).then_inc(st, 16)
                    s.dma_start(out=outsp[k1, :, 1, :].rearrange("k (s e) -> k s e", s=2),
                                in_=stg2[64:98, :]).then_inc(st, 16)
                S.emit("sync", fst)
                S.bump("st", 32)
            pe_need2 = C["pe"]
            def fbar(s, pe_need2=pe_need2):
                s.wait_ge(pe, pe_need2)
            S.emit("sync", fbar)
        return outsp

    fwd(True)
    if phases >= 2:
        fwd(False)
    if phases < 3:
        final_st = C["st"]
        def f(s, final_st=final_st):
            s.wait_ge(st, final_st)
        S.emit("sync", f)
        block_cm = p.nc.Block(); block = block_cm.__enter__()
        S.install(block)
        block_cm.__exit__(None, None, None)
        return p

    pw_st_need = C["st"]
    for dh in range(4):
        ld_now = C["ld"]
        vv_prev = C["vv"]
        def fld(s, dh=dh, pw_st_need=pw_st_need, vv_prev=vv_prev):
            s.wait_ge(st, pw_st_need)
            if dh >= 1:
                s.wait_ge(vv, vv_prev)
            s.dma_start(out=zt[:], in_=Zsp[:, :, :, dh * 256:(dh + 1) * 256]).then_inc(ld, 16)
            s.dma_start(out=ft[:], in_=Fsp[:, :, :, dh * 256:(dh + 1) * 256]).then_inc(ld, 16)
        S.emit("sync", fld)
        ld_now = S.bump("ld", 32)
        z3 = zt.rearrange("k (a c e) -> k a c e", a=K2F, c=2)
        f3 = ft.rearrange("k (a c e) -> k a c e", a=K2F, c=2)
        c3 = ct.rearrange("k (a c e) -> k a c e", a=K2F, c=2)
        vv_base = C["vv"]
        def fpw(v, ld_now=ld_now, z3=z3, f3=f3, c3=c3, dh=dh, vv_base=vv_base):
            v.wait_ge(ld, ld_now)
            v.tensor_tensor(c3[:, :, 0, :], z3[:, :, 0, :], f3[:, :, 0, :], ALU.mult).then_inc(vv, 1)
            v.tensor_tensor(c3[:, :, 1, :], z3[:, :, 1, :], f3[:, :, 1, :], ALU.mult).then_inc(vv, 1)
            v.wait_ge(vv, vv_base + 2)
            v.tensor_tensor(c3[:, :, 0, :], c3[:, :, 0, :], c3[:, :, 1, :], ALU.subtract).then_inc(vv, 1)
            v.tensor_tensor(z3[:, :, 0, :], z3[:, :, 0, :], f3[:, :, 1, :], ALU.mult).then_inc(vv, 1)
            v.tensor_tensor(z3[:, :, 1, :], z3[:, :, 1, :], f3[:, :, 0, :], ALU.mult).then_inc(vv, 1)
            v.wait_ge(vv, vv_base + 5)
            v.tensor_tensor(c3[:, :, 1, :], z3[:, :, 0, :], z3[:, :, 1, :], ALU.add).then_inc(vv, 1)
            v.wait_ge(vv, vv_base + 6)
            v.tensor_scalar(c3[0:1, 0, :, :].rearrange("k c e -> k (c e)"),
                            c3[0:1, 0, :, :].rearrange("k c e -> k (c e)"),
                            0.5, None, ALU.mult).then_inc(vv, 1)
        S.emit("vector", fpw)
        vv_now = S.bump("vv", 7)
        def fstc(s, dh=dh, vv_now=vv_now):
            s.wait_ge(vv, vv_now)
            s.dma_start(out=Csp[:, :, :, dh * 256:(dh + 1) * 256], in_=ct[:]).then_inc(st, 16)
        S.emit("sync", fstc)
        S.bump("st", 16)
    pw_done_st = C["st"]
    if phases < 4:
        def f(s, pw_done_st=pw_done_st):
            s.wait_ge(st, pw_done_st)
        S.emit("sync", f)
        block_cm = p.nc.Block(); block = block_cm.__enter__()
        S.install(block)
        block_cm.__exit__(None, None, None)
        return p

    def f(s, pw_done_st=pw_done_st):
        s.wait_ge(st, pw_done_st)
        s.dma_start(out=c2k[:], in_=Csp[0:1, 16, 0, :]).then_inc(ld, 16)
    S.emit("sync", f)
    c2k_ld = S.bump("ld", 16)
    def f(sc, c2k_ld=c2k_ld):
        sc.wait_ge(ld, c2k_ld)
        sc.mul(out=c2kp[:], in_=c2k[:], mul=0.5)
        sc.mul(out=c2km[:], in_=c2k[:], mul=-0.5).then_inc(ev, 1)
    S.emit("scalar", f)
    c2k_ev = S.bump("ev", 1)

    for kc in range(4):
        pe_guard = C["pe"]
        ld_now = C["ld"]
        def fld(s, kc=kc, pw_done_st=pw_done_st, pe_guard=pe_guard):
            s.wait_ge(st, pw_done_st)
            s.wait_ge(pe, pe_guard)
            s.dma_start(out=cti[:],
                        in_=Csp.rearrange("k a c (s e) -> (a c s) k e", s=4)
                        [0:128, kc * 32:(kc + 1) * 32, :]).then_inc(ld, 16)
        S.emit("sync", fld)
        ld_now = S.bump("ld", 16)
        for co in range(2):
            for js in range(16):
                ev_need = C["ev"]
                def fmm(t, co=co, js=js, ld_now=ld_now, ev_need=ev_need):
                    t.wait_ge(ld, ld_now)
                    t.wait_ge(ev, ev_need)
                    t.matmul(ps[5][:], s1sb[:, co * 128:(co + 1) * 128],
                             cti[:, js * 512:(js + 1) * 512],
                             start=True, stop=True).then_inc(pe, 1)
                S.emit("tensor", fmm)
                pe_now = S.bump("pe", 1)
                st_need = C["st"]
                def fev(sc, pe_now=pe_now, st_need=st_need):
                    sc.wait_ge(pe, pe_now)
                    sc.wait_ge(st, st_need)
                    sc.copy(out=stg3[:], in_=ps[5][:]).then_inc(ev, 1)
                S.emit("scalar", fev)
                ev_now = S.bump("ev", 1)
                k1b = kc * 32 + js * 2
                def fst(s, co=co, k1b=k1b, ev_now=ev_now):
                    s.wait_ge(ev, ev_now)
                    s.dma_start(out=Bsp[:, co, :, k1b:k1b + 2, :]
                                .rearrange("a s k e -> a s (k e)"),
                                in_=stg3[:]).then_inc(st, 16)
                S.emit("sync", fst)
                S.bump("st", 16)
    inv1_done_st = C["st"]
    if phases < 5:
        def f(s, inv1_done_st=inv1_done_st):
            s.wait_ge(st, inv1_done_st)
            s.wait_ge(ev, C_ev_final)
        C_ev_final = C["ev"]
        S.emit("sync", f)
        block_cm = p.nc.Block(); block = block_cm.__enter__()
        S.install(block)
        block_cm.__exit__(None, None, None)
        return p

    for t5 in range(N2):
        ld_now = C["ld"]
        pe_guard = C["pe"]; vv_guard = C["vv"]
        def fld(s, t5=t5, inv1_done_st=inv1_done_st, pe_guard=pe_guard, vv_guard=vv_guard):
            s.wait_ge(st, inv1_done_st)
            s.wait_ge(pe, pe_guard)
            s.wait_ge(vv, vv_guard)
            s.dma_start(out=g2t[:, 0:128], in_=g2re[t5]).then_inc(ld, 16)
            s.dma_start(out=g2t[:, 128:256], in_=g2nim[t5]).then_inc(ld, 16)
            s.dma_start(out=bre[:], in_=Bsp[t5, 0].rearrange("s k e -> k s e")).then_inc(ld, 16)
            s.dma_start(out=bim[:], in_=Bsp[t5, 1].rearrange("s k e -> k s e")).then_inc(ld, 16)
            s.dma_start(out=pg[:], in_=ptok.rearrange("(m b) e -> m (b e)", b=N2)
                        [:, t5 * 3072 + (stage + 1) * D: t5 * 3072 + (stage + 2) * D]).then_inc(ld, 16)
        S.emit("sync", fld)
        ld_now = S.bump("ld", 80)
        for dh in range(2):
            vv_need = C["vv"] - (1 + dh) if t5 > 0 or dh > 0 else 0
            vv_need = max(0, C["vv"] - 2 + dh) if (t5 > 0) else (C["vv"] if dh == 1 else 0)
            vv_need = C["vv"]
            def fmm(t, dh=dh, t5=t5, ld_now=ld_now, vv_need=vv_need, c2k_ev=c2k_ev):
                t.wait_ge(ld, ld_now)
                t.wait_ge(ev, c2k_ev)
                t.wait_ge(vv, vv_need)
                t.matmul(ps[4 + dh][:], g2t[:, 0:128], bre[:, dh * 512:(dh + 1) * 512],
                         start=True, stop=False)
                t.matmul(ps[4 + dh][:], g2t[:, 128:256], bim[:, dh * 512:(dh + 1) * 512],
                         start=False, stop=False)
                t.matmul(ps[4 + dh][:], ones1[:],
                         (c2kp if t5 % 2 == 0 else c2km)[:, dh * 512:(dh + 1) * 512],
                         start=False, stop=True).then_inc(pe, 1)
            S.emit("tensor", fmm)
            pe_now = S.bump("pe", 1)
            def fgate(v, dh=dh, pe_now=pe_now):
                v.wait_ge(pe, pe_now)
                v.tensor_tensor(ot[:, dh * 512:(dh + 1) * 512], ps[4 + dh][:],
                                pg[:, dh * 512:(dh + 1) * 512], ALU.mult).then_inc(vv, 1)
            S.emit("vector", fgate)
            S.bump("vv", 1)
        vv_now = C["vv"]
        def fst(s, t5=t5, vv_now=vv_now):
            s.wait_ge(vv, vv_now)
            s.dma_start(out=vout.rearrange("(m b) d -> m (b d)", b=N2)
                        [:, t5 * D:(t5 + 1) * D], in_=ot[:]).then_inc(st, 16)
        S.emit("sync", fst)
        S.bump("st", 16)
    final_st = C["st"]
    def f(s, final_st=final_st):
        s.wait_ge(st, final_st)
    S.emit("sync", f)

    block_cm = p.nc.Block(); block = block_cm.__enter__()
    S.install(block)
    block_cm.__exit__(None, None, None)
    return p


def build_l0ffn(dbg=False):
    p = Prog("l0ffn")
    xnT = p.inp("xnT", [D, T], BF16)
    Wr = p.inp("W_r", [D, D]); Wk = p.inp("W_k", [D, D]); Wv = p.inp("W_v", [D, D])
    mur = p.inp("mu_r", [D]); muk = p.inp("mu_k", [D])
    ident = p.inp("IDENT", [128, 128], BF16)
    gtok = p.out("g_tok", [T, D], BF16)
    if dbg:
        dwt = p.out("dwt", [D, D], BF16)
        dxr = p.out("dxr", [D, 1024], BF16)
        drt = p.out("drt", [D, 1024], BF16)
        dka = p.out("dka", [D, 1024], BF16)

    KC = 8
    wload = p.sb("wload", [128, D], BF16)
    wT = {n: p.sb(f"wT{n}", [128, KC * D], BF16) for n in ["r", "k", "v"]}
    mu = p.sb("mu", [128, 16], F32)
    idt = p.sb("idt", [128, 128], BF16)
    xnc = p.sb("xnc", [128, KC * 1025], BF16)
    tmp = p.sb("tmpv", [128, 1024], F32)
    xr = p.sb("xr", [128, KC * 1024], BF16)
    xk = p.sb("xk", [128, KC * 1024], BF16)
    rt = p.sb("rt", [128, KC * 1024], BF16)
    kact = p.sb("kact", [128, KC * 1024], BF16)
    sq = p.sb("sq", [128, 512], F32)
    gt = p.sb("gt", [128, KC * 1024], BF16)
    trs = p.sb("trs2", [128, 128], BF16)
    pst = p.ps("pst2", [128, 128], BF16)
    pss = [p.ps(f"psf{j}", [128, 512], F32) for j in range(2)]

    S = Sched(p)
    ld = S.sem("ld"); pe = S.sem("pe"); ev = S.sem("ev"); st = S.sem("st"); vv = S.sem("vv")
    C = S.cnt

    def f(s):
        s.dma_start(out=idt[:], in_=ident[:]).then_inc(ld, 16)
    S.emit("sync", f)
    S.bump("ld", 16)
    def f(g):
        g.dma_start(out=mu[:, 0:8], in_=mur.rearrange("(g p) -> p g", p=128), allow_slow_non_contiguous=True).then_inc(ld, 16)
        g.dma_start(out=mu[:, 8:16], in_=muk.rearrange("(g p) -> p g", p=128), allow_slow_non_contiguous=True).then_inc(ld, 16)
    S.emit("gpsimd", f)
    S.bump("ld", 32)

    for n, W in [("r", Wr), ("k", Wk), ("v", Wv)]:
        for bi in range(8):
            ld_now = C["ld"]
            pe_g = C["pe"]
            def f(g, W=W, bi=bi, pe_g=pe_g):
                g.wait_ge(pe, pe_g)
                g.dma_start(out=wload[:], in_=W[bi * 128:(bi + 1) * 128, :]).then_inc(ld, 16)
            S.emit("gpsimd", f)
            ld_now = S.bump("ld", 16)
            for bj in range(8):
                ev_need = C["ev"]
                def fmm(t, bj=bj, ld_now=ld_now, ev_need=ev_need):
                    t.wait_ge(ld, ld_now)
                    t.wait_ge(ev, ev_need)
                    t.transpose(pst[:], wload[:, bj * 128:(bj + 1) * 128], idt[:]).then_inc(pe, 1)
                S.emit("tensor", fmm)
                pe_now = S.bump("pe", 1)
                def fev(sc, n=n, bi=bi, bj=bj, pe_now=pe_now):
                    sc.wait_ge(pe, pe_now)
                    sc.copy(out=wT[n][:, bj * 1024 + bi * 128: bj * 1024 + (bi + 1) * 128],
                            in_=pst[:]).then_inc(ev, 1)
                S.emit("scalar", fev)
                S.bump("ev", 1)
    wprep_ev = C["ev"]
    if dbg:
        def fdw(s, wprep_ev=wprep_ev):
            s.wait_ge(ev, wprep_ev)
            for bj in range(8):
                s.dma_start(out=dwt[bj * 128:(bj + 1) * 128, :],
                            in_=wT["r"][:, bj * 1024:(bj + 1) * 1024]).then_inc(st, 16)
        S.emit("sync", fdw)
        S.bump("st", 16 * 8)

    NCH = 4; TC = 1024
    for ch in range(NCH):
        t0 = ch * TC
        ld_now = C["ld"]
        vv_g = C["vv"]
        pe_gd = C["pe"]
        def f(s, ch=ch, t0=t0, vv_g=vv_g, pe_gd=pe_gd):
            s.wait_ge(vv, vv_g)
            s.wait_ge(pe, pe_gd)
            for g in range(KC):
                if ch == 0:
                    s.dma_start(out=xnc[:, g * 1025 + 1: (g + 1) * 1025],
                                in_=xnT[g * 128:(g + 1) * 128, 0:TC]).then_inc(ld, 16)
                else:
                    s.dma_start(out=xnc[:, g * 1025: (g + 1) * 1025],
                                in_=xnT[g * 128:(g + 1) * 128, t0 - 1: t0 + TC]).then_inc(ld, 16)
        S.emit("sync", f)
        ld_now = S.bump("ld", 16 * KC)
        if ch == 0:
            def f(v, ld_now=ld_now):
                v.wait_ge(ld, ld_now)
                for g in range(KC):
                    v.memset(xnc[:, g * 1025: g * 1025 + 1], 0.0)
                v.engine_nop().then_inc(vv, 1)
            S.emit("vector", f)
            S.bump("vv", 1)
        for g in range(KC):
            vv_base = C["vv"]
            ev_g = C["ev"]
            def f(v, g=g, ld_now=ld_now, vv_base=vv_base, ev_g=ev_g):
                v.wait_ge(ld, ld_now)
                if g == 0:
                    v.wait_ge(vv, vv_base)
                    v.wait_ge(ev, ev_g)
                xp = xnc[:, g * 1025: g * 1025 + 1024]
                xc = xnc[:, g * 1025 + 1: g * 1025 + 1025]
                v.tensor_tensor(tmp[:], xc, xp, ALU.subtract).then_inc(vv, 1)
                v.wait_ge(vv, vv_base + 1)
                v.scalar_tensor_tensor(xr[:, g * 1024:(g + 1) * 1024], tmp[:],
                                       mu[:, g:g + 1], xp, ALU.mult, ALU.add).then_inc(vv, 1)
                v.scalar_tensor_tensor(xk[:, g * 1024:(g + 1) * 1024], tmp[:],
                                       mu[:, 8 + g:9 + g], xp, ALU.mult, ALU.add).then_inc(vv, 1)
            S.emit("vector", f)
            S.bump("vv", 3)
        mix_vv = C["vv"]

        for (nm, src, dstname) in [("r", xr, "rt"), ("k", xk, "kact")]:
            for bi in range(8):
                for j in range(2):
                    ev_need = C["ev"]
                    def fmm(t, nm=nm, src=src, bi=bi, j=j, mix_vv=mix_vv, ev_need=ev_need,
                            wprep_ev=wprep_ev):
                        t.wait_ge(vv, mix_vv)
                        t.wait_ge(ev, max(ev_need, wprep_ev))
                        for kb in range(8):
                            mm = t.matmul(pss[j][:],
                                     wT[nm][:, kb * 1024 + bi * 128: kb * 1024 + (bi + 1) * 128],
                                     src[:, kb * 1024 + j * 512: kb * 1024 + j * 512 + 512],
                                     start=(kb == 0), stop=(kb == 7))
                        mm.then_inc(pe, 1)
                    S.emit("tensor", fmm)
                    pe_now = S.bump("pe", 1)
                    vv_need = C["vv"]
                    def fev(sc, nm=nm, dstname=dstname, bi=bi, j=j, pe_now=pe_now, vv_need=vv_need):
                        sc.wait_ge(pe, pe_now)
                        sc.wait_ge(vv, vv_need)
                        dst = (rt if dstname == "rt" else kact)
                        piece = dst[:, bi * 1024 + j * 512: bi * 1024 + j * 512 + 512]
                        if nm == "r":
                            sc.activation(out=piece, in_=pss[j][:], func=AF.Sigmoid).then_inc(ev, 1)
                        else:
                            sc.activation(out=sq[:], in_=pss[j][:], func=AF.Relu)
                            sc.activation(out=piece, in_=sq[:], func=AF.Square).then_inc(ev, 1)
                    S.emit("scalar", fev)
                    S.bump("ev", 1)
        rk_ev = C["ev"]
        for bi in range(8):
            for j in range(2):
                ev_need = C["ev"]
                def fmm(t, bi=bi, j=j, rk_ev=rk_ev, ev_need=ev_need):
                    t.wait_ge(ev, max(rk_ev, ev_need))
                    for kb in range(8):
                        mm = t.matmul(pss[j][:],
                                 wT["v"][:, kb * 1024 + bi * 128: kb * 1024 + (bi + 1) * 128],
                                 kact[:, kb * 1024 + j * 512: kb * 1024 + j * 512 + 512],
                                 start=(kb == 0), stop=(kb == 7))
                    mm.then_inc(pe, 1)
                S.emit("tensor", fmm)
                pe_now = S.bump("pe", 1)
                ev_need2 = C["ev"]
                def fgv(v, bi=bi, j=j, pe_now=pe_now, ev_need2=ev_need2):
                    v.wait_ge(pe, pe_now)
                    v.wait_ge(ev, ev_need2)
                    piece = rt[:, bi * 1024 + j * 512: bi * 1024 + j * 512 + 512]
                    v.tensor_tensor(tmp[:, 0:512], pss[j][:], piece, ALU.mult).then_inc(vv, 1)
                S.emit("vector", fgv)
                vv_now = S.bump("vv", 1)
                pe_g2 = C["pe"]
                def fge(sc, bi=bi, j=j, vv_now=vv_now, pe_g2=pe_g2):
                    sc.wait_ge(vv, vv_now)
                    sc.wait_ge(pe, pe_g2)
                    sc.activation(out=gt[:, bi * 1024 + j * 512: bi * 1024 + j * 512 + 512],
                                  in_=tmp[:, 0:512], func=AF.Silu).then_inc(ev, 1)
                S.emit("scalar", fge)
                S.bump("ev", 1)
        g_ev = C["ev"]
        if dbg and ch == 0:
            def fdbg(s, g_ev=g_ev, mix_vv=mix_vv):
                s.wait_ge(ev, g_ev)
                s.wait_ge(vv, mix_vv)
                for g in range(KC):
                    s.dma_start(out=dxr[g * 128:(g + 1) * 128, :],
                                in_=xr[:, g * 1024:(g + 1) * 1024]).then_inc(st, 16)
                    s.dma_start(out=drt[g * 128:(g + 1) * 128, :],
                                in_=rt[:, g * 1024:(g + 1) * 1024]).then_inc(st, 16)
                    s.dma_start(out=dka[g * 128:(g + 1) * 128, :],
                                in_=kact[:, g * 1024:(g + 1) * 1024]).then_inc(st, 16)
            S.emit("sync", fdbg)
            S.bump("st", 16 * 24)
        for it in range(8):
            for jd in range(8):
                ev_need = C["ev"]
                def fmm(t, it=it, jd=jd, g_ev=g_ev, ev_need=ev_need):
                    t.wait_ge(ev, max(g_ev, ev_need))
                    t.transpose(pst[:], gt[:, jd * 1024 + it * 128: jd * 1024 + (it + 1) * 128],
                                idt[:]).then_inc(pe, 1)
                S.emit("tensor", fmm)
                pe_now = S.bump("pe", 1)
                st_need = C["st"]
                def fev(sc, pe_now=pe_now, st_need=st_need):
                    sc.wait_ge(pe, pe_now)
                    sc.wait_ge(st, st_need)
                    sc.copy(out=trs[:], in_=pst[:]).then_inc(ev, 1)
                S.emit("scalar", fev)
                ev_now = S.bump("ev", 1)
                def fst(s, it=it, jd=jd, t0=t0, ev_now=ev_now):
                    s.wait_ge(ev, ev_now)
                    s.dma_start(out=gtok[t0 + it * 128: t0 + (it + 1) * 128,
                                         jd * 128:(jd + 1) * 128], in_=trs[:]).then_inc(st, 16)
                S.emit("sync", fst)
                S.bump("st", 16)
    final_st = C["st"]
    def f(s, final_st=final_st):
        s.wait_ge(st, final_st)
    S.emit("sync", f)

    block_cm = p.nc.Block(); block = block_cm.__enter__()
    S.install(block)
    block_cm.__exit__(None, None, None)
    return p


def build_resid_ln():
    p = Prog("residln")
    x = p.inp("x", [T, D])
    stok = p.inp("s_tok", [T, D], BF16)
    gtok = p.inp("g_tok", [T, D], BF16)
    ident = p.inp("IDENT", [128, 128], BF16)
    x1o = p.out("x1", [T, D])
    xnT = p.out("xn1T", [D, T], BF16)

    xt = p.sb("xt", [128, D], F32)
    stt = p.sb("stt", [128, D], BF16)
    gtt = p.sb("gtt", [128, D], BF16)
    sg = p.sb("sg", [128, D], BF16)
    x1t = p.sb("x1t", [128, D], F32)
    xnb = p.sb("xnb", [128, D], BF16)
    s1 = p.sb("s1", [128, 1], F32)
    mneg = p.sb("mneg", [128, 1], F32)
    msq = p.sb("msq", [128, 1], F32)
    ssq = p.sb("ssq", [128, 1], F32)
    var = p.sb("var", [128, 1], F32)
    rstd = p.sb("rstd", [128, 1], F32)
    dummy = p.sb("scrd", [128, D], F32)
    eps = p.sb("eps", [128, 1], F32)
    idt = p.sb("idt", [128, 128], BF16)
    trs = p.sb("trs", [128, 128], BF16)
    pst = p.ps("pst", [128, 128], BF16)

    S = Sched(p)
    ld = S.sem("ld"); sc = S.sem("sc"); ve = S.sem("ve")
    pe = S.sem("pe"); ev = S.sem("ev"); st = S.sem("st")
    C = S.cnt
    NT = T // 128; ND = D // 128

    def f(s):
        s.dma_start(out=idt[:], in_=ident[:]).then_inc(ld, 16)
    S.emit("sync", f)
    S.bump("ld", 16)
    def f(g):
        g.memset(eps[:], 1e-5).then_inc(ve, 1)
    S.emit("gpsimd", f)
    S.bump("ve", 1)

    for i in range(NT):
        ld_now = C["ld"]
        ve_g = C["ve"]; pe_g = C["pe"]; sc_gl = C["sc"]
        def f(s, i=i, ve_g=ve_g, pe_g=pe_g, sc_gl=sc_gl):
            s.wait_ge(ve, ve_g)
            s.wait_ge(pe, pe_g)
            s.wait_ge(sc, sc_gl)
            s.dma_start(out=xt[:], in_=x[i * 128:(i + 1) * 128, :]).then_inc(ld, 16)
            s.dma_start(out=stt[:], in_=stok[i * 128:(i + 1) * 128, :]).then_inc(ld, 16)
            s.dma_start(out=gtt[:], in_=gtok[i * 128:(i + 1) * 128, :]).then_inc(ld, 16)
        S.emit("sync", f)
        ld_now = S.bump("ld", 48)
        st_g = C["st"]
        vbase = C["ve"]
        def fv(v, ld_now=ld_now, vbase=vbase, st_g=st_g):
            v.wait_ge(ld, ld_now)
            v.wait_ge(st, st_g)
            v.tensor_tensor(sg[:], stt[:], gtt[:], ALU.mult).then_inc(ve, 1)
            v.wait_ge(ve, vbase + 1)
            v.tensor_tensor(x1t[:], xt[:], sg[:], ALU.add).then_inc(ve, 1)
        S.emit("vector", fv)
        ve_now = S.bump("ve", 2)
        def fst1(s, i=i, ve_now=ve_now):
            s.wait_ge(ve, ve_now)
            s.dma_start(out=x1o[i * 128:(i + 1) * 128, :], in_=x1t[:]).then_inc(st, 16)
        S.emit("sync", fst1)
        S.bump("st", 16)
        sc_b = C["sc"]
        def fsc1(scl, ve_now=ve_now, sc_b=sc_b):
            scl.wait_ge(ve, ve_now)
            scl.activation(out=dummy[:], in_=x1t[:], func=AF.Copy, accum_out=s1[:])
            scl.mul(out=mneg[:], in_=s1[:], mul=-1.0 / D)
            scl.activation(out=msq[:], in_=mneg[:], func=AF.Square)
            scl.activation(out=dummy[:], in_=x1t[:], func=AF.Square, accum_out=ssq[:]).then_inc(sc, 1)
        S.emit("scalar", fsc1)
        sc_now = S.bump("sc", 1)
        def fv2(v, sc_now=sc_now):
            v.wait_ge(sc, sc_now)
            v.tensor_scalar(var[:], ssq[:], 1.0 / D, msq[:], ALU.mult, ALU.subtract).then_inc(ve, 1)
        S.emit("vector", fv2)
        ve2 = S.bump("ve", 1)
        def fsc2(scl, ve2=ve2):
            scl.wait_ge(ve, ve2)
            scl.activation(out=rstd[:], in_=var[:], func=AF.Ln, bias=eps[:])
            scl.activation(out=rstd[:], in_=rstd[:], func=AF.Exp, scale=-0.5).then_inc(sc, 1)
        S.emit("scalar", fsc2)
        sc2 = S.bump("sc", 1)
        pe_g2 = C["pe"]
        def fv3(v, sc2=sc2, pe_g2=pe_g2):
            v.wait_ge(sc, sc2)
            v.wait_ge(pe, pe_g2)
            v.tensor_scalar(xnb[:], x1t[:], mneg[:], rstd[:], ALU.add, ALU.mult).then_inc(ve, 1)
        S.emit("vector", fv3)
        ve3 = S.bump("ve", 1)
        for j in range(ND):
            ev_need = C["ev"]
            def fmm(t, j=j, ve3=ve3, ev_need=ev_need):
                t.wait_ge(ve, ve3)
                t.wait_ge(ev, ev_need)
                t.transpose(pst[:], xnb[:, j * 128:(j + 1) * 128], idt[:]).then_inc(pe, 1)
            S.emit("tensor", fmm)
            pe_now = S.bump("pe", 1)
            st_need = C["st"]
            def fev(scl, pe_now=pe_now, st_need=st_need):
                scl.wait_ge(pe, pe_now)
                scl.wait_ge(st, st_need)
                scl.copy(out=trs[:], in_=pst[:]).then_inc(ev, 1)
            S.emit("scalar", fev)
            ev_now = S.bump("ev", 1)
            def fst(s, i=i, j=j, ev_now=ev_now):
                s.wait_ge(ev, ev_now)
                s.dma_start(out=xnT[j * 128:(j + 1) * 128, i * 128:(i + 1) * 128],
                            in_=trs[:]).then_inc(st, 16)
            S.emit("sync", fst)
            S.bump("st", 16)
    final_st = C["st"]
    def f(s, final_st=final_st):
        s.wait_ge(st, final_st)
    S.emit("sync", f)
    block_cm = p.nc.Block(); block = block_cm.__enter__()
    S.install(block)
    block_cm.__exit__(None, None, None)
    return p


def build_l1proj():
    p = Prog("l1proj")
    xnT = p.inp("xn1T", [D, T], BF16)
    W = p.inp("W_xproj", [D, 1056])
    b = p.inp("b_xproj", [1056])
    dT = p.out("deltaT", [D, T])
    bc = p.out("bc", [32, T])

    KC = 8
    xsb = [p.sb(f"x{k}", [128, T], BF16) for k in range(KC)]
    wsb = [p.sb(f"w{k}", [128, 1056], BF16) for k in range(KC)]
    bcol = p.sb("bcol", [128, 9], F32)
    t1 = p.sb("t1", [128, 512], F32)
    t2 = p.sb("t2", [128, 512], F32)
    stg = p.sb("stg", [128, 512], F32)
    pss = [p.ps(f"ps{j}", [128, 512], F32) for j in range(2)]

    S = Sched(p)
    ld = S.sem("ld"); pe = S.sem("pe"); ev = S.sem("ev"); st = S.sem("st"); vv = S.sem("vv")
    C = S.cnt
    def f(g):
        for k in range(KC):
            g.dma_start(out=wsb[k][:], in_=W[k * 128:(k + 1) * 128, :]).then_inc(ld, 16)
        g.dma_start(out=bcol[:, 0:8], in_=b[0:1024].rearrange("(g p) -> p g", p=128),
                    allow_slow_non_contiguous=True).then_inc(ld, 16)
        g.dma_start(out=bcol[0:32, 8:9], in_=b[1024:1056].unsqueeze(1),
                    allow_slow_non_contiguous=True).then_inc(ld, 16)
    S.emit("gpsimd", f)
    S.bump("ld", 16 * (KC + 2))
    def f(s):
        for k in range(KC):
            s.dma_start(out=xsb[k][:], in_=xnT[k * 128:(k + 1) * 128, :]).then_inc(ld, 16)
    S.emit("sync", f)
    ld_all = S.bump("ld", 16 * KC)

    NJ = T // 512
    for i in range(9):
        hi = 128 if i < 8 else 32
        for j in range(NJ):
            jb = j % 2
            ev_need = C["ev"]
            vv_need = C["vv"]
            def fmm(t, i=i, j=j, jb=jb, hi=hi, ld_all=ld_all, ev_need=ev_need, vv_need=vv_need):
                t.wait_ge(ld, ld_all)
                t.wait_ge(ev, ev_need)
                t.wait_ge(vv, vv_need)
                for k in range(KC):
                    mm = t.matmul(pss[jb][0:hi, :], wsb[k][:, i * 128:i * 128 + hi],
                                  xsb[k][:, j * 512:(j + 1) * 512],
                                  start=(k == 0), stop=(k == KC - 1))
                mm.then_inc(pe, 1)
            S.emit("tensor", fmm)
            pe_now = S.bump("pe", 1)
            if i < 8:
                st_need = C["st"]
                def fev(scl, i=i, jb=jb, pe_now=pe_now, st_need=st_need):
                    scl.wait_ge(pe, pe_now)
                    scl.wait_ge(st, st_need)
                    scl.activation(out=t1[:], in_=pss[jb][:], func=AF.Abs, bias=bcol[:, i:i + 1])
                    scl.activation(out=t1[:], in_=t1[:], func=AF.Exp, scale=-1.0)
                    scl.activation(out=t1[:], in_=t1[:], func=AF.Ln, bias=1.0)
                    scl.activation(out=t2[:], in_=pss[jb][:], func=AF.Relu,
                                   bias=bcol[:, i:i + 1]).then_inc(ev, 1)
                S.emit("scalar", fev)
                ev_now = S.bump("ev", 1)
                def fv(v, ev_now=ev_now):
                    v.wait_ge(ev, ev_now)
                    v.tensor_tensor(stg[:], t1[:], t2[:], ALU.add).then_inc(vv, 1)
                S.emit("vector", fv)
                vv_now = S.bump("vv", 1)
                def fst(s, i=i, j=j, vv_now=vv_now):
                    s.wait_ge(vv, vv_now)
                    s.dma_start(out=dT[i * 128:(i + 1) * 128, j * 512:(j + 1) * 512],
                                in_=stg[:]).then_inc(st, 16)
                S.emit("sync", fst)
                S.bump("st", 16)
            else:
                st_need = C["st"]
                def fev(v, jb=jb, pe_now=pe_now, st_need=st_need):
                    v.wait_ge(pe, pe_now)
                    v.wait_ge(st, st_need)
                    v.tensor_scalar(stg[0:32, :], pss[jb][0:32, :], bcol[0:32, 8:9], None,
                                    ALU.add).then_inc(vv, 1)
                S.emit("vector", fev)
                vv_now = S.bump("vv", 1)
                def fst(s, j=j, vv_now=vv_now):
                    s.wait_ge(vv, vv_now)
                    s.dma_start(out=bc[:, j * 512:(j + 1) * 512], in_=stg[0:32, :]).then_inc(st, 16)
                S.emit("sync", fst)
                S.bump("st", 16)
    final_st = C["st"]
    def f(s, final_st=final_st):
        s.wait_ge(st, final_st)
    S.emit("sync", f)
    block_cm = p.nc.Block(); block = block_cm.__enter__()
    S.install(block)
    block_cm.__exit__(None, None, None)
    return p


def build_scan():
    p = Prog("scan")
    dT = p.inp("deltaT", [D, T])
    xnT = p.inp("xn1T", [D, T], BF16)
    bc = p.inp("bc", [32, T])
    Alog = p.inp("A_log", [D, 16])
    Dp = p.inp("Dp", [D])
    yT = p.out("yT", [D, T], BF16)

    at = p.sb("at", [128, 16 * 8], F32)
    dpc = p.sb("dpc", [128, 8], F32)
    dg = p.sb("dg", [128, T], F32)
    xg = p.sb("xg", [128, T], BF16)
    dx = p.sb("dx", [128, T], BF16)
    brep = p.sb("brep", [128, T], BF16)
    crep = p.sb("crep", [128, T], BF16)
    atile = p.sb("atile", [128, T], F32)
    util = p.sb("util", [128, T], BF16)
    htile = p.sb("htile", [128, T], BF16)
    yacc = p.sb("yacc", [128, T], F32)
    yout = p.sb("yout", [128, T], BF16)

    S = Sched(p)
    ld = S.sem("ld"); sc = S.sem("sc"); vv = S.sem("vv"); st = S.sem("st")
    C = S.cnt

    def f(g):
        g.dma_start(out=at[:], in_=Alog.rearrange("(g p) n -> p g n", p=128),
                    allow_slow_non_contiguous=True).then_inc(ld, 16)
        g.dma_start(out=dpc[:], in_=Dp.rearrange("(g p) -> p g", p=128),
                    allow_slow_non_contiguous=True).then_inc(ld, 16)
    S.emit("gpsimd", f)
    S.bump("ld", 32)
    def f(scl):
        scl.wait_ge(ld, 32)
        scl.activation(out=at[:], in_=at[:], func=AF.Exp, scale=1.0).then_inc(sc, 1)
    S.emit("scalar", f)
    S.bump("sc", 1)
    def f(v):
        v.wait_ge(sc, 1)
        v.tensor_scalar(at[:], at[:], -1.0, None, ALU.mult).then_inc(vv, 1)
    S.emit("vector", f)
    S.bump("vv", 1)
    aprep_vv = C["vv"]

    for g in range(8):
        ld_now = C["ld"]
        vv_g = C["vv"]; sc_g = C["sc"]
        def f(s, g=g, vv_g=vv_g, sc_g=sc_g):
            s.wait_ge(vv, vv_g)
            s.wait_ge(sc, sc_g)
            s.dma_start(out=dg[:], in_=dT[g * 128:(g + 1) * 128, :]).then_inc(ld, 16)
            s.dma_start(out=xg[:], in_=xnT[g * 128:(g + 1) * 128, :]).then_inc(ld, 16)
        S.emit("sync", f)
        ld_now = S.bump("ld", 32)
        vbase = C["vv"]
        def f(v, ld_now=ld_now, vbase=vbase):
            v.wait_ge(ld, ld_now)
            v.tensor_tensor(dx[:], dg[:], xg[:], ALU.mult).then_inc(vv, 1)
        S.emit("vector", f)
        S.bump("vv", 1)
        for n in range(16):
            ld_bc = C["ld"]
            vv_bc = C["vv"]
            def f(gp, n=n, vv_bc=vv_bc):
                gp.wait_ge(vv, vv_bc)
                gp.dma_start(out=brep[:], in_=bc[n:n + 1, :].broadcast_to([128, T])).then_inc(ld, 16)
                gp.dma_start(out=crep[:], in_=bc[16 + n:17 + n, :].broadcast_to([128, T])).then_inc(ld, 16)
            S.emit("gpsimd", f)
            ld_bc = S.bump("ld", 32)
            sc_a = C["sc"]
            vv_a = C["vv"]
            def f(scl, g=g, n=n, ld_now=ld_now, vv_a=vv_a):
                scl.wait_ge(ld, ld_now)
                scl.wait_ge(vv, max(vv_a, aprep_vv))
                scl.activation(out=atile[:], in_=dg[:], func=AF.Exp,
                               scale=at[:, g * 16 + n: g * 16 + n + 1]).then_inc(sc, 1)
            S.emit("scalar", f)
            sc_now = S.bump("sc", 1)
            vb = C["vv"]
            def f(v, g=g, n=n, ld_bc=ld_bc, sc_now=sc_now, vb=vb):
                v.wait_ge(ld, ld_bc)
                v.wait_ge(sc, sc_now)
                v.tensor_tensor(util[:], dx[:], brep[:], ALU.mult).then_inc(vv, 1)
                v.wait_ge(vv, vb + 1)
                v.tensor_tensor_scan(htile[:], atile[:], util[:], 0.0,
                                     ALU.mult, ALU.add).then_inc(vv, 1)
                v.wait_ge(vv, vb + 2)
                if n == 0:
                    v.tensor_tensor(yacc[:], htile[:], crep[:], ALU.mult).then_inc(vv, 1)
                else:
                    v.tensor_tensor(util[:], htile[:], crep[:], ALU.mult).then_inc(vv, 1)
                    v.wait_ge(vv, vb + 3)
                    v.tensor_tensor(yacc[:], yacc[:], util[:], ALU.add).then_inc(vv, 1)
            S.emit("vector", f)
            S.bump("vv", 3 if n == 0 else 4)
        st_g = C["st"]
        vb2 = C["vv"]
        def f(v, g=g, st_g=st_g, vb2=vb2):
            v.wait_ge(st, st_g)
            v.wait_ge(vv, vb2)
            v.scalar_tensor_tensor(yout[:], xg[:], dpc[:, g:g + 1], yacc[:],
                                   ALU.mult, ALU.add).then_inc(vv, 1)
        S.emit("vector", f)
        vv_y = S.bump("vv", 1)
        def f(s, g=g, vv_y=vv_y):
            s.wait_ge(vv, vv_y)
            s.dma_start(out=yT[g * 128:(g + 1) * 128, :], in_=yout[:]).then_inc(st, 16)
        S.emit("sync", f)
        S.bump("st", 16)
    final_st = C["st"]
    def f(s, final_st=final_st):
        s.wait_ge(st, final_st)
    S.emit("sync", f)
    block_cm = p.nc.Block(); block = block_cm.__enter__()
    S.install(block)
    block_cm.__exit__(None, None, None)
    return p


def build_finale():
    p = Prog("finale")
    x1 = p.inp("x1", [T, D])
    xnT = p.inp("xn1T", [D, T], BF16)
    yT = p.inp("yT", [D, T], BF16)
    Wout = p.inp("W_out", [D, D])
    bout = p.inp("b_out", [D])
    weight = p.inp("weight", [D, D])
    disha = p.inp("disha", [64, D])
    ident = p.inp("IDENT", [128, 128], BF16)
    OUT = p.out("out", [T, D])

    KC = 8
    wo = [p.sb(f"wo{k}", [128, D], BF16) for k in range(KC)]
    we = [p.sb(f"we{k}", [128, D], BF16) for k in range(KC)]
    wload = p.sb("wload", [128, D], BF16)
    dr = p.sb("dr", [128, D], BF16)
    brep = p.sb("brep", [128, D], BF16)
    idt = p.sb("idt", [128, 128], BF16)
    lhs_y = p.sb("lhs_y", [128, D], BF16)
    lhs_x = p.sb("lhs_x", [128, D], BF16)
    s1t = p.sb("s1t", [128, D], BF16)
    g1t = p.sb("g1t", [128, D], BF16)
    x1t = p.sb("x1t", [128, D], F32)
    sgt = p.sb("sgt", [128, D], BF16)
    ott = p.sb("ott", [128, D], F32)
    pst = p.ps("pst", [128, 128], BF16)
    pss = [p.ps(f"ps{j}", [128, 512], F32) for j in range(4)]

    S = Sched(p)
    ld = S.sem("ld"); pe = S.sem("pe"); ev = S.sem("ev"); st = S.sem("st"); vv = S.sem("vv")
    C = S.cnt

    def f(s):
        s.dma_start(out=idt[:], in_=ident[:]).then_inc(ld, 16)
    S.emit("sync", f)
    S.bump("ld", 16)
    def f(g):
        for k in range(KC):
            g.dma_start(out=wo[k][:], in_=Wout[k * 128:(k + 1) * 128, :]).then_inc(ld, 16)
        g.dma_start(out=brep[:], in_=bout[None, :].broadcast_to([128, D])).then_inc(ld, 16)
        g.dma_start(out=dr[0:64, :], in_=disha[:]).then_inc(ld, 16)
        g.dma_start(out=dr[64:128, :], in_=disha[:]).then_inc(ld, 16)
    S.emit("gpsimd", f)
    S.bump("ld", 16 * (KC + 3))

    for bi in range(8):
        ld_now = C["ld"]
        pe_g = C["pe"]
        def f(g, bi=bi, pe_g=pe_g):
            g.wait_ge(pe, pe_g)
            g.dma_start(out=wload[:], in_=weight[bi * 128:(bi + 1) * 128, :]).then_inc(ld, 16)
        S.emit("gpsimd", f)
        ld_now = S.bump("ld", 16)
        for bj in range(8):
            ev_need = C["ev"]
            def fmm(t, bj=bj, ld_now=ld_now, ev_need=ev_need):
                t.wait_ge(ld, ld_now)
                t.wait_ge(ev, ev_need)
                t.transpose(pst[:], wload[:, bj * 128:(bj + 1) * 128], idt[:]).then_inc(pe, 1)
            S.emit("tensor", fmm)
            pe_now = S.bump("pe", 1)
            def fev(scl, bi=bi, bj=bj, pe_now=pe_now):
                scl.wait_ge(pe, pe_now)
                scl.copy(out=we[bj][:, bi * 128:(bi + 1) * 128], in_=pst[:]).then_inc(ev, 1)
            S.emit("scalar", fev)
            S.bump("ev", 1)
    wt_ev = C["ev"]
    def f(v, wt_ev=wt_ev):
        v.wait_ge(ev, wt_ev)
        v.wait_ge(ld, 16 * (1 + KC + 3))
        for k in range(KC):
            v.tensor_tensor(we[k][:], we[k][:], dr[:], ALU.add)
        v.engine_nop().then_inc(vv, 1)
    S.emit("vector", f)
    weff_vv = S.bump("vv", 1)

    NT = T // 128
    for i in range(NT):
        ld_now = C["ld"]
        pe_g = C["pe"]; vv_g = C["vv"]
        def f(s, i=i, pe_g=pe_g, vv_g=vv_g):
            s.wait_ge(pe, pe_g)
            s.wait_ge(vv, vv_g)
            for k in range(KC):
                s.dma_start(out=lhs_y[:, k * 128:(k + 1) * 128],
                            in_=yT[k * 128:(k + 1) * 128, i * 128:(i + 1) * 128]).then_inc(ld, 16)
                s.dma_start(out=lhs_x[:, k * 128:(k + 1) * 128],
                            in_=xnT[k * 128:(k + 1) * 128, i * 128:(i + 1) * 128]).then_inc(ld, 16)
            s.dma_start(out=x1t[:], in_=x1[i * 128:(i + 1) * 128, :]).then_inc(ld, 16)
        S.emit("sync", f)
        ld_now = S.bump("ld", 16 * (2 * KC + 1))
        for j in range(2):
            ev_need = C["ev"]
            def fmm(t, i=i, j=j, ld_now=ld_now, ev_need=ev_need, weff_vv=weff_vv):
                t.wait_ge(ld, ld_now)
                t.wait_ge(vv, weff_vv)
                t.wait_ge(ev, ev_need)
                for k in range(KC):
                    mm = t.matmul(pss[j][:], lhs_y[:, k * 128:(k + 1) * 128],
                                  wo[k][:, j * 512:(j + 1) * 512],
                                  start=(k == 0), stop=(k == KC - 1))
                mm.then_inc(pe, 1)
                for k in range(KC):
                    mm = t.matmul(pss[2 + j][:], lhs_x[:, k * 128:(k + 1) * 128],
                                  we[k][:, j * 512:(j + 1) * 512],
                                  start=(k == 0), stop=(k == KC - 1))
                mm.then_inc(pe, 1)
            S.emit("tensor", fmm)
            pe_now = S.bump("pe", 2)
            vb = C["vv"]
            def fev(v, j=j, pe_now=pe_now, vb=vb):
                v.wait_ge(pe, pe_now)
                v.tensor_tensor(s1t[:, j * 512:(j + 1) * 512], pss[j][:],
                                brep[:, j * 512:(j + 1) * 512], ALU.add).then_inc(vv, 1)
            S.emit("vector", fev)
            S.bump("vv", 1)
            def fg(scl, j=j, pe_now=pe_now):
                scl.wait_ge(pe, pe_now)
                scl.activation(out=g1t[:, j * 512:(j + 1) * 512], in_=pss[2 + j][:],
                               func=AF.Silu).then_inc(ev, 1)
            S.emit("scalar", fg)
            S.bump("ev", 1)
        mm_done_vv = C["vv"]; mm_done_ev = C["ev"]
        st_g = C["st"]
        vb2 = C["vv"]
        def fout(v, mm_done_vv=mm_done_vv, mm_done_ev=mm_done_ev, st_g=st_g, vb2=vb2):
            v.wait_ge(vv, mm_done_vv)
            v.wait_ge(ev, mm_done_ev)
            v.wait_ge(st, st_g)
            v.tensor_tensor(sgt[:], s1t[:], g1t[:], ALU.mult).then_inc(vv, 1)
            v.wait_ge(vv, vb2 + 1)
            v.tensor_tensor(ott[:], x1t[:], sgt[:], ALU.add).then_inc(vv, 1)
        S.emit("vector", fout)
        vv_out = S.bump("vv", 2)
        def fst(s, i=i, vv_out=vv_out):
            s.wait_ge(vv, vv_out)
            s.dma_start(out=OUT[i * 128:(i + 1) * 128, :], in_=ott[:]).then_inc(st, 16)
        S.emit("sync", fst)
        S.bump("st", 16)
    final_st = C["st"]
    def f(s, final_st=final_st):
        s.wait_ge(st, final_st)
    S.emit("sync", f)
    block_cm = p.nc.Block(); block = block_cm.__enter__()
    S.install(block)
    block_cm.__exit__(None, None, None)
    return p


def kernel_multilaunch(**inputs):
    ident = CONST["IDENT"]
    x = np.ascontiguousarray(np.asarray(inputs["L0_attn_x"], dtype=np.float32))
    nco = x.shape[0]
    cores = list(range(nco))

    def rl(prog, maps):
        return run_launch(prog, maps)

    p_ln = build_ln("ln0")
    r = rl(p_ln, [{"x": x[c], "IDENT": ident} for c in cores])
    xnT = [np.asarray(r.results[c]["xnT"]) for c in cores]

    p_pr = build_l0proj()
    r = rl(p_pr, [{"xnT": xnT[c], "proj_w": inputs["L0_attn_proj_w"],
                   "proj_b": inputs["L0_attn_proj_b"]} for c in cores])
    ptok = [np.asarray(r.results[c]["p_tok"]) for c in cores]

    consts_h = {k: CONST[k] for k in ["F128RE", "F128IM", "S2FWD", "S2FILT",
                                      "S1INV", "G2RE", "G2NIM", "SIGNONE"]}
    filt = np.asarray(inputs["L0_attn_filter_time"], dtype=np.float32)

    p_h0 = build_hyena(0)
    r = rl(p_h0, [{"vsrc": ptok[c], "p_tok": ptok[c], "filt": filt[0, 0], **consts_h}
                  for c in cores])
    v2 = [np.asarray(r.results[c]["vout"]) for c in cores]

    p_h1 = build_hyena(1)
    r = rl(p_h1, [{"vsrc": v2[c], "p_tok": ptok[c], "filt": filt[1, 0], **consts_h}
                  for c in cores])
    stok = [np.asarray(r.results[c]["vout"]) for c in cores]

    p_ff = build_l0ffn()
    r = rl(p_ff, [{"xnT": xnT[c], "W_r": inputs["L0_ffn_W_r"], "W_k": inputs["L0_ffn_W_k"],
                   "W_v": inputs["L0_ffn_W_v"], "mu_r": inputs["L0_ffn_mu_r"],
                   "mu_k": inputs["L0_ffn_mu_k"], "IDENT": ident} for c in cores])
    gtok = [np.asarray(r.results[c]["g_tok"]) for c in cores]

    p_rl = build_resid_ln()
    r = rl(p_rl, [{"x": x[c], "s_tok": stok[c], "g_tok": gtok[c], "IDENT": ident}
                  for c in cores])
    x1 = [np.asarray(r.results[c]["x1"]) for c in cores]
    xn1T = [np.asarray(r.results[c]["xn1T"]) for c in cores]

    p_l1 = build_l1proj()
    r = rl(p_l1, [{"xn1T": xn1T[c], "W_xproj": inputs["L1_attn_W_xproj"],
                   "b_xproj": inputs["L1_attn_b_xproj"]} for c in cores])
    deltaT = [np.asarray(r.results[c]["deltaT"]) for c in cores]
    bc = [np.asarray(r.results[c]["bc"]) for c in cores]

    p_sc = build_scan()
    r = rl(p_sc, [{"deltaT": deltaT[c], "xn1T": xn1T[c], "bc": bc[c],
                   "A_log": inputs["L1_attn_A_log"], "Dp": inputs["L1_attn_Dp"]}
                  for c in cores])
    yT = [np.asarray(r.results[c]["yT"]) for c in cores]

    p_fi = build_finale()
    r = rl(p_fi, [{"x1": x1[c], "xn1T": xn1T[c], "yT": yT[c],
                   "W_out": inputs["L1_attn_W_out"], "b_out": inputs["L1_attn_b_out"],
                   "weight": inputs["L1_ffn_weight"], "disha": inputs["L1_ffn_disha"],
                   "IDENT": ident} for c in cores])
    out = np.stack([np.asarray(r.results[c]["out"], dtype=np.float32) for c in cores])
    return out


class Pool:
    def __init__(self, p, name, cols, dtype):
        self.ap = p.sb(name, [128, cols], dtype)
        self.cols = cols
        self.off = 0

    def alloc(self, n):
        assert self.off + n <= self.cols, (self.off, n, self.cols)
        ap = self.ap[:, self.off:self.off + n]
        self.off += n
        return ap

    def reset(self):
        self.off = 0

    def reset_to(self, off):
        self.off = off


def S_barrier(S, sems):
    snap = {k: S.cnt[k] for k in S.cnt}
    for eng in ["sync", "gpsimd", "scalar", "vector", "tensor"]:
        def f(e, snap=snap, sems=sems):
            for k, v in snap.items():
                if v > 0:
                    e.wait_ge(sems[k], v)
        S.emit(eng, f)


def build_mega(dbg=False):
    p = Prog("mega")
    x_in = p.inp("x", [T, D])
    pw = p.inp("proj_w", [D, 3072])
    pb = p.inp("proj_b", [3072])
    filt_in = p.inp("filt", [2, 2048, D])
    WrT_in = p.inp("W_rT", [D, D]); WkT_in = p.inp("W_kT", [D, D]); WvT_in = p.inp("W_vT", [D, D])
    mur_in = p.inp("mu_r", [D]); muk_in = p.inp("mu_k", [D])
    Alog_in = p.inp("A_log", [D, 16]); Dp_in = p.inp("Dp", [D])
    Wxp_in = p.inp("W_xproj", [D, 1056]); bxp_in = p.inp("b_xproj", [1056])
    Wout_in = p.inp("W_out", [D, D]); bout_in = p.inp("b_out", [D])
    weiT_in = p.inp("weightT", [D, D]); dis_in = p.inp("disha", [64, D])
    ident_in = p.inp("IDENT", [128, 128], BF16)
    f128re_in = p.inp("F128RE", [128, 128], BF16)
    f128im_in = p.inp("F128IM", [128, 128], BF16)
    s2fwd_in = p.inp("S2FWD", [128, 128, 128], BF16)
    s2filt_in = p.inp("S2FILT", [128, 128, 128], BF16)
    s1inv_in = p.inp("S1INV", [2, 128, 128], BF16)
    g2re_in = p.inp("G2RE", [32, 128, 128], BF16)
    g2nim_in = p.inp("G2NIM", [32, 128, 128], BF16)
    signone_in = p.inp("SIGNONE", [2, 128], BF16)
    bfly1_in = p.inp("BFLY1", [128, 128], BF16)
    bfly2_in = p.inp("BFLY2", [128, 128], BF16)
    bflyh1_in = p.inp("BFLYH1", [128, 128], BF16)
    bflyh2_in = p.inp("BFLYH2", [128, 128], BF16)
    OUT = p.out("out", [T, D])

    dram = lambda n, sh, dt=BF16: p.nc.dram_tensor(n, list(sh), dt, kind="Internal").ap()
    xnT_s = dram("xnT_s", [D, T])
    ptok_s = dram("ptok_s", [T, 3072])
    fw1 = dram("fw1", [128, 128, 512])
    Fsp = dram("Fsp_s", [136, 128, 256])
    Zsp = dram("Zsp_s", [136, 128, 256])
    Csp = dram("Csp_s", [136, 128, 256])
    Bsp = dram("Bsp_s", [N2, 2, 128, D])
    v2_s = dram("v2_s", [T, D])
    stok_s = dram("stok_s", [T, D])
    gtok_s = dram("gtok_s", [T, D])
    x1_s = dram("x1_s", [T, D], F32)
    xn1T_s = dram("xn1T_s", [D, T])
    g1_s = dram("g1_s", [T, D])
    if dbg:
        dT_s = p.out("dT_s", [D, T])
        bc_s = p.out("bc_s", [32, T], BF16)
        yT_s = p.out("yT_s", [D, T], BF16)
        xn1T_d = p.out("xn1T_d", [D, T], BF16)
    else:
        dT_s = dram("dT_s", [D, T], F32)
        bc_s = dram("bc_s", [32, T], BF16)
        yT_s = dram("yT_s", [D, T])
    c2_s = dram("c2_s", [4, 256])

    PB = Pool(p, "PB", 59392, BF16)
    PF = Pool(p, "PF", 16640, F32)
    idt = p.sb("idt", [128, 128], BF16)
    sm = p.sb("smalls", [128, 256], F32)
    sm2 = p.sb("smalls2", [128, 160], F32)
    c2row = p.sb("c2row", [1, 3 * D], BF16)
    ones1 = p.sb("ones1", [1, 128], BF16)
    pst = p.ps("pst", [128, 128], BF16)
    pst2 = p.ps("pst2", [128, 128], BF16)
    ps = [p.ps(f"ps{i}", [128, 512], F32) for i in range(6)]

    S = Sched(p)
    sems = {}
    for nsem in ["ld", "pe", "ev", "st", "vv", "sc", "lg", "st2", "sg"]:
        sems[nsem] = S.sem(nsem)
    ld = sems["ld"]; pe = sems["pe"]; ev = sems["ev"]
    st = sems["st"]; vv = sems["vv"]; sc = sems["sc"]
    lg = sems["lg"]; st2 = sems["st2"]; sg = sems["sg"]
    C = S.cnt

    def bar():
        S_barrier(S, sems)

    def f(s):
        s.dma_start(out=idt[:], in_=ident_in[:]).then_inc(ld, 16)
        s.dma_start(out=ones1[:], in_=signone_in[0:1, :]).then_inc(ld, 16)
    S.emit("sync", f)
    S.bump("ld", 32)
    eps = sm[:, 0:1]
    def f(g):
        g.memset(eps, 1e-5).then_inc(vv, 1)
    S.emit("gpsimd", f)
    S.bump("vv", 1)

    def emit_ln(x_dram, xnT_dram, resid=None):
        PB.reset(); PF.reset()
        res = [PB.alloc(T) for _ in range(8)]
        xnb = [PB.alloc(1024) for _ in range(2)]
        stt = [PB.alloc(1024) for _ in range(2)] if resid else None
        gtt = [PB.alloc(1024) for _ in range(2)] if resid else None
        sgt = PB.alloc(1024) if resid else None
        xt = [PF.alloc(1024) for _ in range(2)]
        x1t = [PF.alloc(1024) for _ in range(2)] if resid else xt
        scrd = PF.alloc(1024)
        stats = [sm[:, 1 + 8 * q: 9 + 8 * q] for q in range(2)]
        NT = T // 128; ND = D // 128
        ev_done = {}
        vv_done = {}
        for i in range(NT):
            par = i % 2
            s1 = stats[par][:, 0:1]; mneg = stats[par][:, 1:2]; msq = stats[par][:, 2:3]
            ssq = stats[par][:, 3:4]; var = stats[par][:, 4:5]; rstd = stats[par][:, 5:6]
            ld_now = C["ld"]
            vv_g = vv_done.get(i - 2, 0)
            sc_g = C["sc"] - 4 if i >= 2 else 0
            def f(s, i=i, par=par, vv_g=vv_g, sc_g=sc_g, x_dram=x_dram, resid=resid):
                if vv_g > 0:
                    s.wait_ge(vv, vv_g)
                if sc_g > 0:
                    s.wait_ge(sc, sc_g)
                s.dma_start(out=xt[par][:], in_=x_dram[i * 128:(i + 1) * 128, :]).then_inc(ld, 16)
                if resid:
                    s.dma_start(out=stt[par][:], in_=resid[0][i * 128:(i + 1) * 128, :]).then_inc(ld, 16)
                    s.dma_start(out=gtt[par][:], in_=resid[1][i * 128:(i + 1) * 128, :]).then_inc(ld, 16)
            S.emit("sync", f)
            ld_now = S.bump("ld", 48 if resid else 16)
            if resid:
                st_g = C["st"] - 16 if i >= 2 else 0
                vb = C["vv"]
                def fv(v, par=par, ld_now=ld_now, vb=vb, st_g=st_g):
                    v.wait_ge(ld, ld_now)
                    if st_g > 0:
                        v.wait_ge(st, st_g)
                    v.tensor_tensor(sgt[:], stt[par][:], gtt[par][:], ALU.mult).then_inc(vv, 1)
                    v.wait_ge(vv, vb + 1)
                    v.tensor_tensor(x1t[par][:], xt[par][:], sgt[:], ALU.add).then_inc(vv, 1)
                S.emit("vector", fv)
                ve_now = S.bump("vv", 2)
                def fst1(s, i=i, par=par, ve_now=ve_now, resid=resid):
                    s.wait_ge(vv, ve_now)
                    s.dma_start(out=resid[2][i * 128:(i + 1) * 128, :],
                                in_=x1t[par][:]).then_inc(st, 16)
                S.emit("sync", fst1)
                S.bump("st", 16)
                ready = ve_now
                rsem = vv
            else:
                ready = ld_now
                rsem = ld
            def fsc1(scl, par=par, ready=ready, rsem=rsem, s1=s1, mneg=mneg, msq=msq, ssq=ssq):
                scl.wait_ge(rsem, ready)
                scl.activation(out=scrd[:], in_=x1t[par][:], func=AF.Copy, accum_out=s1)
                scl.mul(out=mneg, in_=s1, mul=-1.0 / D)
                scl.activation(out=msq, in_=mneg, func=AF.Square)
                scl.activation(out=scrd[:], in_=x1t[par][:], func=AF.Square, accum_out=ssq).then_inc(sc, 1)
            S.emit("scalar", fsc1)
            sc_now = S.bump("sc", 1)
            def fv2(v, sc_now=sc_now, var=var, ssq=ssq, msq=msq):
                v.wait_ge(sc, sc_now)
                v.tensor_scalar(var, ssq, 1.0 / D, msq, ALU.mult, ALU.subtract).then_inc(vv, 1)
            S.emit("vector", fv2)
            ve2 = S.bump("vv", 1)
            def fsc2(scl, ve2=ve2, var=var, rstd=rstd):
                scl.wait_ge(vv, ve2)
                scl.activation(out=rstd, in_=var, func=AF.Ln, bias=eps)
                scl.activation(out=rstd, in_=rstd, func=AF.Exp, scale=-0.5).then_inc(sc, 1)
            S.emit("scalar", fsc2)
            sc2 = S.bump("sc", 1)
            pe_g2 = C["pe"] - ND if i >= 2 else 0
            def fv3(v, par=par, sc2=sc2, pe_g2=pe_g2, mneg=mneg, rstd=rstd):
                v.wait_ge(sc, sc2)
                if pe_g2 > 0:
                    v.wait_ge(pe, pe_g2)
                v.tensor_scalar(xnb[par][:], x1t[par][:], mneg, rstd, ALU.add, ALU.mult).then_inc(vv, 1)
            S.emit("vector", f_ := fv3)
            ve3 = S.bump("vv", 1)
            vv_done[i] = ve3
            for j in range(ND):
                pp = (i * ND + j) % 2
                pstj = pst if pp == 0 else pst2
                ev_need = ev_done.get(i * ND + j - 2, 0)
                def fmm(t, j=j, pstj=pstj, ve3=ve3, ev_need=ev_need, par=par):
                    t.wait_ge(vv, ve3)
                    if ev_need > 0:
                        t.wait_ge(ev, ev_need)
                    t.transpose(pstj[:], xnb[par][:, j * 128:(j + 1) * 128], idt[:]).then_inc(pe, 1)
                S.emit("tensor", fmm)
                pe_now = S.bump("pe", 1)
                def fev(scl, i=i, j=j, pstj=pstj, pe_now=pe_now):
                    scl.wait_ge(pe, pe_now)
                    scl.copy(out=res[j][:, i * 128:(i + 1) * 128], in_=pstj[:]).then_inc(ev, 1)
                S.emit("scalar", fev)
                ev_done[i * ND + j] = S.bump("ev", 1)
        fin_ev = C["ev"]
        def fstall(s, fin_ev=fin_ev, xnT_dram=xnT_dram):
            s.wait_ge(ev, fin_ev)
            for j in range(8):
                s.dma_start(out=xnT_dram[j * 128:(j + 1) * 128, :], in_=res[j][:]).then_inc(st, 16)
        S.emit("sync", fstall)
        S.bump("st", 16 * 8)
        return res

    xn_res = emit_ln(x_in, xnT_s)
    bar()

    def emit_proj():
        PB.reset_to(8 * T)
        wsb = [PB.alloc(1536) for _ in range(8)]
        brep = PB.alloc(3072)
        stg = [PB.alloc(1536) for _ in range(2)]
        NJ = 3
        def f(g):
            g.dma_start(out=brep[:], in_=pb[None, :].broadcast_to([128, 3072])).then_inc(lg, 16)
        S.emit("gpsimd", f)
        b_lg = S.bump("lg", 16)
        for douth in range(2):
            pe_gw = C["pe"]
            def fw(g, douth=douth, pe_gw=pe_gw):
                g.wait_ge(pe, pe_gw)
                for k in range(8):
                    g.dma_start(out=wsb[k][:],
                                in_=pw[k * 128:(k + 1) * 128,
                                       douth * 1536:(douth + 1) * 1536]).then_inc(lg, 16)
            S.emit("gpsimd", fw)
            w_lg = S.bump("lg", 16 * 8)
            for i in range(T // 128):
                vv_need = C["vv"]
                def fmm(t, i=i, w_lg=w_lg, vv_need=vv_need):
                    t.wait_ge(lg, w_lg)
                    t.wait_ge(vv, vv_need)
                    for k in range(8):
                        for j in range(NJ):
                            t.matmul(ps[j][:], xn_res[k][:, i * 128:(i + 1) * 128],
                                     wsb[k][:, j * 512:(j + 1) * 512],
                                     start=(k == 0), stop=(k == 7)).then_inc(pe, 1)
                S.emit("tensor", fmm)
                pe_now = S.bump("pe", 8 * NJ)
                par = i % 2
                st_need = C["st"] - 16 if i >= 2 else 0
                def fevv(v, par=par, douth=douth, pe_now=pe_now, st_need=st_need, b_lg=b_lg):
                    v.wait_ge(pe, pe_now)
                    v.wait_ge(lg, b_lg)
                    if st_need > 0:
                        v.wait_ge(st, st_need)
                    for j in range(NJ):
                        v.tensor_tensor(stg[par][:, j * 512:(j + 1) * 512], ps[j][:],
                                        brep[:, douth * 1536 + j * 512: douth * 1536 + (j + 1) * 512],
                                        ALU.add).then_inc(vv, 1 if j == NJ - 1 else 0)
                S.emit("vector", fevv)
                vv_now = S.bump("vv", 1)
                def fst(s, i=i, par=par, douth=douth, vv_now=vv_now):
                    s.wait_ge(vv, vv_now)
                    s.dma_start(out=ptok_s[i * 128:(i + 1) * 128,
                                           douth * 1536:(douth + 1) * 1536],
                                in_=stg[par][:]).then_inc(st, 16)
                S.emit("sync", fst)
                S.bump("st", 16)
    emit_proj()


    bar()

    def emit_hyena(stage):
        PB.reset(); PF.reset()
        la = PB.alloc(N2 * 1024)
        mid = [la[:, 0:8192], la[:, 8192:16384]]
        s2sb = [la[:, 16384:18432], la[:, 18432:20480]]
        stg2 = [la[:, 20480:24576], la[:, 24576:28672]]
        stg1 = [PB.alloc(2 * 1024) for _ in range(2)]
        s1sb = PB.alloc(256)
        g2t = [PB.alloc(256) for _ in range(4)]
        bre = [PB.alloc(1024) for _ in range(4)]
        bim = [PB.alloc(1024) for _ in range(4)]
        pg = [PB.alloc(1024) for _ in range(4)]
        ot = [PB.alloc(1024) for _ in range(4)]
        cre = PB.alloc(128)
        cim = PB.alloc(128)
        DQ = 256
        zt = la[:, 0:K2F * 2 * DQ]
        ft_ = la[:, K2F * 2 * DQ:2 * K2F * 2 * DQ]
        ct_ = la[:, 2 * K2F * 2 * DQ:3 * K2F * 2 * DQ]
        cti = [la[:, 0:32 * 256], la[:, 32 * 256:64 * 256]]
        stg3 = [la[:, 64 * 256:64 * 256 + 2048], la[:, 64 * 256 + 2048:64 * 256 + 4096]]
        c2k = c2row[:, 0:D]; c2kp = c2row[:, D:2 * D]; c2km = c2row[:, 2 * D:3 * D]
        filt = filt_in[stage]
        vsrc = ptok_s if stage == 0 else v2_s
        vout = v2_s if stage == 0 else stok_s

        def f(s):
            s.dma_start(out=cre[:], in_=f128re_in[:]).then_inc(ld, 16)
            s.dma_start(out=cim[:], in_=f128im_in[:]).then_inc(ld, 16)
            s.dma_start(out=s1sb[:, 0:128], in_=s1inv_in[0]).then_inc(ld, 16)
            s.dma_start(out=s1sb[:, 128:256], in_=s1inv_in[1]).then_inc(ld, 16)
        S.emit("sync", f)
        S.bump("ld", 64)
        def f(g):
            g.dma_start(out=la[0:64, :], in_=filt.rearrange("(a b) d -> a (b d)", b=N2)).then_inc(lg, 16)
        S.emit("gpsimd", f)
        S.bump("lg", 16)

        def fwd(src_is_filter):
            kh = 64 if src_is_filter else 128
            outsp = Fsp if src_is_filter else Zsp
            s2c = s2filt_in if src_is_filter else s2fwd_in
            if not src_is_filter:
                pe_need = C["pe"]
                st_need0 = C["st2"]
                sg_need0 = C["sg"]
                vla = vsrc[:, 0:D].rearrange("(a b) d -> a b d", b=N2) if stage == 0 \
                    else vsrc.rearrange("(a b) d -> a (b d)", b=N2)
                def f(s, pe_need=pe_need, st_need0=st_need0, sg_need0=sg_need0, vla=vla):
                    s.wait_ge(pe, pe_need)
                    s.wait_ge(st2, st_need0)
                    if sg_need0 > 0:
                        s.wait_ge(sg, sg_need0)
                    s.dma_start(out=la[:], in_=vla).then_inc(ld, 16)
                S.emit("sync", f)
                S.bump("ld", 16)
            ld_need = C["ld"]; lg_need = C["lg"]
            ev_base = C["ev"]; st_base = C["st2"]
            for n2 in range(N2):
                par = n2 % 2
                ev_need = ev_base + 4 * n2
                def fmm(t, n2=n2, ld_need=ld_need, lg_need=lg_need, ev_need=ev_need, kh=kh):
                    t.wait_ge(ld, ld_need)
                    t.wait_ge(lg, lg_need)
                    if ev_need > 0:
                        t.wait_ge(ev, ev_need)
                    for dh in range(2):
                        t.matmul(ps[0 + dh][:], cre[0:kh, :],
                                 la[0:kh, n2 * D + dh * 512: n2 * D + dh * 512 + 512],
                                 start=True, stop=True).then_inc(pe, 1)
                        t.matmul(ps[2 + dh][:], cim[0:kh, :],
                                 la[0:kh, n2 * D + dh * 512: n2 * D + dh * 512 + 512],
                                 start=True, stop=True).then_inc(pe, 1)
                S.emit("tensor", fmm)
                pe_now = S.bump("pe", 4)
                st_need = st_base + 16 * (n2 - 1) if n2 >= 2 else 0
                def fev(v, par=par, pe_now=pe_now, st_need=st_need):
                    v.wait_ge(pe, pe_now)
                    if st_need > 0:
                        v.wait_ge(st2, st_need)
                    v.tensor_copy(stg1[par][:, 0:512], ps[0][:])
                    v.tensor_copy(stg1[par][:, 512:1024], ps[1][:])
                    v.tensor_copy(stg1[par][:, 1024:1536], ps[2][:])
                    v.tensor_copy(stg1[par][:, 1536:2048], ps[3][:]).then_inc(ev, 4)
                S.emit("vector", fev)
                ev_now = S.bump("ev", 4)
                def fst(g, n2=n2, par=par, ev_now=ev_now):
                    g.wait_ge(ev, ev_now)
                    for c_ in range(2):
                        g.dma_start(out=fw1[n2 * 4 + c_ * 2: n2 * 4 + c_ * 2 + 2, :, :]
                                    .rearrange("s k e -> k s e"),
                                    in_=stg1[par][:, c_ * 1024:(c_ + 1) * 1024]
                                    .rearrange("k (s e) -> k s e", s=2)).then_inc(st2, 16)
                S.emit("scalar", fst)
                S.bump("st2", 32)
            s1_done_st = C["st2"]
            for kc in range(8):
                par = kc % 2
                pe_guard = C["pe"] - 16 if kc >= 2 else 0
                def fldc(s, kc=kc, par=par, s1_done_st=s1_done_st, s2c=s2c, pe_guard=pe_guard):
                    s.wait_ge(st2, s1_done_st)
                    if pe_guard > 0:
                        s.wait_ge(pe, pe_guard)
                    s.dma_start(out=s2sb[par][:],
                                in_=s2c[kc * 16:(kc + 1) * 16, :, :]
                                .rearrange("k p i -> p k i")).then_inc(ld, 16)
                    s.dma_start(out=mid[par][:],
                                in_=fw1[:, kc * 16:(kc + 1) * 16, :]).then_inc(ld, 16)
                S.emit("sync", fldc)
                ldc_now = S.bump("ld", 32)
                for half in range(2):
                    spar = (kc * 2 + half) % 2
                    stb = stg2[spar]
                    qname = "st2"
                    qsem = st2
                    for k1h in range(8):
                        k1i = half * 8 + k1h
                        k1 = kc * 16 + k1i
                        bank = 2 + (k1i % 4)
                        ev_need = C["ev"] if (kc == 0 and half == 0 and k1h <= 3) else C["ev"] - 3
                        def fmm(t, par=par, k1i=k1i, bank=bank, ldc_now=ldc_now, ev_need=ev_need):
                            t.wait_ge(ld, ldc_now)
                            if ev_need > 0:
                                t.wait_ge(ev, ev_need)
                            t.matmul(ps[bank][:], s2sb[par][:, k1i * 128:(k1i + 1) * 128],
                                     mid[par][:, k1i * 512:(k1i + 1) * 512],
                                     start=True, stop=True).then_inc(pe, 1)
                        S.emit("tensor", fmm)
                        pe_now = S.bump("pe", 1)
                        st_need = C["st2"] - 256 if (kc * 2 + half) >= 2 else 0
                        def fev(v, stb=stb, k1h=k1h, bank=bank, pe_now=pe_now,
                                st_need=st_need, qsem=qsem):
                            v.wait_ge(pe, pe_now)
                            if st_need > 0 and k1h == 0:
                                v.wait_ge(qsem, st_need)
                            v.tensor_copy(stb[:, k1h * 512:(k1h + 1) * 512], ps[bank][:]).then_inc(ev, 1)
                        S.emit("vector", fev)
                        ev_now = S.bump("ev", 1)
                    k1b = kc * 16 + half * 8
                    ev_now = C["ev"]
                    def fst(scl, k1b=k1b, stb=stb, ev_now=ev_now, outsp=outsp, qsem=qsem):
                        scl.wait_ge(ev, ev_now)
                        for co in range(2):
                            for s2 in range(2):
                                for eh in range(2):
                                    scl.dma_start(
                                        out=outsp.rearrange("r k e -> r (k e)")
                                        [co * 64 + s2 * 2 + eh: co * 64 + s2 * 2 + eh + 61: 4,
                                         k1b * 256:(k1b + 8) * 256]
                                        .rearrange("r (q e) -> r q e", q=8),
                                        in_=stb[co * 64 + s2 * 32: co * 64 + s2 * 32 + 16, :]
                                        .rearrange("p (q h e) -> p q h e", q=8, h=2)
                                        [:, :, eh, :]).then_inc(st2, 16)
                                    scl.dma_start(
                                        out=outsp[128 + co * 4 + s2 * 2 + eh,
                                                  k1b:k1b + 8, :].unsqueeze(0),
                                        in_=stb[co * 64 + s2 * 32 + 16: co * 64 + s2 * 32 + 17, :]
                                        .rearrange("p (q h e) -> p q h e", q=8, h=2)
                                        [:, :, eh, :]).then_inc(qsem, 16)
                    S.emit("scalar", fst)
                    S.bump(qname, 256)

        fwd(True)
        fwd(False)

        pwz = la[:, 0:4096]
        pwf = la[:, 4096:8192]
        pwfs = la[:, 8192:12288]
        pwt1 = la[:, 12288:16384]
        pwt2 = la[:, 16384:20480]
        pwct = la[:, 20480:24576]
        zhi = la[:, 24576:24832]
        fhi = la[:, 24832:25088]
        fshi = la[:, 25088:25344]
        thi1 = la[:, 25344:25600]
        thi2 = la[:, 25600:25856]
        c2st = la[:, 25856:26112]
        bfly = PB.alloc(512)
        pw_st_need = C["st2"]
        pw_sg_need = C["sg"]
        def fldb(s):
            s.dma_start(out=bfly[:, 0:128], in_=bfly1_in[:]).then_inc(ld, 16)
            s.dma_start(out=bfly[:, 128:256], in_=bfly2_in[:]).then_inc(ld, 16)
            s.dma_start(out=bfly[:, 256:384], in_=bflyh1_in[:]).then_inc(ld, 16)
            s.dma_start(out=bfly[:, 384:512], in_=bflyh2_in[:]).then_inc(ld, 16)
        S.emit("sync", fldb)
        bf_ld = S.bump("ld", 64)
        Zv = Zsp.rearrange("r k e -> r (k e)")
        Fv = Fsp.rearrange("r k e -> r (k e)")
        Cv = Csp.rearrange("r k e -> r (k e)")
        for kc in range(8):
            c0 = kc * 16 * 256
            vv_prev = C["vv"]
            pe_prev = C["pe"]
            def fld(s, kc=kc, c0=c0, pw_st_need=pw_st_need, pw_sg_need=pw_sg_need,
                    vv_prev=vv_prev, pe_prev=pe_prev):
                s.wait_ge(st2, pw_st_need)
                if pw_sg_need > 0:
                    s.wait_ge(sg, pw_sg_need)
                if kc >= 1:
                    s.wait_ge(vv, vv_prev)
                    s.wait_ge(pe, pe_prev)
                s.dma_start(out=pwz[:], in_=Zv[0:128, c0:c0 + 4096]).then_inc(ld, 16)
                s.dma_start(out=pwf[:], in_=Fv[0:128, c0:c0 + 4096]).then_inc(ld, 16)
                s.dma_start(out=pwfs[0:64, :], in_=Fv[64:128, c0:c0 + 4096]).then_inc(ld, 16)
                s.dma_start(out=pwfs[64:128, :], in_=Fv[0:64, c0:c0 + 4096]).then_inc(ld, 16)
            S.emit("sync", fld)
            ld_now = S.bump("ld", 64)
            vb = C["vv"]
            def fmul(v, ld_now=ld_now, vb=vb):
                v.wait_ge(ld, ld_now)
                v.tensor_tensor(pwt1[:], pwz[:], pwf[:], ALU.mult).then_inc(vv, 1)
                v.tensor_tensor(pwt2[:], pwz[:], pwfs[:], ALU.mult).then_inc(vv, 1)
            S.emit("vector", fmul)
            mul_vv = S.bump("vv", 2)
            for js in range(8):
                bank = 4 + (js % 2)
                ev_need = C["ev"]
                def fbf(t, js=js, bank=bank, mul_vv=mul_vv, ev_need=ev_need, bf_ld=bf_ld):
                    t.wait_ge(vv, mul_vv)
                    t.wait_ge(ld, bf_ld)
                    t.wait_ge(ev, ev_need)
                    t.matmul(ps[bank][:], bfly[:, 0:128], pwt1[:, js * 512:(js + 1) * 512],
                             start=True, stop=False)
                    t.matmul(ps[bank][:], bfly[:, 128:256], pwt2[:, js * 512:(js + 1) * 512],
                             start=False, stop=True).then_inc(pe, 1)
                S.emit("tensor", fbf)
                pe_now = S.bump("pe", 1)
                st_need2 = C["st"]
                def fev(scl, js=js, bank=bank, pe_now=pe_now, st_need2=st_need2):
                    scl.wait_ge(pe, pe_now)
                    if js == 0:
                        scl.wait_ge(st, st_need2)
                    scl.copy(out=pwct[:, js * 512:(js + 1) * 512], in_=ps[bank][:]).then_inc(ev, 1)
                S.emit("scalar", fev)
                S.bump("ev", 1)
            ev_h = C["ev"]
            if kc == 0:
                def fhv(v, ev_h=ev_h):
                    v.wait_ge(ev, ev_h)
                    v.tensor_scalar(pwct[0:4, 0:256], pwct[0:4, 0:256], 0.5, None, ALU.mult).then_inc(vv, 1)
                    v.tensor_scalar(pwct[64:68, 0:256], pwct[64:68, 0:256], 0.5, None, ALU.mult).then_inc(vv, 1)
                S.emit("vector", fhv)
                S.bump("vv", 2)
            vv_now = C["vv"]; ev_now = C["ev"]
            def fstc(s, kc=kc, c0=c0, vv_now=vv_now, ev_now=ev_now):
                s.wait_ge(vv, vv_now)
                s.wait_ge(ev, ev_now)
                s.dma_start(out=Cv[0:128, c0:c0 + 4096], in_=pwct[:]).then_inc(st, 16)
            S.emit("sync", fstc)
            S.bump("st", 16)
        def fldh(s, pw_st_need=pw_st_need, pw_sg_need=pw_sg_need):
            s.wait_ge(st2, pw_st_need)
            if pw_sg_need > 0:
                s.wait_ge(sg, pw_sg_need)
            s.dma_start(out=zhi[0:8, :], in_=Zsp[128:136, 0, :]).then_inc(ld, 16)
            s.dma_start(out=fhi[0:8, :], in_=Fsp[128:136, 0, :]).then_inc(ld, 16)
            s.dma_start(out=fshi[0:4, :], in_=Fsp[132:136, 0, :]).then_inc(ld, 16)
            s.dma_start(out=fshi[4:8, :], in_=Fsp[128:132, 0, :]).then_inc(ld, 16)
        S.emit("sync", fldh)
        hi_ld = S.bump("ld", 64)
        vbh = C["vv"]
        def fmh(v, hi_ld=hi_ld, vbh=vbh):
            v.wait_ge(ld, hi_ld)
            v.tensor_tensor(thi1[0:8, :], zhi[0:8, :], fhi[0:8, :], ALU.mult).then_inc(vv, 1)
            v.tensor_tensor(thi2[0:8, :], zhi[0:8, :], fshi[0:8, :], ALU.mult).then_inc(vv, 1)
        S.emit("vector", fmh)
        mh_vv = S.bump("vv", 2)
        ev_need = C["ev"]
        def fbh(t, mh_vv=mh_vv, ev_need=ev_need, bf_ld=bf_ld):
            t.wait_ge(vv, mh_vv)
            t.wait_ge(ev, ev_need)
            t.matmul(ps[4][0:8, 0:256], bfly[0:8, 256:264], thi1[0:8, :],
                     start=True, stop=False)
            t.matmul(ps[4][0:8, 0:256], bfly[0:8, 384:392], thi2[0:8, :],
                     start=False, stop=True).then_inc(pe, 1)
        S.emit("tensor", fbh)
        peh = S.bump("pe", 1)
        def fevh(scl, peh=peh):
            scl.wait_ge(pe, peh)
            scl.copy(out=c2st[0:8, :], in_=ps[4][0:8, 0:256]).then_inc(ev, 1)
        S.emit("scalar", fevh)
        evh = S.bump("ev", 1)
        def fgat(s, evh=evh):
            s.wait_ge(ev, evh)
            s.dma_start(out=c2_s[:], in_=c2st[0:4, :]).then_inc(ld, 16)
        S.emit("sync", fgat)
        gat_ld = S.bump("ld", 16)
        def fgat2(s, gat_ld=gat_ld):
            s.wait_ge(ld, gat_ld)
            s.dma_start(out=c2k[:], in_=c2_s.rearrange("s e -> (s e)").unsqueeze(0)).then_inc(ld, 16)
        S.emit("sync", fgat2)
        c2k_ld = S.bump("ld", 16)
        def f(scl, c2k_ld=c2k_ld):
            scl.wait_ge(ld, c2k_ld)
            scl.mul(out=c2kp[:], in_=c2k[:], mul=0.5)
            scl.mul(out=c2km[:], in_=c2k[:], mul=-0.5).then_inc(ev, 1)
        S.emit("scalar", f)
        c2k_ev = S.bump("ev", 1)
        pw_done_st = C["st"]

        for kc in range(4):
            par = kc % 2
            pe_guard = C["pe"] - 32 if kc >= 2 else 0
            def fld(s, kc=kc, par=par, pw_done_st=pw_done_st, pe_guard=pe_guard):
                s.wait_ge(st, pw_done_st)
                if pe_guard > 0:
                    s.wait_ge(pe, pe_guard)
                s.dma_start(out=cti[par][:],
                            in_=Csp[0:128, kc * 32:(kc + 1) * 32, :]).then_inc(ld, 16)
            S.emit("sync", fld)
            ld_now = S.bump("ld", 16)
            for co in range(2):
                for jg in range(4):
                    m = kc * 8 + co * 4 + jg
                    spar = m % 2
                    stb = stg3[spar]
                    qname = "st2"
                    qsem = st2
                    for jh in range(4):
                        js = jg * 4 + jh
                        bank = 2 + (js % 4)
                        ev_need = C["ev"] if (kc == 0 and co == 0 and js <= 3) else C["ev"] - 3
                        def fmm(t, par=par, co=co, js=js, bank=bank, ld_now=ld_now, ev_need=ev_need):
                            t.wait_ge(ld, ld_now)
                            if ev_need > 0:
                                t.wait_ge(ev, ev_need)
                            t.matmul(ps[bank][:], s1sb[:, co * 128:(co + 1) * 128],
                                     cti[par][:, js * 512:(js + 1) * 512],
                                     start=True, stop=True).then_inc(pe, 1)
                        S.emit("tensor", fmm)
                        pe_now = S.bump("pe", 1)
                        st_need = C["st2"] - 64 if m >= 2 else 0
                        def fev(v, stb=stb, jh=jh, bank=bank, pe_now=pe_now,
                                st_need=st_need, qsem=qsem):
                            v.wait_ge(pe, pe_now)
                            if st_need > 0 and jh == 0:
                                v.wait_ge(qsem, st_need)
                            v.tensor_copy(stb[:, jh * 512:(jh + 1) * 512], ps[bank][:]).then_inc(ev, 1)
                        S.emit("vector", fev)
                        S.bump("ev", 1)
                    ev_now = C["ev"]
                    k1b = kc * 32 + jg * 8
                    def fst(scl, co=co, k1b=k1b, stb=stb, ev_now=ev_now, qsem=qsem):
                        scl.wait_ge(ev, ev_now)
                        for s4 in range(4):
                            scl.dma_start(out=Bsp[:, co, k1b:k1b + 8, s4 * 256:(s4 + 1) * 256],
                                          in_=stb[s4 * 32:(s4 + 1) * 32, :]
                                          .rearrange("a (k e) -> a k e", k=8)).then_inc(qsem, 16)
                    S.emit("scalar", fst)
                    S.bump(qname, 64)
        inv1_done_st = C["st2"]
        inv1_done_sg = C["sg"]

        gate_vv = {}
        mm_pe = {}
        st_done = {}
        inv1_ev = C["ev"]
        for t5 in range(N2):
            par = t5 % 4
            pe_guard = mm_pe.get(t5 - 4, 0)
            vv_guard = gate_vv.get((t5 - 4, 1), 0)
            def fld(s, t5=t5, par=par, inv1_done_st=inv1_done_st,
                    pe_guard=pe_guard, vv_guard=vv_guard):
                s.wait_ge(st2, inv1_done_st)
                if pe_guard > 0:
                    s.wait_ge(pe, pe_guard)
                if vv_guard > 0:
                    s.wait_ge(vv, vv_guard)
                s.dma_start(out=g2t[par][:, 0:128], in_=g2re_in[t5]).then_inc(ld, 16)
                s.dma_start(out=g2t[par][:, 128:256], in_=g2nim_in[t5]).then_inc(ld, 16)
                s.dma_start(out=bre[par][:], in_=Bsp[t5, 0]).then_inc(ld, 16)
                s.dma_start(out=bim[par][:], in_=Bsp[t5, 1]).then_inc(ld, 16)
                s.dma_start(out=pg[par][:], in_=ptok_s.rearrange("(m b) e -> m (b e)", b=N2)
                            [:, t5 * 3072 + (stage + 1) * D: t5 * 3072 + (stage + 2) * D]).then_inc(ld, 16)
            S.emit("sync", fld)
            ld_now = S.bump("ld", 80)
            for dh in range(2):
                bank = 2 + ((t5 * 2 + dh) % 4)
                vv_need = gate_vv.get((t5 - 2, dh), 0)
                def fmm(t, dh=dh, t5=t5, par=par, bank=bank, ld_now=ld_now, vv_need=vv_need,
                        c2k_ev=c2k_ev, inv1_ev=inv1_ev):
                    t.wait_ge(ld, ld_now)
                    t.wait_ge(ev, max(c2k_ev, inv1_ev))
                    if vv_need > 0:
                        t.wait_ge(vv, vv_need)
                    t.matmul(ps[bank][:], g2t[par][:, 0:128], bre[par][:, dh * 512:(dh + 1) * 512],
                             start=True, stop=False)
                    t.matmul(ps[bank][:], g2t[par][:, 128:256], bim[par][:, dh * 512:(dh + 1) * 512],
                             start=False, stop=False)
                    t.matmul(ps[bank][:], ones1[:],
                             (c2kp if t5 % 2 == 0 else c2km)[:, dh * 512:(dh + 1) * 512],
                             start=False, stop=True).then_inc(pe, 1)
                S.emit("tensor", fmm)
                pe_now = S.bump("pe", 1)
                st_g = st_done.get(t5 - 4, 0)
                def fgate(v, dh=dh, par=par, bank=bank, pe_now=pe_now, st_g=st_g):
                    v.wait_ge(pe, pe_now)
                    if st_g > 0 and dh == 0:
                        v.wait_ge(st, st_g)
                    v.tensor_tensor(ot[par][:, dh * 512:(dh + 1) * 512], ps[bank][:],
                                    pg[par][:, dh * 512:(dh + 1) * 512], ALU.mult).then_inc(vv, 1)
                S.emit("vector", fgate)
                gate_vv[(t5, dh)] = S.bump("vv", 1)
            mm_pe[t5] = C["pe"]
            vv_now = C["vv"]
            def fst(s, t5=t5, par=par, vv_now=vv_now, vout=vout):
                s.wait_ge(vv, vv_now)
                s.dma_start(out=vout.rearrange("(m b) d -> m (b d)", b=N2)
                            [:, t5 * D:(t5 + 1) * D], in_=ot[par][:]).then_inc(st, 16)
            S.emit("sync", fst)
            st_done[t5] = S.bump("st", 16)

    emit_hyena(0)
    bar()
    emit_hyena(1)
    bar()

    def emit_l0ffn():
        PB.reset(); PF.reset()
        wTr = PB.alloc(8 * 1024)
        wTk = PB.alloc(8 * 1024)
        wTv = PB.alloc(8 * 1024)
        wT = {"r": wTr, "k": wTk, "v": wTv}
        wload = PB.alloc(1024)
        xnc = PB.alloc(8 * 1025)
        xr = PB.alloc(8 * 1024)
        xk = PB.alloc(8 * 1024)
        rt = PB.alloc(8 * 1024)
        kact = xr
        gt = xnc
        trs = PB.alloc(256)
        tmp = PF.alloc(1024)
        sq = PF.alloc(512)
        mu = sm2[:, 0:16]
        def f(g):
            g.dma_start(out=mu[:, 0:8], in_=mur_in.rearrange("(g p) -> p g", p=128),
                        allow_slow_non_contiguous=True).then_inc(lg, 16)
            g.dma_start(out=mu[:, 8:16], in_=muk_in.rearrange("(g p) -> p g", p=128),
                        allow_slow_non_contiguous=True).then_inc(lg, 16)
        S.emit("gpsimd", f)
        mu_lg = S.bump("lg", 32)
        def fw(g):
            for n, W in [("r", WrT_in), ("k", WkT_in), ("v", WvT_in)]:
                for bj in range(8):
                    g.dma_start(out=wT[n][:, bj * 1024:(bj + 1) * 1024],
                                in_=W[bj * 128:(bj + 1) * 128, :]).then_inc(lg, 16)
        S.emit("gpsimd", fw)
        wprep_lg = S.bump("lg", 16 * 24)
        NCH = 4; TC = 1024
        for ch in range(NCH):
            t0 = ch * TC
            vv_g = C["vv"]; pe_gd = C["pe"]; ev_g0 = C["ev"]
            def f(s, ch=ch, t0=t0, vv_g=vv_g, pe_gd=pe_gd, ev_g0=ev_g0):
                s.wait_ge(vv, vv_g)
                s.wait_ge(pe, pe_gd)
                s.wait_ge(ev, ev_g0)
                for g in range(8):
                    if ch == 0:
                        s.dma_start(out=xnc[:, g * 1025 + 1: (g + 1) * 1025],
                                    in_=xnT_s[g * 128:(g + 1) * 128, 0:TC]).then_inc(ld, 16)
                    else:
                        s.dma_start(out=xnc[:, g * 1025: (g + 1) * 1025],
                                    in_=xnT_s[g * 128:(g + 1) * 128, t0 - 1: t0 + TC]).then_inc(ld, 16)
            S.emit("sync", f)
            ld_now = S.bump("ld", 16 * 8)
            if ch == 0:
                def f(v, ld_now=ld_now):
                    v.wait_ge(ld, ld_now)
                    for g in range(8):
                        v.memset(xnc[:, g * 1025: g * 1025 + 1], 0.0)
                    v.engine_nop().then_inc(vv, 1)
                S.emit("vector", f)
                S.bump("vv", 1)
            for g in range(8):
                vv_base = C["vv"]
                ev_g = C["ev"]
                def f(v, g=g, ld_now=ld_now, vv_base=vv_base, ev_g=ev_g):
                    v.wait_ge(ld, ld_now)
                    if g == 0:
                        v.wait_ge(vv, vv_base)
                        v.wait_ge(ev, ev_g)
                        v.wait_ge(lg, mu_lg)
                    xp = xnc[:, g * 1025: g * 1025 + 1024]
                    xc = xnc[:, g * 1025 + 1: g * 1025 + 1025]
                    v.tensor_tensor(tmp[:], xc, xp, ALU.subtract).then_inc(vv, 1)
                    v.wait_ge(vv, vv_base + 1)
                    v.scalar_tensor_tensor(xr[:, g * 1024:(g + 1) * 1024], tmp[:],
                                           mu[:, g:g + 1], xp, ALU.mult, ALU.add).then_inc(vv, 1)
                    v.scalar_tensor_tensor(xk[:, g * 1024:(g + 1) * 1024], tmp[:],
                                           mu[:, 8 + g:9 + g], xp, ALU.mult, ALU.add).then_inc(vv, 1)
                S.emit("vector", f)
                S.bump("vv", 3)
            mix_vv = C["vv"]
            for (nm, srcb, dstb) in [("r", xr, rt), ("k", xk, kact)]:
                for bi in range(8):
                    for j in range(2):
                        ev_need = C["ev"]
                        def fmm(t, nm=nm, srcb=srcb, bi=bi, j=j, mix_vv=mix_vv,
                                ev_need=ev_need, wprep_lg=wprep_lg):
                            t.wait_ge(vv, mix_vv)
                            t.wait_ge(lg, wprep_lg)
                            if ev_need > 0:
                                t.wait_ge(ev, ev_need)
                            for kb in range(8):
                                mm = t.matmul(ps[j][:],
                                              wT[nm][:, kb * 1024 + bi * 128: kb * 1024 + (bi + 1) * 128],
                                              srcb[:, kb * 1024 + j * 512: kb * 1024 + j * 512 + 512],
                                              start=(kb == 0), stop=(kb == 7))
                            mm.then_inc(pe, 1)
                        S.emit("tensor", fmm)
                        pe_now = S.bump("pe", 1)
                        vv_need = C["vv"]
                        def fev(scl, nm=nm, dstb=dstb, bi=bi, j=j, pe_now=pe_now, vv_need=vv_need):
                            scl.wait_ge(pe, pe_now)
                            scl.wait_ge(vv, vv_need)
                            piece = dstb[:, bi * 1024 + j * 512: bi * 1024 + j * 512 + 512]
                            if nm == "r":
                                scl.activation(out=piece, in_=ps[j][:], func=AF.Sigmoid).then_inc(ev, 1)
                            else:
                                scl.activation(out=sq[:], in_=ps[j][:], func=AF.Relu)
                                scl.activation(out=piece, in_=sq[:], func=AF.Square).then_inc(ev, 1)
                        S.emit("scalar", fev)
                        S.bump("ev", 1)
            rk_ev = C["ev"]
            for bi in range(8):
                for j in range(2):
                    ev_need = C["ev"]
                    def fmm(t, bi=bi, j=j, rk_ev=rk_ev, ev_need=ev_need):
                        t.wait_ge(ev, max(rk_ev, ev_need))
                        for kb in range(8):
                            mm = t.matmul(ps[j][:],
                                          wT["v"][:, kb * 1024 + bi * 128: kb * 1024 + (bi + 1) * 128],
                                          kact[:, kb * 1024 + j * 512: kb * 1024 + j * 512 + 512],
                                          start=(kb == 0), stop=(kb == 7))
                        mm.then_inc(pe, 1)
                    S.emit("tensor", fmm)
                    pe_now = S.bump("pe", 1)
                    ev_need2 = C["ev"]
                    def fgv(v, bi=bi, j=j, pe_now=pe_now, ev_need2=ev_need2):
                        v.wait_ge(pe, pe_now)
                        v.wait_ge(ev, ev_need2)
                        piece = rt[:, bi * 1024 + j * 512: bi * 1024 + j * 512 + 512]
                        v.tensor_tensor(tmp[:, 0:512], ps[j][:], piece, ALU.mult).then_inc(vv, 1)
                    S.emit("vector", fgv)
                    vv_now = S.bump("vv", 1)
                    pe_g2 = C["pe"]
                    def fge(scl, bi=bi, j=j, vv_now=vv_now, pe_g2=pe_g2):
                        scl.wait_ge(vv, vv_now)
                        scl.wait_ge(pe, pe_g2)
                        scl.activation(out=gt[:, bi * 1025 + j * 512: bi * 1025 + j * 512 + 512],
                                       in_=tmp[:, 0:512], func=AF.Silu).then_inc(ev, 1)
                    S.emit("scalar", fge)
                    S.bump("ev", 1)
            g_ev = C["ev"]
            for it in range(8):
                for jd in range(8):
                    idx2 = it * 8 + jd
                    pstj = pst if idx2 % 2 == 0 else pst2
                    ev_need = C["ev"] if idx2 <= 1 else C["ev"] - 1
                    def fmm(t, it=it, jd=jd, pstj=pstj, g_ev=g_ev, ev_need=ev_need):
                        t.wait_ge(ev, max(g_ev, ev_need))
                        t.transpose(pstj[:], gt[:, jd * 1025 + it * 128: jd * 1025 + (it + 1) * 128],
                                    idt[:]).then_inc(pe, 1)
                    S.emit("tensor", fmm)
                    pe_now = S.bump("pe", 1)
                    st_need = C["st"] - 16 if idx2 >= 2 else 0
                    trsj = trs[:, 0:128] if idx2 % 2 == 0 else trs[:, 128:256]
                    def fev(scl, pstj=pstj, trsj=trsj, pe_now=pe_now, st_need=st_need):
                        scl.wait_ge(pe, pe_now)
                        if st_need > 0:
                            scl.wait_ge(st, st_need)
                        scl.copy(out=trsj, in_=pstj[:]).then_inc(ev, 1)
                    S.emit("scalar", fev)
                    ev_now = S.bump("ev", 1)
                    def fst(s, it=it, jd=jd, t0=t0, trsj=trsj, ev_now=ev_now):
                        s.wait_ge(ev, ev_now)
                        s.dma_start(out=gtok_s[t0 + it * 128: t0 + (it + 1) * 128,
                                               jd * 128:(jd + 1) * 128], in_=trsj).then_inc(st, 16)
                    S.emit("sync", fst)
                    S.bump("st", 16)
    emit_l0ffn()
    bar()

    xn1_res = emit_ln(x_in, xn1T_s, resid=(stok_s, gtok_s, x1_s))
    bar()
    if dbg:
        def fdbgx(s):
            for jj in range(8):
                s.dma_start(out=xn1T_d[jj * 128:(jj + 1) * 128, :],
                            in_=xn1T_s[jj * 128:(jj + 1) * 128, :]).then_inc(st, 16)
        S.emit("sync", fdbgx)
        S.bump("st", 16 * 8)
        bar()

    def emit_l1proj():
        PB.reset_to(8 * T); PF.reset()
        wsb = [PB.alloc(1056) for _ in range(8)]
        bcst = PB.alloc(512)
        t1 = PF.alloc(512)
        t2 = PF.alloc(512)
        stg = PF.alloc(512)
        bcol = sm2[:, 16:25]
        def f(g):
            for k in range(8):
                g.dma_start(out=wsb[k][:], in_=Wxp_in[k * 128:(k + 1) * 128, :]).then_inc(lg, 16)
            g.dma_start(out=bcol[:, 0:8], in_=bxp_in[0:1024].rearrange("(g p) -> p g", p=128),
                        allow_slow_non_contiguous=True).then_inc(lg, 16)
            g.dma_start(out=bcol[0:32, 8:9], in_=bxp_in[1024:1056].unsqueeze(1),
                        allow_slow_non_contiguous=True).then_inc(lg, 16)
        S.emit("gpsimd", f)
        w_ld = S.bump("lg", 16 * 10)
        for j in range(T // 512):
            ld_now = C["ld"]
            for i in range(9):
                hi = 128 if i < 8 else 32
                jb = i % 2
                ev_need = C["ev"]; vv_need = C["vv"]
                def fmm(t, i=i, j=j, jb=jb, hi=hi, w_ld=w_ld,
                        ev_need=ev_need, vv_need=vv_need):
                    t.wait_ge(lg, w_ld)
                    t.wait_ge(ev, ev_need)
                    t.wait_ge(vv, vv_need)
                    for k in range(8):
                        mm = t.matmul(ps[jb][0:hi, :], wsb[k][:, i * 128:i * 128 + hi],
                                      xn1_res[k][:, j * 512:(j + 1) * 512],
                                      start=(k == 0), stop=(k == 7))
                    mm.then_inc(pe, 1)
                S.emit("tensor", fmm)
                pe_now = S.bump("pe", 1)
                if i < 8:
                    st_need = C["st"]
                    def fev(scl, i=i, jb=jb, pe_now=pe_now, st_need=st_need):
                        scl.wait_ge(pe, pe_now)
                        scl.wait_ge(st, st_need)
                        scl.activation(out=t1[:], in_=ps[jb][:], func=AF.Abs, bias=bcol[:, i:i + 1])
                        scl.activation(out=t1[:], in_=t1[:], func=AF.Exp, scale=-1.0)
                        scl.activation(out=t1[:], in_=t1[:], func=AF.Ln, bias=1.0)
                        scl.activation(out=t2[:], in_=ps[jb][:], func=AF.Relu,
                                       bias=bcol[:, i:i + 1]).then_inc(ev, 1)
                    S.emit("scalar", fev)
                    ev_now = S.bump("ev", 1)
                    def fv(v, ev_now=ev_now):
                        v.wait_ge(ev, ev_now)
                        v.tensor_tensor(stg[:], t1[:], t2[:], ALU.add).then_inc(vv, 1)
                    S.emit("vector", fv)
                    vv_now = S.bump("vv", 1)
                    def fst(s, i=i, j=j, vv_now=vv_now):
                        s.wait_ge(vv, vv_now)
                        s.dma_start(out=dT_s[i * 128:(i + 1) * 128, j * 512:(j + 1) * 512],
                                    in_=stg[:]).then_inc(st, 16)
                    S.emit("sync", fst)
                    S.bump("st", 16)
                else:
                    st_need = C["st"]
                    def fev(v, jb=jb, pe_now=pe_now, st_need=st_need):
                        v.wait_ge(pe, pe_now)
                        v.wait_ge(st, st_need)
                        v.tensor_scalar(bcst[0:32, :], ps[jb][0:32, :], bcol[0:32, 8:9], None,
                                        ALU.add).then_inc(vv, 1)
                    S.emit("vector", fev)
                    vv_now = S.bump("vv", 1)
                    def fst(s, j=j, vv_now=vv_now):
                        s.wait_ge(vv, vv_now)
                        s.dma_start(out=bc_s[:, j * 512:(j + 1) * 512],
                                    in_=bcst[0:32, :]).then_inc(st, 16)
                    S.emit("sync", fst)
                    S.bump("st", 16)
    emit_l1proj()
    bar()

    def emit_scan():
        PB.reset(); PF.reset()
        at_ = sm2[:, 0:128]
        dpc8 = sm2[:, 128:136]
        dx = PB.alloc(T)
        brepb = [PB.alloc(T), PB.alloc(T)]
        crepb = [PB.alloc(T), PB.alloc(T)]
        util = PB.alloc(T)
        htile = PB.alloc(T)
        yout = PB.alloc(T)
        atileb = [PB.alloc(T), PB.alloc(T)]
        PF.reset()
        dgb = [PF.alloc(4096), PF.alloc(4096)]
        yacc = PF.alloc(4096)
        xgb = [PF.alloc(2048).bitcast(BF16), PF.alloc(2048).bitcast(BF16)]
        we = [PB.alloc(1024) for _ in range(8)]
        dr = PB.alloc(1024)
        brep_sb = PB.alloc(1024)
        xnh = [PB.alloc(1024) for _ in range(2)]
        g1t = [PB.alloc(1024) for _ in range(2)]
        wo = [brepb[k // 4][:, (k % 4) * 1024:(k % 4 + 1) * 1024] for k in range(8)]
        yh = [crepb[0][:, q * 1024:(q + 1) * 1024] for q in range(4)]
        s1t = [util[:, 0:1024], util[:, 1024:2048]]
        g1r = [util[:, 2048:3072], util[:, 3072:4096]]
        x1f = crepb[1].bitcast(F32)
        x1t = [x1f[:, 0:1024], x1f[:, 1024:2048]]
        otf = dx.bitcast(F32)
        ott = [otf[:, 0:1024], otf[:, 1024:2048]]
        sgt = htile[:, 0:1024]
        ffn_state = {"mm_pe": {}, "silu_ev": {}, "g1_st": {}, "weadd_vv": 0}

        def emit_weight_prep():
            def f(g):
                g.dma_start(out=dr[0:64, :], in_=dis_in[:]).then_inc(lg, 16)
                g.dma_start(out=dr[64:128, :], in_=dis_in[:]).then_inc(lg, 16)
                g.dma_start(out=brep_sb[:], in_=bout_in[None, :].broadcast_to([128, D])).then_inc(lg, 16)
                for bj in range(8):
                    g.dma_start(out=we[bj][:],
                                in_=weiT_in[bj * 128:(bj + 1) * 128, :]).then_inc(lg, 16)
            S.emit("gpsimd", f)
            ffn_state["dr_lg"] = S.bump("lg", 16 * 11)

        def emit_weadd():
            dr_lg = ffn_state["dr_lg"]
            def f(v, dr_lg=dr_lg):
                v.wait_ge(lg, dr_lg)
                for k in range(8):
                    v.tensor_tensor(we[k][:], we[k][:], dr[:], ALU.add).then_inc(vv, 1 if k == 7 else 0)
            S.emit("vector", f)
            ffn_state["weadd_vv"] = S.bump("vv", 1)

        def emit_ffn_tile(t):
            par = t % 2
            pe_guard = ffn_state["mm_pe"].get(t - 2, 0)
            def fldx(s, t=t, par=par, pe_guard=pe_guard):
                if pe_guard > 0:
                    s.wait_ge(pe, pe_guard)
                for k in range(8):
                    s.dma_start(out=xnh[par][:, k * 128:(k + 1) * 128],
                                in_=xn1T_s[k * 128:(k + 1) * 128,
                                           t * 128:(t + 1) * 128]).then_inc(sg, 16)
            S.emit("scalar", fldx)
            sg_now = S.bump("sg", 16 * 8)
            for j in range(2):
                ev_need = ffn_state["silu_ev"].get((t - 2, j), 0)
                weadd_vv = ffn_state["weadd_vv"]
                def fmm(tn, t=t, j=j, par=par, sg_now=sg_now, ev_need=ev_need, weadd_vv=weadd_vv):
                    tn.wait_ge(sg, sg_now)
                    tn.wait_ge(vv, weadd_vv)
                    if ev_need > 0:
                        tn.wait_ge(ev, ev_need)
                    for k in range(8):
                        mm = tn.matmul(ps[2 + j][:], xnh[par][:, k * 128:(k + 1) * 128],
                                       we[k][:, j * 512:(j + 1) * 512],
                                       start=(k == 0), stop=(k == 7))
                    mm.then_inc(pe, 1)
                S.emit("tensor", fmm)
                pe_now = S.bump("pe", 1)
                st_guard = ffn_state["g1_st"].get(t - 2, 0)
                def fsil(scl, t=t, j=j, par=par, pe_now=pe_now, st_guard=st_guard):
                    scl.wait_ge(pe, pe_now)
                    if st_guard > 0 and j == 0:
                        scl.wait_ge(st2, st_guard)
                    scl.activation(out=g1t[par][:, j * 512:(j + 1) * 512],
                                   in_=ps[2 + j][:], func=AF.Silu).then_inc(ev, 1)
                S.emit("scalar", fsil)
                ffn_state["silu_ev"][(t, j)] = S.bump("ev", 1)
            ffn_state["mm_pe"][t] = C["pe"]
            ev_now = C["ev"]
            def fstg(scl, t=t, par=par, ev_now=ev_now):
                scl.wait_ge(ev, ev_now)
                scl.dma_start(out=g1_s[t * 128:(t + 1) * 128, :], in_=g1t[par][:]).then_inc(st2, 16)
            S.emit("scalar", fstg)
            ffn_state["g1_st"][t] = S.bump("st2", 16)
        def f(g):
            g.dma_start(out=at_[:], in_=Alog_in.rearrange("(g p) n -> p g n", p=128),
                        allow_slow_non_contiguous=True).then_inc(lg, 16)
            g.dma_start(out=dpc8[:], in_=Dp_in.rearrange("(g p) -> p g", p=128),
                        allow_slow_non_contiguous=True).then_inc(lg, 16)
        S.emit("gpsimd", f)
        a_ld = S.bump("lg", 32)
        def f(scl, a_ld=a_ld):
            scl.wait_ge(lg, a_ld)
            scl.activation(out=at_[:], in_=at_[:], func=AF.Exp, scale=1.0).then_inc(sc, 1)
        S.emit("scalar", f)
        sc_now = S.bump("sc", 1)
        def f(v, sc_now=sc_now):
            v.wait_ge(sc, sc_now)
            v.tensor_scalar(at_[:], at_[:], -1.0, None, ALU.mult).then_inc(vv, 1)
        S.emit("vector", f)
        S.bump("vv", 1)
        aprep_vv = C["vv"]
        bc_done = {}
        scan_done = {}
        exp_done = {}
        ystt_done = {}
        ld_g = {}

        def emit_ldg(g):
            par2 = g % 2
            dg = dgb[par2]; xg = xgb[par2]
            vv_g = ystt_done.get(g - 2, 0)
            sc_g = exp_done.get(g - 2, 0)
            def f(s, g=g, dg=dg, xg=xg, vv_g=vv_g, sc_g=sc_g):
                if vv_g > 0:
                    s.wait_ge(vv, vv_g)
                if sc_g > 0:
                    s.wait_ge(sc, sc_g)
                s.dma_start(out=dg[:], in_=dT_s[g * 128:(g + 1) * 128, :]).then_inc(ld, 16)
                s.dma_start(out=xg[:], in_=xn1T_s[g * 128:(g + 1) * 128, :]).then_inc(ld, 16)
            S.emit("sync", f)
            ld_g[g] = S.bump("ld", 32)

        emit_ldg(0)
        for g in range(8):
            par2 = g % 2
            dg = dgb[par2]; xg = xgb[par2]
            ld_now = ld_g[g]
            if g + 1 < 8:
                emit_ldg(g + 1)
            vbdx = C["vv"]
            def f(v, dg=dg, xg=xg, ld_now=ld_now, vbdx=vbdx):
                v.wait_ge(ld, ld_now)
                v.wait_ge(vv, vbdx)
                v.tensor_tensor(dx[:], dg[:], xg[:], ALU.mult).then_inc(vv, 1)
            S.emit("vector", f)
            S.bump("vv", 1)
            for n in range(16):
                par = (g * 16 + n) % 2
                brep = brepb[par]; crep = crepb[par]
                i2 = g * 16 + n - 2
                vv_bc = bc_done.get(i2, 0)
                def f(gp, n=n, par=par, vv_bc=vv_bc):
                    if vv_bc > 0:
                        gp.wait_ge(vv, vv_bc)
                    gp.dma_start(out=brepb[par][:], in_=bc_s[n:n + 1, :].broadcast_to([128, T])).then_inc(lg, 16)
                    gp.dma_start(out=crepb[par][:], in_=bc_s[16 + n:17 + n, :].broadcast_to([128, T])).then_inc(lg, 16)
                S.emit("gpsimd", f)
                ld_bc = S.bump("lg", 32)
                atile = atileb[par]
                vv_a = max(scan_done.get(g * 16 + n - 2, 0), aprep_vv)
                def f(scl, g=g, n=n, dg=dg, ld_now=ld_now, vv_a=vv_a, atile=atile):
                    scl.wait_ge(ld, ld_now)
                    scl.wait_ge(vv, vv_a)
                    scl.activation(out=atile[:], in_=dg[:], func=AF.Exp,
                                   scale=at_[:, g * 16 + n: g * 16 + n + 1]).then_inc(sc, 1)
                S.emit("scalar", f)
                sc_now = S.bump("sc", 1)
                vb = C["vv"]
                def f(v, g=g, n=n, ld_bc=ld_bc, sc_now=sc_now, vb=vb, brep=brep, crep=crep, atile=atile):
                    v.wait_ge(lg, ld_bc)
                    v.tensor_tensor(util[:], dx[:], brep[:], ALU.mult).then_inc(vv, 1)
                    v.wait_ge(vv, vb + 1)
                    v.wait_ge(sc, sc_now)
                    v.tensor_tensor_scan(htile[:], atile[:], util[:], 0.0,
                                         ALU.mult, ALU.add).then_inc(vv, 1)
                    v.wait_ge(vv, vb + 2)
                    if n == 0:
                        v.tensor_tensor(yacc[:], htile[:], crep[:], ALU.mult).then_inc(vv, 1)
                    else:
                        v.tensor_tensor(util[:], htile[:], crep[:], ALU.mult).then_inc(vv, 1)
                        v.wait_ge(vv, vb + 3)
                        v.tensor_tensor(yacc[:], yacc[:], util[:], ALU.add).then_inc(vv, 1)
                S.emit("vector", f)
                scan_done[g * 16 + n] = vb + 2
                bc_done[g * 16 + n] = S.bump("vv", 3 if n == 0 else 4)
            exp_done[g] = C["sc"]
            st_g = C["st"]
            vb2 = C["vv"]
            def f(v, g=g, xg=xg, st_g=st_g, vb2=vb2):
                if st_g > 0:
                    v.wait_ge(st, st_g)
                v.wait_ge(vv, vb2)
                v.scalar_tensor_tensor(yout[:], xg[:], dpc8[:, g:g + 1], yacc[:],
                                       ALU.mult, ALU.add).then_inc(vv, 1)
            S.emit("vector", f)
            vv_y = S.bump("vv", 1)
            ystt_done[g] = vv_y
            def f(scl, g=g, vv_y=vv_y):
                scl.wait_ge(vv, vv_y)
                scl.dma_start(out=yT_s[g * 128:(g + 1) * 128, :], in_=yout[:]).then_inc(st, 16)
            S.emit("scalar", f)
            S.bump("st", 16)
            if g == 0:
                emit_weight_prep()
                emit_weadd()
            else:
                for t_ in range((g - 1) * 5, min(32, g * 5)):
                    emit_ffn_tile(t_)

        scan_st = C["st"]
        def fwo(g, scan_st=scan_st):
            g.wait_ge(st, scan_st)
            for k in range(8):
                g.dma_start(out=wo[k][:], in_=Wout_in[k * 128:(k + 1) * 128, :]).then_inc(lg, 16)
        S.emit("gpsimd", fwo)
        wo_lg = S.bump("lg", 16 * 8)
        op_pe = {}
        s1_vv = {}
        comb_vv = {}
        out_st = {}
        pending_store = [None]
        for t in range(32):
            par = t % 2
            ypar = t % 4
            pe_g = op_pe.get(t - 4, 0)
            def fldy(s, t=t, ypar=ypar, scan_st=scan_st, pe_g=pe_g):
                s.wait_ge(st, scan_st)
                if pe_g > 0:
                    s.wait_ge(pe, pe_g)
                for k in range(8):
                    s.dma_start(out=yh[ypar][:, k * 128:(k + 1) * 128],
                                in_=yT_s[k * 128:(k + 1) * 128,
                                         t * 128:(t + 1) * 128]).then_inc(ld, 16)
            S.emit("sync", fldy)
            ldy_now = S.bump("ld", 16 * 8)
            cg = comb_vv.get(t - 2, 0)
            def fldx1(s, t=t, par=par, cg=cg):
                if cg > 0:
                    s.wait_ge(vv, cg)
                s.dma_start(out=x1t[par][:], in_=x1_s[t * 128:(t + 1) * 128, :]).then_inc(ld, 16)
            S.emit("sync", fldx1)
            ldx_now = S.bump("ld", 16)
            g1_need = ffn_state["g1_st"][t]
            def fldg1(scl, t=t, par=par, cg=cg, g1_need=g1_need):
                scl.wait_ge(st2, g1_need)
                if cg > 0:
                    scl.wait_ge(vv, cg)
                scl.dma_start(out=g1r[par][:], in_=g1_s[t * 128:(t + 1) * 128, :]).then_inc(ld, 16)
            S.emit("sync", fldg1)
            sgg_now = S.bump("ld", 16)
            for j in range(2):
                vv_need = s1_vv.get((t - 2, j), 0)
                def fmo(tn, t=t, j=j, ypar=ypar, ldy_now=ldy_now, vv_need=vv_need, wo_lg=wo_lg):
                    tn.wait_ge(ld, ldy_now)
                    tn.wait_ge(lg, wo_lg)
                    if vv_need > 0:
                        tn.wait_ge(vv, vv_need)
                    for k in range(8):
                        mm = tn.matmul(ps[j][:], yh[ypar][:, k * 128:(k + 1) * 128],
                                       wo[k][:, j * 512:(j + 1) * 512],
                                       start=(k == 0), stop=(k == 7))
                    mm.then_inc(pe, 1)
                S.emit("tensor", fmo)
                pe_now = S.bump("pe", 1)
                def fse(v, j=j, par=par, pe_now=pe_now):
                    v.wait_ge(pe, pe_now)
                    v.tensor_tensor(s1t[par][:, j * 512:(j + 1) * 512], ps[j][:],
                                    brep_sb[:, j * 512:(j + 1) * 512], ALU.add).then_inc(vv, 1)
                S.emit("vector", fse)
                s1_vv[(t, j)] = S.bump("vv", 1)
            op_pe[t] = C["pe"]
            sev = s1_vv[(t, 1)]
            stg_ = out_st.get(t - 2, 0)
            vb = C["vv"]
            def fcomb(v, t=t, par=par, sev=sev, stg_=stg_, vb=vb,
                      ldx_now=ldx_now, sgg_now=sgg_now):
                v.wait_ge(vv, sev)
                v.wait_ge(ld, max(ldx_now, sgg_now))
                if stg_ > 0:
                    v.wait_ge(st, stg_)
                v.tensor_tensor(sgt[:], s1t[par][:], g1r[par][:], ALU.mult).then_inc(vv, 1)
                v.wait_ge(vv, vb + 1)
                v.tensor_tensor(ott[par][:], x1t[par][:], sgt[:], ALU.add).then_inc(vv, 1)
            S.emit("vector", fcomb)
            comb_vv[t] = S.bump("vv", 2)
            def fsto(s, t=t, par=par, cvv=comb_vv[t]):
                s.wait_ge(vv, cvv)
                s.dma_start(out=OUT[t * 128:(t + 1) * 128, :], in_=ott[par][:]).then_inc(st, 16)
            pending_store.append(fsto)
            fprev = pending_store.pop(0)
            if fprev is not None:
                S.emit("sync", fprev)
                out_st[t - 1] = S.bump("st", 16)
        S.emit("sync", pending_store.pop(0))
        out_st[31] = S.bump("st", 16)
    emit_scan()

    def emit_finale():
        PB.reset(); PF.reset()
        wo = [PB.alloc(1024) for _ in range(8)]
        we = [PB.alloc(1024) for _ in range(8)]
        wload = PB.alloc(1024)
        dr = PB.alloc(1024)
        brep = PB.alloc(1024)
        yh = [PB.alloc(2048) for _ in range(8)]
        xnh = [PB.alloc(2048) for _ in range(8)]
        s1t = PB.alloc(1024)
        g1t = PB.alloc(1024)
        sgt = PB.alloc(1024)
        x1t = [PF.alloc(1024) for _ in range(2)]
        ott = [PF.alloc(1024) for _ in range(2)]
        def f(g):
            for k in range(8):
                g.dma_start(out=wo[k][:], in_=Wout_in[k * 128:(k + 1) * 128, :]).then_inc(lg, 16)
            g.dma_start(out=brep[:], in_=bout_in[None, :].broadcast_to([128, D])).then_inc(lg, 16)
            g.dma_start(out=dr[0:64, :], in_=dis_in[:]).then_inc(lg, 16)
            g.dma_start(out=dr[64:128, :], in_=dis_in[:]).then_inc(lg, 16)
        S.emit("gpsimd", f)
        w_ld = S.bump("lg", 16 * 11)
        for bi in range(8):
            pe_g = C["pe"]
            def f(g, bi=bi, pe_g=pe_g):
                g.wait_ge(pe, pe_g)
                g.dma_start(out=wload[:], in_=wei_in[bi * 128:(bi + 1) * 128, :]).then_inc(lg, 16)
            S.emit("gpsimd", f)
            lg_now = S.bump("lg", 16)
            for bj in range(8):
                ev_need = C["ev"]
                def fmm(t, bj=bj, lg_now=lg_now, ev_need=ev_need):
                    t.wait_ge(lg, lg_now)
                    t.wait_ge(ev, ev_need)
                    t.transpose(pst[:], wload[:, bj * 128:(bj + 1) * 128], idt[:]).then_inc(pe, 1)
                S.emit("tensor", fmm)
                pe_now = S.bump("pe", 1)
                def fev(scl, bi=bi, bj=bj, pe_now=pe_now):
                    scl.wait_ge(pe, pe_now)
                    scl.copy(out=we[bj][:, bi * 128:(bi + 1) * 128], in_=pst[:]).then_inc(ev, 1)
                S.emit("scalar", fev)
                S.bump("ev", 1)
        wt_ev = C["ev"]
        def f(v, wt_ev=wt_ev, w_ld=w_ld):
            v.wait_ge(ev, wt_ev)
            v.wait_ge(lg, w_ld)
            for k in range(8):
                v.tensor_tensor(we[k][:], we[k][:], dr[:], ALU.add).then_inc(vv, 1 if k == 7 else 0)
        S.emit("vector", f)
        weff_vv = S.bump("vv", 1)
        fout_done = {}
        for half in range(2):
            pe_g = C["pe"]
            def fldh(s, half=half, pe_g=pe_g):
                s.wait_ge(pe, pe_g)
                for k in range(8):
                    s.dma_start(out=yh[k][:],
                                in_=yT_s[k * 128:(k + 1) * 128,
                                         half * 2048:(half + 1) * 2048]).then_inc(ld, 16)
                    s.dma_start(out=xnh[k][:],
                                in_=xn1T_s[k * 128:(k + 1) * 128,
                                           half * 2048:(half + 1) * 2048]).then_inc(ld, 16)
            S.emit("sync", fldh)
            ldh = S.bump("ld", 16 * 16)
            for il in range(16):
                i = half * 16 + il
                par = i % 2
                ld_now = C["ld"]
                x1_guard = fout_done.get(i - 2, 0)
                def fldx(s, i=i, par=par, x1_guard=x1_guard):
                    if x1_guard > 0:
                        s.wait_ge(vv, x1_guard)
                    s.dma_start(out=x1t[par][:], in_=x1_s[i * 128:(i + 1) * 128, :]).then_inc(ld, 16)
                S.emit("sync", fldx)
                ld_now = S.bump("ld", 16)
                for j in range(2):
                    ev_need = C["ev"]
                    vv_need = C["vv"]
                    def fmm(t, il=il, j=j, ldh=ldh, ev_need=ev_need, vv_need=vv_need, weff_vv=weff_vv):
                        t.wait_ge(ld, ldh)
                        t.wait_ge(vv, max(weff_vv, vv_need))
                        t.wait_ge(ev, ev_need)
                        for k in range(8):
                            mm = t.matmul(ps[j][:], yh[k][:, il * 128:(il + 1) * 128],
                                          wo[k][:, j * 512:(j + 1) * 512],
                                          start=(k == 0), stop=(k == 7))
                        mm.then_inc(pe, 1)
                        for k in range(8):
                            mm = t.matmul(ps[2 + j][:], xnh[k][:, il * 128:(il + 1) * 128],
                                          we[k][:, j * 512:(j + 1) * 512],
                                          start=(k == 0), stop=(k == 7))
                        mm.then_inc(pe, 1)
                    S.emit("tensor", fmm)
                    pe_now = S.bump("pe", 2)
                    vv_g2 = C["vv"]
                    def fev(v, j=j, pe_now=pe_now, vv_g2=vv_g2):
                        v.wait_ge(pe, pe_now)
                        if j == 0:
                            v.wait_ge(vv, vv_g2)
                        v.tensor_tensor(s1t[:, j * 512:(j + 1) * 512], ps[j][:],
                                        brep[:, j * 512:(j + 1) * 512], ALU.add).then_inc(vv, 1)
                    S.emit("vector", fev)
                    S.bump("vv", 1)
                    vv_g3 = C["vv"]
                    def fg(scl, j=j, pe_now=pe_now, vv_g3=vv_g3):
                        scl.wait_ge(pe, pe_now)
                        if j == 0:
                            scl.wait_ge(vv, vv_g3)
                        scl.activation(out=g1t[:, j * 512:(j + 1) * 512], in_=ps[2 + j][:],
                                       func=AF.Silu).then_inc(ev, 1)
                    S.emit("scalar", fg)
                    S.bump("ev", 1)
                mm_done_vv = C["vv"]; mm_done_ev = C["ev"]
                st_g = C["st"] - 16 if i >= 2 else 0
                vb2 = C["vv"]
                def fout(v, par=par, ld_now=ld_now, mm_done_vv=mm_done_vv,
                         mm_done_ev=mm_done_ev, st_g=st_g, vb2=vb2):
                    v.wait_ge(vv, mm_done_vv)
                    v.wait_ge(ev, mm_done_ev)
                    v.wait_ge(ld, ld_now)
                    if st_g > 0:
                        v.wait_ge(st, st_g)
                    v.tensor_tensor(sgt[:], s1t[:], g1t[:], ALU.mult).then_inc(vv, 1)
                    v.wait_ge(vv, vb2 + 1)
                    v.tensor_tensor(ott[par][:], x1t[par][:], sgt[:], ALU.add).then_inc(vv, 1)
                S.emit("vector", fout)
                vv_out = S.bump("vv", 2)
                fout_done[i] = vv_out
                def fst(s, i=i, par=par, vv_out=vv_out):
                    s.wait_ge(vv, vv_out)
                    s.dma_start(out=OUT[i * 128:(i + 1) * 128, :], in_=ott[par][:]).then_inc(st, 16)
                S.emit("sync", fst)
                S.bump("st", 16)


    final_st = C["st"]
    def f(s, final_st=final_st):
        s.wait_ge(st, final_st)
    S.emit("sync", f)

    block_cm = p.nc.Block(); block = block_cm.__enter__()
    S.install(block)
    block_cm.__exit__(None, None, None)
    return p


def kernel_fused(**inputs):
    ident = CONST["IDENT"]
    x = np.ascontiguousarray(np.asarray(inputs["L0_attn_x"], dtype=np.float32))
    nco = x.shape[0]
    p = build_mega()
    filt = np.ascontiguousarray(np.asarray(inputs["L0_attn_filter_time"],
                                           dtype=np.float32)[:, 0])
    tr = lambda a: np.ascontiguousarray(np.asarray(a, dtype=np.float32).T)
    base = {"proj_w": inputs["L0_attn_proj_w"], "proj_b": inputs["L0_attn_proj_b"],
            "filt": filt,
            "W_rT": tr(inputs["L0_ffn_W_r"]), "W_kT": tr(inputs["L0_ffn_W_k"]),
            "W_vT": tr(inputs["L0_ffn_W_v"]), "mu_r": inputs["L0_ffn_mu_r"],
            "mu_k": inputs["L0_ffn_mu_k"], "A_log": inputs["L1_attn_A_log"],
            "Dp": inputs["L1_attn_Dp"], "W_xproj": inputs["L1_attn_W_xproj"],
            "b_xproj": inputs["L1_attn_b_xproj"], "W_out": inputs["L1_attn_W_out"],
            "b_out": inputs["L1_attn_b_out"], "weightT": tr(inputs["L1_ffn_weight"]),
            "disha": inputs["L1_ffn_disha"], "IDENT": ident,
            **{k: CONST[k] for k in ["F128RE", "F128IM", "S2FWD", "S2FILT",
                                     "S1INV", "G2RE", "G2NIM", "SIGNONE",
                                     "BFLY1", "BFLY2", "BFLYH1", "BFLYH2"]}}
    maps = [{"x": x[c], **base} for c in range(nco)]
    res = run_launch(p, maps)
    return np.stack([np.asarray(res.results[c]["out"], dtype=np.float32)
                     for c in range(nco)])


def kernel(**inputs):
    return kernel_fused(**inputs)
